# version 01
# baseline (speedup 1.0000x reference)
import os
import numpy as np
import ml_dtypes
import concourse.bass as bass
import concourse.mybir as mybir
from concourse.bass_utils import run_bass_kernel_spmd
from concourse.bass_isa import ReduceOp
from contextlib import ExitStack

F32 = mybir.dt.float32
F16 = mybir.dt.float16
F8 = mybir.dt.float8e4
AF = mybir.ActivationFunctionType
ALU = mybir.AluOpType
AX = mybir.AxisListType

ENGINES = ("tensor", "vector", "scalar", "gpsimd", "sync")
D = 1024
H = 16
FF = 4096
DI = 2048
EPS = 1e-5
MAGIC = float(1.5 * 2.0 ** 23)
GROUPS = [[0, 1], [2, 3], [4, 5], [6, 7]]


_COLLAPSE_PREFIX = ("pb",)
_COLLAPSE_NAMES = {"scr", "scr3", "fscr", "wres", "w8s", "xq8", "hqb", "lra_s",
                   "qmzv", "qbv", "w1v", "rqv", "szv", "egv", "mkv", "xs1v",
                   "afc_s", "apr_s", "masks", "lncol", "convw", "halos"}


def _norm(n):
    s = str(n)
    head, _, tail = s.rpartition("_")
    return head if head and tail.isdigit() else s


def _key(x):
    if isinstance(x, tuple):
        if x and not isinstance(x[0], (str, int, tuple)):
            n0 = getattr(x[0], "name", None)
            if n0 is not None:
                b = _norm(n0)
                if b in _COLLAPSE_NAMES or b.startswith(_COLLAPSE_PREFIX):
                    return b
        return tuple(_key(e) for e in x)
    if isinstance(x, (str, int)):
        return x
    n = getattr(x, "name", None)
    if n is None:
        return id(x)
    b = _norm(n)
    return b if (b in _COLLAPSE_NAMES or b.startswith(_COLLAPSE_PREFIX)) else n


class _Op:
    __slots__ = ("eng", "fn", "reads", "writes", "kind", "sem_key", "idx",
                 "waits", "need_inc", "drain_before", "tag")

    def __init__(self, eng, fn, reads, writes, kind, sem_key, tag=""):
        self.eng, self.fn, self.reads, self.writes = eng, fn, reads, writes
        self.kind, self.sem_key, self.tag = kind, sem_key, tag
        self.idx = -1
        self.waits = []
        self.need_inc = False
        self.drain_before = False


class Prog:
    def __init__(self, nc):
        self.nc = nc
        self.ops = []
        self.dma_sem_of_buf = {}
        self.n_dma_sems = 0

    def add(self, eng, fn, reads=(), writes=(), kind="c", sem_key=None, tag=""):
        op = _Op(eng, fn, tuple(_key(r) for r in reads),
                 tuple(_key(w) for w in writes), kind, sem_key, tag)
        op.idx = len(self.ops)
        self.ops.append(op)
        return op

    def c(self, eng, fn, reads=(), writes=(), tag=""):
        return self.add(eng, fn, reads, writes, "c", tag=tag)

    def dma(self, queue, fn, reads=(), writes=(), key=None, tag=""):
        if key is None:
            assert writes
            key = writes[0]
        key = _key(key)
        if key not in self.dma_sem_of_buf:
            self.dma_sem_of_buf[key] = f"ds{self.n_dma_sems}"
            self.n_dma_sems += 1
        return self.add(queue, fn, reads, writes, "d", self.dma_sem_of_buf[key], tag=tag)

    def cc(self, fn, reads=(), writes=(), tag=""):
        return self.add("gpsimd", fn, reads, writes, "cc", tag=tag)

    def fence(self, reads, writes, tag="fence"):
        self.c("vector", lambda h: h.memset(self.dummy, 0.0),
               reads=tuple(reads), writes=tuple(writes) + ("nopd",), tag=tag)

    def final_wait(self, queue, bufs):
        self.c(queue, lambda h: h.memset(self.dummy, 0.0),
               reads=tuple(bufs), writes=("nopd",), tag="final")

    def schedule(self):
        last_writer = {}
        readers_since = {}
        edges = {}
        for op in self.ops:
            deps = set()
            for r in op.reads:
                w = last_writer.get(r)
                if w is not None:
                    deps.add(w)
            for w_ in op.writes:
                pw = last_writer.get(w_)
                if pw is not None:
                    deps.add(pw)
                for rd in readers_since.get(w_, ()):
                    deps.add(rd)
            deps.discard(op)
            pl = [d for d in deps
                  if not (op.kind == "d" and d.kind == "d"
                          and d.sem_key == op.sem_key and d.eng == op.eng)]
            if pl:
                edges[op.idx] = pl
            for w_ in op.writes:
                last_writer[w_] = op
                readers_since[w_] = []
            for r in op.reads:
                readers_since.setdefault(r, []).append(op)
        for ci, pl in edges.items():
            c = self.ops[ci]
            for pp in pl:
                if pp.kind in ("d", "cc"):
                    continue
                if pp.eng == c.eng and c.kind == "c":
                    continue
                pp.need_inc = True
        inc_no = {e: 0 for e in ENGINES}
        inc_value = {}
        dval = {}
        for op in self.ops:
            if op.kind == "c" and op.need_inc:
                inc_no[op.eng] += 1
                inc_value[op.idx] = inc_no[op.eng]
            elif op.kind == "d":
                dval[op.idx] = dval.get(op.sem_key, 0) + 16
                dval[op.sem_key] = dval[op.idx]
            elif op.kind == "cc":
                dval[op.idx] = dval.get("__cc__", 0) + 1
                dval["__cc__"] = dval[op.idx]
        wm = {}
        pos = {}
        cnt = {e: 0 for e in ENGINES}
        for op in self.ops:
            pos[op.idx] = cnt[op.eng]
            cnt[op.eng] += 1
        drained = {e: -1 for e in ENGINES}
        for op in self.ops:
            for pp in edges.get(op.idx, ()):
                if pp.kind == "d":
                    k = ("dma", pp.sem_key)
                    v = dval[pp.idx]
                    if wm.get((op.eng,) + k, -1) < v:
                        op.waits.append((k, v))
                        wm[(op.eng,) + k] = v
                elif pp.kind == "cc":
                    v = dval[pp.idx]
                    if wm.get((op.eng, "cc"), -1) < v:
                        op.waits.append((("cc", "__cc__"), v))
                        wm[(op.eng, "cc")] = v
                elif pp.eng == op.eng and op.kind == "c":
                    if pp.eng != "tensor" and pos[pp.idx] > drained[pp.eng]:
                        op.drain_before = True
                else:
                    v = inc_value[pp.idx]
                    if wm.get((op.eng, "eng", pp.eng), -1) < v:
                        op.waits.append((("eng", pp.eng), v))
                        wm[(op.eng, "eng", pp.eng)] = v
            if op.drain_before:
                drained[op.eng] = pos[op.idx] - 1
            if len(op.waits) > 1:
                agg = {}
                for (k, v) in op.waits:
                    agg[k] = max(agg.get(k, 0), v)
                op.waits = list(agg.items())
        return inc_no

    def emit(self, stack):
        nc = self.nc
        self.schedule()
        if os.environ.get("PROG_DEBUG"):
            for k, name in self.dma_sem_of_buf.items():
                print("SEM", name, k)
        sems = {}
        for e in ENGINES:
            sems[("eng", e)] = stack.enter_context(nc.semaphore(f"pg_{e}"))
        for k, name in self.dma_sem_of_buf.items():
            sems[("dma", name)] = stack.enter_context(nc.semaphore(name))
        sems[("cc", "__cc__")] = stack.enter_context(nc.semaphore("ccsem"))
        by_eng = {e: [] for e in ENGINES}
        for op in self.ops:
            by_eng[op.eng].append(op)
        block = stack.enter_context(nc.Block())

        def runner(ename):
            def _run(hh):
                mysem = sems[("eng", ename)]
                for op in by_eng[ename]:
                    for (skey, val) in op.waits:
                        hh.wait_ge(sems[skey], val)
                    if op.drain_before:
                        hh.drain()
                    inst = op.fn(hh)
                    if op.kind == "d":
                        inst.then_inc(sems[("dma", op.sem_key)], 16)
                    elif op.kind == "cc":
                        inst.then_inc(sems[("cc", "__cc__")], 1)
                    elif op.need_inc:
                        inst.then_inc(mysem, 1)
            return _run

        for e in ENGINES:
            if by_eng[e]:
                getattr(block, e)(runner(e))


def bc_ap(handle, row, width):
    return bass.AP(handle, row * width, [[1, 1], [0, 128], [1, width]])


def hpk(g):
    return ((g // 2) % 2) * 64, (g % 2) * 256 + (g // 4) * 64



def build_program(T):
    NCH = T // 128
    TS = min(512, T)
    NT4 = T // TS
    TB = T // 4
    TSB = min(512, TB)
    NTB = TB // TSB
    NLVL = 7

    nc = bass.Bass("TRN2", target_bir_lowering=False)

    def din(name, shape, dt=F32):
        return nc.dram_tensor(name, shape, dt, kind="ExternalInput").ap()

    x_in = din("x_in", [D, T])
    role_in = din("role_in", [1, 1])
    trilm = din("trilm", [128, 128], F16)
    trium = din("trium", [128, 128], F16)
    mask_in = din("mask_in", [D, T], F16)
    lr_in = din("lr_in", [1, H])
    wqT = din("wqT", [D, D]); wkT = din("wkT", [D, D]); wvT = din("wvT", [D, D])
    woT = din("woT", [D, D])
    ln1w = din("ln1w", [1, D]); ln1b = din("ln1b", [1, D])
    ln2w = din("ln2w", [1, D]); ln2b = din("ln2b", [1, D])
    w1T = din("w1T", [D, FF]); b1v = din("b1v", [1, FF])
    w2T = din("w2T", [FF, D]); b2v = din("b2v", [1, D])
    WfcT = din("WfcT", [D, FF]); bfcv = din("bfcv", [1, FF])
    AfcT = din("AfcT", [D, 16]); BfcT = din("BfcT", [16, FF])
    WprojT = din("WprojT", [FF, D]); bprv = din("bprv", [1, D])
    AprojT = din("AprojT", [FF, 16]); BprojT = din("BprojT", [16, D])
    WiT = din("WiT", [D, 2 * DI])
    Wcv = din("Wcv", [DI, 4])
    bconv = din("bconv", [1, DI])
    WdtT = din("WdtT", [DI, DI]); bdtv = din("bdtv", [1, DI])
    WBT = din("WBT", [DI, DI]); WCT = din("WCT", [DI, DI])
    WoT1 = din("WoT1", [DI, D])
    A_log = din("A_log", [1, DI])
    w1fT = din("w1fT", [D, FF]); b1f = din("b1f", [1, FF])
    w2fT = din("w2fT", [FF, D]); b2f = din("b2f", [1, D])
    out_t = nc.dram_tensor("out", [D, T], F32, kind="ExternalOutput").ap()

    cc_s_in = nc.dram_tensor("cc_s_in", [128, 512], F32)
    cc_s_out = nc.dram_tensor("cc_s_out", [256, 512], F32)
    cc_e_in = nc.dram_tensor("cc_e_in", [4, 2 * DI], F32)
    cc_e_out = nc.dram_tensor("cc_e_out", [8, 2 * DI], F32)
    cc_h_in = nc.dram_tensor("cc_h_in", [128, 16], F32)
    cc_h_out = nc.dram_tensor("cc_h_out", [256, 16], F32)
    qmz_d = nc.dram_tensor("qmz_d", [128, 8 * NCH * 256], F16)
    s0_d = nc.dram_tensor("s0_d", [D, T], F16)
    s1_d = nc.dram_tensor("s1_d", [D, T], F16)
    y0_d = nc.dram_tensor("y0_d", [128, 16 * T], F16)
    gc_d = nc.dram_tensor("gc_d", [128, 16 * T], F16)
    w116_d = nc.dram_tensor("w116_d", [D, FF], F16)
    w216_d = nc.dram_tensor("w216_d", [FF, D], F16)
    rfc_d = nc.dram_tensor("rfc_d", [D, FF], F16)
    rpr_d = nc.dram_tensor("rpr_d", [FF, D], F16)
    q1_d = nc.dram_tensor("q1_d", [D, FF], F8)
    q2_d = nc.dram_tensor("q2_d", [FF, D], F8)
    wi16_d = nc.dram_tensor("wi16_d", [D, 2 * DI], F16)
    wdt16_d = nc.dram_tensor("wdt16_d", [DI, DI], F16)
    wb16_d = nc.dram_tensor("wb16_d", [DI, DI], F16)
    wc16_d = nc.dram_tensor("wc16_d", [DI, DI], F16)
    wo116_d = nc.dram_tensor("wo116_d", [DI, D], F16)

    p = Prog(nc)
    st_ = ExitStack()

    def sb(name, shape, dt):
        return st_.enter_context(nc.sbuf_tensor(name, shape, dt))

    def vt(name, shape, dt, backing, byte_off):
        h = nc.virtual_sbuf_tensor(name, shape, dt)
        nc._tensor(h.name, h.shape, h.dtype, type="SB")
        mloc = nc.lookup_mloc(h)
        mloc.allocated = not nc.target_bir_lowering
        mloc.addr = nc.lookup_mloc(backing).addr + byte_off
        return h

    with st_:
        A1 = sb("A1", [128, 8 * T], F16)
        A2 = sb("A2", [128, 8 * T], F16)
        A3 = sb("A3", [128, 8 * T], F16)
        A4 = sb("A4", [128, 8 * T], F16)
        ARC = sb("ARC", [128, 4 * T], F16)
        XNb = [sb("XN0", [128, 256], F16), sb("XN1", [128, 256], F16)]
        NTb = [sb("NT0", [128, 128], F16), sb("NT1", [128, 128], F16)]
        KQb = [sb("KQu0", [128, 128], F16), sb("KQu1", [128, 128], F16)]
        rows = sb("rows", [128, 3 * T], F32)
        bca = sb("bca", [128, 2 * T], F16)
        qmax = sb("qmax", [1, 132], F32)
        nopd = sb("nopd", [1, 2], F32)
        p.dummy = nopd[0:1, 0:2]
        onec = sb("onec", [128, 2], F16)
        bcols = sb("bcols", [128, 176], F32)
        lncol = sb("lncol", [128, 32], F32)
        rolec = sb("rolec", [128, 1], F32)
        wres = sb("wres", [128, 4096], F16)
        scr = sb("scr", [128, T], F16)
        scr3 = sb("scr3", [128, max(T // 2, 512)], F16)
        fscr = sb("fscr", [128, 4 * TB], F32)
        zbuf = sb("zbuf", [128, TB], F32)
        qsc = sb("qsc", [128, 12], F32)
        convw = sb("convw", [128, 64], F32)
        accol = sb("accol", [128, 16], F32)
        hcol = sb("hcol", [128, 16], F32)
        hini = sb("hini", [128, 16], F32)
        ucol = sb("ucol", [128, 16], F32)
        uprevc = sb("uprevc", [128, 16], F32)
        xce = sb("xce", [128, 16], F16)
        halos = sb("halos", [128, 64], F16)
        pb = [st_.enter_context(nc.psum_tensor(f"pb{i}", [128, 512], F32)) for i in range(8)]

        xs1v = vt("xs1v", [128, T], F16, A3, 8 * T)
        qmzv = vt("qmzv", [128, 2 * T], F16, A3, 10 * T)
        masks = vt("masks", [128, 256], F16, A3, 14 * T)
        if T >= 2048:
            sch = vt("sch", [128, 512], F32, A3, 8 * T)
            schb = vt("schb", [128, 512], F16, A3, 8 * T + 2048)
            wch = vt("wch", [128, 512], F32, A3, 8 * T + 3072)
            wchb = vt("wchb", [128, 512], F16, A3, 8 * T + 5120)
            sini = vt("sini", [128, 512], F32, A3, 8 * T + 6144)
            qbv = vt("qbv", [128, 2048], F16, A3, 8 * T + 8192)
        else:
            sch = sb("sch", [128, 512], F32)
            schb = sb("schb", [128, 512], F16)
            wch = sb("wch", [128, 512], F32)
            wchb = sb("wchb", [128, 512], F16)
            sini = sb("sini", [128, 512], F32)
            qbv = sb("qbv", [128, 2048], F16)
        w1v = vt("w1v", [128, 8 * TB], F16, rows, 4 * T)
        bcs = vt("bcs", [1, T], F16, rows, 8 * T)
        if 12 * T >= 2 * T + 11520 + 4 * T:
            lra_s = vt("lra_s", [16, T], F16, rows, 0)
            bpr_s = vt("bpr_s", [16, D], F16, rows, 2 * T)
            afc_s = vt("afc_s", [128, 128], F16, rows, 2 * T + 2048)
            apr_s = vt("apr_s", [128, 512], F16, rows, 2 * T + 2304)
            bfc_s = vt("bfc_s", [16, FF], F16, rows, 2 * T + 3328)
        else:
            lra_s = sb("lra_s", [16, T], F16)
            bpr_s = sb("bpr_s", [16, D], F16)
            afc_s = sb("afc_s", [128, 128], F16)
            apr_s = sb("apr_s", [128, 512], F16)
            bfc_s = sb("bfc_s", [16, FF], F16)
        rqv = vt("rqv", [128, T], F16, bca, 0)
        szv = vt("szv", [128, 8 * TB], F16, bca, 0)
        egv = (vt("egv", [4, 4096], F16, bca, 0) if 4 * T >= 8192
               else sb("egv", [4, 4096], F16))
        mkv = vt("mkv", [128, 8 * TB], F16, A2, 0)
        if 10 * T + 8192 <= 16 * T:
            hqb = vt("hqb", [128, 32 * TB], F8, A4, 0)
            w8s = vt("w8s", [128, 8192], F8, A4, 8 * T)
            xq8 = vt("xq8", [128, 8 * TB], F8, A4, 10 * T + 8192 - 2 * T)
        else:
            hqb = sb("hqb", [128, 32 * TB], F8)
            w8s = sb("w8s", [128, 8192], F8)
            xq8 = sb("xq8", [128, 8 * TB], F8)

        R_MAP = {0: (0, 0), 2: (0, 1),
                 1: (32, 0), 13: (32, 1), 3: (32, 2),
                 14: (64, 0), 15: (64, 1), 9: (64, 2),
                 8: (96, 0), 7: (96, 1)}

        def R(i):
            b, j = R_MAP[i]
            return rows[b:b + 1, j * T:(j + 1) * T]

        def dmal(dst, src, buf, queue="gpsimd", reads=(), tag=""):
            p.dma(queue, lambda h, d=dst, s=src: h.dma_start(out=d, in_=s),
                  reads=reads, writes=[buf], key=buf, tag=tag)

        def vop(fn, reads=(), writes=(), tag=""):
            p.c("vector", fn, reads, writes, tag)

        def aop(fn, reads=(), writes=(), tag=""):
            p.c("scalar", fn, reads, writes, tag)

        def top(fn, reads=(), writes=(), tag=""):
            p.c("tensor", fn, reads, writes, tag)

        def bcast(row_i, slot, tag):
            p.c("vector", lambda h, r=row_i: h.tensor_copy(bcs[0:1, 0:T], R(r)),
                reads=[rows], writes=[bcs], tag=tag + "c")
            p.dma("gpsimd", lambda h, s=slot: h.dma_start(
                out=bca[:, s * T:(s + 1) * T],
                in_=bass.AP(bcs, 0, [[1, 1], [0, 128], [1, T]])),
                reads=[bcs], writes=[(bca, slot)], key=(bca, slot), tag=tag)

        def rsqrt_row(v_row, o_row, tag):
            vop(lambda h: h.tensor_scalar_add(R(13), R(v_row), EPS),
                reads=[rows], writes=[rows], tag=tag + "a")
            vop(lambda h: h.reciprocal(R(13), R(13)),
                reads=[rows], writes=[rows], tag=tag + "r")
            aop(lambda h: h.activation(R(o_row), R(13), AF.Sqrt),
                reads=[rows], writes=[rows], tag=tag + "s")

        def negmul_tile(tag):
            vop(lambda h: h.scalar_tensor_tensor(
                bca[:, T:2 * T], bca[:, T:2 * T], -1.0, bca[:, 0:T],
                ALU.mult, ALU.mult),
                reads=[(bca, 0), (bca, 1)], writes=[(bca, 1)], tag=tag)

        def stats_of(srcbuf, src_key, m_row, v_row, bank, tag):
            skeys = list(src_key) if isinstance(src_key, list) else [src_key]
            mb_, mj = R_MAP[m_row]
            for t4 in range(NT4):
                for f in range(8):
                    top(lambda h, f_=f, t_=t4, s=srcbuf: h.matmul(
                        bank[0:1, 0:TS], onec[:, 0:1],
                        s[:, f_ * T + t_ * TS: f_ * T + (t_ + 1) * TS],
                        start=(f_ == 0), stop=(f_ == 7)),
                        reads=skeys + [onec], writes=[(bank, "st")], tag=f"{tag}s{f}.{t4}")
                vop(lambda h, t_=t4: h.tensor_scalar_mul(
                    rows[mb_:mb_ + 1, mj * T + t_ * TS: mj * T + (t_ + 1) * TS],
                    bank[0:1, 0:TS], 1.0 / D),
                    reads=[(bank, "st")], writes=[rows], tag=f"{tag}m{t4}")
            for t4 in range(NT4):
                for f in range(8):
                    aop(lambda h, f_=f, t_=t4, s=srcbuf: h.activation(
                        scr3[:, 0:TS], s[:, f_ * T + t_ * TS: f_ * T + (t_ + 1) * TS],
                        AF.Square),
                        reads=skeys, writes=[scr3], tag=f"{tag}q{f}.{t4}")
                    top(lambda h, f_=f: h.matmul(
                        bank[64:65, 0:TS], onec[:, 0:1], scr3[:, 0:TS],
                        start=(f_ == 0), stop=(f_ == 7)),
                        reads=[scr3, onec], writes=[(bank, "sq")], tag=f"{tag}k{f}.{t4}")
                vop(lambda h, t_=t4: h.tensor_scalar_mul(
                    rows[64:65, T + t_ * TS: T + (t_ + 1) * TS],
                    bank[64:65, 0:TS], 1.0 / D),
                    reads=[(bank, "sq")], writes=[rows], tag=f"{tag}e{t4}")
            vop(lambda h: h.tensor_mul(R(14), R(m_row), R(m_row)),
                reads=[rows], writes=[rows], tag=f"{tag}2")
            vop(lambda h: h.tensor_sub(R(v_row), R(15), R(14)),
                reads=[rows], writes=[rows], tag=f"{tag}v")

        def g3(srch, O_tot, row0, nrow, col0, W):
            return bass.AP(srch, row0 * O_tot + col0,
                           [[O_tot, 128], [128 * O_tot, nrow // 128], [1, W]])

        def d3(dsth, doff, nch, W, dcols):
            return bass.AP(dsth, doff, [[dcols, 128], [W, nch], [1, W]])

        def ldg(dsth, doff, dcols, srch, O_tot, row0, nrow, col0, W, buf,
                queue="gpsimd", tag="", rd=()):
            p.dma(queue, lambda h: h.dma_start(
                out=d3(dsth, doff, nrow // 128, W, dcols),
                in_=g3(srch, O_tot, row0, nrow, col0, W)),
                reads=tuple(rd), writes=[buf], key=buf, tag=tag)


        dmal(masks[:, 0:128], trilm, (masks, 0), tag="tril")
        dmal(masks[:, 128:256], trium, (masks, 1), tag="triu")
        p.dma("gpsimd", lambda h: h.dma_start(out=rolec[:, 0:1], in_=bc_ap(role_in.tensor, 0, 1)),
              writes=[rolec], tag="role")
        vop(lambda h: h.memset(onec[:, 0:2], 1.0), writes=[onec], tag="one")
        vop(lambda h: h.memset(zbuf[:, :], 0.0), writes=[zbuf], tag="zb")
        p.dma("gpsimd", lambda h: h.dma_start(out=bcols[:, 0:H], in_=bc_ap(lr_in.tensor, 0, H)),
              writes=[(bcols, "lr")], tag="lr")
        vop(lambda h: h.tensor_scalar_mul(bcols[:, 16:16 + H], bcols[:, 0:H], 0.01),
            reads=[(bcols, "lr")], writes=[(bcols, "beta")], tag="beta")

        def ldcols(off, src, nm, n=8, dst=None):
            d_ = lncol if dst is None else dst
            p.dma("gpsimd", lambda h, a=d_[:, off:off + n],
                  b=bass.AP(src.tensor, 0, [[1, 128], [128, n]]): h.dma_start(
                      out=a, in_=b, allow_slow_non_contiguous=True),
                  writes=[(d_, off)], key=(d_, off), tag=nm)
        ldcols(0, ln1w, "l1w"); ldcols(8, ln1b, "l1b")
        ldcols(16, ln2w, "l2w"); ldcols(24, ln2b, "l2b")

        xbuf = [scr, xs1v]
        for f in range(8):
            sl = f % 2
            dmal(xbuf[sl][:, 0:T], x_in[f * 128:(f + 1) * 128, :],
                 (xbuf[sl], "x"), tag=f"xs{f}")
            for t4 in range(NT4):
                top(lambda h, f_=f, t_=t4, s_=sl: h.matmul(
                    pb[t_][0:1, 0:TS], onec[:, 0:1],
                    xbuf[s_][:, t_ * TS:(t_ + 1) * TS],
                    start=(f_ == 0), stop=(f_ == 7)),
                    reads=[(xbuf[sl], "x"), onec], writes=[(pb[t4], "st")], tag=f"As{f}.{t4}")
            for t4 in range(NT4):
                aop(lambda h, s_=sl, t_=t4: h.activation(
                    scr3[:, 0:TS], xbuf[s_][:, t_ * TS:(t_ + 1) * TS], AF.Square),
                    reads=[(xbuf[sl], "x")], writes=[scr3], tag=f"Aq{f}.{t4}")
                top(lambda h, f_=f, t_=t4: h.matmul(
                    pb[t_][64:65, 0:TS], onec[:, 0:1],
                    scr3[:, 0:TS],
                    start=(f_ == 0), stop=(f_ == 7)),
                    reads=[scr3, onec], writes=[(pb[t4], "st")], tag=f"Ak{f}.{t4}")
        for t4 in range(NT4):
            vop(lambda h, t_=t4: h.tensor_scalar_mul(
                rows[0:1, t_ * TS:(t_ + 1) * TS], pb[t_][0:1, 0:TS], 1.0 / D),
                reads=[(pb[t4], "st")], writes=[rows], tag=f"Am{t4}")
            vop(lambda h, t_=t4: h.tensor_scalar_mul(
                rows[64:65, T + t_ * TS: T + (t_ + 1) * TS],
                pb[t_][64:65, 0:TS], 1.0 / D),
                reads=[(pb[t4], "st")], writes=[rows], tag=f"Ae{t4}")
        vop(lambda h: h.tensor_mul(R(14), R(0), R(0)),
            reads=[rows], writes=[rows], tag="A2")
        vop(lambda h: h.tensor_sub(R(1), R(15), R(14)),
            reads=[rows], writes=[rows], tag="Av")
        rsqrt_row(1, 2, "rs1")
        vop(lambda h: h.tensor_scalar_add(R(13), R(1), EPS),
            reads=[rows], writes=[rows], tag="r2a")
        vop(lambda h: h.reciprocal(R(13), R(13)),
            reads=[rows], writes=[rows], tag="r2b")
        vop(lambda h: h.tensor_mul(R(13), R(1), R(13)),
            reads=[rows], writes=[rows], tag="r2c")
        vop(lambda h: h.tensor_scalar_add(R(13), R(13), EPS),
            reads=[rows], writes=[rows], tag="r2d")
        vop(lambda h: h.reciprocal(R(13), R(13)),
            reads=[rows], writes=[rows], tag="r2e")
        aop(lambda h: h.activation(R(3), R(13), AF.Sqrt),
            reads=[rows], writes=[rows], tag="r2f")
        bcast(2, 0, "b_r1")
        bcast(3, 1, "b_r2")
        vop(lambda h: h.tensor_mul(bca[:, 0:T], bca[:, 0:T], bca[:, T:2 * T]),
            reads=[(bca, 0), (bca, 1)], writes=[(bca, 0)], tag="al1t")
        bcast(0, 1, "b_m1")
        negmul_tile("ma1t")
        SW3 = max(T // 2, 512)
        for f in range(8):
            sl = f % 2
            dmal(xbuf[sl][:, 0:T], x_in[f * 128:(f + 1) * 128, :],
                 (xbuf[sl], "x"), tag=f"xt{f}")
            for c2 in range(0, T, SW3):
                W3 = min(SW3, T - c2)
                vop(lambda h, s_=sl, c_=c2, W_=W3: h.tensor_mul(
                    scr3[:, 0:W_], xbuf[s_][:, c_:c_ + W_], bca[:, c_:c_ + W_]),
                    reads=[(xbuf[sl], "x"), (bca, 0)], writes=[scr3], tag=f"ta{f}.{c2}")
                vop(lambda h, c_=c2, W_=W3: h.tensor_add(
                    scr3[:, 0:W_], scr3[:, 0:W_], bca[:, T + c_:T + c_ + W_]),
                    reads=[scr3, (bca, 1)], writes=[scr3], tag=f"tb{f}.{c2}")
                vop(lambda h, f_=f, c_=c2, W_=W3: h.tensor_scalar(
                    A1[:, f_ * T + c_:f_ * T + c_ + W_], scr3[:, 0:W_],
                    lncol[:, f_:f_ + 1], lncol[:, 8 + f_:9 + f_], ALU.mult, ALU.add),
                    reads=[scr3, (lncol, 0), (lncol, 8)], writes=[(A1, "t0c")], tag=f"tc{f}.{c2}")
        bcast(2, 0, "b_rs1")
        bcast(0, 1, "b_m1b")
        negmul_tile("mr1t")

        kvt_off = 2 * T
        hal_off = 0
        gal_off = 4 * T

        for hp in range(8):
            sq, sk, sv_ = [((hp * 3 + j) % 4) * 1024 for j in range(3)]
            for (so, wT_ap, nm) in ((sq, wqT, "q"), (sk, wkT, "k"), (sv_, wvT, "v")):
                ldg(wres, so, 4096, wT_ap.tensor, D, 0, D, hp * 128, 128,
                    (wres, "qkv", hp, nm), tag=f"w{nm}{hp}")
            for (so, dsto, nm, bank) in ((sq, 0, "q", pb[0]), (sk, T, "k", pb[1])):
                for t4 in range(NT4):
                    for f in range(8):
                        top(lambda h, f_=f, t_=t4, w_=so, bk=bank: h.matmul(
                            bk[:, 0:TS], wres[:, w_ + f_ * 128: w_ + (f_ + 1) * 128],
                            A1[:, f_ * T + t_ * TS: f_ * T + (t_ + 1) * TS],
                            start=(f_ == 0), stop=(f_ == 7)),
                            reads=[(wres, "qkv", hp, nm), (A1, "t0c")], writes=[(bank, "qk")],
                            tag=f"{nm}{hp}.{t4}.{f}")
                    vop(lambda h, t_=t4, o_=dsto, bk=bank: h.tensor_copy(
                        A3[:, o_ + t_ * TS: o_ + (t_ + 1) * TS], bk[:, 0:TS]),
                        reads=[(bank, "qk")], writes=[(A3, "qk", dsto)], tag=f"{nm}e{hp}.{t4}")
            for c in range(NCH):
                for f in range(8):
                    top(lambda h, c_=c, f_=f, w_=sk: h.matmul(
                        pb[2][:, 0:128],
                        A1[:, f_ * T + c_ * 128: f_ * T + (c_ + 1) * 128],
                        wres[:, w_ + f_ * 128: w_ + (f_ + 1) * 128],
                        start=(f_ == 0), stop=(f_ == 7)),
                        reads=[(wres, "qkv", hp, "k"), (A1, "t0c")], writes=[(pb[2], "kt")],
                        tag=f"kt{hp}.{c}.{f}")
                for f in range(8):
                    top(lambda h, c_=c, f_=f, w_=sv_: h.matmul(
                        pb[2][:, 128:256],
                        A1[:, f_ * T + c_ * 128: f_ * T + (c_ + 1) * 128],
                        wres[:, w_ + f_ * 128: w_ + (f_ + 1) * 128],
                        start=(f_ == 0), stop=(f_ == 7)),
                        reads=[(wres, "qkv", hp, "v"), (A1, "t0c")], writes=[(pb[2], "kt")],
                        tag=f"vt{hp}.{c}.{f}")
                vop(lambda h, c_=c: h.tensor_copy(
                    A3[:, kvt_off + c_ * 256: kvt_off + (c_ + 1) * 256], pb[2][:, 0:256]),
                    reads=[(pb[2], "kt")], writes=[(A3, "kvt")], tag=f"kv{hp}.{c}")
            for c in range(NCH):
                for hh in range(2):
                    hg = hp * 2 + hh
                    Xn, Nt, Kq = XNb[hh], NTb[hh], KQb[hh]
                    pG = pb[3 + hh]
                    pA = pb[5 + hh]
                    pD = pb[7]
                    kfm = A3[hh * 64:hh * 64 + 64, T + c * 128: T + (c + 1) * 128]
                    qfm = A3[hh * 64:hh * 64 + 64, c * 128:(c + 1) * 128]
                    ktm = A3[:, kvt_off + c * 256 + hh * 64: kvt_off + c * 256 + hh * 64 + 64]
                    vtm = A3[:, kvt_off + c * 256 + 128 + hh * 64: kvt_off + c * 256 + 128 + hh * 64 + 64]
                    bco = bcols[:, 16 + hg:17 + hg]
                    pp_, cc_ = hpk(hg)
                    top(lambda h, a=kfm, g=pG: h.matmul(g[:, 0:128], a, a, start=True, stop=True),
                        reads=[(A3, "qk", T)], writes=[(pG, "g")], tag=f"KK{hp}.{c}.{hh}")
                    top(lambda h, a=kfm, b=qfm, g=pG: h.matmul(g[:, 128:256], a, b, start=True, stop=True),
                        reads=[(A3, "qk", T), (A3, "qk", 0)], writes=[(pG, "g")], tag=f"KQ{hp}.{c}.{hh}")
                    vop(lambda h, g=pG, b=bco, X=Xn: h.scalar_tensor_tensor(
                        X[:, 128:256], g[:, 0:128], b, masks[:, 0:128], ALU.mult, ALU.mult),
                        reads=[(pG, "g"), (masks, 0), (bcols, "beta")], writes=[Xn], tag=f"mN{hp}.{c}.{hh}")
                    vop(lambda h, g=pG, b=bco, N=Nt: h.scalar_tensor_tensor(
                        N[:, :], g[:, 0:128], b, masks[:, 128:256], ALU.mult, ALU.mult),
                        reads=[(pG, "g"), (masks, 1), (bcols, "beta")], writes=[Nt], tag=f"mT{hp}.{c}.{hh}")
                    vop(lambda h, g=pG, K=Kq: h.tensor_mul(K[:, :], g[:, 128:256], masks[:, 128:256]),
                        reads=[(pG, "g"), (masks, 1)], writes=[Kq], tag=f"mQ{hp}.{c}.{hh}")
                    vop(lambda h, a=ktm, b=bco, X=Xn: h.tensor_scalar_mul(X[:, 0:64], a, b),
                        reads=[(A3, "kvt"), (bcols, "beta")], writes=[Xn], tag=f"Xa{hp}.{c}.{hh}")
                    vop(lambda h, a=vtm, b=bco, X=Xn: h.tensor_scalar_mul(X[:, 64:128], a, b),
                        reads=[(A3, "kvt"), (bcols, "beta")], writes=[Xn], tag=f"Xb{hp}.{c}.{hh}")
                    for k in range(NLVL):
                        last = (k == NLVL - 1)
                        ncols = 128 if last else 256
                        top(lambda h, n_=ncols, N=Nt, X=Xn, A=pA: h.matmul(
                            A[:, 0:n_], N[:, :], X[:, 0:n_], start=True, stop=True),
                            reads=[Nt, Xn], writes=[(pA, "a")], tag=f"sA{hp}.{c}.{hh}.{k}")
                        if not last:
                            top(lambda h, N=Nt, X=Xn, A=pA: h.matmul(
                                A[:, 256:384], X[:, 128:256], N[:, :], start=True, stop=True),
                                reads=[Xn, Nt], writes=[(pA, "a")], tag=f"sB{hp}.{c}.{hh}.{k}")
                        xo = ALU.subtract if k == 0 else ALU.add
                        vop(lambda h, o=xo, X=Xn, A=pA: h.tensor_tensor(
                            X[:, 0:128], X[:, 0:128], A[:, 0:128], o),
                            reads=[(pA, "a"), Xn], writes=[Xn], tag=f"sX{hp}.{c}.{hh}.{k}")
                        if not last:
                            vop(lambda h, X=Xn, A=pA: h.tensor_copy(X[:, 128:256], A[:, 128:256]),
                                reads=[(pA, "a")], writes=[Xn], tag=f"sN{hp}.{c}.{hh}.{k}")
                            vop(lambda h, N=Nt, A=pA: h.tensor_copy(N[:, :], A[:, 256:384]),
                                reads=[(pA, "a")], writes=[Nt], tag=f"sT{hp}.{c}.{hh}.{k}")
                    top(lambda h, a=ktm, X=Xn, dd=pD: h.matmul(dd[0:64, 0:64], X[:, 0:64], a,
                                                               start=True, stop=True),
                        reads=[Xn, (A3, "kvt")], writes=[(pD, "d")], tag=f"Ht{hp}.{c}.{hh}")
                    top(lambda h, a=ktm, X=Xn, dd=pD: h.matmul(dd[0:64, 64:128], a, X[:, 64:128],
                                                               start=True, stop=True),
                        reads=[Xn, (A3, "kvt")], writes=[(pD, "d")], tag=f"Gt{hp}.{c}.{hh}")
                    vop(lambda h, P_=pp_, C_=cc_, c_=c, dd=pD: h.tensor_copy(
                        A4[P_:P_ + 64, hal_off + c_ * 512 + C_: hal_off + c_ * 512 + C_ + 64],
                        dd[0:64, 0:64]),
                        reads=[(pD, "d")], writes=[(A4, "hal")], tag=f"He{hp}.{c}.{hh}")
                    vop(lambda h, P_=pp_, C_=cc_, c_=c, dd=pD: h.tensor_copy(
                        A4[P_:P_ + 64, gal_off + c_ * 512 + C_: gal_off + c_ * 512 + C_ + 64],
                        dd[0:64, 64:128]),
                        reads=[(pD, "d")], writes=[(A4, "gal")], tag=f"Ge{hp}.{c}.{hh}")
                    top(lambda h, X=Xn, K=Kq, dd=pD: h.matmul(dd[0:64, 128:256], X[:, 0:64], K[:, :],
                                                              start=True, stop=True),
                        reads=[Xn, Kq], writes=[(pD, "d")], tag=f"Z{hp}.{c}.{hh}")
                    vop(lambda h, q=qfm, P_=pp_, g=hg, c_=c, dd=pD: h.tensor_sub(
                        qmzv[P_:P_ + 64, c_ * 256 + (g % 2) * 128: c_ * 256 + (g % 2) * 128 + 128],
                        q, dd[0:64, 128:256]),
                        reads=[(pD, "d"), (A3, "qk", 0)], writes=[(qmzv, "qmz")], tag=f"QZ{hp}.{c}.{hh}")
                    top(lambda h, X=Xn, K=Kq, dd=pD: h.matmul(dd[0:64, 256:384], X[:, 64:128], K[:, :],
                                                              start=True, stop=True),
                        reads=[Xn, Kq], writes=[(pD, "d")], tag=f"Op{hp}.{c}.{hh}")
                    vop(lambda h, g=hg, c_=c, dd=pD: h.tensor_copy(
                        A2[(g % 2) * 64:(g % 2) * 64 + 64,
                           (g // 2) * T + c_ * 128:(g // 2) * T + (c_ + 1) * 128],
                        dd[0:64, 256:384]),
                        reads=[(pD, "d")], writes=[(A2, "oall")], tag=f"Oe{hp}.{c}.{hh}")
            hpp = (hp % 2) * 64
            p.dma("sync", lambda h, hp_=hp, pp2=hpp: h.dma_start(
                out=qmz_d.ap()[pp2:pp2 + 64, hp_ * NCH * 256:(hp_ + 1) * NCH * 256],
                in_=qmzv[pp2:pp2 + 64, 0:NCH * 256]),
                reads=[(qmzv, "qmz")], writes=[("d", "qmz")], key=("d", "qmz"), tag=f"qz{hp}")

        p.fence([("d", "qmz")], [sch, schb, wch, wchb, sini, "xs1v"], tag="fchain")
        vop(lambda h: h.memset(sch[:, :], 0.0), writes=[sch], tag="Sz")
        vop(lambda h: h.memset(schb[:, :], 0.0), writes=[schb], tag="Szb")
        for c in range(NCH):
            vop(lambda h, c_=c: h.tensor_copy(ARC[:, c_ * 512:(c_ + 1) * 512], schb[:, :]),
                reads=[schb], writes=[(ARC, "s0log")], tag=f"sl{c}")
            for g in range(H):
                bk = pb[g % 2]
                pg, cg = hpk(g)
                top(lambda h, g_=g, c_=c, b_=bk, P_=pg, C_=cg: h.matmul(
                    b_[P_:P_ + 64, (g_ // 4) * 64:(g_ // 4) * 64 + 64],
                    A4[P_:P_ + 64, hal_off + c_ * 512 + C_: hal_off + c_ * 512 + C_ + 64],
                    schb[P_:P_ + 64, C_:C_ + 64],
                    start=True, stop=True),
                    reads=[(A4, "hal"), schb], writes=[(bk, "hs")], tag=f"HS{c}.{g}")
            vop(lambda h: h.tensor_sub(sch[:, 0:256], sch[:, 0:256], pb[0][:, 0:256]),
                reads=[(pb[0], "hs"), sch], writes=[sch], tag=f"Su0.{c}")
            vop(lambda h: h.tensor_sub(sch[:, 256:512], sch[:, 256:512], pb[1][:, 0:256]),
                reads=[(pb[1], "hs"), sch], writes=[sch], tag=f"Su1.{c}")
            vop(lambda h, c_=c: h.tensor_add(
                sch[:, :], sch[:, :], A4[:, gal_off + c_ * 512: gal_off + (c_ + 1) * 512]),
                reads=[(A4, "gal"), sch], writes=[sch], tag=f"Sg{c}")
            vop(lambda h: h.tensor_copy(schb[:, :], sch[:, :]), reads=[sch], writes=[schb],
                tag=f"Sc{c}")
        p.dma("sync", lambda h: h.dma_start(out=cc_s_in.ap(), in_=sch[:, :]),
              reads=[sch], writes=[("d", "ccs")], tag="sstg")
        p.cc(lambda h: h.collective_compute(
            "AllGather", ALU.bypass, replica_groups=GROUPS,
            ins=[cc_s_in.ap().opt()], outs=[cc_s_out.ap().opt()]),
            reads=[("d", "ccs")], writes=[("d", "ccso")], tag="sag")
        p.dma("sync", lambda h: h.dma_start(out=sini[:, :], in_=cc_s_out.ap()[0:128, :]),
              reads=[("d", "ccso")], writes=[sini], tag="srcv")
        vop(lambda h: h.tensor_scalar_mul(sini[:, :], sini[:, :], rolec[:, 0:1]),
            reads=[sini, rolec], writes=[sini], tag="smsk")
        vop(lambda h: h.tensor_copy(wch[:, :], sini[:, :]), reads=[sini], writes=[wch], tag="W0")
        vop(lambda h: h.tensor_copy(wchb[:, :], wch[:, :]), reads=[wch], writes=[wchb], tag="W0b")
        for c in range(NCH):
            vop(lambda h, c_=c: h.tensor_add(schb[:, :], ARC[:, c_ * 512:(c_ + 1) * 512], wchb[:, :]),
                reads=[(ARC, "s0log"), wchb], writes=[schb], tag=f"S0c{c}")
            p.dma("gpsimd", lambda h, c_=c: h.dma_start(
                out=bass.AP(qbv, 0, [[2048, 64], [512, 4], [1, 256]]),
                in_=bass.AP(qmz_d.ap().tensor, c_ * 256,
                            [[8 * NCH * 256, 64], [2 * NCH * 256, 4], [1, 256]])),
                reads=[("d", "qmz")], writes=[(qbv, "qb")], key=(qbv, "qb"), tag=f"qr{c}")
            p.dma("gpsimd", lambda h, c_=c: h.dma_start(
                out=bass.AP(qbv, 64 * 2048 + 256, [[2048, 64], [512, 4], [1, 256]]),
                in_=bass.AP(qmz_d.ap().tensor,
                            64 * 8 * NCH * 256 + NCH * 256 + c_ * 256,
                            [[8 * NCH * 256, 64], [2 * NCH * 256, 4], [1, 256]])),
                reads=[("d", "qmz")], writes=[(qbv, "qb")], key=(qbv, "qb"), tag=f"qro{c}")
            for g in range(H):
                bk = pb[2 + g % 2]
                pg, cg = hpk(g)
                top(lambda h, g_=g, b_=bk, P_=pg, C_=cg: h.matmul(
                    b_[P_:P_ + 64, (g_ // 4) * 128:(g_ // 4) * 128 + 128],
                    schb[P_:P_ + 64, C_:C_ + 64],
                    qbv[P_:P_ + 64, (g_ // 2) * 256 + (g_ % 2) * 128: (g_ // 2) * 256 + (g_ % 2) * 128 + 128],
                    start=True, stop=True),
                    reads=[schb, (qbv, "qb")], writes=[(bk, "of")], tag=f"Of{c}.{g}")
            for g in range(H):
                bk = pb[2 + g % 2]
                pg, cg = hpk(g)
                vop(lambda h, g_=g, c_=c, b_=bk, P_=pg: h.tensor_add(
                    A2[(g_ % 2) * 64:(g_ % 2) * 64 + 64,
                       (g_ // 2) * T + c_ * 128:(g_ // 2) * T + (c_ + 1) * 128],
                    A2[(g_ % 2) * 64:(g_ % 2) * 64 + 64,
                       (g_ // 2) * T + c_ * 128:(g_ // 2) * T + (c_ + 1) * 128],
                    b_[P_:P_ + 64, (g_ // 4) * 128:(g_ // 4) * 128 + 128]),
                    reads=[(bk, "of"), (A2, "oall")], writes=[(A2, "oall")], tag=f"Oa{c}.{g}")
            if c < NCH - 1:
                for g in range(H):
                    bk = pb[g % 2]
                    pg, cg = hpk(g)
                    top(lambda h, g_=g, c_=c, b_=bk, P_=pg, C_=cg: h.matmul(
                        b_[P_:P_ + 64, (g_ // 4) * 64:(g_ // 4) * 64 + 64],
                        A4[P_:P_ + 64, hal_off + c_ * 512 + C_: hal_off + c_ * 512 + C_ + 64],
                        wchb[P_:P_ + 64, C_:C_ + 64],
                        start=True, stop=True),
                        reads=[(A4, "hal"), wchb], writes=[(bk, "hw")], tag=f"HW{c}.{g}")
                vop(lambda h: h.tensor_sub(wch[:, 0:256], wch[:, 0:256], pb[0][:, 0:256]),
                    reads=[(pb[0], "hw"), wch], writes=[wch], tag=f"Wu0.{c}")
                vop(lambda h: h.tensor_sub(wch[:, 256:512], wch[:, 256:512], pb[1][:, 0:256]),
                    reads=[(pb[1], "hw"), wch], writes=[wch], tag=f"Wu1.{c}")
                vop(lambda h: h.tensor_copy(wchb[:, :], wch[:, :]), reads=[wch], writes=[wchb],
                    tag=f"Wb{c}")

        p.fence([(A3, "qk", 0), (A3, "qk", T), (A3, "kvt")],
                [(A3, "x"), "xs1v", "qmzv", "masks", sch, schb, wch, wchb, sini,
                 "qbv"], tag="fA3x")
        p.fence([(wres, "qkv", 0, "q")], [wres], tag="fwres")
        for f in range(8):
            dmal(A3[:, f * T:(f + 1) * T], x_in[f * 128:(f + 1) * 128, :], (A3, "x", f), tag=f"xr{f}")
        p.fence([(A1, "t0c")], [(A1, "xnc")], tag="fA1x")
        for f in range(8):
            for c2 in range(0, T, SW3):
                W3 = min(SW3, T - c2)
                vop(lambda h, f_=f, c_=c2, W_=W3: h.tensor_mul(
                    scr3[:, 0:W_], A3[:, f_ * T + c_:f_ * T + c_ + W_], bca[:, c_:c_ + W_]),
                    reads=[(A3, "x", f), (bca, 0)], writes=[scr3], tag=f"xna{f}.{c2}")
                vop(lambda h, f_=f, c_=c2, W_=W3: h.tensor_add(
                    A1[:, f_ * T + c_:f_ * T + c_ + W_], scr3[:, 0:W_], bca[:, T + c_:T + c_ + W_]),
                    reads=[scr3, (bca, 1)], writes=[(A1, "xnc")], tag=f"xnb{f}.{c2}")
        p.fence([(A4, "hal"), (A4, "gal")], [(A4, "u")], tag="fA4u")
        for dg in range(4):
            so = (dg % 2) * 2048
            ldg(wres, so, 4096, woT.tensor, D, 0, D, dg * 256, 256, (wres, "wo", dg), tag=f"lwo{dg}")
            for oi in range(2):
                dt_i = dg * 2 + oi
                for t4 in range(NT4):
                    for f in range(8):
                        top(lambda h, f_=f, t_=t4, o_=oi, s_=so: h.matmul(
                            pb[4][:, 0:TS], wres[:, s_ + f_ * 256 + o_ * 128: s_ + f_ * 256 + (o_ + 1) * 128],
                            A2[:, f_ * T + t_ * TS: f_ * T + (t_ + 1) * TS],
                            start=(f_ == 0), stop=(f_ == 7)),
                            reads=[(wres, "wo", dg), (A2, "oall")], writes=[(pb[4], "wo")],
                            tag=f"wo{dt_i}.{t4}.{f}")
                    vop(lambda h, d_=dt_i, t_=t4: h.tensor_add(
                        A4[:, d_ * T + t_ * TS: d_ * T + (t_ + 1) * TS], pb[4][:, 0:TS],
                        A1[:, d_ * T + t_ * TS: d_ * T + (t_ + 1) * TS]),
                        reads=[(pb[4], "wo"), (A1, "xnc")], writes=[(A4, "u")], tag=f"u{dt_i}.{t4}")
        stats_of(A4, (A4, "u"), 7, 8, pb[0], "su")
        rsqrt_row(8, 9, "al2")
        bcast(9, 0, "b_al2")
        bcast(7, 1, "b_m2")
        negmul_tile("ma2t")
        p.dma("gpsimd", lambda h, a=bcols[:, 32:64], b=bass.AP(b1v.tensor, 0, [[1, 128], [128, 32]]): h.dma_start(out=a, in_=b, allow_slow_non_contiguous=True), writes=[(bcols, "b1")], key=(bcols, "b1"), tag="b1")
        p.dma("gpsimd", lambda h, a=bcols[:, 64:72], b=bass.AP(b2v.tensor, 0, [[1, 128], [128, 8]]): h.dma_start(out=a, in_=b, allow_slow_non_contiguous=True), writes=[(bcols, "b2")], key=(bcols, "b2"), tag="b2")
        def bounce16(srcT, dstD, K_, O_, nn):
            PC = min(O_, T)
            for kk in range(K_ // 128):
                for ob in range(0, O_, PC):
                    dmal(scr[:, 0:PC], srcT[kk * 128:(kk + 1) * 128, ob:ob + PC], (scr, "wc"),
                         tag=f"{nn}l{kk}.{ob}")
                    p.dma("sync", lambda h, k_=kk, o_=ob, D_=dstD, P_=PC: h.dma_start(
                        out=D_.ap()[k_ * 128:(k_ + 1) * 128, o_:o_ + P_], in_=scr[:, 0:P_]),
                        reads=[(scr, "wc")], writes=[("d", nn)], key=("d", nn),
                        tag=f"{nn}s{kk}.{ob}")
        bounce16(w1T, w116_d, D, FF, "w1c")
        bounce16(w2T, w216_d, FF, D, "w2c")
        p.fence([(A2, "oall")], [(A2, "ghm")], tag="fA2g")
        p.fence([], [rows, "w1v"], tag="fw1v")
        for tb in range(4):
            for f in range(8):
                vop(lambda h, f_=f, b_=tb: h.tensor_mul(
                    scr3[:, 0:TB], A4[:, f_ * T + b_ * TB: f_ * T + (b_ + 1) * TB], bca[:, b_ * TB:(b_ + 1) * TB]),
                    reads=[(A4, "u"), (bca, 0)], writes=[scr3], tag=f"wia{tb}.{f}")
                vop(lambda h, f_=f, b_=tb: h.tensor_add(
                    scr3[:, 0:TB], scr3[:, 0:TB], bca[:, T + b_ * TB: T + (b_ + 1) * TB]),
                    reads=[scr3, (bca, 1)], writes=[scr3], tag=f"wib{tb}.{f}")
                vop(lambda h, f_=f: h.tensor_scalar(
                    w1v[:, f_ * TB:(f_ + 1) * TB], scr3[:, 0:TB],
                    lncol[:, 16 + f_:17 + f_], lncol[:, 24 + f_:25 + f_], ALU.mult, ALU.add),
                    reads=[scr3, (lncol, 16), (lncol, 24)], writes=[(w1v, "w1in", tb)], tag=f"wic{tb}.{f}")
            for hgrp in range(16):
                so = (hgrp % 2) * 2048
                ldg(wres, so, 4096, w116_d, FF, 0, D, hgrp * 256, 256,
                    (wres, "w1g", tb, hgrp), tag=f"lw1{tb}.{hgrp}", rd=[("d", "w1c")])
                for ho in range(2):
                    ht = hgrp * 2 + ho
                    for t4 in range(NTB):
                        for f in range(8):
                            top(lambda h, f_=f, o_=ho, t_=t4, s_=so: h.matmul(
                                pb[5][:, 0:TSB],
                                wres[:, s_ + f_ * 256 + o_ * 128: s_ + f_ * 256 + (o_ + 1) * 128],
                                w1v[:, f_ * TB + t_ * TSB: f_ * TB + (t_ + 1) * TSB],
                                start=(f_ == 0), stop=(f_ == 7)),
                                reads=[(wres, "w1g", tb, hgrp), (w1v, "w1in", tb)], writes=[(pb[5], "w1")],
                                tag=f"w1m{tb}.{ht}.{t4}.{f}")
                        aop(lambda h, h_=ht, t_=t4: h.activation(
                            A2[:, h_ * TB + t_ * TSB: h_ * TB + (t_ + 1) * TSB], pb[5][:, 0:TSB],
                            AF.Gelu, bias=bcols[:, 32 + h_:33 + h_]),
                            reads=[(pb[5], "w1"), (bcols, "b1")], writes=[(A2, "ghm")],
                            tag=f"w1g{tb}.{ht}.{t4}")
            for dt_i in range(8):
                for hf in range(2):
                    so = ((dt_i * 2 + hf) % 2) * 2048
                    ldg(wres, so, 4096, w216_d, D, hf * 2048, 2048, dt_i * 128, 128,
                        (wres, "w2g", tb, dt_i, hf), tag=f"lw2{tb}.{dt_i}.{hf}", rd=[("d", "w2c")])
                    for t4 in range(NTB):
                        for hc in range(16):
                            top(lambda h, hc_=hc, t_=t4, h2=hf, s_=so: h.matmul(
                                pb[6][:, 0:TSB],
                                wres[:, s_ + hc_ * 128: s_ + (hc_ + 1) * 128],
                                A2[:, (h2 * 16 + hc_) * TB + t_ * TSB: (h2 * 16 + hc_) * TB + (t_ + 1) * TSB],
                                start=(h2 == 0 and hc_ == 0), stop=(h2 == 1 and hc_ == 15)),
                                reads=[(wres, "w2g", tb, dt_i, hf), (A2, "ghm")], writes=[(pb[6], "w2")],
                                tag=f"w2m{tb}.{dt_i}.{hf}.{t4}.{hc}")
                if True:
                    for t4 in range(NTB):
                        aop(lambda h, d_=dt_i, t_=t4: h.activation(
                            scr3[:, 0:TSB], pb[6][:, 0:TSB], AF.Identity,
                            bias=bcols[:, 64 + d_:65 + d_]),
                            reads=[(pb[6], "w2"), (bcols, "b2")], writes=[scr3], tag=f"s0a{tb}.{dt_i}.{t4}")
                        vop(lambda h, d_=dt_i, t_=t4, b_=tb: h.tensor_add(
                            scr[:, 0:TSB], scr3[:, 0:TSB],
                            A4[:, d_ * T + b_ * TB + t_ * TSB: d_ * T + b_ * TB + (t_ + 1) * TSB]),
                            reads=[scr3, (A4, "u")], writes=[(scr, "s0")], tag=f"s0b{tb}.{dt_i}.{t4}")
                        p.dma("sync", lambda h, d_=dt_i, t_=t4, b_=tb: h.dma_start(
                            out=s0_d.ap()[d_ * 128:(d_ + 1) * 128,
                                          b_ * TB + t_ * TSB: b_ * TB + (t_ + 1) * TSB],
                            in_=scr[:, 0:TSB]),
                            reads=[(scr, "s0")], writes=[("d", "s0")], key=("d", "s0"),
                            tag=f"s0d{tb}.{dt_i}.{t4}")

        def quant_prep(wT_ap, K_, O_, scol, dstD, nn):
            PC = min(O_, T)
            vop(lambda h: h.memset(qsc[:, scol + 2:scol + 3], 0.0), writes=[(qsc, scol, "mx")],
                tag=nn + "z")
            for kk in range(K_ // 128):
                for ob in range(0, O_, PC):
                    dmal(scr[:, 0:PC], wT_ap[kk * 128:(kk + 1) * 128, ob:ob + PC], (scr, "wc"),
                         tag=f"{nn}a{kk}.{ob}")
                    vop(lambda h, P_=PC: h.reduce_max(qsc[:, scol + 3:scol + 4], scr[:, 0:P_],
                                                      axis=AX.X, apply_absolute_value=True),
                        reads=[(scr, "wc")], writes=[(qsc, scol, "t")], tag=f"{nn}r{kk}.{ob}")
                    vop(lambda h: h.tensor_max(qsc[:, scol + 2:scol + 3], qsc[:, scol + 2:scol + 3],
                                               qsc[:, scol + 3:scol + 4]),
                        reads=[(qsc, scol, "t"), (qsc, scol, "mx")], writes=[(qsc, scol, "mx")],
                        tag=f"{nn}m{kk}.{ob}")
            p.dma("gpsimd", lambda h: h.dma_start(
                out=qmax[0:1, 4:132],
                in_=bass.AP(qsc, scol + 2, [[12, 128], [1, 1]]),
                allow_slow_non_contiguous=True),
                reads=[(qsc, scol, "mx")], writes=[qmax], key=qmax, tag=f"{nn}tr")
            vop(lambda h: h.reduce_max(qmax[0:1, 0:1], qmax[0:1, 4:132], axis=AX.X),
                reads=[qmax], writes=[qmax], tag=f"{nn}rm")
            p.dma("gpsimd", lambda h: h.dma_start(
                out=qsc[:, scol + 4:scol + 5],
                in_=bass.AP(qmax, 0, [[1, 1], [0, 128], [1, 1]])),
                reads=[qmax], writes=[(qsc, scol, "gm")], key=(qsc, scol, "gm"),
                tag=f"{nn}ar")
            vop(lambda h: h.tensor_scalar_mul(qsc[:, scol:scol + 1], qsc[:, scol + 4:scol + 5],
                                              1.0 / 127.0),
                reads=[(qsc, scol, "gm")], writes=[(qsc, scol, "s")], tag=f"{nn}s")
            vop(lambda h: h.reciprocal(qsc[:, scol + 1:scol + 2], qsc[:, scol:scol + 1]),
                reads=[(qsc, scol, "s")], writes=[(qsc, scol, "rs")], tag=f"{nn}rs")
            for kk in range(K_ // 128):
                for ob in range(0, O_, PC):
                    dmal(scr[:, 0:PC], wT_ap[kk * 128:(kk + 1) * 128, ob:ob + PC], (scr, "wc"),
                         tag=f"{nn}b{kk}.{ob}")
                    S3 = max(T // 2, 512)
                    for o2 in range(0, PC, S3):
                        W2 = min(S3, PC - o2)
                        aop(lambda h, o_=o2, W_=W2: h.activation(
                            scr3[:, 0:W_], scr[:, o_:o_ + W_], AF.Identity,
                            scale=qsc[:, scol + 1:scol + 2]),
                            reads=[(scr, "wc"), (qsc, scol, "rs")], writes=[scr3], tag=f"{nn}q{kk}.{ob}.{o2}")
                        vop(lambda h, o_=o2, W_=W2: h.tensor_scalar(
                            rqv[:, o_:o_ + W_], scr3[:, 0:W_], MAGIC, MAGIC, ALU.add, ALU.subtract),
                            reads=[scr3], writes=[(rqv, "rq")], tag=f"{nn}w{kk}.{ob}.{o2}")
                    p.dma("sync", lambda h, k_=kk, o_=ob, D_=dstD, P_=PC: h.dma_start(
                        out=D_.ap()[k_ * 128:(k_ + 1) * 128, o_:o_ + P_], in_=rqv[:, 0:P_]),
                        reads=[(rqv, "rq")], writes=[("d", nn)], key=("d", nn),
                        tag=f"{nn}o{kk}.{ob}")
        p.fence([], [(bca, 0), (bca, 1), "rqv"], tag="frq")
        quant_prep(WfcT, D, FF, 0, rfc_d, "qfc")
        quant_prep(WprojT, FF, D, 4, rpr_d, "qpr")
        p.fence([], ["w1v", rows, "afc_s", "apr_s", bfc_s, bpr_s, "lra_s"],
                tag="fEsm")
        afc_v = afc_s[:, :]
        apr_v = apr_s[:, :]
        bfcs_v = bfc_s[:, :]
        bprs_v = bpr_s[:, :]
        lra_v = lra_s[:, :]
        for f in range(8):
            dmal(afc_v[:, f * 16:(f + 1) * 16], AfcT[f * 128:(f + 1) * 128, :], (afc_s, f),
                 tag=f"afc{f}")
        for hc in range(32):
            dmal(apr_v[:, hc * 16:(hc + 1) * 16], AprojT[hc * 128:(hc + 1) * 128, :],
                 (apr_s, hc), tag=f"apr{hc}")
        dmal(bfcs_v, BfcT, bfc_s, tag="bfc")
        dmal(bprs_v, BprojT, bpr_s, tag="bpr")
        p.dma("gpsimd", lambda h, a=bcols[:, 72:104], b=bass.AP(bfcv.tensor, 0, [[1, 128], [128, 32]]): h.dma_start(out=a, in_=b, allow_slow_non_contiguous=True), writes=[(bcols, "bfc")], key=(bcols, "bfc"), tag="bfcc")
        p.dma("gpsimd", lambda h, a=bcols[:, 104:112], b=bass.AP(bprv.tensor, 0, [[1, 128], [128, 8]]): h.dma_start(out=a, in_=b, allow_slow_non_contiguous=True), writes=[(bcols, "bpr")], key=(bcols, "bpr"), tag="bprc")
        for t4 in range(NT4):
            for f in range(8):
                top(lambda h, f_=f, t_=t4: h.matmul(
                    pb[0][0:16, 0:TS], afc_v[:, f_ * 16:(f_ + 1) * 16],
                    A1[:, f_ * T + t_ * TS: f_ * T + (t_ + 1) * TS],
                    start=(f_ == 0), stop=(f_ == 7)),
                    reads=[(afc_s, f), (A1, "xnc")], writes=[(pb[0], "lm")],
                    tag=f"lm{t4}.{f}")
            vop(lambda h, t_=t4: h.tensor_scalar_mul(lra_v[:, t_ * TS:(t_ + 1) * TS],
                                                     pb[0][0:16, 0:TS], 2.0),
                reads=[(pb[0], "lm")], writes=[(lra_s, "a")], tag=f"lme{t4}")
        p.fence([(A2, "ghm")], [(A2, "gf")], tag="fA2f")
        for tb in range(4):
            for hgrp in range(16):
                so = (hgrp % 2) * 2048
                ldg(wres, so, 4096, rfc_d, FF, 0, D, hgrp * 256, 256,
                    (wres, "rfc", tb, hgrp), tag=f"lrf{tb}.{hgrp}", rd=[("d", "qfc")])
                for ho in range(2):
                    ht = hgrp * 2 + ho
                    for t4 in range(NTB):
                        for f in range(8):
                            top(lambda h, f_=f, o_=ho, t_=t4, b_=tb, s_=so: h.matmul(
                                pb[5][:, 0:TSB],
                                wres[:, s_ + f_ * 256 + o_ * 128: s_ + f_ * 256 + (o_ + 1) * 128],
                                A1[:, f_ * T + b_ * TB + t_ * TSB: f_ * T + b_ * TB + (t_ + 1) * TSB],
                                start=(f_ == 0), stop=(f_ == 7)),
                                reads=[(wres, "rfc", tb, hgrp), (A1, "xnc")], writes=[(pb[5], "fc")],
                                tag=f"fcm{tb}.{ht}.{t4}.{f}")
                        top(lambda h, h_=ht, t_=t4, b_=tb: h.matmul(
                            pb[3][:, 0:TSB], bfcs_v[:, h_ * 128:(h_ + 1) * 128],
                            lra_v[:, b_ * TB + t_ * TSB: b_ * TB + (t_ + 1) * TSB],
                            start=True, stop=True),
                            reads=[bfc_s, (lra_s, "a")], writes=[(pb[3], "fl")],
                            tag=f"fcl{tb}.{ht}.{t4}")
                        aop(lambda h, t_=t4: h.activation(
                            scr3[:, 0:TSB], pb[5][:, 0:TSB], AF.Identity, scale=qsc[:, 0:1]),
                            reads=[(pb[5], "fc"), (qsc, 0, "s")], writes=[scr3], tag=f"fcs{tb}.{ht}.{t4}")
                        vop(lambda h, t_=t4: h.tensor_add(scr3[:, 0:TSB], scr3[:, 0:TSB], pb[3][:, 0:TSB]),
                            reads=[scr3, (pb[3], "fl")], writes=[scr3], tag=f"fca{tb}.{ht}.{t4}")
                        aop(lambda h, h_=ht, t_=t4: h.activation(
                            A2[:, h_ * TB + t_ * TSB: h_ * TB + (t_ + 1) * TSB], scr3[:, 0:TSB],
                            AF.Gelu, bias=bcols[:, 72 + h_:73 + h_]),
                            reads=[scr3, (bcols, "bfc")], writes=[(A2, "gf")],
                            tag=f"fcg{tb}.{ht}.{t4}")
            for t4 in range(NTB):
                for hc in range(32):
                    top(lambda h, hc_=hc, t_=t4: h.matmul(
                        pb[0][0:16, 0:TSB], apr_v[:, hc_ * 16:(hc_ + 1) * 16],
                        A2[:, hc_ * TB + t_ * TSB: hc_ * TB + (t_ + 1) * TSB],
                        start=(hc_ == 0), stop=(hc_ == 31)),
                        reads=[(apr_s, hc), (A2, "gf")], writes=[(pb[0], "lm2")],
                        tag=f"l2{tb}.{t4}.{hc}")
                vop(lambda h, t_=t4, b_=tb: h.tensor_scalar_mul(
                    lra_v[:, b_ * TB + t_ * TSB: b_ * TB + (t_ + 1) * TSB], pb[0][0:16, 0:TSB], 2.0),
                    reads=[(pb[0], "lm2")], writes=[(lra_s, "b", tb)], tag=f"l2e{tb}.{t4}")
            for dgp in range(4):
                p.dma("gpsimd", lambda h, b_=tb, g_=dgp: h.dma_start(
                    out=d3(scr, 0, 2, TB, T),
                    in_=g3(s0_d, T, g_ * 256, 256, b_ * TB, TB)),
                    reads=[("d", "s0")], writes=[(scr, "s0b")], key=(scr, "s0b"), tag=f"s0r{tb}.{dgp}")
                for do in range(2):
                    dt_i = dgp * 2 + do
                    for hf in range(2):
                        so = ((dt_i * 2 + hf) % 2) * 2048
                        ldg(wres, so, 4096, rpr_d, D, hf * 2048, 2048, dt_i * 128, 128,
                            (wres, "rpr", tb, dt_i, hf), tag=f"lrp{tb}.{dt_i}.{hf}", rd=[("d", "qpr")])
                        for t4 in range(NTB):
                            for hc in range(16):
                                top(lambda h, hc_=hc, t_=t4, h2=hf, s_=so: h.matmul(
                                    pb[6][:, 0:TSB],
                                    wres[:, s_ + hc_ * 128: s_ + (hc_ + 1) * 128],
                                    A2[:, (h2 * 16 + hc_) * TB + t_ * TSB: (h2 * 16 + hc_) * TB + (t_ + 1) * TSB],
                                    start=(h2 == 0 and hc_ == 0), stop=(h2 == 1 and hc_ == 15)),
                                    reads=[(wres, "rpr", tb, dt_i, hf), (A2, "gf")], writes=[(pb[6], "pr")],
                                    tag=f"prm{tb}.{dt_i}.{hf}.{t4}.{hc}")
                    for t4 in range(NTB):
                        top(lambda h, d_=dt_i, t_=t4, b_=tb: h.matmul(
                            pb[4][:, 0:TSB], bprs_v[:, d_ * 128:(d_ + 1) * 128],
                            lra_v[:, b_ * TB + t_ * TSB: b_ * TB + (t_ + 1) * TSB],
                            start=True, stop=True),
                            reads=[bpr_s, (lra_s, "b", tb)], writes=[(pb[4], "pl")],
                            tag=f"prl{tb}.{dt_i}.{t4}")
                        aop(lambda h, t_=t4: h.activation(
                            scr3[:, 0:TSB], pb[6][:, 0:TSB], AF.Identity, scale=qsc[:, 4:5]),
                            reads=[(pb[6], "pr"), (qsc, 4, "s")], writes=[scr3], tag=f"prs{tb}.{dt_i}.{t4}")
                        vop(lambda h, t_=t4: h.tensor_add(scr3[:, 0:TSB], scr3[:, 0:TSB], pb[4][:, 0:TSB]),
                            reads=[scr3, (pb[4], "pl")], writes=[scr3], tag=f"pra{tb}.{dt_i}.{t4}")
                        aop(lambda h, d_=dt_i, t_=t4: h.activation(
                            rqv[:, 0:TSB],
                            scr3[:, 0:TSB], AF.Silu, bias=bcols[:, 104 + d_:105 + d_]),
                            reads=[scr3, (bcols, "bpr")], writes=[(rqv, "rq")], tag=f"prg{tb}.{dt_i}.{t4}")
                        vop(lambda h, d_=dt_i, t_=t4: h.tensor_mul(
                            scr3[:, 0:TSB], rqv[:, 0:TSB],
                            scr[:, (d_ % 2) * TB + t_ * TSB: (d_ % 2) * TB + (t_ + 1) * TSB]),
                            reads=[(rqv, "rq"), (scr, "s0b")], writes=[scr3], tag=f"g0m{tb}.{dt_i}.{t4}")
                        vop(lambda h, d_=dt_i, t_=t4, b_=tb: h.tensor_add(
                            A3[:, d_ * T + b_ * TB + t_ * TSB: d_ * T + b_ * TB + (t_ + 1) * TSB],
                            A3[:, d_ * T + b_ * TB + t_ * TSB: d_ * T + b_ * TB + (t_ + 1) * TSB],
                            scr3[:, 0:TSB]),
                            reads=[scr3, (A3, "x", dt_i)], writes=[(A3, "x", dt_i)],
                            tag=f"g0a{tb}.{dt_i}.{t4}")

        stats_of(A3, [(A3, "x", f) for f in range(8)], 0, 1, pb[0], "s1")
        rsqrt_row(1, 2, "rsL1")
        p.fence([], ["rqv", (bca, 0), (bca, 1)], tag="fFt")
        bcast(2, 0, "b_rsl1")
        bcast(0, 1, "b_ml1")
        negmul_tile("mrl1t")
        p.fence([(A1, "xnc")], [(A1, "x1c")], tag="fA1f")
        for f in range(8):
            for c2 in range(0, T, SW3):
                W3 = min(SW3, T - c2)
                vop(lambda h, f_=f, c_=c2, W_=W3: h.tensor_mul(
                    scr3[:, 0:W_], A3[:, f_ * T + c_:f_ * T + c_ + W_], bca[:, c_:c_ + W_]),
                    reads=[(A3, "x", f), (bca, 0)], writes=[scr3], tag=f"f1a{f}.{c2}")
                vop(lambda h, f_=f, c_=c2, W_=W3: h.tensor_add(
                    A1[:, f_ * T + c_:f_ * T + c_ + W_], scr3[:, 0:W_], bca[:, T + c_:T + c_ + W_]),
                    reads=[scr3, (bca, 1)], writes=[(A1, "x1c")], tag=f"f1b{f}.{c2}")
        p.dma("gpsimd", lambda h: h.dma_start(
            out=convw[:, 0:64], in_=bass.AP(Wcv.tensor, 0, [[4, 128], [512, 16], [1, 4]]),
            allow_slow_non_contiguous=True),
            writes=[convw], key=convw, tag="cw")
        p.dma("gpsimd", lambda h, a=bcols[:, 112:128], b=bass.AP(bconv.tensor, 0, [[1, 128], [128, 16]]): h.dma_start(out=a, in_=b, allow_slow_non_contiguous=True), writes=[(bcols, "bcv")], key=(bcols, "bcv"), tag="bcv")
        p.dma("gpsimd", lambda h, a=bcols[:, 128:144], b=bass.AP(bdtv.tensor, 0, [[1, 128], [128, 16]]): h.dma_start(out=a, in_=b, allow_slow_non_contiguous=True), writes=[(bcols, "bdt")], key=(bcols, "bdt"), tag="bdt")
        p.dma("gpsimd", lambda h, a=bcols[:, 144:160], b=bass.AP(A_log.tensor, 0, [[1, 128], [128, 16]]): h.dma_start(out=a, in_=b, allow_slow_non_contiguous=True), writes=[(bcols, "alog")], key=(bcols, "alog"), tag="alog")
        aop(lambda h: h.activation(bcols[:, 160:176], bcols[:, 144:160], AF.Exp),
            reads=[(bcols, "alog")], writes=[(bcols, "Ae")], tag="Aexp")
        vop(lambda h: h.tensor_scalar_mul(bcols[:, 160:176], bcols[:, 160:176], -1.0),
            reads=[(bcols, "Ae")], writes=[(bcols, "Ae")], tag="Aneg")
        bounce16(WiT, wi16_d, D, 2 * DI, "wic")
        bounce16(WdtT, wdt16_d, DI, DI, "dtc")
        bounce16(WBT, wb16_d, DI, DI, "wbc")
        bounce16(WCT, wc16_d, DI, DI, "wcc")
        bounce16(WoT1, wo116_d, DI, D, "woc")
        p.fence([], [(bca, 0), (bca, 1), "egv"], tag="feg")
        for og2 in range(16):
            so = (og2 % 2) * 2048
            ldg(wres, so, 4096, wi16_d, 2 * DI, 0, D, og2 * 256, 256,
                (wres, "wie", og2), tag=f"we{og2}", rd=[("d", "wic")])
            for f in range(8):
                top(lambda h, f_=f, s_=so: h.matmul(
                    pb[1][0:4, 0:256], A1[:, f_ * T + T - 4: f_ * T + T],
                    wres[:, s_ + f_ * 256: s_ + (f_ + 1) * 256],
                    start=(f_ == 0), stop=(f_ == 7)),
                    reads=[(wres, "wie", og2), (A1, "x1c")], writes=[(pb[1], "eg")], tag=f"eg{og2}.{f}")
            vop(lambda h, o_=og2: h.tensor_copy(egv[0:4, o_ * 256:(o_ + 1) * 256], pb[1][0:4, 0:256]),
                reads=[(pb[1], "eg")], writes=[(egv, "eg")], tag=f"ege{og2}")
        p.dma("gpsimd", lambda h: h.dma_start(out=cc_e_in.ap(), in_=egv[0:4, 0:4096]),
              reads=[(egv, "eg")], writes=[("d", "cce")], tag="estg")
        p.cc(lambda h: h.collective_compute(
            "AllGather", ALU.bypass, replica_groups=GROUPS,
            ins=[cc_e_in.ap().opt()], outs=[cc_e_out.ap().opt()]),
            reads=[("d", "cce")], writes=[("d", "cceo")], tag="eag")
        for j in range(4):
            p.dma("gpsimd", lambda h, j_=j: h.dma_start(
                out=bass.AP(halos, j_, [[64, 128], [4, 16]]),
                in_=bass.AP(cc_e_out.ap().tensor, j_ * 2 * DI, [[1, 128], [128, 16]]),
                allow_slow_non_contiguous=True),
                reads=[("d", "cceo")], writes=[halos], key=halos, tag=f"ercv{j}")
        vop(lambda h: h.tensor_scalar_mul(halos[:, :], halos[:, :], rolec[:, 0:1]),
            reads=[halos, rolec], writes=[halos], tag="emsk")
        for oc in range(16):
            vop(lambda h, o_=oc: h.tensor_scalar_mul(
                xce[:, o_:o_ + 1], halos[:, o_ * 4:o_ * 4 + 1], convw[:, o_ * 4:o_ * 4 + 1]),
                reads=[halos, convw], writes=[xce], tag=f"xce{oc}.0")
            for j in range(1, 4):
                vop(lambda h, o_=oc, j_=j: h.scalar_tensor_tensor(
                    xce[:, o_:o_ + 1], halos[:, o_ * 4 + j_:o_ * 4 + j_ + 1],
                    convw[:, o_ * 4 + j_:o_ * 4 + j_ + 1], xce[:, o_:o_ + 1],
                    ALU.mult, ALU.add),
                    reads=[halos, convw, xce], writes=[xce], tag=f"xce{oc}.{j}")
            aop(lambda h, o_=oc: h.activation(xce[:, o_:o_ + 1], xce[:, o_:o_ + 1], AF.Silu,
                                              bias=bcols[:, 112 + o_:113 + o_]),
                reads=[xce, (bcols, "bcv")], writes=[xce], tag=f"xcs{oc}")
        vop(lambda h: h.memset(accol[:, :], 1.0), writes=[accol], tag="ac1")
        vop(lambda h: h.memset(hcol[:, :], 0.0), writes=[hcol], tag="hc0")
        vop(lambda h: h.memset(ucol[:, :], 0.0), writes=[ucol], tag="uc0")
        p.fence([], ["egv", "szv"], tag="fsz")
        x1o = 0
        dto = 4 * T
        p.fence([(A4, "u")], [(A4, "x1"), (A4, "dtc")], tag="fA4f")
        for blk in range(4):
            if blk > 0:
                for oc in range(16):
                    vop(lambda h, o_=oc: h.tensor_copy(
                        halos[:, o_ * 4 + 1:o_ * 4 + 4],
                        A4[:, x1o + o_ * TB + TB - 3: x1o + o_ * TB + TB]),
                        reads=[(A4, "x1")], writes=[halos], tag=f"hup{blk}.{oc}")
            for og2 in range(8):
                so = (og2 % 2) * 2048
                ldg(wres, so, 4096, wi16_d, 2 * DI, 0, D, og2 * 256, 256,
                    (wres, "wix", blk, og2), tag=f"wx{blk}.{og2}", rd=[("d", "wic")])
                for oi in range(2):
                    oc = og2 * 2 + oi
                    for f in range(8):
                        top(lambda h, f_=f, o_=oi, b_=blk, s_=so: h.matmul(
                            pb[2][:, 0:TB],
                            wres[:, s_ + f_ * 256 + o_ * 128: s_ + f_ * 256 + (o_ + 1) * 128],
                            A1[:, f_ * T + b_ * TB: f_ * T + (b_ + 1) * TB],
                            start=(f_ == 0), stop=(f_ == 7)),
                            reads=[(wres, "wix", blk, og2), (A1, "x1c")], writes=[(pb[2], "x1")],
                            tag=f"x1m{blk}.{oc}.{f}")
                    vop(lambda h, o_=oc: h.tensor_copy(
                        A4[:, x1o + o_ * TB: x1o + (o_ + 1) * TB], pb[2][:, 0:TB]),
                        reads=[(pb[2], "x1")], writes=[(A4, "x1")], tag=f"x1e{blk}.{oc}")
            if blk == 0:
                p.fence([(ARC, "s0log")], [(ARC, "xc")], tag="fARC")
            for oc in range(16):
                for j in range(4):
                    nh = 3 - j
                    if nh > 0:
                        if j == 0:
                            vop(lambda h, o_=oc, j_=j, n_=nh: h.tensor_scalar_mul(
                                scr3[:, 0:n_], halos[:, o_ * 4 + 1 + j_:o_ * 4 + 1 + j_ + n_],
                                convw[:, o_ * 4 + j_:o_ * 4 + j_ + 1]),
                                reads=[halos, convw], writes=[scr3], tag=f"cve{blk}.{oc}.{j}")
                        else:
                            vop(lambda h, o_=oc, j_=j, n_=nh: h.scalar_tensor_tensor(
                                scr3[:, 0:n_], halos[:, o_ * 4 + 1 + j_:o_ * 4 + 1 + j_ + n_],
                                convw[:, o_ * 4 + j_:o_ * 4 + j_ + 1], scr3[:, 0:n_],
                                ALU.mult, ALU.add),
                                reads=[halos, convw, scr3], writes=[scr3], tag=f"cve{blk}.{oc}.{j}")
                    nb = TB - nh
                    if j == 0:
                        vop(lambda h, o_=oc, j_=j, n_=nh, m_=nb: h.tensor_scalar_mul(
                            scr3[:, n_:n_ + m_],
                            A4[:, x1o + o_ * TB: x1o + o_ * TB + m_],
                            convw[:, o_ * 4 + j_:o_ * 4 + j_ + 1]),
                            reads=[(A4, "x1"), convw], writes=[scr3], tag=f"cvb{blk}.{oc}.{j}")
                    else:
                        vop(lambda h, o_=oc, j_=j, n_=nh, m_=nb: h.scalar_tensor_tensor(
                            scr3[:, n_:n_ + m_],
                            A4[:, x1o + o_ * TB: x1o + o_ * TB + m_],
                            convw[:, o_ * 4 + j_:o_ * 4 + j_ + 1], scr3[:, n_:n_ + m_],
                            ALU.mult, ALU.add),
                            reads=[(A4, "x1"), convw, scr3], writes=[scr3],
                            tag=f"cvb{blk}.{oc}.{j}")
                aop(lambda h, o_=oc: h.activation(ARC[:, o_ * TB:(o_ + 1) * TB], scr3[:, 0:TB],
                                                  AF.Silu, bias=bcols[:, 112 + o_:113 + o_]),
                    reads=[scr3, (bcols, "bcv")], writes=[(ARC, "xc")], tag=f"cvs{blk}.{oc}")
            for og in range(4):
                for (mi, mdram, mk) in ((0, wdt16_d, "dt"), (1, wb16_d, "B"), (2, wc16_d, "C"),
                                        (3, wi16_d, "z")):
                    if mk == "z":
                        for oi2 in range(2):
                            so = (oi2 % 2) * 2048
                            ldg(wres, so, 4096, wi16_d, 2 * DI, 0, D,
                                DI + og * 512 + oi2 * 256, 256,
                                (wres, "wz", blk, og, oi2), tag=f"wz{blk}.{og}.{oi2}",
                                rd=[("d", "wic")])
                            for oi in range(2):
                                oc = og * 4 + oi2 * 2 + oi
                                for f in range(8):
                                    top(lambda h, f_=f, o_=oi, b_=blk, s_=so: h.matmul(
                                        pb[3][:, 0:TB],
                                        wres[:, s_ + f_ * 256 + o_ * 128: s_ + f_ * 256 + (o_ + 1) * 128],
                                        A1[:, f_ * T + b_ * TB: f_ * T + (b_ + 1) * TB],
                                        start=(f_ == 0), stop=(f_ == 7)),
                                        reads=[(wres, "wz", blk, og, oi2), (A1, "x1c")], writes=[(pb[3], "z")],
                                        tag=f"zm{blk}.{oc}.{f}")
                                aop(lambda h, o_=oc % 4: h.activation(
                                    szv[:, o_ * TB:(o_ + 1) * TB], pb[3][:, 0:TB], AF.Silu),
                                    reads=[(pb[3], "z")], writes=[(szv, "sz")], tag=f"ze{blk}.{oc}")
                    else:
                        for oi in range(4):
                            oc = og * 4 + oi
                            so = (oc % 2) * 2048
                            ldg(wres, so, 4096, mdram, DI, 0, DI, oc * 128, 128,
                                (wres, "wm", mk, blk, oc), tag=f"wm{mk}{blk}.{oc}",
                                rd=[("d", {"dt": "dtc", "B": "wbc", "C": "wcc"}[mk])])
                            for kk in range(16):
                                top(lambda h, k_=kk, m_=mi, s_=so: h.matmul(
                                    pb[4 + m_ % 2][:, 0:TB],
                                    wres[:, s_ + k_ * 128: s_ + (k_ + 1) * 128],
                                    ARC[:, k_ * TB:(k_ + 1) * TB],
                                    start=(k_ == 0), stop=(k_ == 15)),
                                    reads=[(wres, "wm", mk, blk, oc), (ARC, "xc")],
                                    writes=[(pb[4 + mi % 2], mk)], tag=f"{mk}m{blk}.{oc}.{kk}")
                            if mk == "dt":
                                aop(lambda h, o_=oc, m_=mi: h.activation(
                                    scr3[:, 0:TB], pb[4 + m_ % 2][:, 0:TB], AF.Abs,
                                    bias=bcols[:, 128 + o_:129 + o_]),
                                    reads=[(pb[4 + mi % 2], "dt"), (bcols, "bdt")], writes=[scr3],
                                    tag=f"dta{blk}.{oc}")
                                aop(lambda h: h.activation(scr3[:, 0:TB], scr3[:, 0:TB], AF.Exp,
                                                           scale=-1.0),
                                    reads=[scr3], writes=[scr3], tag=f"dtb{blk}.{oc}")
                                aop(lambda h: h.activation(scr3[:, 0:TB], scr3[:, 0:TB], AF.Ln,
                                                           bias=1.0),
                                    reads=[scr3], writes=[scr3], tag=f"dtc{blk}.{oc}")
                                aop(lambda h, o_=oc, m_=mi: h.activation(
                                    fscr[:, 0:TB], pb[4 + m_ % 2][:, 0:TB], AF.Relu,
                                    bias=bcols[:, 128 + o_:129 + o_]),
                                    reads=[(pb[4 + mi % 2], "dt"), (bcols, "bdt")], writes=[(fscr, "r")],
                                    tag=f"dtd{blk}.{oc}")
                                vop(lambda h, o_=oc: h.tensor_add(
                                    A4[:, dto + o_ * TB: dto + (o_ + 1) * TB], scr3[:, 0:TB],
                                    fscr[:, 0:TB]),
                                    reads=[scr3, (fscr, "r")], writes=[(A4, "dtc")], tag=f"dte{blk}.{oc}")
                            elif mk == "B":
                                if blk == 0:
                                    for kk in range(16):
                                        top(lambda h, k_=kk, oc2=oc, s_=so: h.matmul(
                                            pb[7][:, oc2:oc2 + 1],
                                            wres[:, s_ + k_ * 128: s_ + (k_ + 1) * 128],
                                            xce[:, k_:k_ + 1],
                                            start=(k_ == 0), stop=(k_ == 15)),
                                            reads=[(wres, "wm", "B", blk, oc), xce], writes=[(pb[7], "bv")],
                                            tag=f"bv{oc}.{kk}")
                                aop(lambda h, o_=oi: h.activation(
                                    scr3[:, 0:TB], pb[4 + 1][:, 0:TB], AF.Sigmoid),
                                    reads=[(pb[5], "B")], writes=[scr3], tag=f"Bs{blk}.{oc}")
                                vop(lambda h, o_=oc, oo=oi: h.tensor_mul(
                                    scr[:, oo * TB:(oo + 1) * TB], scr3[:, 0:TB],
                                    ARC[:, o_ * TB:(o_ + 1) * TB]),
                                    reads=[scr3, (ARC, "xc")], writes=[(scr, "u")], tag=f"Bu{blk}.{oc}")
                            else:
                                aop(lambda h, o_=oi: h.activation(
                                    scr3[:, 0:TB],
                                    pb[4][:, 0:TB], AF.Tanh),
                                    reads=[(pb[4], "C")], writes=[scr3], tag=f"Ct{blk}.{oc}")
                                vop(lambda h, o_=oi: h.tensor_copy(
                                    szv[:, 4 * TB + o_ * TB: 4 * TB + (o_ + 1) * TB], scr3[:, 0:TB]),
                                    reads=[scr3], writes=[(szv, "ct")], tag=f"Cte{blk}.{oc}")
                if blk == 0:
                    for oi in range(4):
                        oc = og * 4 + oi
                        aop(lambda h, o_=oc: h.activation(
                            uprevc[:, o_:o_ + 1], pb[7][:, o_:o_ + 1], AF.Sigmoid),
                            reads=[(pb[7], "bv")], writes=[uprevc], tag=f"ups{oc}")
                        vop(lambda h, o_=oc: h.tensor_mul(uprevc[:, o_:o_ + 1], uprevc[:, o_:o_ + 1],
                                                          xce[:, o_:o_ + 1]),
                            reads=[uprevc, xce], writes=[uprevc], tag=f"upm{oc}")
                        vop(lambda h, o_=oc: h.tensor_mul(uprevc[:, o_:o_ + 1], uprevc[:, o_:o_ + 1],
                                                          rolec[:, 0:1]),
                            reads=[uprevc, rolec], writes=[uprevc], tag=f"upr{oc}")
                for oi in range(4):
                    oc = og * 4 + oi
                    vop(lambda h, o_=oc: h.tensor_scalar(
                        fscr[:, TB:2 * TB], A4[:, dto + o_ * TB: dto + (o_ + 1) * TB],
                        bcols[:, 160 + o_:161 + o_], -10.0, ALU.mult, ALU.max),
                        reads=[(A4, "dtc"), (bcols, "Ae")], writes=[(fscr, "la")], tag=f"la{blk}.{oc}")
                    vop(lambda h: h.tensor_scalar_min(fscr[:, TB:2 * TB], fscr[:, TB:2 * TB], -1e-4),
                        reads=[(fscr, "la")], writes=[(fscr, "la")], tag=f"lb{blk}.{oc}")
                    aop(lambda h: h.activation(fscr[:, TB:2 * TB], fscr[:, TB:2 * TB], AF.Exp),
                        reads=[(fscr, "la")], writes=[(fscr, "la")], tag=f"lc{blk}.{oc}")
                    vop(lambda h, o_=oc, oo=oi, b_=blk: h.tensor_copy(
                        fscr[:, 2 * TB + 1: 3 * TB], scr[:, oo * TB: (oo + 1) * TB - 1]),
                        reads=[(scr, "u")], writes=[(fscr, "be")], tag=f"us{blk}.{oc}")
                    vop(lambda h, o_=oc, b_=blk: h.tensor_copy(
                        fscr[:, 2 * TB: 2 * TB + 1],
                        (uprevc[:, o_:o_ + 1] if b_ == 0 else ucol[:, o_:o_ + 1])),
                        reads=[(uprevc if blk == 0 else ucol)], writes=[(fscr, "be")], tag=f"u0{blk}.{oc}")
                    vop(lambda h, oo=oi, o_=oc: h.tensor_copy(
                        ucol[:, o_:o_ + 1], scr[:, (oo + 1) * TB - 1:(oo + 1) * TB]),
                        reads=[(scr, "u")], writes=[ucol], tag=f"ue{blk}.{oc}")
                    vop(lambda h, oo=oi: h.tensor_add(
                        fscr[:, 2 * TB:3 * TB], fscr[:, 2 * TB:3 * TB], scr[:, oo * TB:(oo + 1) * TB]),
                        reads=[(fscr, "be"), (scr, "u")], writes=[(fscr, "be")], tag=f"ua{blk}.{oc}")
                    vop(lambda h, o_=oc: h.scalar_tensor_tensor(
                        fscr[:, 2 * TB:3 * TB], fscr[:, 2 * TB:3 * TB], 0.5,
                        A4[:, dto + o_ * TB: dto + (o_ + 1) * TB], ALU.mult, ALU.mult),
                        reads=[(fscr, "be"), (A4, "dtc")], writes=[(fscr, "be")], tag=f"ub{blk}.{oc}")
                    vop(lambda h, o_=oc: h.tensor_tensor_scan(
                        fscr[:, 0:TB], fscr[:, TB:2 * TB], fscr[:, 2 * TB:3 * TB],
                        hcol[:, o_:o_ + 1], ALU.mult, ALU.add),
                        reads=[(fscr, "la"), (fscr, "be"), hcol], writes=[(fscr, "h0")], tag=f"sc{blk}.{oc}")
                    vop(lambda h, o_=oc: h.tensor_copy(hcol[:, o_:o_ + 1], fscr[:, TB - 1:TB]),
                        reads=[(fscr, "h0")], writes=[hcol], tag=f"sce{blk}.{oc}")
                    vop(lambda h, o_=oc: h.tensor_tensor_scan(
                        fscr[:, 3 * TB:4 * TB], fscr[:, TB:2 * TB], zbuf[:, 0:TB],
                        accol[:, o_:o_ + 1], ALU.mult, ALU.add),
                        reads=[(fscr, "la"), zbuf, accol], writes=[(fscr, "ac")], tag=f"sa{blk}.{oc}")
                    vop(lambda h, o_=oc: h.tensor_copy(accol[:, o_:o_ + 1], fscr[:, 4 * TB - 1:4 * TB]),
                        reads=[(fscr, "ac")], writes=[accol], tag=f"sae{blk}.{oc}")
                    vop(lambda h, oo=oi: h.tensor_mul(
                        scr3[:, 0:TB], szv[:, 4 * TB + oo * TB: 4 * TB + (oo + 1) * TB],
                        szv[:, oo * TB:(oo + 1) * TB]),
                        reads=[(szv, "ct"), (szv, "sz")], writes=[scr3], tag=f"yw{blk}.{oc}")
                    vop(lambda h, o_=oc: h.tensor_mul(
                        A2[:, o_ * TB:(o_ + 1) * TB], fscr[:, 0:TB], scr3[:, 0:TB]),
                        reads=[(fscr, "h0"), scr3], writes=[(A2, "y0")], tag=f"y0{blk}.{oc}")
                    vop(lambda h, o_=oc: h.tensor_mul(
                        A2[:, 4 * T + o_ * TB: 4 * T + (o_ + 1) * TB],
                        fscr[:, 3 * TB:4 * TB], scr3[:, 0:TB]),
                        reads=[(fscr, "ac"), scr3], writes=[(A2, "gc")], tag=f"gc{blk}.{oc}")
            p.dma("sync", lambda h, b_=blk: h.dma_start(
                out=y0_d.ap()[:, b_ * 16 * TB:(b_ + 1) * 16 * TB], in_=A2[:, 0:16 * TB]),
                reads=[(A2, "y0")], writes=[("d", "y0")], key=("d", "y0"), tag=f"y0d{blk}")
            p.dma("sync", lambda h, b_=blk: h.dma_start(
                out=gc_d.ap()[:, b_ * 16 * TB:(b_ + 1) * 16 * TB], in_=A2[:, 4 * T:4 * T + 16 * TB]),
                reads=[(A2, "gc")], writes=[("d", "gc")], key=("d", "gc"), tag=f"gcd{blk}")
        p.dma("sync", lambda h: h.dma_start(out=cc_h_in.ap(), in_=hcol[:, :]),
              reads=[hcol], writes=[("d", "cch")], tag="hstg")
        p.cc(lambda h: h.collective_compute(
            "AllGather", ALU.bypass, replica_groups=GROUPS,
            ins=[cc_h_in.ap().opt()], outs=[cc_h_out.ap().opt()]),
            reads=[("d", "cch")], writes=[("d", "ccho")], tag="hag")
        p.dma("sync", lambda h: h.dma_start(out=hini[:, :], in_=cc_h_out.ap()[0:128, :]),
              reads=[("d", "ccho")], writes=[hini], tag="hrcv")
        vop(lambda h: h.tensor_scalar_mul(hini[:, :], hini[:, :], rolec[:, 0:1]),
            reads=[hini, rolec], writes=[hini], tag="hmsk")
        for blk in range(4):
            p.dma("gpsimd", lambda h, b_=blk: h.dma_start(
                out=A2[:, 0:16 * TB], in_=y0_d.ap()[:, b_ * 16 * TB:(b_ + 1) * 16 * TB]),
                reads=[("d", "y0")], writes=[(A2, "y0")], key=(A2, "y0"), tag=f"y0r{blk}")
            p.dma("gpsimd", lambda h, b_=blk: h.dma_start(
                out=A2[:, 4 * T:4 * T + 16 * TB], in_=gc_d.ap()[:, b_ * 16 * TB:(b_ + 1) * 16 * TB]),
                reads=[("d", "gc")], writes=[(A2, "gc")], key=(A2, "gc"), tag=f"gcr{blk}")
            for og in range(4):
                for hf in range(2):
                    so = ((og * 2 + hf) % 2) * 2048
                    ldg(wres, so, 4096, wo116_d, D, og * 512, 512, hf * 512, 512,
                        (wres, "wog", blk, og, hf), tag=f"wog{blk}.{og}.{hf}",
                        rd=[("d", "woc")])
                    for oi in range(4):
                        oc = og * 4 + oi
                        vop(lambda h, o_=oc: h.scalar_tensor_tensor(
                            scr3[:, 0:TB], A2[:, 4 * T + o_ * TB: 4 * T + (o_ + 1) * TB],
                            hini[:, o_:o_ + 1], A2[:, o_ * TB:(o_ + 1) * TB], ALU.mult, ALU.add),
                            reads=[(A2, "y0"), (A2, "gc"), hini], writes=[scr3], tag=f"y{blk}.{oc}.{hf}")
                        vop(lambda h: h.tensor_copy(scr[:, 0:TB], scr3[:, 0:TB]),
                            reads=[scr3], writes=[(scr, "y16")], tag=f"y16{blk}.{oc}.{hf}")
                        for d4 in range(4):
                            dt_i = hf * 4 + d4
                            top(lambda h, o_=oi, d_=d4, b2=dt_i, oc_=oc, s_=so: h.matmul(
                                pb[b2][:, 0:TB], wres[:, s_ + o_ * 512 + d_ * 128: s_ + o_ * 512 + (d_ + 1) * 128],
                                scr[:, 0:TB],
                                start=(oc_ == 0), stop=(oc_ == 15)),
                                reads=[(wres, "wog", blk, og, hf), (scr, "y16")], writes=[(pb[dt_i], "s1")],
                                tag=f"s1m{blk}.{oc}.{dt_i}")
            for dt_i in range(8):
                vop(lambda h, d_=dt_i: h.tensor_copy(scr3[:, 0:TB], pb[d_][:, 0:TB]),
                    reads=[(pb[dt_i], "s1")], writes=[scr3], tag=f"s1e{blk}.{dt_i}")
                p.dma("sync", lambda h, d_=dt_i, b_=blk: h.dma_start(
                    out=s1_d.ap()[d_ * 128:(d_ + 1) * 128, b_ * TB:(b_ + 1) * TB], in_=scr3[:, 0:TB]),
                    reads=[scr3], writes=[("d", "s1")], key=("d", "s1"),
                    tag=f"s1d{blk}.{d_ if False else dt_i}")

        def quant8(wT_ap, K_, O_, scol, dstD, nn):
            PC = min(O_, T)
            for kk in range(K_ // 128):
                for ob in range(0, O_, PC):
                    dmal(scr[:, 0:PC], wT_ap[kk * 128:(kk + 1) * 128, ob:ob + PC], (scr, "wc"),
                         tag=f"{nn}a{kk}.{ob}")
                    vop(lambda h, P_=PC: h.tensor_copy(w8s[:, 0:P_], scr[:, 0:P_]),
                        reads=[(scr, "wc")], writes=[(w8s, "q")], tag=f"{nn}c{kk}.{ob}")
                    p.dma("sync", lambda h, k_=kk, o_=ob, D_=dstD, P_=PC: h.dma_start(
                        out=D_.ap()[k_ * 128:(k_ + 1) * 128, o_:o_ + P_], in_=w8s[:, 0:P_]),
                        reads=[(w8s, "q")], writes=[("d", nn)], key=("d", nn),
                        tag=f"{nn}o{kk}.{ob}")
        p.fence([], [(A2, "y0"), (A2, "gc"), (mkv, "mk"),
                     (A4, "x1"), (A4, "dtc"), "hqb", "w8s", "xq8"], tag="fG")
        quant8(w1fT, D, FF, 8, q1_d, "q1")
        quant8(w2fT, FF, D, 8, q2_d, "q2")
        p.dma("gpsimd", lambda h, a=bcols[:, 32:64], b=bass.AP(b1f.tensor, 0, [[1, 128], [128, 32]]): h.dma_start(out=a, in_=b, allow_slow_non_contiguous=True), writes=[(bcols, "b1")], key=(bcols, "b1"), tag="b1f")
        p.dma("gpsimd", lambda h, a=bcols[:, 64:72], b=bass.AP(b2f.tensor, 0, [[1, 128], [128, 8]]): h.dma_start(out=a, in_=b, allow_slow_non_contiguous=True), writes=[(bcols, "b2")], key=(bcols, "b2"), tag="b2f")
        outw = 0
        for tb in range(4):
            for f in range(8):
                vop(lambda h, f_=f, b_=tb: h.tensor_copy(
                    xq8[:, f_ * TB:(f_ + 1) * TB],
                    A1[:, f_ * T + b_ * TB: f_ * T + (b_ + 1) * TB]),
                    reads=[(A1, "x1c")], writes=[(xq8, tb)], tag=f"xq{tb}.{f}")
            for hgrp in range(4):
                ldg(w8s, 0, 8192, q1_d, FF, 0, D, hgrp * 1024, 1024, (w8s, "g1"), tag=f"g1{tb}.{hgrp}", rd=[("d", "q1")])
                for ho in range(8):
                    ht = hgrp * 8 + ho
                    for t4 in range(NTB):
                        for f in range(8):
                            top(lambda h, f_=f, o_=ho, t_=t4: h.matmul(
                                pb[5][:, 0:TSB],
                                w8s[:, f_ * 1024 + o_ * 128: f_ * 1024 + (o_ + 1) * 128],
                                xq8[:, f_ * TB + t_ * TSB: f_ * TB + (t_ + 1) * TSB],
                                start=(f_ == 0), stop=(f_ == 7)),
                                reads=[(w8s, "g1"), (xq8, tb)], writes=[(pb[5], "f1")],
                                tag=f"f1m{tb}.{ht}.{t4}.{f}")
                        aop(lambda h, h_=ht, t_=t4: h.activation(
                            scr3[:, 0:TSB], pb[5][:, 0:TSB], AF.Identity,
                            bias=bcols[:, 32 + h_:33 + h_]),
                            reads=[(pb[5], "f1"), (bcols, "b1")], writes=[scr3], tag=f"f1q{tb}.{ht}.{t4}")
                        aop(lambda h, h_=ht, t_=t4: h.activation(
                            hqb[:, h_ * TB + t_ * TSB: h_ * TB + (t_ + 1) * TSB], scr3[:, 0:TSB],
                            AF.Relu),
                            reads=[scr3], writes=[(hqb, tb)], tag=f"f1r{tb}.{ht}.{t4}")
            p.dma("gpsimd", lambda h, b_=tb: h.dma_start(
                out=d3(mkv, 0, 8, TB, 8 * TB),
                in_=g3(mask_in.tensor, T, 0, D, b_ * TB, TB)),
                reads=[], writes=[(mkv, "mk")], key=(mkv, "mk"), tag=f"mk{tb}")
            for dgp in range(4):
                p.dma("gpsimd", lambda h, b_=tb, g_=dgp: h.dma_start(
                    out=d3(scr, 0, 2, TB, T),
                    in_=g3(s1_d, T, g_ * 256, 256, b_ * TB, TB)),
                    reads=[("d", "s1")], writes=[(scr, "s1b")], key=(scr, "s1b"), tag=f"s1r{tb}.{dgp}")
                ldg(w8s, 0, 8192, q2_d, D, 0, FF, dgp * 256, 256, (w8s, "g2"), tag=f"g2{tb}.{dgp}", rd=[("d", "q2")])
                for do in range(2):
                    dt_i = dgp * 2 + do
                    for t4 in range(NTB):
                        for hc in range(32):
                            top(lambda h, hc_=hc, t_=t4, d_=do: h.matmul(
                                pb[6][:, 0:TSB],
                                w8s[:, hc_ * 256 + d_ * 128: hc_ * 256 + (d_ + 1) * 128],
                                hqb[:, hc_ * TB + t_ * TSB: hc_ * TB + (t_ + 1) * TSB],
                                start=(hc_ == 0), stop=(hc_ == 31)),
                                reads=[(w8s, "g2"), (hqb, tb)], writes=[(pb[6], "f2")],
                                tag=f"f2m{tb}.{dt_i}.{t4}.{hc}")
                        aop(lambda h, d_=dt_i, t_=t4: h.activation(
                            scr3[:, 0:TSB], pb[6][:, 0:TSB], AF.Identity,
                            bias=bcols[:, 64 + d_:65 + d_]),
                            reads=[(pb[6], "f2"), (bcols, "b2")], writes=[scr3], tag=f"f2q{tb}.{dt_i}.{t4}")
                        vop(lambda h, d_=dt_i, t_=t4: h.tensor_mul(
                            scr3[:, 0:TSB], scr3[:, 0:TSB],
                            mkv[:, d_ * TB + t_ * TSB: d_ * TB + (t_ + 1) * TSB]),
                            reads=[scr3, (mkv, "mk")], writes=[scr3], tag=f"f2d{tb}.{dt_i}.{t4}")
                        aop(lambda h: h.activation(scr3[:, 0:TSB], scr3[:, 0:TSB], AF.Silu),
                            reads=[scr3], writes=[scr3], tag=f"f2s{tb}.{dt_i}.{t4}")
                        vop(lambda h, d_=dt_i, t_=t4: h.tensor_mul(
                            scr3[:, 0:TSB], scr3[:, 0:TSB],
                            scr[:, (d_ % 2) * TB + t_ * TSB: (d_ % 2) * TB + (t_ + 1) * TSB]),
                            reads=[scr3, (scr, "s1b")], writes=[scr3], tag=f"f2g{tb}.{dt_i}.{t4}")
                        vop(lambda h, d_=dt_i, t_=t4, b_=tb: h.tensor_add(
                            A3[:, d_ * T + b_ * TB + t_ * TSB: d_ * T + b_ * TB + (t_ + 1) * TSB],
                            A3[:, d_ * T + b_ * TB + t_ * TSB: d_ * T + b_ * TB + (t_ + 1) * TSB],
                            scr3[:, 0:TSB]),
                            reads=[scr3, (A3, "x", dt_i)], writes=[(A3, "x", dt_i), (A3, "xf", dt_i, tb)],
                            tag=f"g1a{tb}.{dt_i}.{t4}")
                    p.dma("gpsimd", lambda h, d_=dt_i, b_=tb: h.dma_start(
                        out=out_t[d_ * 128:(d_ + 1) * 128, b_ * TB:(b_ + 1) * TB],
                        in_=A3[:, d_ * T + b_ * TB: d_ * T + (b_ + 1) * TB]),
                        reads=[(A3, "xf", dt_i, tb)], writes=[("d", "outw", tb * 8 + dt_i)],
                        key=("d", "outw"), tag=f"out{tb}.{dt_i}")
        p.final_wait("gpsimd", [("d", "outw", i) for i in range(32)])
        p.emit(st_)
    return nc


_CACHE = {}


def _prep_inputs(inputs):
    f32 = np.float32
    f16 = np.float16
    tr = lambda a: np.ascontiguousarray(np.asarray(a, dtype=f32).T)
    w = {}
    w["wqT"] = tr(inputs["L0_attn_wq"]); w["wkT"] = tr(inputs["L0_attn_wk"])
    w["wvT"] = tr(inputs["L0_attn_wv"]); w["woT"] = tr(inputs["L0_attn_wo"])
    w["ln1w"] = np.asarray(inputs["L0_attn_ln1_w"], f32).reshape(1, -1)
    w["ln1b"] = np.asarray(inputs["L0_attn_ln1_b"], f32).reshape(1, -1)
    w["ln2w"] = np.asarray(inputs["L0_attn_ln2_w"], f32).reshape(1, -1)
    w["ln2b"] = np.asarray(inputs["L0_attn_ln2_b"], f32).reshape(1, -1)
    w["w1T"] = tr(inputs["L0_attn_w1"]); w["b1v"] = np.asarray(inputs["L0_attn_b1"], f32).reshape(1, -1)
    w["w2T"] = tr(inputs["L0_attn_w2"]); w["b2v"] = np.asarray(inputs["L0_attn_b2"], f32).reshape(1, -1)
    w["WfcT"] = tr(inputs["L0_ffn_W_fc"]); w["bfcv"] = np.asarray(inputs["L0_ffn_b_fc"], f32).reshape(1, -1)
    w["AfcT"] = tr(inputs["L0_ffn_A_fc"]); w["BfcT"] = np.ascontiguousarray(np.asarray(inputs["L0_ffn_B_fc"], f32).T)
    w["WprojT"] = tr(inputs["L0_ffn_W_proj"]); w["bprv"] = np.asarray(inputs["L0_ffn_b_proj"], f32).reshape(1, -1)
    w["AprojT"] = tr(inputs["L0_ffn_A_proj"]); w["BprojT"] = np.ascontiguousarray(np.asarray(inputs["L0_ffn_B_proj"], f32).T)
    w["WiT"] = tr(inputs["L1_attn_Wi"])
    w["Wcv"] = np.ascontiguousarray(np.asarray(inputs["L1_attn_Wconv"], f32).reshape(DI, 4))
    w["bconv"] = np.asarray(inputs["L1_attn_bconv"], f32).reshape(1, -1)
    w["WdtT"] = tr(inputs["L1_attn_Wdt"]); w["bdtv"] = np.asarray(inputs["L1_attn_bdt"], f32).reshape(1, -1)
    w["WBT"] = tr(inputs["L1_attn_WB"]); w["WCT"] = tr(inputs["L1_attn_WC"])
    w["WoT1"] = tr(inputs["L1_attn_Wo"])
    w["A_log"] = np.asarray(inputs["L1_attn_A_log"], f32).reshape(1, -1)
    w["w1fT"] = tr(inputs["L1_ffn_w1"]); w["b1f"] = np.asarray(inputs["L1_ffn_b1"], f32).reshape(1, -1)
    w["w2fT"] = tr(inputs["L1_ffn_w2"]); w["b2f"] = np.asarray(inputs["L1_ffn_b2"], f32).reshape(1, -1)
    w["lr_in"] = np.asarray(inputs["L0_attn_lr_scale"], f32).reshape(1, -1)
    w["trilm"] = np.tril(np.ones((128, 128), f16), -1)
    w["trium"] = np.triu(np.ones((128, 128), f16), 1)
    return w


def _dropout_mask(shape):
    import jax
    try:
        dev = jax.local_devices(backend="cpu")[0]
        with jax.default_device(dev):
            keep = np.asarray(jax.random.bernoulli(jax.random.key(42), 0.8, shape))
    except Exception:
        keep = np.asarray(jax.random.bernoulli(jax.random.key(42), 0.8, shape))
    return (keep.astype(np.float32) * 1.25).astype(np.float16)


def kernel(**inputs):
    x = np.asarray(inputs["L0_attn_x"], np.float32)
    B, L, D_ = x.shape
    T = L // 2
    w = _prep_inputs(inputs)
    mask = _dropout_mask((B, L, D_))
    if T not in _CACHE:
        _CACHE[T] = build_program(T)
    nc = _CACHE[T]
    in_maps = []
    for c in range(8):
        b, hf = c // 2, c % 2
        m = dict(w)
        m["x_in"] = np.ascontiguousarray(x[b, hf * T:(hf + 1) * T, :].T)
        m["mask_in"] = np.ascontiguousarray(mask[b, hf * T:(hf + 1) * T, :].T)
        m["role_in"] = np.full((1, 1), float(hf), np.float32)
        in_maps.append(m)
    res = run_bass_kernel_spmd(nc, in_maps, core_ids=list(range(8)))
    out = np.empty((B, L, D_), np.float32)
    for c in range(8):
        b, hf = c // 2, c % 2
        out[b, hf * T:(hf + 1) * T, :] = res.results[c]["out"].T
    return out



# version 03
# speedup vs baseline: 1.6582x; 1.6582x over previous
import os
import numpy as np
import ml_dtypes
import concourse.bass as bass
import concourse.mybir as mybir
from concourse.bass_utils import run_bass_kernel_spmd
from concourse.bass_isa import ReduceOp
from contextlib import ExitStack

F32 = mybir.dt.float32
F16 = mybir.dt.float16
F8 = mybir.dt.float8e4
AF = mybir.ActivationFunctionType
ALU = mybir.AluOpType
AX = mybir.AxisListType

ENGINES = ("tensor", "vector", "scalar", "gpsimd", "sync")
D = 1024
H = 16
FF = 4096
DI = 2048
EPS = 1e-5
MAGIC = float(1.5 * 2.0 ** 23)
GROUPS = [[0, 1], [2, 3], [4, 5], [6, 7]]


_COLLAPSE_PREFIX = ("pb",)
_COLLAPSE_NAMES = {"scr", "scr3", "fscr", "w8s", "xq8", "hqb", "lra_s",
                   "qmzv", "qbv", "w1v", "rqv", "szv", "egv", "mkv", "xs1v",
                   "afc_s", "apr_s", "masks", "lncol", "convw", "halos"}


def _norm(n):
    s = str(n)
    head, _, tail = s.rpartition("_")
    return head if head and tail.isdigit() else s


def _key(x):
    if isinstance(x, tuple):
        if x and not isinstance(x[0], (str, int, tuple)):
            n0 = getattr(x[0], "name", None)
            if n0 is not None:
                b = _norm(n0)
                if b in _COLLAPSE_NAMES or b.startswith(_COLLAPSE_PREFIX):
                    return b
        return tuple(_key(e) for e in x)
    if isinstance(x, (str, int)):
        return x
    n = getattr(x, "name", None)
    if n is None:
        return id(x)
    b = _norm(n)
    return b if (b in _COLLAPSE_NAMES or b.startswith(_COLLAPSE_PREFIX)) else n


class _Op:
    __slots__ = ("eng", "fn", "reads", "writes", "kind", "sem_key", "idx",
                 "waits", "need_inc", "drain_before", "tag")

    def __init__(self, eng, fn, reads, writes, kind, sem_key, tag=""):
        self.eng, self.fn, self.reads, self.writes = eng, fn, reads, writes
        self.kind, self.sem_key, self.tag = kind, sem_key, tag
        self.idx = -1
        self.waits = []
        self.need_inc = False
        self.drain_before = False


class Prog:
    def __init__(self, nc):
        self.nc = nc
        self.ops = []
        self.dma_sem_of_buf = {}
        self.n_dma_sems = 0

    def add(self, eng, fn, reads=(), writes=(), kind="c", sem_key=None, tag=""):
        op = _Op(eng, fn, tuple(_key(r) for r in reads),
                 tuple(_key(w) for w in writes), kind, sem_key, tag)
        op.idx = len(self.ops)
        self.ops.append(op)
        return op

    def c(self, eng, fn, reads=(), writes=(), tag=""):
        return self.add(eng, fn, reads, writes, "c", tag=tag)

    def dma(self, queue, fn, reads=(), writes=(), key=None, tag=""):
        if key is None:
            assert writes
            key = writes[0]
        key = _key(key)
        if key not in self.dma_sem_of_buf:
            self.dma_sem_of_buf[key] = f"ds{self.n_dma_sems}"
            self.n_dma_sems += 1
        return self.add(queue, fn, reads, writes, "d", self.dma_sem_of_buf[key], tag=tag)

    def cc(self, fn, reads=(), writes=(), tag=""):
        return self.add("gpsimd", fn, reads, writes, "cc", tag=tag)

    def fence(self, reads, writes, tag="fence"):
        self.c("vector", lambda h: h.memset(self.dummy, 0.0),
               reads=tuple(reads), writes=tuple(writes) + ("nopd",), tag=tag)

    def final_wait(self, queue, bufs):
        self.c(queue, lambda h: h.memset(self.dummy, 0.0),
               reads=tuple(bufs), writes=("nopd",), tag="final")

    def schedule(self):
        last_writer = {}
        readers_since = {}
        edges = {}
        for op in self.ops:
            deps = set()
            for r in op.reads:
                w = last_writer.get(r)
                if w is not None:
                    deps.add(w)
            for w_ in op.writes:
                pw = last_writer.get(w_)
                if pw is not None:
                    deps.add(pw)
                for rd in readers_since.get(w_, ()):
                    deps.add(rd)
            deps.discard(op)
            pl = [d for d in deps
                  if not (op.kind == "d" and d.kind == "d"
                          and d.sem_key == op.sem_key and d.eng == op.eng)]
            if pl:
                edges[op.idx] = pl
            for w_ in op.writes:
                last_writer[w_] = op
                readers_since[w_] = []
            for r in op.reads:
                readers_since.setdefault(r, []).append(op)
        for ci, pl in edges.items():
            c = self.ops[ci]
            for pp in pl:
                if pp.kind in ("d", "cc"):
                    continue
                if pp.eng == c.eng and c.kind == "c":
                    continue
                pp.need_inc = True
        inc_no = {e: 0 for e in ENGINES}
        inc_value = {}
        dval = {}
        for op in self.ops:
            if op.kind == "c" and op.need_inc:
                inc_no[op.eng] += 1
                inc_value[op.idx] = inc_no[op.eng]
            elif op.kind == "d":
                dval[op.idx] = dval.get(op.sem_key, 0) + 16
                dval[op.sem_key] = dval[op.idx]
            elif op.kind == "cc":
                dval[op.idx] = dval.get("__cc__", 0) + 1
                dval["__cc__"] = dval[op.idx]
        wm = {}
        pos = {}
        cnt = {e: 0 for e in ENGINES}
        for op in self.ops:
            pos[op.idx] = cnt[op.eng]
            cnt[op.eng] += 1
        drained = {e: -1 for e in ENGINES}
        for op in self.ops:
            for pp in edges.get(op.idx, ()):
                if pp.kind == "d":
                    k = ("dma", pp.sem_key)
                    v = dval[pp.idx]
                    if wm.get((op.eng,) + k, -1) < v:
                        op.waits.append((k, v))
                        wm[(op.eng,) + k] = v
                elif pp.kind == "cc":
                    v = dval[pp.idx]
                    if wm.get((op.eng, "cc"), -1) < v:
                        op.waits.append((("cc", "__cc__"), v))
                        wm[(op.eng, "cc")] = v
                elif pp.eng == op.eng and op.kind == "c":
                    if pp.eng != "tensor" and pos[pp.idx] > drained[pp.eng]:
                        op.drain_before = True
                else:
                    v = inc_value[pp.idx]
                    if wm.get((op.eng, "eng", pp.eng), -1) < v:
                        op.waits.append((("eng", pp.eng), v))
                        wm[(op.eng, "eng", pp.eng)] = v
            if op.drain_before:
                drained[op.eng] = pos[op.idx] - 1
            if len(op.waits) > 1:
                agg = {}
                for (k, v) in op.waits:
                    agg[k] = max(agg.get(k, 0), v)
                op.waits = list(agg.items())
        return inc_no

    def emit(self, stack):
        nc = self.nc
        self.schedule()
        if os.environ.get("PROG_DEBUG"):
            for k, name in self.dma_sem_of_buf.items():
                print("SEM", name, k)
        sems = {}
        for e in ENGINES:
            sems[("eng", e)] = stack.enter_context(nc.semaphore(f"pg_{e}"))
        for k, name in self.dma_sem_of_buf.items():
            sems[("dma", name)] = stack.enter_context(nc.semaphore(name))
        sems[("cc", "__cc__")] = stack.enter_context(nc.semaphore("ccsem"))
        by_eng = {e: [] for e in ENGINES}
        for op in self.ops:
            by_eng[op.eng].append(op)
        block = stack.enter_context(nc.Block())

        def runner(ename):
            def _run(hh):
                mysem = sems[("eng", ename)]
                for op in by_eng[ename]:
                    for (skey, val) in op.waits:
                        hh.wait_ge(sems[skey], val)
                    if op.drain_before:
                        hh.drain()
                    inst = op.fn(hh)
                    if op.kind == "d":
                        inst.then_inc(sems[("dma", op.sem_key)], 16)
                    elif op.kind == "cc":
                        inst.then_inc(sems[("cc", "__cc__")], 1)
                    elif op.need_inc:
                        inst.then_inc(mysem, 1)
            return _run

        for e in ENGINES:
            if by_eng[e]:
                getattr(block, e)(runner(e))


def bc_ap(handle, row, width):
    return bass.AP(handle, row * width, [[1, 1], [0, 128], [1, width]])


def hpk(g):
    return ((g // 2) % 2) * 64, (g % 2) * 256 + (g // 4) * 64



def build_program(T):
    NCH = T // 128
    TS = min(512, T)
    NT4 = T // TS
    TB = T // 4
    TSB = min(512, TB)
    NTB = TB // TSB
    NLVL = 7

    nc = bass.Bass("TRN2", target_bir_lowering=False)

    def din(name, shape, dt=F32):
        return nc.dram_tensor(name, shape, dt, kind="ExternalInput").ap()

    x_in = din("x_in", [D, T])
    role_in = din("role_in", [1, 1])
    trilm = din("trilm", [128, 128], F16)
    trium = din("trium", [128, 128], F16)
    mask_in = din("mask_in", [D, T], F16)
    lr_in = din("lr_in", [1, H])
    wqT = din("wqT", [D, D]); wkT = din("wkT", [D, D]); wvT = din("wvT", [D, D])
    woT = din("woT", [D, D])
    ln1w = din("ln1w", [1, D]); ln1b = din("ln1b", [1, D])
    ln2w = din("ln2w", [1, D]); ln2b = din("ln2b", [1, D])
    w1T = din("w1T", [D, FF]); b1v = din("b1v", [1, FF])
    w2T = din("w2T", [FF, D]); b2v = din("b2v", [1, D])
    WfcT = din("WfcT", [D, FF]); bfcv = din("bfcv", [1, FF])
    AfcT = din("AfcT", [D, 16]); BfcT = din("BfcT", [16, FF])
    WprojT = din("WprojT", [FF, D]); bprv = din("bprv", [1, D])
    AprojT = din("AprojT", [FF, 16]); BprojT = din("BprojT", [16, D])
    WiT = din("WiT", [D, 2 * DI])
    Wcv = din("Wcv", [DI, 4])
    bconv = din("bconv", [1, DI])
    WdtT = din("WdtT", [DI, DI]); bdtv = din("bdtv", [1, DI])
    WBT = din("WBT", [DI, DI]); WCT = din("WCT", [DI, DI])
    WoT1 = din("WoT1", [DI, D])
    A_log = din("A_log", [1, DI])
    w1fT = din("w1fT", [D, FF]); b1f = din("b1f", [1, FF])
    w2fT = din("w2fT", [FF, D]); b2f = din("b2f", [1, D])
    out_t = nc.dram_tensor("out", [D, T], F32, kind="ExternalOutput").ap()

    cc_s_in = nc.dram_tensor("cc_s_in", [128, 512], F32)
    cc_s_out = nc.dram_tensor("cc_s_out", [256, 512], F32)
    cc_e_in = nc.dram_tensor("cc_e_in", [4, 2 * DI], F32)
    cc_e_out = nc.dram_tensor("cc_e_out", [8, 2 * DI], F32)
    cc_h_in = nc.dram_tensor("cc_h_in", [128, 16], F32)
    cc_h_out = nc.dram_tensor("cc_h_out", [256, 16], F32)
    qmz_d = nc.dram_tensor("qmz_d", [128, 8 * NCH * 256], F16)
    s0_d = nc.dram_tensor("s0_d", [D, T], F16)
    s1_d = nc.dram_tensor("s1_d", [D, T], F16)
    y0_d = nc.dram_tensor("y0_d", [128, 16 * T], F16)
    gc_d = nc.dram_tensor("gc_d", [128, 16 * T], F16)
    w116_d = nc.dram_tensor("w116_d", [D, FF], F16)
    w216_d = nc.dram_tensor("w216_d", [FF, D], F16)
    rfc_d = nc.dram_tensor("rfc_d", [D, FF], F16)
    rpr_d = nc.dram_tensor("rpr_d", [FF, D], F16)
    q1_d = nc.dram_tensor("q1_d", [D, FF], F8)
    q2_d = nc.dram_tensor("q2_d", [FF, D], F8)
    wi16_d = nc.dram_tensor("wi16_d", [D, 2 * DI], F16)
    wdt16_d = nc.dram_tensor("wdt16_d", [DI, DI], F16)
    wb16_d = nc.dram_tensor("wb16_d", [DI, DI], F16)
    wc16_d = nc.dram_tensor("wc16_d", [DI, DI], F16)
    wo116_d = nc.dram_tensor("wo116_d", [DI, D], F16)

    p = Prog(nc)
    st_ = ExitStack()

    def sb(name, shape, dt):
        return st_.enter_context(nc.sbuf_tensor(name, shape, dt))

    def vt(name, shape, dt, backing, byte_off):
        h = nc.virtual_sbuf_tensor(name, shape, dt)
        nc._tensor(h.name, h.shape, h.dtype, type="SB")
        mloc = nc.lookup_mloc(h)
        mloc.allocated = not nc.target_bir_lowering
        mloc.addr = nc.lookup_mloc(backing).addr + byte_off
        return h

    with st_:
        A1 = sb("A1", [128, 8 * T], F16)
        A2 = sb("A2", [128, 8 * T], F16)
        A3 = sb("A3", [128, 8 * T], F16)
        A4 = sb("A4", [128, 8 * T], F16)
        ARC = sb("ARC", [128, 4 * T], F16)
        XNb = [sb("XN0", [128, 256], F16), sb("XN1", [128, 256], F16)]
        NTb = [sb("NT0", [128, 128], F16), sb("NT1", [128, 128], F16)]
        KQb = [sb("KQu0", [128, 128], F16), sb("KQu1", [128, 128], F16)]
        rows = sb("rows", [128, 3 * T], F32)
        bca = sb("bca", [128, 2 * T], F16)
        qmax = sb("qmax", [1, 132], F32)
        nopd = sb("nopd", [1, 2], F32)
        p.dummy = nopd[0:1, 0:2]
        onec = sb("onec", [128, 2], F16)
        bcols = sb("bcols", [128, 176], F32)
        lncol = sb("lncol", [128, 32], F32)
        rolec = sb("rolec", [128, 1], F32)
        wres = sb("wres", [128, 4096], F16)
        scr = sb("scr", [128, T], F16)
        scr3 = sb("scr3", [128, max(T // 2, 512)], F16)
        fscr = sb("fscr", [128, 4 * TB], F32)
        zbuf = sb("zbuf", [128, TB], F32)
        qsc = sb("qsc", [128, 12], F32)
        convw = sb("convw", [128, 64], F32)
        accol = sb("accol", [128, 16], F32)
        hcol = sb("hcol", [128, 16], F32)
        hini = sb("hini", [128, 16], F32)
        ucol = sb("ucol", [128, 16], F32)
        uprevc = sb("uprevc", [128, 16], F32)
        xce = sb("xce", [128, 16], F16)
        halos = sb("halos", [128, 64], F16)
        pb = [st_.enter_context(nc.psum_tensor(f"pb{i}", [128, 512], F32)) for i in range(8)]

        xs1v = vt("xs1v", [128, T], F16, A3, 8 * T)
        qmzv = vt("qmzv", [128, 2 * T], F16, A3, 10 * T)
        masks = vt("masks", [128, 256], F16, A3, 14 * T)
        if T >= 2048:
            sch = vt("sch", [128, 512], F32, A3, 8 * T)
            schb = vt("schb", [128, 512], F16, A3, 8 * T + 2048)
            wch = vt("wch", [128, 512], F32, A3, 8 * T + 3072)
            wchb = vt("wchb", [128, 512], F16, A3, 8 * T + 5120)
            sini = vt("sini", [128, 512], F32, A3, 8 * T + 6144)
            qbv = vt("qbv", [128, 2048], F16, A3, 8 * T + 8192)
        else:
            sch = sb("sch", [128, 512], F32)
            schb = sb("schb", [128, 512], F16)
            wch = sb("wch", [128, 512], F32)
            wchb = sb("wchb", [128, 512], F16)
            sini = sb("sini", [128, 512], F32)
            qbv = sb("qbv", [128, 2048], F16)
        w1v = vt("w1v", [128, 8 * TB], F16, rows, 4 * T)
        bcs = vt("bcs", [1, T], F16, rows, 8 * T)
        if 12 * T >= 2 * T + 11520 + 4 * T:
            lra_s = vt("lra_s", [16, T], F16, rows, 0)
            bpr_s = vt("bpr_s", [16, D], F16, rows, 2 * T)
            afc_s = vt("afc_s", [128, 128], F16, rows, 2 * T + 2048)
            apr_s = vt("apr_s", [128, 512], F16, rows, 2 * T + 2304)
            bfc_s = vt("bfc_s", [16, FF], F16, rows, 2 * T + 3328)
        else:
            lra_s = sb("lra_s", [16, T], F16)
            bpr_s = sb("bpr_s", [16, D], F16)
            afc_s = sb("afc_s", [128, 128], F16)
            apr_s = sb("apr_s", [128, 512], F16)
            bfc_s = sb("bfc_s", [16, FF], F16)
        rqv = vt("rqv", [128, T], F16, bca, 0)
        szv = vt("szv", [128, 8 * TB], F16, bca, 0)
        egv = (vt("egv", [4, 4096], F16, bca, 0) if 4 * T >= 8192
               else sb("egv", [4, 4096], F16))
        mkv = vt("mkv", [128, 8 * TB], F16, A2, 0)
        if 10 * T + 8192 <= 16 * T:
            hqb = vt("hqb", [128, 32 * TB], F8, A4, 0)
            w8s = vt("w8s", [128, 8192], F8, A4, 8 * T)
            xq8 = vt("xq8", [128, 8 * TB], F8, A4, 10 * T + 8192 - 2 * T)
        else:
            hqb = sb("hqb", [128, 32 * TB], F8)
            w8s = sb("w8s", [128, 8192], F8)
            xq8 = sb("xq8", [128, 8 * TB], F8)

        R_MAP = {0: (0, 0), 2: (0, 1),
                 1: (32, 0), 13: (32, 1), 3: (32, 2),
                 14: (64, 0), 15: (64, 1), 9: (64, 2),
                 8: (96, 0), 7: (96, 1)}

        def R(i):
            b, j = R_MAP[i]
            return rows[b:b + 1, j * T:(j + 1) * T]

        def dmal(dst, src, buf, queue="gpsimd", reads=(), tag=""):
            p.dma(queue, lambda h, d=dst, s=src: h.dma_start(out=d, in_=s),
                  reads=reads, writes=[buf], key=buf, tag=tag)

        def vop(fn, reads=(), writes=(), tag=""):
            p.c("vector", fn, reads, writes, tag)

        def aop(fn, reads=(), writes=(), tag=""):
            p.c("scalar", fn, reads, writes, tag)

        def top(fn, reads=(), writes=(), tag=""):
            p.c("tensor", fn, reads, writes, tag)

        def bcast(row_i, slot, tag):
            p.c("vector", lambda h, r=row_i: h.tensor_copy(bcs[0:1, 0:T], R(r)),
                reads=[rows], writes=[bcs], tag=tag + "c")
            p.dma("gpsimd", lambda h, s=slot: h.dma_start(
                out=bca[:, s * T:(s + 1) * T],
                in_=bass.AP(bcs, 0, [[1, 1], [0, 128], [1, T]])),
                reads=[bcs], writes=[(bca, slot)], key=(bca, slot), tag=tag)

        def rsqrt_row(v_row, o_row, tag):
            vop(lambda h: h.tensor_scalar_add(R(13), R(v_row), EPS),
                reads=[rows], writes=[rows], tag=tag + "a")
            vop(lambda h: h.reciprocal(R(13), R(13)),
                reads=[rows], writes=[rows], tag=tag + "r")
            aop(lambda h: h.activation(R(o_row), R(13), AF.Sqrt),
                reads=[rows], writes=[rows], tag=tag + "s")

        def negmul_tile(tag):
            vop(lambda h: h.scalar_tensor_tensor(
                bca[:, T:2 * T], bca[:, T:2 * T], -1.0, bca[:, 0:T],
                ALU.mult, ALU.mult),
                reads=[(bca, 0), (bca, 1)], writes=[(bca, 1)], tag=tag)

        def stats_of(srcbuf, src_key, m_row, v_row, bank, tag):
            skeys = list(src_key) if isinstance(src_key, list) else [src_key]
            mb_, mj = R_MAP[m_row]
            for t4 in range(NT4):
                for f in range(8):
                    top(lambda h, f_=f, t_=t4, s=srcbuf: h.matmul(
                        bank[0:1, 0:TS], onec[:, 0:1],
                        s[:, f_ * T + t_ * TS: f_ * T + (t_ + 1) * TS],
                        start=(f_ == 0), stop=(f_ == 7)),
                        reads=skeys + [onec], writes=[(bank, "st")], tag=f"{tag}s{f}.{t4}")
                vop(lambda h, t_=t4: h.tensor_scalar_mul(
                    rows[mb_:mb_ + 1, mj * T + t_ * TS: mj * T + (t_ + 1) * TS],
                    bank[0:1, 0:TS], 1.0 / D),
                    reads=[(bank, "st")], writes=[rows], tag=f"{tag}m{t4}")
            for t4 in range(NT4):
                for f in range(8):
                    aop(lambda h, f_=f, t_=t4, s=srcbuf: h.activation(
                        scr3[:, 0:TS], s[:, f_ * T + t_ * TS: f_ * T + (t_ + 1) * TS],
                        AF.Square),
                        reads=skeys, writes=[scr3], tag=f"{tag}q{f}.{t4}")
                    top(lambda h, f_=f: h.matmul(
                        bank[64:65, 0:TS], onec[:, 0:1], scr3[:, 0:TS],
                        start=(f_ == 0), stop=(f_ == 7)),
                        reads=[scr3, onec], writes=[(bank, "sq")], tag=f"{tag}k{f}.{t4}")
                vop(lambda h, t_=t4: h.tensor_scalar_mul(
                    rows[64:65, T + t_ * TS: T + (t_ + 1) * TS],
                    bank[64:65, 0:TS], 1.0 / D),
                    reads=[(bank, "sq")], writes=[rows], tag=f"{tag}e{t4}")
            vop(lambda h: h.tensor_mul(R(14), R(m_row), R(m_row)),
                reads=[rows], writes=[rows], tag=f"{tag}2")
            vop(lambda h: h.tensor_sub(R(v_row), R(15), R(14)),
                reads=[rows], writes=[rows], tag=f"{tag}v")

        def g3(srch, O_tot, row0, nrow, col0, W):
            return bass.AP(srch, row0 * O_tot + col0,
                           [[O_tot, 128], [128 * O_tot, nrow // 128], [1, W]])

        def d3(dsth, doff, nch, W, dcols):
            return bass.AP(dsth, doff, [[dcols, 128], [W, nch], [1, W]])

        def ldg(dsth, doff, dcols, srch, O_tot, row0, nrow, col0, W, buf,
                queue="gpsimd", tag="", rd=()):
            p.dma(queue, lambda h: h.dma_start(
                out=d3(dsth, doff, nrow // 128, W, dcols),
                in_=g3(srch, O_tot, row0, nrow, col0, W)),
                reads=tuple(rd), writes=[buf], key=buf, tag=tag)


        dmal(masks[:, 0:128], trilm, (masks, 0), tag="tril")
        dmal(masks[:, 128:256], trium, (masks, 1), tag="triu")
        p.dma("gpsimd", lambda h: h.dma_start(out=rolec[:, 0:1], in_=bc_ap(role_in.tensor, 0, 1)),
              writes=[rolec], tag="role")
        vop(lambda h: h.memset(onec[:, 0:2], 1.0), writes=[onec], tag="one")
        vop(lambda h: h.memset(zbuf[:, :], 0.0), writes=[zbuf], tag="zb")
        p.dma("gpsimd", lambda h: h.dma_start(out=bcols[:, 0:H], in_=bc_ap(lr_in.tensor, 0, H)),
              writes=[(bcols, "lr")], tag="lr")
        vop(lambda h: h.tensor_scalar_mul(bcols[:, 16:16 + H], bcols[:, 0:H], 0.01),
            reads=[(bcols, "lr")], writes=[(bcols, "beta")], tag="beta")

        def ldcols(off, src, nm, n=8, dst=None):
            d_ = lncol if dst is None else dst
            p.dma("gpsimd", lambda h, a=d_[:, off:off + n],
                  b=bass.AP(src.tensor, 0, [[1, 128], [128, n]]): h.dma_start(
                      out=a, in_=b, allow_slow_non_contiguous=True),
                  writes=[(d_, off)], key=(d_, off), tag=nm)
        ldcols(0, ln1w, "l1w"); ldcols(8, ln1b, "l1b")
        ldcols(16, ln2w, "l2w"); ldcols(24, ln2b, "l2b")

        xbuf = [scr, xs1v]
        for f in range(8):
            sl = f % 2
            dmal(xbuf[sl][:, 0:T], x_in[f * 128:(f + 1) * 128, :],
                 (xbuf[sl], "x"), tag=f"xs{f}")
            for t4 in range(NT4):
                top(lambda h, f_=f, t_=t4, s_=sl: h.matmul(
                    pb[t_][0:1, 0:TS], onec[:, 0:1],
                    xbuf[s_][:, t_ * TS:(t_ + 1) * TS],
                    start=(f_ == 0), stop=(f_ == 7)),
                    reads=[(xbuf[sl], "x"), onec], writes=[(pb[t4], "st")], tag=f"As{f}.{t4}")
            for t4 in range(NT4):
                aop(lambda h, s_=sl, t_=t4: h.activation(
                    scr3[:, 0:TS], xbuf[s_][:, t_ * TS:(t_ + 1) * TS], AF.Square),
                    reads=[(xbuf[sl], "x")], writes=[scr3], tag=f"Aq{f}.{t4}")
                top(lambda h, f_=f, t_=t4: h.matmul(
                    pb[t_][64:65, 0:TS], onec[:, 0:1],
                    scr3[:, 0:TS],
                    start=(f_ == 0), stop=(f_ == 7)),
                    reads=[scr3, onec], writes=[(pb[t4], "st")], tag=f"Ak{f}.{t4}")
        for t4 in range(NT4):
            vop(lambda h, t_=t4: h.tensor_scalar_mul(
                rows[0:1, t_ * TS:(t_ + 1) * TS], pb[t_][0:1, 0:TS], 1.0 / D),
                reads=[(pb[t4], "st")], writes=[rows], tag=f"Am{t4}")
            vop(lambda h, t_=t4: h.tensor_scalar_mul(
                rows[64:65, T + t_ * TS: T + (t_ + 1) * TS],
                pb[t_][64:65, 0:TS], 1.0 / D),
                reads=[(pb[t4], "st")], writes=[rows], tag=f"Ae{t4}")
        vop(lambda h: h.tensor_mul(R(14), R(0), R(0)),
            reads=[rows], writes=[rows], tag="A2")
        vop(lambda h: h.tensor_sub(R(1), R(15), R(14)),
            reads=[rows], writes=[rows], tag="Av")
        rsqrt_row(1, 2, "rs1")
        vop(lambda h: h.tensor_scalar_add(R(13), R(1), EPS),
            reads=[rows], writes=[rows], tag="r2a")
        vop(lambda h: h.reciprocal(R(13), R(13)),
            reads=[rows], writes=[rows], tag="r2b")
        vop(lambda h: h.tensor_mul(R(13), R(1), R(13)),
            reads=[rows], writes=[rows], tag="r2c")
        vop(lambda h: h.tensor_scalar_add(R(13), R(13), EPS),
            reads=[rows], writes=[rows], tag="r2d")
        vop(lambda h: h.reciprocal(R(13), R(13)),
            reads=[rows], writes=[rows], tag="r2e")
        aop(lambda h: h.activation(R(3), R(13), AF.Sqrt),
            reads=[rows], writes=[rows], tag="r2f")
        bcast(2, 0, "b_r1")
        bcast(3, 1, "b_r2")
        vop(lambda h: h.tensor_mul(bca[:, 0:T], bca[:, 0:T], bca[:, T:2 * T]),
            reads=[(bca, 0), (bca, 1)], writes=[(bca, 0)], tag="al1t")
        bcast(0, 1, "b_m1")
        negmul_tile("ma1t")
        SW3 = max(T // 2, 512)
        for f in range(8):
            sl = f % 2
            dmal(xbuf[sl][:, 0:T], x_in[f * 128:(f + 1) * 128, :],
                 (xbuf[sl], "x"), tag=f"xt{f}")
            for c2 in range(0, T, SW3):
                W3 = min(SW3, T - c2)
                vop(lambda h, s_=sl, c_=c2, W_=W3: h.tensor_mul(
                    scr3[:, 0:W_], xbuf[s_][:, c_:c_ + W_], bca[:, c_:c_ + W_]),
                    reads=[(xbuf[sl], "x"), (bca, 0)], writes=[scr3], tag=f"ta{f}.{c2}")
                vop(lambda h, c_=c2, W_=W3: h.tensor_add(
                    scr3[:, 0:W_], scr3[:, 0:W_], bca[:, T + c_:T + c_ + W_]),
                    reads=[scr3, (bca, 1)], writes=[scr3], tag=f"tb{f}.{c2}")
                vop(lambda h, f_=f, c_=c2, W_=W3: h.tensor_scalar(
                    A1[:, f_ * T + c_:f_ * T + c_ + W_], scr3[:, 0:W_],
                    lncol[:, f_:f_ + 1], lncol[:, 8 + f_:9 + f_], ALU.mult, ALU.add),
                    reads=[scr3, (lncol, 0), (lncol, 8)], writes=[(A1, "t0c")], tag=f"tc{f}.{c2}")
        bcast(2, 0, "b_rs1")
        bcast(0, 1, "b_m1b")
        negmul_tile("mr1t")

        kvt_off = 2 * T
        hal_off = 0
        gal_off = 4 * T

        for hp in range(8):
            sq, sk, sv_ = [((hp * 3 + j) % 4) * 1024 for j in range(3)]
            for (so, wT_ap, nm) in ((sq, wqT, "q"), (sk, wkT, "k"), (sv_, wvT, "v")):
                ldg(wres, so, 4096, wT_ap.tensor, D, 0, D, hp * 128, 128,
                    ("wq", so // 1024), tag=f"w{nm}{hp}")
            for (so, dsto, nm, bank) in ((sq, 0, "q", pb[0]), (sk, T, "k", pb[1])):
                for t4 in range(NT4):
                    for f in range(8):
                        top(lambda h, f_=f, t_=t4, w_=so, bk=bank: h.matmul(
                            bk[:, 0:TS], wres[:, w_ + f_ * 128: w_ + (f_ + 1) * 128],
                            A1[:, f_ * T + t_ * TS: f_ * T + (t_ + 1) * TS],
                            start=(f_ == 0), stop=(f_ == 7)),
                            reads=[("wq", so // 1024), (A1, "t0c")], writes=[(bank, "qk")],
                            tag=f"{nm}{hp}.{t4}.{f}")
                    vop(lambda h, t_=t4, o_=dsto, bk=bank: h.tensor_copy(
                        A3[:, o_ + t_ * TS: o_ + (t_ + 1) * TS], bk[:, 0:TS]),
                        reads=[(bank, "qk")], writes=[(A3, "qk", dsto)], tag=f"{nm}e{hp}.{t4}")
            def kvt_block(c):
                for f in range(8):
                    top(lambda h, c_=c, f_=f, w_=sk: h.matmul(
                        pb[2][:, 0:128],
                        A1[:, f_ * T + c_ * 128: f_ * T + (c_ + 1) * 128],
                        wres[:, w_ + f_ * 128: w_ + (f_ + 1) * 128],
                        start=(f_ == 0), stop=(f_ == 7)),
                        reads=[("wq", sk // 1024), (A1, "t0c")], writes=[(pb[2], "kt")],
                        tag=f"kt{hp}.{c}.{f}")
                for f in range(8):
                    top(lambda h, c_=c, f_=f, w_=sv_: h.matmul(
                        pb[2][:, 128:256],
                        A1[:, f_ * T + c_ * 128: f_ * T + (c_ + 1) * 128],
                        wres[:, w_ + f_ * 128: w_ + (f_ + 1) * 128],
                        start=(f_ == 0), stop=(f_ == 7)),
                        reads=[("wq", sv_ // 1024), (A1, "t0c")], writes=[(pb[2], "kt")],
                        tag=f"vt{hp}.{c}.{f}")
                vop(lambda h, c_=c: h.tensor_copy(
                    A3[:, kvt_off + c_ * 256: kvt_off + (c_ + 1) * 256], pb[2][:, 0:256]),
                    reads=[(pb[2], "kt")], writes=[(A3, "kvt", c)], tag=f"kv{hp}.{c}")

            def unit_steps(c, hh):
                hg = hp * 2 + hh
                Xn, Nt, Kq = XNb[hh], NTb[hh], KQb[hh]
                pG = pb[3 + hh]
                pA = pb[5 + hh]
                pD = pb[7] if hh == 0 else pb[0]
                kfm = A3[hh * 64:hh * 64 + 64, T + c * 128: T + (c + 1) * 128]
                qfm = A3[hh * 64:hh * 64 + 64, c * 128:(c + 1) * 128]
                ktm = A3[:, kvt_off + c * 256 + hh * 64: kvt_off + c * 256 + hh * 64 + 64]
                vtm = A3[:, kvt_off + c * 256 + 128 + hh * 64: kvt_off + c * 256 + 128 + hh * 64 + 64]
                bco = bcols[:, 16 + hg:17 + hg]
                pp_, cc_ = hpk(hg)
                steps = []

                def s_gram():
                    top(lambda h, a=kfm, g=pG: h.matmul(g[:, 0:128], a, a, start=True, stop=True),
                        reads=[(A3, "qk", T)], writes=[(pG, "g")], tag=f"KK{hp}.{c}.{hh}")
                    top(lambda h, a=kfm, b=qfm, g=pG: h.matmul(g[:, 128:256], a, b, start=True, stop=True),
                        reads=[(A3, "qk", T), (A3, "qk", 0)], writes=[(pG, "g")], tag=f"KQ{hp}.{c}.{hh}")
                steps.append(s_gram)

                def s_mask():
                    vop(lambda h, g=pG, b=bco, X=Xn: h.scalar_tensor_tensor(
                        X[:, 128:256], g[:, 0:128], b, masks[:, 0:128], ALU.mult, ALU.mult),
                        reads=[(pG, "g"), (masks, 0), (bcols, "beta")], writes=[Xn], tag=f"mN{hp}.{c}.{hh}")
                    vop(lambda h, g=pG, b=bco, N=Nt: h.scalar_tensor_tensor(
                        N[:, :], g[:, 0:128], b, masks[:, 128:256], ALU.mult, ALU.mult),
                        reads=[(pG, "g"), (masks, 1), (bcols, "beta")], writes=[Nt], tag=f"mT{hp}.{c}.{hh}")
                    vop(lambda h, g=pG, K=Kq: h.tensor_mul(K[:, :], g[:, 128:256], masks[:, 128:256]),
                        reads=[(pG, "g"), (masks, 1)], writes=[Kq], tag=f"mQ{hp}.{c}.{hh}")
                    vop(lambda h, a=ktm, b=bco, X=Xn: h.tensor_scalar_mul(X[:, 0:64], a, b),
                        reads=[(A3, "kvt", c), (bcols, "beta")], writes=[Xn], tag=f"Xa{hp}.{c}.{hh}")
                    vop(lambda h, a=vtm, b=bco, X=Xn: h.tensor_scalar_mul(X[:, 64:128], a, b),
                        reads=[(A3, "kvt", c), (bcols, "beta")], writes=[Xn], tag=f"Xb{hp}.{c}.{hh}")
                steps.append(s_mask)

                for k in range(NLVL):
                    last = (k == NLVL - 1)
                    ncols = 128 if last else 256

                    def s_mm(k=k, last=last, ncols=ncols):
                        top(lambda h, n_=ncols, N=Nt, X=Xn, A=pA: h.matmul(
                            A[:, 0:n_], N[:, :], X[:, 0:n_], start=True, stop=True),
                            reads=[Nt, Xn], writes=[(pA, "a")], tag=f"sA{hp}.{c}.{hh}.{k}")
                        if not last:
                            top(lambda h, N=Nt, X=Xn, A=pA: h.matmul(
                                A[:, 256:384], X[:, 128:256], N[:, :], start=True, stop=True),
                                reads=[Xn, Nt], writes=[(pA, "a")], tag=f"sB{hp}.{c}.{hh}.{k}")
                    steps.append(s_mm)

                    def s_up(k=k, last=last):
                        xo = ALU.subtract if k == 0 else ALU.add
                        vop(lambda h, o=xo, X=Xn, A=pA: h.tensor_tensor(
                            X[:, 0:128], X[:, 0:128], A[:, 0:128], o),
                            reads=[(pA, "a"), Xn], writes=[Xn], tag=f"sX{hp}.{c}.{hh}.{k}")
                        if not last:
                            vop(lambda h, X=Xn, A=pA: h.tensor_copy(X[:, 128:256], A[:, 128:256]),
                                reads=[(pA, "a")], writes=[Xn], tag=f"sN{hp}.{c}.{hh}.{k}")
                            vop(lambda h, N=Nt, A=pA: h.tensor_copy(N[:, :], A[:, 256:384]),
                                reads=[(pA, "a")], writes=[Nt], tag=f"sT{hp}.{c}.{hh}.{k}")
                    steps.append(s_up)

                def s_hg():
                    top(lambda h, a=ktm, X=Xn, dd=pD: h.matmul(dd[0:64, 0:64], X[:, 0:64], a,
                                                               start=True, stop=True),
                        reads=[Xn, (A3, "kvt", c)], writes=[(pD, "d")], tag=f"Ht{hp}.{c}.{hh}")
                    top(lambda h, a=ktm, X=Xn, dd=pD: h.matmul(dd[0:64, 64:128], a, X[:, 64:128],
                                                               start=True, stop=True),
                        reads=[Xn, (A3, "kvt", c)], writes=[(pD, "d")], tag=f"Gt{hp}.{c}.{hh}")
                steps.append(s_hg)

                def s_hge():
                    vop(lambda h, P_=pp_, C_=cc_, c_=c, dd=pD: h.tensor_copy(
                        A4[P_:P_ + 64, hal_off + c_ * 512 + C_: hal_off + c_ * 512 + C_ + 64],
                        dd[0:64, 0:64]),
                        reads=[(pD, "d")], writes=[(A4, "hal")], tag=f"He{hp}.{c}.{hh}")
                    vop(lambda h, P_=pp_, C_=cc_, c_=c, dd=pD: h.tensor_copy(
                        A4[P_:P_ + 64, gal_off + c_ * 512 + C_: gal_off + c_ * 512 + C_ + 64],
                        dd[0:64, 64:128]),
                        reads=[(pD, "d")], writes=[(A4, "gal")], tag=f"Ge{hp}.{c}.{hh}")
                steps.append(s_hge)

                def s_zo():
                    top(lambda h, X=Xn, K=Kq, dd=pD: h.matmul(dd[0:64, 128:256], X[:, 0:64], K[:, :],
                                                              start=True, stop=True),
                        reads=[Xn, Kq], writes=[(pD, "d")], tag=f"Z{hp}.{c}.{hh}")
                    top(lambda h, X=Xn, K=Kq, dd=pD: h.matmul(dd[0:64, 256:384], X[:, 64:128], K[:, :],
                                                              start=True, stop=True),
                        reads=[Xn, Kq], writes=[(pD, "d")], tag=f"Op{hp}.{c}.{hh}")
                steps.append(s_zo)

                def s_ze():
                    vop(lambda h, q=qfm, P_=pp_, g=hg, c_=c, dd=pD: h.tensor_sub(
                        qmzv[P_:P_ + 64, c_ * 256 + (g % 2) * 128: c_ * 256 + (g % 2) * 128 + 128],
                        q, dd[0:64, 128:256]),
                        reads=[(pD, "d"), (A3, "qk", 0)], writes=[(qmzv, "qmz")], tag=f"QZ{hp}.{c}.{hh}")
                    vop(lambda h, g=hg, c_=c, dd=pD: h.tensor_copy(
                        A2[(g % 2) * 64:(g % 2) * 64 + 64,
                           (g // 2) * T + c_ * 128:(g // 2) * T + (c_ + 1) * 128],
                        dd[0:64, 256:384]),
                        reads=[(pD, "d")], writes=[(A2, "oall")], tag=f"Oe{hp}.{c}.{hh}")
                steps.append(s_ze)
                return steps

            kvt_block(0)
            for c in range(NCH):
                s0 = unit_steps(c, 0)
                s1 = unit_steps(c, 1)
                for si, (a, b) in enumerate(zip(s0, s1)):
                    a()
                    if si == 1 and c + 1 < NCH:
                        kvt_block(c + 1)
                    b()
            hpp = (hp % 2) * 64
            p.dma("sync", lambda h, hp_=hp, pp2=hpp: h.dma_start(
                out=qmz_d.ap()[pp2:pp2 + 64, hp_ * NCH * 256:(hp_ + 1) * NCH * 256],
                in_=qmzv[pp2:pp2 + 64, 0:NCH * 256]),
                reads=[(qmzv, "qmz")], writes=[("d", "qmz")], key=("d", "qmz"), tag=f"qz{hp}")

        p.fence([("d", "qmz")], [sch, schb, wch, wchb, sini, "xs1v"], tag="fchain")
        vop(lambda h: h.memset(sch[:, :], 0.0), writes=[sch], tag="Sz")
        vop(lambda h: h.memset(schb[:, :], 0.0), writes=[schb], tag="Szb")
        for c in range(NCH):
            vop(lambda h, c_=c: h.tensor_copy(ARC[:, c_ * 512:(c_ + 1) * 512], schb[:, :]),
                reads=[schb], writes=[(ARC, "s0log")], tag=f"sl{c}")
            for g in range(H):
                bk = pb[g % 2]
                pg, cg = hpk(g)
                top(lambda h, g_=g, c_=c, b_=bk, P_=pg, C_=cg: h.matmul(
                    b_[P_:P_ + 64, (g_ // 4) * 64:(g_ // 4) * 64 + 64],
                    A4[P_:P_ + 64, hal_off + c_ * 512 + C_: hal_off + c_ * 512 + C_ + 64],
                    schb[P_:P_ + 64, C_:C_ + 64],
                    start=True, stop=True),
                    reads=[(A4, "hal"), schb], writes=[(bk, "hs")], tag=f"HS{c}.{g}")
            vop(lambda h: h.tensor_sub(sch[:, 0:256], sch[:, 0:256], pb[0][:, 0:256]),
                reads=[(pb[0], "hs"), sch], writes=[sch], tag=f"Su0.{c}")
            vop(lambda h: h.tensor_sub(sch[:, 256:512], sch[:, 256:512], pb[1][:, 0:256]),
                reads=[(pb[1], "hs"), sch], writes=[sch], tag=f"Su1.{c}")
            vop(lambda h, c_=c: h.tensor_add(
                sch[:, :], sch[:, :], A4[:, gal_off + c_ * 512: gal_off + (c_ + 1) * 512]),
                reads=[(A4, "gal"), sch], writes=[sch], tag=f"Sg{c}")
            vop(lambda h: h.tensor_copy(schb[:, :], sch[:, :]), reads=[sch], writes=[schb],
                tag=f"Sc{c}")
        p.dma("sync", lambda h: h.dma_start(out=cc_s_in.ap(), in_=sch[:, :]),
              reads=[sch], writes=[("d", "ccs")], tag="sstg")
        p.cc(lambda h: h.collective_compute(
            "AllGather", ALU.bypass, replica_groups=GROUPS,
            ins=[cc_s_in.ap().opt()], outs=[cc_s_out.ap().opt()]),
            reads=[("d", "ccs")], writes=[("d", "ccso")], tag="sag")
        p.dma("sync", lambda h: h.dma_start(out=sini[:, :], in_=cc_s_out.ap()[0:128, :]),
              reads=[("d", "ccso")], writes=[sini], tag="srcv")
        vop(lambda h: h.tensor_scalar_mul(sini[:, :], sini[:, :], rolec[:, 0:1]),
            reads=[sini, rolec], writes=[sini], tag="smsk")
        vop(lambda h: h.tensor_copy(wch[:, :], sini[:, :]), reads=[sini], writes=[wch], tag="W0")
        vop(lambda h: h.tensor_copy(wchb[:, :], wch[:, :]), reads=[wch], writes=[wchb], tag="W0b")
        for c in range(NCH):
            vop(lambda h, c_=c: h.tensor_add(schb[:, :], ARC[:, c_ * 512:(c_ + 1) * 512], wchb[:, :]),
                reads=[(ARC, "s0log"), wchb], writes=[schb], tag=f"S0c{c}")
            p.dma("gpsimd", lambda h, c_=c: h.dma_start(
                out=bass.AP(qbv, 0, [[2048, 64], [512, 4], [1, 256]]),
                in_=bass.AP(qmz_d.ap().tensor, c_ * 256,
                            [[8 * NCH * 256, 64], [2 * NCH * 256, 4], [1, 256]])),
                reads=[("d", "qmz")], writes=[(qbv, "qb")], key=(qbv, "qb"), tag=f"qr{c}")
            p.dma("gpsimd", lambda h, c_=c: h.dma_start(
                out=bass.AP(qbv, 64 * 2048 + 256, [[2048, 64], [512, 4], [1, 256]]),
                in_=bass.AP(qmz_d.ap().tensor,
                            64 * 8 * NCH * 256 + NCH * 256 + c_ * 256,
                            [[8 * NCH * 256, 64], [2 * NCH * 256, 4], [1, 256]])),
                reads=[("d", "qmz")], writes=[(qbv, "qb")], key=(qbv, "qb"), tag=f"qro{c}")
            for g in range(H):
                bk = pb[2 + g % 2]
                pg, cg = hpk(g)
                top(lambda h, g_=g, b_=bk, P_=pg, C_=cg: h.matmul(
                    b_[P_:P_ + 64, (g_ // 4) * 128:(g_ // 4) * 128 + 128],
                    schb[P_:P_ + 64, C_:C_ + 64],
                    qbv[P_:P_ + 64, (g_ // 2) * 256 + (g_ % 2) * 128: (g_ // 2) * 256 + (g_ % 2) * 128 + 128],
                    start=True, stop=True),
                    reads=[schb, (qbv, "qb")], writes=[(bk, "of")], tag=f"Of{c}.{g}")
            for g in range(H):
                bk = pb[2 + g % 2]
                pg, cg = hpk(g)
                vop(lambda h, g_=g, c_=c, b_=bk, P_=pg: h.tensor_add(
                    A2[(g_ % 2) * 64:(g_ % 2) * 64 + 64,
                       (g_ // 2) * T + c_ * 128:(g_ // 2) * T + (c_ + 1) * 128],
                    A2[(g_ % 2) * 64:(g_ % 2) * 64 + 64,
                       (g_ // 2) * T + c_ * 128:(g_ // 2) * T + (c_ + 1) * 128],
                    b_[P_:P_ + 64, (g_ // 4) * 128:(g_ // 4) * 128 + 128]),
                    reads=[(bk, "of"), (A2, "oall")], writes=[(A2, "oall")], tag=f"Oa{c}.{g}")
            if c < NCH - 1:
                for g in range(H):
                    bk = pb[g % 2]
                    pg, cg = hpk(g)
                    top(lambda h, g_=g, c_=c, b_=bk, P_=pg, C_=cg: h.matmul(
                        b_[P_:P_ + 64, (g_ // 4) * 64:(g_ // 4) * 64 + 64],
                        A4[P_:P_ + 64, hal_off + c_ * 512 + C_: hal_off + c_ * 512 + C_ + 64],
                        wchb[P_:P_ + 64, C_:C_ + 64],
                        start=True, stop=True),
                        reads=[(A4, "hal"), wchb], writes=[(bk, "hw")], tag=f"HW{c}.{g}")
                vop(lambda h: h.tensor_sub(wch[:, 0:256], wch[:, 0:256], pb[0][:, 0:256]),
                    reads=[(pb[0], "hw"), wch], writes=[wch], tag=f"Wu0.{c}")
                vop(lambda h: h.tensor_sub(wch[:, 256:512], wch[:, 256:512], pb[1][:, 0:256]),
                    reads=[(pb[1], "hw"), wch], writes=[wch], tag=f"Wu1.{c}")
                vop(lambda h: h.tensor_copy(wchb[:, :], wch[:, :]), reads=[wch], writes=[wchb],
                    tag=f"Wb{c}")

        p.fence([(A3, "qk", 0), (A3, "qk", T), (A3, "kvt")],
                [(A3, "x"), "xs1v", "qmzv", "masks", sch, schb, wch, wchb, sini,
                 "qbv"], tag="fA3x")
        p.fence([], [("wq", 0), ("wq", 1), ("wq", 2), ("wq", 3),
                     ("ws", 0), ("ws", 1)], tag="fwres")
        for f in range(8):
            dmal(A3[:, f * T:(f + 1) * T], x_in[f * 128:(f + 1) * 128, :], (A3, "x", f), tag=f"xr{f}")
        p.fence([(A1, "t0c")], [(A1, "xnc")], tag="fA1x")
        for f in range(8):
            for c2 in range(0, T, SW3):
                W3 = min(SW3, T - c2)
                vop(lambda h, f_=f, c_=c2, W_=W3: h.tensor_mul(
                    scr3[:, 0:W_], A3[:, f_ * T + c_:f_ * T + c_ + W_], bca[:, c_:c_ + W_]),
                    reads=[(A3, "x", f), (bca, 0)], writes=[scr3], tag=f"xna{f}.{c2}")
                vop(lambda h, f_=f, c_=c2, W_=W3: h.tensor_add(
                    A1[:, f_ * T + c_:f_ * T + c_ + W_], scr3[:, 0:W_], bca[:, T + c_:T + c_ + W_]),
                    reads=[scr3, (bca, 1)], writes=[(A1, "xnc")], tag=f"xnb{f}.{c2}")
        p.fence([(A4, "hal"), (A4, "gal")], [(A4, "u")], tag="fA4u")
        for dg in range(4):
            so = (dg % 2) * 2048
            ldg(wres, so, 4096, woT.tensor, D, 0, D, dg * 256, 256, ("ws", so // 2048), tag=f"lwo{dg}")
            for oi in range(2):
                dt_i = dg * 2 + oi
                for t4 in range(NT4):
                    for f in range(8):
                        top(lambda h, f_=f, t_=t4, o_=oi, s_=so: h.matmul(
                            pb[4][:, 0:TS], wres[:, s_ + f_ * 256 + o_ * 128: s_ + f_ * 256 + (o_ + 1) * 128],
                            A2[:, f_ * T + t_ * TS: f_ * T + (t_ + 1) * TS],
                            start=(f_ == 0), stop=(f_ == 7)),
                            reads=[("ws", so // 2048), (A2, "oall")], writes=[(pb[4], "wo")],
                            tag=f"wo{dt_i}.{t4}.{f}")
                    vop(lambda h, d_=dt_i, t_=t4: h.tensor_add(
                        A4[:, d_ * T + t_ * TS: d_ * T + (t_ + 1) * TS], pb[4][:, 0:TS],
                        A1[:, d_ * T + t_ * TS: d_ * T + (t_ + 1) * TS]),
                        reads=[(pb[4], "wo"), (A1, "xnc")], writes=[(A4, "u")], tag=f"u{dt_i}.{t4}")
        stats_of(A4, (A4, "u"), 7, 8, pb[0], "su")
        rsqrt_row(8, 9, "al2")
        bcast(9, 0, "b_al2")
        bcast(7, 1, "b_m2")
        negmul_tile("ma2t")
        p.dma("gpsimd", lambda h, a=bcols[:, 32:64], b=bass.AP(b1v.tensor, 0, [[1, 128], [128, 32]]): h.dma_start(out=a, in_=b, allow_slow_non_contiguous=True), writes=[(bcols, "b1")], key=(bcols, "b1"), tag="b1")
        p.dma("gpsimd", lambda h, a=bcols[:, 64:72], b=bass.AP(b2v.tensor, 0, [[1, 128], [128, 8]]): h.dma_start(out=a, in_=b, allow_slow_non_contiguous=True), writes=[(bcols, "b2")], key=(bcols, "b2"), tag="b2")
        def bounce16(srcT, dstD, K_, O_, nn):
            for kk in range(4):
                R4 = K_ // 4
                p.dma("gpsimd", lambda h, k_=kk, D_=dstD, R_=R4: h.dma_start(
                    out=D_.ap()[k_ * R_:(k_ + 1) * R_, :],
                    in_=srcT[k_ * R_:(k_ + 1) * R_, :]),
                    writes=[("d", nn)], key=("d", nn), tag=f"{nn}s{kk}")
        bounce16(w1T, w116_d, D, FF, "w1c")
        bounce16(w2T, w216_d, FF, D, "w2c")
        p.fence([(A2, "oall")], [(A2, "ghm")], tag="fA2g")
        p.fence([], [rows, "w1v"], tag="fw1v")
        for tb in range(4):
            for f in range(8):
                vop(lambda h, f_=f, b_=tb: h.tensor_mul(
                    scr3[:, 0:TB], A4[:, f_ * T + b_ * TB: f_ * T + (b_ + 1) * TB], bca[:, b_ * TB:(b_ + 1) * TB]),
                    reads=[(A4, "u"), (bca, 0)], writes=[scr3], tag=f"wia{tb}.{f}")
                vop(lambda h, f_=f, b_=tb: h.tensor_add(
                    scr3[:, 0:TB], scr3[:, 0:TB], bca[:, T + b_ * TB: T + (b_ + 1) * TB]),
                    reads=[scr3, (bca, 1)], writes=[scr3], tag=f"wib{tb}.{f}")
                vop(lambda h, f_=f: h.tensor_scalar(
                    w1v[:, f_ * TB:(f_ + 1) * TB], scr3[:, 0:TB],
                    lncol[:, 16 + f_:17 + f_], lncol[:, 24 + f_:25 + f_], ALU.mult, ALU.add),
                    reads=[scr3, (lncol, 16), (lncol, 24)], writes=[(w1v, "w1in", tb)], tag=f"wic{tb}.{f}")
            for hgrp in range(16):
                so = (hgrp % 2) * 2048
                ldg(wres, so, 4096, w116_d, FF, 0, D, hgrp * 256, 256,
                    ("ws", so // 2048), tag=f"lw1{tb}.{hgrp}", rd=[("d", "w1c")])
                for ho in range(2):
                    ht = hgrp * 2 + ho
                    wb = pb[5 if ht % 2 == 0 else 7]
                    for t4 in range(NTB):
                        for f in range(8):
                            top(lambda h, f_=f, o_=ho, t_=t4, s_=so, b_=wb: h.matmul(
                                b_[:, 0:TSB],
                                wres[:, s_ + f_ * 256 + o_ * 128: s_ + f_ * 256 + (o_ + 1) * 128],
                                w1v[:, f_ * TB + t_ * TSB: f_ * TB + (t_ + 1) * TSB],
                                start=(f_ == 0), stop=(f_ == 7)),
                                reads=[("ws", so // 2048), (w1v, "w1in", tb)], writes=[(wb, "w1")],
                                tag=f"w1m{tb}.{ht}.{t4}.{f}")
                        aop(lambda h, h_=ht, t_=t4, b_=wb: h.activation(
                            A2[:, h_ * TB + t_ * TSB: h_ * TB + (t_ + 1) * TSB], b_[:, 0:TSB],
                            AF.Gelu, bias=bcols[:, 32 + h_:33 + h_]),
                            reads=[(wb, "w1"), (bcols, "b1")], writes=[(A2, "ghm")],
                            tag=f"w1g{tb}.{ht}.{t4}")
            for dt_i in range(8):
                w2b = pb[6 if dt_i % 2 == 0 else 2]
                for hf in range(2):
                    so = ((dt_i * 2 + hf) % 2) * 2048
                    ldg(wres, so, 4096, w216_d, D, hf * 2048, 2048, dt_i * 128, 128,
                        ("ws", so // 2048), tag=f"lw2{tb}.{dt_i}.{hf}", rd=[("d", "w2c")])
                    for t4 in range(NTB):
                        for hc in range(16):
                            top(lambda h, hc_=hc, t_=t4, h2=hf, s_=so, b_=w2b: h.matmul(
                                b_[:, 0:TSB],
                                wres[:, s_ + hc_ * 128: s_ + (hc_ + 1) * 128],
                                A2[:, (h2 * 16 + hc_) * TB + t_ * TSB: (h2 * 16 + hc_) * TB + (t_ + 1) * TSB],
                                start=(h2 == 0 and hc_ == 0), stop=(h2 == 1 and hc_ == 15)),
                                reads=[("ws", so // 2048), (A2, "ghm")], writes=[(w2b, "w2")],
                                tag=f"w2m{tb}.{dt_i}.{hf}.{t4}.{hc}")
                if True:
                    for t4 in range(NTB):
                        aop(lambda h, d_=dt_i, t_=t4, b_=w2b: h.activation(
                            scr3[:, 0:TSB], b_[:, 0:TSB], AF.Identity,
                            bias=bcols[:, 64 + d_:65 + d_]),
                            reads=[(w2b, "w2"), (bcols, "b2")], writes=[scr3], tag=f"s0a{tb}.{dt_i}.{t4}")
                        vop(lambda h, d_=dt_i, t_=t4, b_=tb: h.tensor_add(
                            scr[:, 0:TSB], scr3[:, 0:TSB],
                            A4[:, d_ * T + b_ * TB + t_ * TSB: d_ * T + b_ * TB + (t_ + 1) * TSB]),
                            reads=[scr3, (A4, "u")], writes=[(scr, "s0")], tag=f"s0b{tb}.{dt_i}.{t4}")
                        p.dma("sync", lambda h, d_=dt_i, t_=t4, b_=tb: h.dma_start(
                            out=s0_d.ap()[d_ * 128:(d_ + 1) * 128,
                                          b_ * TB + t_ * TSB: b_ * TB + (t_ + 1) * TSB],
                            in_=scr[:, 0:TSB]),
                            reads=[(scr, "s0")], writes=[("d", "s0")], key=("d", "s0"),
                            tag=f"s0d{tb}.{dt_i}.{t4}")

        def quant_prep(wT_ap, K_, O_, scol, dstD, nn):
            PC = min(O_, T)
            vop(lambda h: h.memset(qsc[:, scol + 2:scol + 3], 0.0), writes=[(qsc, scol, "mx")],
                tag=nn + "z")
            for kk in range(K_ // 128):
                for ob in range(0, O_, PC):
                    dmal(scr[:, 0:PC], wT_ap[kk * 128:(kk + 1) * 128, ob:ob + PC], (scr, "wc"),
                         tag=f"{nn}a{kk}.{ob}")
                    vop(lambda h, P_=PC: h.reduce_max(qsc[:, scol + 3:scol + 4], scr[:, 0:P_],
                                                      axis=AX.X, apply_absolute_value=True),
                        reads=[(scr, "wc")], writes=[(qsc, scol, "t")], tag=f"{nn}r{kk}.{ob}")
                    vop(lambda h: h.tensor_max(qsc[:, scol + 2:scol + 3], qsc[:, scol + 2:scol + 3],
                                               qsc[:, scol + 3:scol + 4]),
                        reads=[(qsc, scol, "t"), (qsc, scol, "mx")], writes=[(qsc, scol, "mx")],
                        tag=f"{nn}m{kk}.{ob}")
            p.dma("gpsimd", lambda h: h.dma_start(
                out=qmax[0:1, 4:132],
                in_=bass.AP(qsc, scol + 2, [[12, 128], [1, 1]]),
                allow_slow_non_contiguous=True),
                reads=[(qsc, scol, "mx")], writes=[qmax], key=qmax, tag=f"{nn}tr")
            vop(lambda h: h.reduce_max(qmax[0:1, 0:1], qmax[0:1, 4:132], axis=AX.X),
                reads=[qmax], writes=[qmax], tag=f"{nn}rm")
            p.dma("gpsimd", lambda h: h.dma_start(
                out=qsc[:, scol + 4:scol + 5],
                in_=bass.AP(qmax, 0, [[1, 1], [0, 128], [1, 1]])),
                reads=[qmax], writes=[(qsc, scol, "gm")], key=(qsc, scol, "gm"),
                tag=f"{nn}ar")
            vop(lambda h: h.tensor_scalar_mul(qsc[:, scol:scol + 1], qsc[:, scol + 4:scol + 5],
                                              1.0 / 127.0),
                reads=[(qsc, scol, "gm")], writes=[(qsc, scol, "s")], tag=f"{nn}s")
            vop(lambda h: h.reciprocal(qsc[:, scol + 1:scol + 2], qsc[:, scol:scol + 1]),
                reads=[(qsc, scol, "s")], writes=[(qsc, scol, "rs")], tag=f"{nn}rs")
            for kk in range(K_ // 128):
                for ob in range(0, O_, PC):
                    dmal(scr[:, 0:PC], wT_ap[kk * 128:(kk + 1) * 128, ob:ob + PC], (scr, "wc"),
                         tag=f"{nn}b{kk}.{ob}")
                    S3 = max(T // 2, 512)
                    for o2 in range(0, PC, S3):
                        W2 = min(S3, PC - o2)
                        aop(lambda h, o_=o2, W_=W2: h.activation(
                            scr3[:, 0:W_], scr[:, o_:o_ + W_], AF.Identity,
                            scale=qsc[:, scol + 1:scol + 2]),
                            reads=[(scr, "wc"), (qsc, scol, "rs")], writes=[scr3], tag=f"{nn}q{kk}.{ob}.{o2}")
                        vop(lambda h, o_=o2, W_=W2: h.tensor_scalar(
                            rqv[:, o_:o_ + W_], scr3[:, 0:W_], MAGIC, MAGIC, ALU.add, ALU.subtract),
                            reads=[scr3], writes=[(rqv, "rq")], tag=f"{nn}w{kk}.{ob}.{o2}")
                    p.dma("sync", lambda h, k_=kk, o_=ob, D_=dstD, P_=PC: h.dma_start(
                        out=D_.ap()[k_ * 128:(k_ + 1) * 128, o_:o_ + P_], in_=rqv[:, 0:P_]),
                        reads=[(rqv, "rq")], writes=[("d", nn)], key=("d", nn),
                        tag=f"{nn}o{kk}.{ob}")
        p.fence([], [(bca, 0), (bca, 1), "rqv"], tag="frq")
        quant_prep(WfcT, D, FF, 0, rfc_d, "qfc")
        quant_prep(WprojT, FF, D, 4, rpr_d, "qpr")
        p.fence([], ["w1v", rows, "afc_s", "apr_s", bfc_s, bpr_s, "lra_s"],
                tag="fEsm")
        afc_v = afc_s[:, :]
        apr_v = apr_s[:, :]
        bfcs_v = bfc_s[:, :]
        bprs_v = bpr_s[:, :]
        lra_v = lra_s[:, :]
        for f in range(8):
            dmal(afc_v[:, f * 16:(f + 1) * 16], AfcT[f * 128:(f + 1) * 128, :], (afc_s, f),
                 tag=f"afc{f}")
        for hc in range(32):
            dmal(apr_v[:, hc * 16:(hc + 1) * 16], AprojT[hc * 128:(hc + 1) * 128, :],
                 (apr_s, hc), tag=f"apr{hc}")
        dmal(bfcs_v, BfcT, bfc_s, tag="bfc")
        dmal(bprs_v, BprojT, bpr_s, tag="bpr")
        p.dma("gpsimd", lambda h, a=bcols[:, 72:104], b=bass.AP(bfcv.tensor, 0, [[1, 128], [128, 32]]): h.dma_start(out=a, in_=b, allow_slow_non_contiguous=True), writes=[(bcols, "bfc")], key=(bcols, "bfc"), tag="bfcc")
        p.dma("gpsimd", lambda h, a=bcols[:, 104:112], b=bass.AP(bprv.tensor, 0, [[1, 128], [128, 8]]): h.dma_start(out=a, in_=b, allow_slow_non_contiguous=True), writes=[(bcols, "bpr")], key=(bcols, "bpr"), tag="bprc")
        for t4 in range(NT4):
            for f in range(8):
                top(lambda h, f_=f, t_=t4: h.matmul(
                    pb[0][0:16, 0:TS], afc_v[:, f_ * 16:(f_ + 1) * 16],
                    A1[:, f_ * T + t_ * TS: f_ * T + (t_ + 1) * TS],
                    start=(f_ == 0), stop=(f_ == 7)),
                    reads=[(afc_s, f), (A1, "xnc")], writes=[(pb[0], "lm")],
                    tag=f"lm{t4}.{f}")
            vop(lambda h, t_=t4: h.tensor_scalar_mul(lra_v[:, t_ * TS:(t_ + 1) * TS],
                                                     pb[0][0:16, 0:TS], 2.0),
                reads=[(pb[0], "lm")], writes=[(lra_s, "a")], tag=f"lme{t4}")
        p.fence([(A2, "ghm")], [(A2, "gf")], tag="fA2f")
        for tb in range(4):
            for hgrp in range(16):
                so = (hgrp % 2) * 2048
                ldg(wres, so, 4096, rfc_d, FF, 0, D, hgrp * 256, 256,
                    ("ws", so // 2048), tag=f"lrf{tb}.{hgrp}", rd=[("d", "qfc")])
                for ho in range(2):
                    ht = hgrp * 2 + ho
                    fcb = pb[5 if ht % 2 == 0 else 7]
                    for t4 in range(NTB):
                        for f in range(8):
                            top(lambda h, f_=f, o_=ho, t_=t4, b_=tb, s_=so, k_=fcb: h.matmul(
                                k_[:, 0:TSB],
                                wres[:, s_ + f_ * 256 + o_ * 128: s_ + f_ * 256 + (o_ + 1) * 128],
                                A1[:, f_ * T + b_ * TB + t_ * TSB: f_ * T + b_ * TB + (t_ + 1) * TSB],
                                start=(f_ == 0), stop=(f_ == 7)),
                                reads=[("ws", so // 2048), (A1, "xnc")], writes=[(fcb, "fc")],
                                tag=f"fcm{tb}.{ht}.{t4}.{f}")
                        top(lambda h, h_=ht, t_=t4, b_=tb: h.matmul(
                            pb[3][:, 0:TSB], bfcs_v[:, h_ * 128:(h_ + 1) * 128],
                            lra_v[:, b_ * TB + t_ * TSB: b_ * TB + (t_ + 1) * TSB],
                            start=True, stop=True),
                            reads=[bfc_s, (lra_s, "a")], writes=[(pb[3], "fl")],
                            tag=f"fcl{tb}.{ht}.{t4}")
                        aop(lambda h, t_=t4, k_=fcb: h.activation(
                            scr3[:, 0:TSB], k_[:, 0:TSB], AF.Identity, scale=qsc[:, 0:1]),
                            reads=[(fcb, "fc"), (qsc, 0, "s")], writes=[scr3], tag=f"fcs{tb}.{ht}.{t4}")
                        vop(lambda h, t_=t4: h.tensor_add(scr3[:, 0:TSB], scr3[:, 0:TSB], pb[3][:, 0:TSB]),
                            reads=[scr3, (pb[3], "fl")], writes=[scr3], tag=f"fca{tb}.{ht}.{t4}")
                        aop(lambda h, h_=ht, t_=t4: h.activation(
                            A2[:, h_ * TB + t_ * TSB: h_ * TB + (t_ + 1) * TSB], scr3[:, 0:TSB],
                            AF.Gelu, bias=bcols[:, 72 + h_:73 + h_]),
                            reads=[scr3, (bcols, "bfc")], writes=[(A2, "gf")],
                            tag=f"fcg{tb}.{ht}.{t4}")
            for t4 in range(NTB):
                for hc in range(32):
                    top(lambda h, hc_=hc, t_=t4: h.matmul(
                        pb[0][0:16, 0:TSB], apr_v[:, hc_ * 16:(hc_ + 1) * 16],
                        A2[:, hc_ * TB + t_ * TSB: hc_ * TB + (t_ + 1) * TSB],
                        start=(hc_ == 0), stop=(hc_ == 31)),
                        reads=[(apr_s, hc), (A2, "gf")], writes=[(pb[0], "lm2")],
                        tag=f"l2{tb}.{t4}.{hc}")
                vop(lambda h, t_=t4, b_=tb: h.tensor_scalar_mul(
                    lra_v[:, b_ * TB + t_ * TSB: b_ * TB + (t_ + 1) * TSB], pb[0][0:16, 0:TSB], 2.0),
                    reads=[(pb[0], "lm2")], writes=[(lra_s, "b", tb)], tag=f"l2e{tb}.{t4}")
            for dgp in range(4):
                p.dma("gpsimd", lambda h, b_=tb, g_=dgp: h.dma_start(
                    out=d3(scr, 0, 2, TB, T),
                    in_=g3(s0_d, T, g_ * 256, 256, b_ * TB, TB)),
                    reads=[("d", "s0")], writes=[(scr, "s0b")], key=(scr, "s0b"), tag=f"s0r{tb}.{dgp}")
                for do in range(2):
                    dt_i = dgp * 2 + do
                    prb = pb[6 if dt_i % 2 == 0 else 2]
                    for hf in range(2):
                        so = ((dt_i * 2 + hf) % 2) * 2048
                        ldg(wres, so, 4096, rpr_d, D, hf * 2048, 2048, dt_i * 128, 128,
                            ("ws", so // 2048), tag=f"lrp{tb}.{dt_i}.{hf}", rd=[("d", "qpr")])
                        for t4 in range(NTB):
                            for hc in range(16):
                                top(lambda h, hc_=hc, t_=t4, h2=hf, s_=so, k_=prb: h.matmul(
                                    k_[:, 0:TSB],
                                    wres[:, s_ + hc_ * 128: s_ + (hc_ + 1) * 128],
                                    A2[:, (h2 * 16 + hc_) * TB + t_ * TSB: (h2 * 16 + hc_) * TB + (t_ + 1) * TSB],
                                    start=(h2 == 0 and hc_ == 0), stop=(h2 == 1 and hc_ == 15)),
                                    reads=[("ws", so // 2048), (A2, "gf")], writes=[(prb, "pr")],
                                    tag=f"prm{tb}.{dt_i}.{hf}.{t4}.{hc}")
                    for t4 in range(NTB):
                        top(lambda h, d_=dt_i, t_=t4, b_=tb: h.matmul(
                            pb[4][:, 0:TSB], bprs_v[:, d_ * 128:(d_ + 1) * 128],
                            lra_v[:, b_ * TB + t_ * TSB: b_ * TB + (t_ + 1) * TSB],
                            start=True, stop=True),
                            reads=[bpr_s, (lra_s, "b", tb)], writes=[(pb[4], "pl")],
                            tag=f"prl{tb}.{dt_i}.{t4}")
                        aop(lambda h, t_=t4, k_=prb: h.activation(
                            scr3[:, 0:TSB], k_[:, 0:TSB], AF.Identity, scale=qsc[:, 4:5]),
                            reads=[(prb, "pr"), (qsc, 4, "s")], writes=[scr3], tag=f"prs{tb}.{dt_i}.{t4}")
                        vop(lambda h, t_=t4: h.tensor_add(scr3[:, 0:TSB], scr3[:, 0:TSB], pb[4][:, 0:TSB]),
                            reads=[scr3, (pb[4], "pl")], writes=[scr3], tag=f"pra{tb}.{dt_i}.{t4}")
                        aop(lambda h, d_=dt_i, t_=t4: h.activation(
                            rqv[:, 0:TSB],
                            scr3[:, 0:TSB], AF.Silu, bias=bcols[:, 104 + d_:105 + d_]),
                            reads=[scr3, (bcols, "bpr")], writes=[(rqv, "rq")], tag=f"prg{tb}.{dt_i}.{t4}")
                        vop(lambda h, d_=dt_i, t_=t4: h.tensor_mul(
                            scr3[:, 0:TSB], rqv[:, 0:TSB],
                            scr[:, (d_ % 2) * TB + t_ * TSB: (d_ % 2) * TB + (t_ + 1) * TSB]),
                            reads=[(rqv, "rq"), (scr, "s0b")], writes=[scr3], tag=f"g0m{tb}.{dt_i}.{t4}")
                        vop(lambda h, d_=dt_i, t_=t4, b_=tb: h.tensor_add(
                            A3[:, d_ * T + b_ * TB + t_ * TSB: d_ * T + b_ * TB + (t_ + 1) * TSB],
                            A3[:, d_ * T + b_ * TB + t_ * TSB: d_ * T + b_ * TB + (t_ + 1) * TSB],
                            scr3[:, 0:TSB]),
                            reads=[scr3, (A3, "x", dt_i)], writes=[(A3, "x", dt_i)],
                            tag=f"g0a{tb}.{dt_i}.{t4}")

        stats_of(A3, [(A3, "x", f) for f in range(8)], 0, 1, pb[0], "s1")
        rsqrt_row(1, 2, "rsL1")
        p.fence([], ["rqv", (bca, 0), (bca, 1)], tag="fFt")
        bcast(2, 0, "b_rsl1")
        bcast(0, 1, "b_ml1")
        negmul_tile("mrl1t")
        p.fence([(A1, "xnc")], [(A1, "x1c")], tag="fA1f")
        for f in range(8):
            for c2 in range(0, T, SW3):
                W3 = min(SW3, T - c2)
                vop(lambda h, f_=f, c_=c2, W_=W3: h.tensor_mul(
                    scr3[:, 0:W_], A3[:, f_ * T + c_:f_ * T + c_ + W_], bca[:, c_:c_ + W_]),
                    reads=[(A3, "x", f), (bca, 0)], writes=[scr3], tag=f"f1a{f}.{c2}")
                vop(lambda h, f_=f, c_=c2, W_=W3: h.tensor_add(
                    A1[:, f_ * T + c_:f_ * T + c_ + W_], scr3[:, 0:W_], bca[:, T + c_:T + c_ + W_]),
                    reads=[scr3, (bca, 1)], writes=[(A1, "x1c")], tag=f"f1b{f}.{c2}")
        p.dma("gpsimd", lambda h: h.dma_start(
            out=convw[:, 0:64], in_=bass.AP(Wcv.tensor, 0, [[4, 128], [512, 16], [1, 4]]),
            allow_slow_non_contiguous=True),
            writes=[convw], key=convw, tag="cw")
        p.dma("gpsimd", lambda h, a=bcols[:, 112:128], b=bass.AP(bconv.tensor, 0, [[1, 128], [128, 16]]): h.dma_start(out=a, in_=b, allow_slow_non_contiguous=True), writes=[(bcols, "bcv")], key=(bcols, "bcv"), tag="bcv")
        p.dma("gpsimd", lambda h, a=bcols[:, 128:144], b=bass.AP(bdtv.tensor, 0, [[1, 128], [128, 16]]): h.dma_start(out=a, in_=b, allow_slow_non_contiguous=True), writes=[(bcols, "bdt")], key=(bcols, "bdt"), tag="bdt")
        p.dma("gpsimd", lambda h, a=bcols[:, 144:160], b=bass.AP(A_log.tensor, 0, [[1, 128], [128, 16]]): h.dma_start(out=a, in_=b, allow_slow_non_contiguous=True), writes=[(bcols, "alog")], key=(bcols, "alog"), tag="alog")
        aop(lambda h: h.activation(bcols[:, 160:176], bcols[:, 144:160], AF.Exp),
            reads=[(bcols, "alog")], writes=[(bcols, "Ae")], tag="Aexp")
        vop(lambda h: h.tensor_scalar_mul(bcols[:, 160:176], bcols[:, 160:176], -1.0),
            reads=[(bcols, "Ae")], writes=[(bcols, "Ae")], tag="Aneg")
        bounce16(WiT, wi16_d, D, 2 * DI, "wic")
        bounce16(WdtT, wdt16_d, DI, DI, "dtc")
        bounce16(WBT, wb16_d, DI, DI, "wbc")
        bounce16(WCT, wc16_d, DI, DI, "wcc")
        bounce16(WoT1, wo116_d, DI, D, "woc")
        p.fence([], [(bca, 0), (bca, 1), "egv"], tag="feg")
        for og2 in range(16):
            so = (og2 % 2) * 2048
            ldg(wres, so, 4096, wi16_d, 2 * DI, 0, D, og2 * 256, 256,
                ("ws", so // 2048), tag=f"we{og2}", rd=[("d", "wic")])
            for f in range(8):
                top(lambda h, f_=f, s_=so: h.matmul(
                    pb[1][0:4, 0:256], A1[:, f_ * T + T - 4: f_ * T + T],
                    wres[:, s_ + f_ * 256: s_ + (f_ + 1) * 256],
                    start=(f_ == 0), stop=(f_ == 7)),
                    reads=[("ws", so // 2048), (A1, "x1c")], writes=[(pb[1], "eg")], tag=f"eg{og2}.{f}")
            vop(lambda h, o_=og2: h.tensor_copy(egv[0:4, o_ * 256:(o_ + 1) * 256], pb[1][0:4, 0:256]),
                reads=[(pb[1], "eg")], writes=[(egv, "eg")], tag=f"ege{og2}")
        p.dma("gpsimd", lambda h: h.dma_start(out=cc_e_in.ap(), in_=egv[0:4, 0:4096]),
              reads=[(egv, "eg")], writes=[("d", "cce")], tag="estg")
        p.cc(lambda h: h.collective_compute(
            "AllGather", ALU.bypass, replica_groups=GROUPS,
            ins=[cc_e_in.ap().opt()], outs=[cc_e_out.ap().opt()]),
            reads=[("d", "cce")], writes=[("d", "cceo")], tag="eag")
        for j in range(4):
            p.dma("gpsimd", lambda h, j_=j: h.dma_start(
                out=bass.AP(halos, j_, [[64, 128], [4, 16]]),
                in_=bass.AP(cc_e_out.ap().tensor, j_ * 2 * DI, [[1, 128], [128, 16]]),
                allow_slow_non_contiguous=True),
                reads=[("d", "cceo")], writes=[halos], key=halos, tag=f"ercv{j}")
        vop(lambda h: h.tensor_scalar_mul(halos[:, :], halos[:, :], rolec[:, 0:1]),
            reads=[halos, rolec], writes=[halos], tag="emsk")
        for oc in range(16):
            vop(lambda h, o_=oc: h.tensor_scalar_mul(
                xce[:, o_:o_ + 1], halos[:, o_ * 4:o_ * 4 + 1], convw[:, o_ * 4:o_ * 4 + 1]),
                reads=[halos, convw], writes=[xce], tag=f"xce{oc}.0")
            for j in range(1, 4):
                vop(lambda h, o_=oc, j_=j: h.scalar_tensor_tensor(
                    xce[:, o_:o_ + 1], halos[:, o_ * 4 + j_:o_ * 4 + j_ + 1],
                    convw[:, o_ * 4 + j_:o_ * 4 + j_ + 1], xce[:, o_:o_ + 1],
                    ALU.mult, ALU.add),
                    reads=[halos, convw, xce], writes=[xce], tag=f"xce{oc}.{j}")
            aop(lambda h, o_=oc: h.activation(xce[:, o_:o_ + 1], xce[:, o_:o_ + 1], AF.Silu,
                                              bias=bcols[:, 112 + o_:113 + o_]),
                reads=[xce, (bcols, "bcv")], writes=[xce], tag=f"xcs{oc}")
        vop(lambda h: h.memset(accol[:, :], 1.0), writes=[accol], tag="ac1")
        vop(lambda h: h.memset(hcol[:, :], 0.0), writes=[hcol], tag="hc0")
        vop(lambda h: h.memset(ucol[:, :], 0.0), writes=[ucol], tag="uc0")
        p.fence([], ["egv", "szv"], tag="fsz")
        x1o = 0
        dto = 4 * T
        p.fence([(A4, "u")], [(A4, "x1"), (A4, "dtc")], tag="fA4f")
        for blk in range(4):
            if blk > 0:
                for oc in range(16):
                    vop(lambda h, o_=oc: h.tensor_copy(
                        halos[:, o_ * 4 + 1:o_ * 4 + 4],
                        A4[:, x1o + o_ * TB + TB - 3: x1o + o_ * TB + TB]),
                        reads=[(A4, "x1")], writes=[halos], tag=f"hup{blk}.{oc}")
            for og2 in range(8):
                so = (og2 % 2) * 2048
                ldg(wres, so, 4096, wi16_d, 2 * DI, 0, D, og2 * 256, 256,
                    ("ws", so // 2048), tag=f"wx{blk}.{og2}", rd=[("d", "wic")])
                for oi in range(2):
                    oc = og2 * 2 + oi
                    xb_ = pb[2 if oc % 2 == 0 else 0]
                    for f in range(8):
                        top(lambda h, f_=f, o_=oi, b_=blk, s_=so, k_=xb_: h.matmul(
                            k_[:, 0:TB],
                            wres[:, s_ + f_ * 256 + o_ * 128: s_ + f_ * 256 + (o_ + 1) * 128],
                            A1[:, f_ * T + b_ * TB: f_ * T + (b_ + 1) * TB],
                            start=(f_ == 0), stop=(f_ == 7)),
                            reads=[("ws", so // 2048), (A1, "x1c")], writes=[(xb_, "x1")],
                            tag=f"x1m{blk}.{oc}.{f}")
                    vop(lambda h, o_=oc, k_=xb_: h.tensor_copy(
                        A4[:, x1o + o_ * TB: x1o + (o_ + 1) * TB], k_[:, 0:TB]),
                        reads=[(xb_, "x1")], writes=[(A4, "x1")], tag=f"x1e{blk}.{oc}")
            if blk == 0:
                p.fence([(ARC, "s0log")], [(ARC, "xc")], tag="fARC")
            for oc in range(16):
                for j in range(4):
                    nh = 3 - j
                    if nh > 0:
                        if j == 0:
                            vop(lambda h, o_=oc, j_=j, n_=nh: h.tensor_scalar_mul(
                                scr3[:, 0:n_], halos[:, o_ * 4 + 1 + j_:o_ * 4 + 1 + j_ + n_],
                                convw[:, o_ * 4 + j_:o_ * 4 + j_ + 1]),
                                reads=[halos, convw], writes=[scr3], tag=f"cve{blk}.{oc}.{j}")
                        else:
                            vop(lambda h, o_=oc, j_=j, n_=nh: h.scalar_tensor_tensor(
                                scr3[:, 0:n_], halos[:, o_ * 4 + 1 + j_:o_ * 4 + 1 + j_ + n_],
                                convw[:, o_ * 4 + j_:o_ * 4 + j_ + 1], scr3[:, 0:n_],
                                ALU.mult, ALU.add),
                                reads=[halos, convw, scr3], writes=[scr3], tag=f"cve{blk}.{oc}.{j}")
                    nb = TB - nh
                    if j == 0:
                        vop(lambda h, o_=oc, j_=j, n_=nh, m_=nb: h.tensor_scalar_mul(
                            scr3[:, n_:n_ + m_],
                            A4[:, x1o + o_ * TB: x1o + o_ * TB + m_],
                            convw[:, o_ * 4 + j_:o_ * 4 + j_ + 1]),
                            reads=[(A4, "x1"), convw], writes=[scr3], tag=f"cvb{blk}.{oc}.{j}")
                    else:
                        vop(lambda h, o_=oc, j_=j, n_=nh, m_=nb: h.scalar_tensor_tensor(
                            scr3[:, n_:n_ + m_],
                            A4[:, x1o + o_ * TB: x1o + o_ * TB + m_],
                            convw[:, o_ * 4 + j_:o_ * 4 + j_ + 1], scr3[:, n_:n_ + m_],
                            ALU.mult, ALU.add),
                            reads=[(A4, "x1"), convw, scr3], writes=[scr3],
                            tag=f"cvb{blk}.{oc}.{j}")
                aop(lambda h, o_=oc: h.activation(ARC[:, o_ * TB:(o_ + 1) * TB], scr3[:, 0:TB],
                                                  AF.Silu, bias=bcols[:, 112 + o_:113 + o_]),
                    reads=[scr3, (bcols, "bcv")], writes=[(ARC, "xc")], tag=f"cvs{blk}.{oc}")
            for og in range(4):
                for (mi, mdram, mk) in ((0, wdt16_d, "dt"), (1, wb16_d, "B"), (2, wc16_d, "C"),
                                        (3, wi16_d, "z")):
                    if mk == "z":
                        for oi2 in range(2):
                            so = (oi2 % 2) * 2048
                            ldg(wres, so, 4096, wi16_d, 2 * DI, 0, D,
                                DI + og * 512 + oi2 * 256, 256,
                                ("ws", so // 2048), tag=f"wz{blk}.{og}.{oi2}",
                                rd=[("d", "wic")])
                            for oi in range(2):
                                oc = og * 4 + oi2 * 2 + oi
                                zb_ = pb[3 if oc % 2 == 0 else 0]
                                for f in range(8):
                                    top(lambda h, f_=f, o_=oi, b_=blk, s_=so, k_=zb_: h.matmul(
                                        k_[:, 0:TB],
                                        wres[:, s_ + f_ * 256 + o_ * 128: s_ + f_ * 256 + (o_ + 1) * 128],
                                        A1[:, f_ * T + b_ * TB: f_ * T + (b_ + 1) * TB],
                                        start=(f_ == 0), stop=(f_ == 7)),
                                        reads=[("ws", so // 2048), (A1, "x1c")], writes=[(zb_, "z")],
                                        tag=f"zm{blk}.{oc}.{f}")
                                aop(lambda h, o_=oc % 4, k_=zb_: h.activation(
                                    szv[:, o_ * TB:(o_ + 1) * TB], k_[:, 0:TB], AF.Silu),
                                    reads=[(zb_, "z")], writes=[(szv, "sz")], tag=f"ze{blk}.{oc}")
                    else:
                        for oi in range(4):
                            oc = og * 4 + oi
                            so = (oc % 2) * 2048
                            if mk == "B":
                                mb_ = pb[5 if oc % 2 == 0 else 1]
                            else:
                                mb_ = pb[4 if oc % 2 == 0 else 6]
                            ldg(wres, so, 4096, mdram, DI, 0, DI, oc * 128, 128,
                                ("ws", so // 2048), tag=f"wm{mk}{blk}.{oc}",
                                rd=[("d", {"dt": "dtc", "B": "wbc", "C": "wcc"}[mk])])
                            for kk in range(16):
                                top(lambda h, k_=kk, s_=so, b_=mb_: h.matmul(
                                    b_[:, 0:TB],
                                    wres[:, s_ + k_ * 128: s_ + (k_ + 1) * 128],
                                    ARC[:, k_ * TB:(k_ + 1) * TB],
                                    start=(k_ == 0), stop=(k_ == 15)),
                                    reads=[("ws", so // 2048), (ARC, "xc")],
                                    writes=[(mb_, mk)], tag=f"{mk}m{blk}.{oc}.{kk}")
                            if mk == "dt":
                                aop(lambda h, o_=oc, b_=mb_: h.activation(
                                    scr3[:, 0:TB], b_[:, 0:TB], AF.Abs,
                                    bias=bcols[:, 128 + o_:129 + o_]),
                                    reads=[(mb_, "dt"), (bcols, "bdt")], writes=[scr3],
                                    tag=f"dta{blk}.{oc}")
                                aop(lambda h: h.activation(scr3[:, 0:TB], scr3[:, 0:TB], AF.Exp,
                                                           scale=-1.0),
                                    reads=[scr3], writes=[scr3], tag=f"dtb{blk}.{oc}")
                                aop(lambda h: h.activation(scr3[:, 0:TB], scr3[:, 0:TB], AF.Ln,
                                                           bias=1.0),
                                    reads=[scr3], writes=[scr3], tag=f"dtc{blk}.{oc}")
                                aop(lambda h, o_=oc, b_=mb_: h.activation(
                                    fscr[:, 0:TB], b_[:, 0:TB], AF.Relu,
                                    bias=bcols[:, 128 + o_:129 + o_]),
                                    reads=[(mb_, "dt"), (bcols, "bdt")], writes=[(fscr, "r")],
                                    tag=f"dtd{blk}.{oc}")
                                vop(lambda h, o_=oc: h.tensor_add(
                                    A4[:, dto + o_ * TB: dto + (o_ + 1) * TB], scr3[:, 0:TB],
                                    fscr[:, 0:TB]),
                                    reads=[scr3, (fscr, "r")], writes=[(A4, "dtc")], tag=f"dte{blk}.{oc}")
                            elif mk == "B":
                                if blk == 0:
                                    for kk in range(16):
                                        top(lambda h, k_=kk, oc2=oc, s_=so: h.matmul(
                                            pb[7][:, oc2:oc2 + 1],
                                            wres[:, s_ + k_ * 128: s_ + (k_ + 1) * 128],
                                            xce[:, k_:k_ + 1],
                                            start=(k_ == 0), stop=(k_ == 15)),
                                            reads=[("ws", so // 2048), xce], writes=[(pb[7], "bv")],
                                            tag=f"bv{oc}.{kk}")
                                aop(lambda h, b_=mb_: h.activation(
                                    scr3[:, 0:TB], b_[:, 0:TB], AF.Sigmoid),
                                    reads=[(mb_, "B")], writes=[scr3], tag=f"Bs{blk}.{oc}")
                                vop(lambda h, o_=oc, oo=oi: h.tensor_mul(
                                    scr[:, oo * TB:(oo + 1) * TB], scr3[:, 0:TB],
                                    ARC[:, o_ * TB:(o_ + 1) * TB]),
                                    reads=[scr3, (ARC, "xc")], writes=[(scr, "u")], tag=f"Bu{blk}.{oc}")
                            else:
                                aop(lambda h, b_=mb_: h.activation(
                                    scr3[:, 0:TB],
                                    b_[:, 0:TB], AF.Tanh),
                                    reads=[(mb_, "C")], writes=[scr3], tag=f"Ct{blk}.{oc}")
                                vop(lambda h, o_=oi: h.tensor_copy(
                                    szv[:, 4 * TB + o_ * TB: 4 * TB + (o_ + 1) * TB], scr3[:, 0:TB]),
                                    reads=[scr3], writes=[(szv, "ct")], tag=f"Cte{blk}.{oc}")
                if blk == 0:
                    for oi in range(4):
                        oc = og * 4 + oi
                        aop(lambda h, o_=oc: h.activation(
                            uprevc[:, o_:o_ + 1], pb[7][:, o_:o_ + 1], AF.Sigmoid),
                            reads=[(pb[7], "bv")], writes=[uprevc], tag=f"ups{oc}")
                        vop(lambda h, o_=oc: h.tensor_mul(uprevc[:, o_:o_ + 1], uprevc[:, o_:o_ + 1],
                                                          xce[:, o_:o_ + 1]),
                            reads=[uprevc, xce], writes=[uprevc], tag=f"upm{oc}")
                        vop(lambda h, o_=oc: h.tensor_mul(uprevc[:, o_:o_ + 1], uprevc[:, o_:o_ + 1],
                                                          rolec[:, 0:1]),
                            reads=[uprevc, rolec], writes=[uprevc], tag=f"upr{oc}")
                for oi in range(4):
                    oc = og * 4 + oi
                    vop(lambda h, o_=oc: h.tensor_scalar(
                        fscr[:, TB:2 * TB], A4[:, dto + o_ * TB: dto + (o_ + 1) * TB],
                        bcols[:, 160 + o_:161 + o_], -10.0, ALU.mult, ALU.max),
                        reads=[(A4, "dtc"), (bcols, "Ae")], writes=[(fscr, "la")], tag=f"la{blk}.{oc}")
                    vop(lambda h: h.tensor_scalar_min(fscr[:, TB:2 * TB], fscr[:, TB:2 * TB], -1e-4),
                        reads=[(fscr, "la")], writes=[(fscr, "la")], tag=f"lb{blk}.{oc}")
                    aop(lambda h: h.activation(fscr[:, TB:2 * TB], fscr[:, TB:2 * TB], AF.Exp),
                        reads=[(fscr, "la")], writes=[(fscr, "la")], tag=f"lc{blk}.{oc}")
                    vop(lambda h, o_=oc, oo=oi, b_=blk: h.tensor_copy(
                        fscr[:, 2 * TB + 1: 3 * TB], scr[:, oo * TB: (oo + 1) * TB - 1]),
                        reads=[(scr, "u")], writes=[(fscr, "be")], tag=f"us{blk}.{oc}")
                    vop(lambda h, o_=oc, b_=blk: h.tensor_copy(
                        fscr[:, 2 * TB: 2 * TB + 1],
                        (uprevc[:, o_:o_ + 1] if b_ == 0 else ucol[:, o_:o_ + 1])),
                        reads=[(uprevc if blk == 0 else ucol)], writes=[(fscr, "be")], tag=f"u0{blk}.{oc}")
                    vop(lambda h, oo=oi, o_=oc: h.tensor_copy(
                        ucol[:, o_:o_ + 1], scr[:, (oo + 1) * TB - 1:(oo + 1) * TB]),
                        reads=[(scr, "u")], writes=[ucol], tag=f"ue{blk}.{oc}")
                    vop(lambda h, oo=oi: h.tensor_add(
                        fscr[:, 2 * TB:3 * TB], fscr[:, 2 * TB:3 * TB], scr[:, oo * TB:(oo + 1) * TB]),
                        reads=[(fscr, "be"), (scr, "u")], writes=[(fscr, "be")], tag=f"ua{blk}.{oc}")
                    vop(lambda h, o_=oc: h.scalar_tensor_tensor(
                        fscr[:, 2 * TB:3 * TB], fscr[:, 2 * TB:3 * TB], 0.5,
                        A4[:, dto + o_ * TB: dto + (o_ + 1) * TB], ALU.mult, ALU.mult),
                        reads=[(fscr, "be"), (A4, "dtc")], writes=[(fscr, "be")], tag=f"ub{blk}.{oc}")
                    vop(lambda h, o_=oc: h.tensor_tensor_scan(
                        fscr[:, 0:TB], fscr[:, TB:2 * TB], fscr[:, 2 * TB:3 * TB],
                        hcol[:, o_:o_ + 1], ALU.mult, ALU.add),
                        reads=[(fscr, "la"), (fscr, "be"), hcol], writes=[(fscr, "h0")], tag=f"sc{blk}.{oc}")
                    vop(lambda h, o_=oc: h.tensor_copy(hcol[:, o_:o_ + 1], fscr[:, TB - 1:TB]),
                        reads=[(fscr, "h0")], writes=[hcol], tag=f"sce{blk}.{oc}")
                    vop(lambda h, o_=oc: h.tensor_tensor_scan(
                        fscr[:, 3 * TB:4 * TB], fscr[:, TB:2 * TB], zbuf[:, 0:TB],
                        accol[:, o_:o_ + 1], ALU.mult, ALU.add),
                        reads=[(fscr, "la"), zbuf, accol], writes=[(fscr, "ac")], tag=f"sa{blk}.{oc}")
                    vop(lambda h, o_=oc: h.tensor_copy(accol[:, o_:o_ + 1], fscr[:, 4 * TB - 1:4 * TB]),
                        reads=[(fscr, "ac")], writes=[accol], tag=f"sae{blk}.{oc}")
                    vop(lambda h, oo=oi: h.tensor_mul(
                        scr3[:, 0:TB], szv[:, 4 * TB + oo * TB: 4 * TB + (oo + 1) * TB],
                        szv[:, oo * TB:(oo + 1) * TB]),
                        reads=[(szv, "ct"), (szv, "sz")], writes=[scr3], tag=f"yw{blk}.{oc}")
                    vop(lambda h, o_=oc: h.tensor_mul(
                        A2[:, o_ * TB:(o_ + 1) * TB], fscr[:, 0:TB], scr3[:, 0:TB]),
                        reads=[(fscr, "h0"), scr3], writes=[(A2, "y0")], tag=f"y0{blk}.{oc}")
                    vop(lambda h, o_=oc: h.tensor_mul(
                        A2[:, 4 * T + o_ * TB: 4 * T + (o_ + 1) * TB],
                        fscr[:, 3 * TB:4 * TB], scr3[:, 0:TB]),
                        reads=[(fscr, "ac"), scr3], writes=[(A2, "gc")], tag=f"gc{blk}.{oc}")
            p.dma("sync", lambda h, b_=blk: h.dma_start(
                out=y0_d.ap()[:, b_ * 16 * TB:(b_ + 1) * 16 * TB], in_=A2[:, 0:16 * TB]),
                reads=[(A2, "y0")], writes=[("d", "y0")], key=("d", "y0"), tag=f"y0d{blk}")
            p.dma("sync", lambda h, b_=blk: h.dma_start(
                out=gc_d.ap()[:, b_ * 16 * TB:(b_ + 1) * 16 * TB], in_=A2[:, 4 * T:4 * T + 16 * TB]),
                reads=[(A2, "gc")], writes=[("d", "gc")], key=("d", "gc"), tag=f"gcd{blk}")
        p.dma("sync", lambda h: h.dma_start(out=cc_h_in.ap(), in_=hcol[:, :]),
              reads=[hcol], writes=[("d", "cch")], tag="hstg")
        p.cc(lambda h: h.collective_compute(
            "AllGather", ALU.bypass, replica_groups=GROUPS,
            ins=[cc_h_in.ap().opt()], outs=[cc_h_out.ap().opt()]),
            reads=[("d", "cch")], writes=[("d", "ccho")], tag="hag")
        p.dma("sync", lambda h: h.dma_start(out=hini[:, :], in_=cc_h_out.ap()[0:128, :]),
              reads=[("d", "ccho")], writes=[hini], tag="hrcv")
        vop(lambda h: h.tensor_scalar_mul(hini[:, :], hini[:, :], rolec[:, 0:1]),
            reads=[hini, rolec], writes=[hini], tag="hmsk")
        for blk in range(4):
            p.dma("gpsimd", lambda h, b_=blk: h.dma_start(
                out=A2[:, 0:16 * TB], in_=y0_d.ap()[:, b_ * 16 * TB:(b_ + 1) * 16 * TB]),
                reads=[("d", "y0")], writes=[(A2, "y0")], key=(A2, "y0"), tag=f"y0r{blk}")
            p.dma("gpsimd", lambda h, b_=blk: h.dma_start(
                out=A2[:, 4 * T:4 * T + 16 * TB], in_=gc_d.ap()[:, b_ * 16 * TB:(b_ + 1) * 16 * TB]),
                reads=[("d", "gc")], writes=[(A2, "gc")], key=(A2, "gc"), tag=f"gcr{blk}")
            for og in range(4):
                for hf in range(2):
                    so = ((og * 2 + hf) % 2) * 2048
                    ldg(wres, so, 4096, wo116_d, D, og * 512, 512, hf * 512, 512,
                        ("ws", so // 2048), tag=f"wog{blk}.{og}.{hf}",
                        rd=[("d", "woc")])
                    for oi in range(4):
                        oc = og * 4 + oi
                        vop(lambda h, o_=oc: h.scalar_tensor_tensor(
                            scr3[:, 0:TB], A2[:, 4 * T + o_ * TB: 4 * T + (o_ + 1) * TB],
                            hini[:, o_:o_ + 1], A2[:, o_ * TB:(o_ + 1) * TB], ALU.mult, ALU.add),
                            reads=[(A2, "y0"), (A2, "gc"), hini], writes=[scr3], tag=f"y{blk}.{oc}.{hf}")
                        vop(lambda h: h.tensor_copy(scr[:, 0:TB], scr3[:, 0:TB]),
                            reads=[scr3], writes=[(scr, "y16")], tag=f"y16{blk}.{oc}.{hf}")
                        for d4 in range(4):
                            dt_i = hf * 4 + d4
                            top(lambda h, o_=oi, d_=d4, b2=dt_i, oc_=oc, s_=so: h.matmul(
                                pb[b2][:, 0:TB], wres[:, s_ + o_ * 512 + d_ * 128: s_ + o_ * 512 + (d_ + 1) * 128],
                                scr[:, 0:TB],
                                start=(oc_ == 0), stop=(oc_ == 15)),
                                reads=[("ws", so // 2048), (scr, "y16")], writes=[(pb[dt_i], "s1")],
                                tag=f"s1m{blk}.{oc}.{dt_i}")
            for dt_i in range(8):
                vop(lambda h, d_=dt_i: h.tensor_copy(scr3[:, 0:TB], pb[d_][:, 0:TB]),
                    reads=[(pb[dt_i], "s1")], writes=[scr3], tag=f"s1e{blk}.{dt_i}")
                p.dma("sync", lambda h, d_=dt_i, b_=blk: h.dma_start(
                    out=s1_d.ap()[d_ * 128:(d_ + 1) * 128, b_ * TB:(b_ + 1) * TB], in_=scr3[:, 0:TB]),
                    reads=[scr3], writes=[("d", "s1")], key=("d", "s1"),
                    tag=f"s1d{blk}.{d_ if False else dt_i}")

        def quant8(wT_ap, K_, O_, scol, dstD, nn):
            for kk in range(4):
                R4 = K_ // 4
                p.dma("gpsimd", lambda h, k_=kk, D_=dstD, R_=R4: h.dma_start(
                    out=D_.ap()[k_ * R_:(k_ + 1) * R_, :],
                    in_=wT_ap[k_ * R_:(k_ + 1) * R_, :]),
                    writes=[("d", nn)], key=("d", nn), tag=f"{nn}o{kk}")
        p.fence([], [(A2, "y0"), (A2, "gc"), (mkv, "mk"),
                     (A4, "x1"), (A4, "dtc"), "hqb", "w8s", "xq8"], tag="fG")
        quant8(w1fT, D, FF, 8, q1_d, "q1")
        quant8(w2fT, FF, D, 8, q2_d, "q2")
        p.dma("gpsimd", lambda h, a=bcols[:, 32:64], b=bass.AP(b1f.tensor, 0, [[1, 128], [128, 32]]): h.dma_start(out=a, in_=b, allow_slow_non_contiguous=True), writes=[(bcols, "b1")], key=(bcols, "b1"), tag="b1f")
        p.dma("gpsimd", lambda h, a=bcols[:, 64:72], b=bass.AP(b2f.tensor, 0, [[1, 128], [128, 8]]): h.dma_start(out=a, in_=b, allow_slow_non_contiguous=True), writes=[(bcols, "b2")], key=(bcols, "b2"), tag="b2f")
        outw = 0
        for tb in range(4):
            for f in range(8):
                vop(lambda h, f_=f, b_=tb: h.tensor_copy(
                    xq8[:, f_ * TB:(f_ + 1) * TB],
                    A1[:, f_ * T + b_ * TB: f_ * T + (b_ + 1) * TB]),
                    reads=[(A1, "x1c")], writes=[(xq8, tb)], tag=f"xq{tb}.{f}")
            for hgrp in range(4):
                ldg(w8s, 0, 8192, q1_d, FF, 0, D, hgrp * 1024, 1024, (w8s, "g1"), tag=f"g1{tb}.{hgrp}", rd=[("d", "q1")])
                for ho in range(8):
                    ht = hgrp * 8 + ho
                    f1b = pb[5 if ht % 2 == 0 else 7]
                    for t4 in range(NTB):
                        for f in range(8):
                            top(lambda h, f_=f, o_=ho, t_=t4, k_=f1b: h.matmul(
                                k_[:, 0:TSB],
                                w8s[:, f_ * 1024 + o_ * 128: f_ * 1024 + (o_ + 1) * 128],
                                xq8[:, f_ * TB + t_ * TSB: f_ * TB + (t_ + 1) * TSB],
                                start=(f_ == 0), stop=(f_ == 7)),
                                reads=[(w8s, "g1"), (xq8, tb)], writes=[(f1b, "f1")],
                                tag=f"f1m{tb}.{ht}.{t4}.{f}")
                        aop(lambda h, h_=ht, t_=t4, k_=f1b: h.activation(
                            scr3[:, 0:TSB], k_[:, 0:TSB], AF.Identity,
                            bias=bcols[:, 32 + h_:33 + h_]),
                            reads=[(f1b, "f1"), (bcols, "b1")], writes=[scr3], tag=f"f1q{tb}.{ht}.{t4}")
                        aop(lambda h, h_=ht, t_=t4: h.activation(
                            hqb[:, h_ * TB + t_ * TSB: h_ * TB + (t_ + 1) * TSB], scr3[:, 0:TSB],
                            AF.Relu),
                            reads=[scr3], writes=[(hqb, tb)], tag=f"f1r{tb}.{ht}.{t4}")
            p.dma("gpsimd", lambda h, b_=tb: h.dma_start(
                out=d3(mkv, 0, 8, TB, 8 * TB),
                in_=g3(mask_in.tensor, T, 0, D, b_ * TB, TB)),
                reads=[], writes=[(mkv, "mk")], key=(mkv, "mk"), tag=f"mk{tb}")
            for dgp in range(4):
                p.dma("gpsimd", lambda h, b_=tb, g_=dgp: h.dma_start(
                    out=d3(scr, 0, 2, TB, T),
                    in_=g3(s1_d, T, g_ * 256, 256, b_ * TB, TB)),
                    reads=[("d", "s1")], writes=[(scr, "s1b")], key=(scr, "s1b"), tag=f"s1r{tb}.{dgp}")
                ldg(w8s, 0, 8192, q2_d, D, 0, FF, dgp * 256, 256, (w8s, "g2"), tag=f"g2{tb}.{dgp}", rd=[("d", "q2")])
                for do in range(2):
                    dt_i = dgp * 2 + do
                    f2b = pb[6 if dt_i % 2 == 0 else 2]
                    for t4 in range(NTB):
                        for hc in range(32):
                            top(lambda h, hc_=hc, t_=t4, d_=do, k_=f2b: h.matmul(
                                k_[:, 0:TSB],
                                w8s[:, hc_ * 256 + d_ * 128: hc_ * 256 + (d_ + 1) * 128],
                                hqb[:, hc_ * TB + t_ * TSB: hc_ * TB + (t_ + 1) * TSB],
                                start=(hc_ == 0), stop=(hc_ == 31)),
                                reads=[(w8s, "g2"), (hqb, tb)], writes=[(f2b, "f2")],
                                tag=f"f2m{tb}.{dt_i}.{t4}.{hc}")
                        aop(lambda h, d_=dt_i, t_=t4, k_=f2b: h.activation(
                            scr3[:, 0:TSB], k_[:, 0:TSB], AF.Identity,
                            bias=bcols[:, 64 + d_:65 + d_]),
                            reads=[(f2b, "f2"), (bcols, "b2")], writes=[scr3], tag=f"f2q{tb}.{dt_i}.{t4}")
                        vop(lambda h, d_=dt_i, t_=t4: h.tensor_mul(
                            scr3[:, 0:TSB], scr3[:, 0:TSB],
                            mkv[:, d_ * TB + t_ * TSB: d_ * TB + (t_ + 1) * TSB]),
                            reads=[scr3, (mkv, "mk")], writes=[scr3], tag=f"f2d{tb}.{dt_i}.{t4}")
                        aop(lambda h: h.activation(scr3[:, 0:TSB], scr3[:, 0:TSB], AF.Silu),
                            reads=[scr3], writes=[scr3], tag=f"f2s{tb}.{dt_i}.{t4}")
                        vop(lambda h, d_=dt_i, t_=t4: h.tensor_mul(
                            scr3[:, 0:TSB], scr3[:, 0:TSB],
                            scr[:, (d_ % 2) * TB + t_ * TSB: (d_ % 2) * TB + (t_ + 1) * TSB]),
                            reads=[scr3, (scr, "s1b")], writes=[scr3], tag=f"f2g{tb}.{dt_i}.{t4}")
                        vop(lambda h, d_=dt_i, t_=t4, b_=tb: h.tensor_add(
                            A3[:, d_ * T + b_ * TB + t_ * TSB: d_ * T + b_ * TB + (t_ + 1) * TSB],
                            A3[:, d_ * T + b_ * TB + t_ * TSB: d_ * T + b_ * TB + (t_ + 1) * TSB],
                            scr3[:, 0:TSB]),
                            reads=[scr3, (A3, "x", dt_i)], writes=[(A3, "x", dt_i), (A3, "xf", dt_i, tb)],
                            tag=f"g1a{tb}.{dt_i}.{t4}")
                    p.dma("gpsimd", lambda h, d_=dt_i, b_=tb: h.dma_start(
                        out=out_t[d_ * 128:(d_ + 1) * 128, b_ * TB:(b_ + 1) * TB],
                        in_=A3[:, d_ * T + b_ * TB: d_ * T + (b_ + 1) * TB]),
                        reads=[(A3, "xf", dt_i, tb)], writes=[("d", "outw", tb * 8 + dt_i)],
                        key=("d", "outw"), tag=f"out{tb}.{dt_i}")
        p.final_wait("gpsimd", [("d", "outw", i) for i in range(32)])
        p.emit(st_)
    return nc


_CACHE = {}


def _prep_inputs(inputs):
    f32 = np.float32
    f16 = np.float16
    tr = lambda a: np.ascontiguousarray(np.asarray(a, dtype=f32).T)
    w = {}
    w["wqT"] = tr(inputs["L0_attn_wq"]); w["wkT"] = tr(inputs["L0_attn_wk"])
    w["wvT"] = tr(inputs["L0_attn_wv"]); w["woT"] = tr(inputs["L0_attn_wo"])
    w["ln1w"] = np.asarray(inputs["L0_attn_ln1_w"], f32).reshape(1, -1)
    w["ln1b"] = np.asarray(inputs["L0_attn_ln1_b"], f32).reshape(1, -1)
    w["ln2w"] = np.asarray(inputs["L0_attn_ln2_w"], f32).reshape(1, -1)
    w["ln2b"] = np.asarray(inputs["L0_attn_ln2_b"], f32).reshape(1, -1)
    w["w1T"] = tr(inputs["L0_attn_w1"]); w["b1v"] = np.asarray(inputs["L0_attn_b1"], f32).reshape(1, -1)
    w["w2T"] = tr(inputs["L0_attn_w2"]); w["b2v"] = np.asarray(inputs["L0_attn_b2"], f32).reshape(1, -1)
    w["WfcT"] = tr(inputs["L0_ffn_W_fc"]); w["bfcv"] = np.asarray(inputs["L0_ffn_b_fc"], f32).reshape(1, -1)
    w["AfcT"] = tr(inputs["L0_ffn_A_fc"]); w["BfcT"] = np.ascontiguousarray(np.asarray(inputs["L0_ffn_B_fc"], f32).T)
    w["WprojT"] = tr(inputs["L0_ffn_W_proj"]); w["bprv"] = np.asarray(inputs["L0_ffn_b_proj"], f32).reshape(1, -1)
    w["AprojT"] = tr(inputs["L0_ffn_A_proj"]); w["BprojT"] = np.ascontiguousarray(np.asarray(inputs["L0_ffn_B_proj"], f32).T)
    w["WiT"] = tr(inputs["L1_attn_Wi"])
    w["Wcv"] = np.ascontiguousarray(np.asarray(inputs["L1_attn_Wconv"], f32).reshape(DI, 4))
    w["bconv"] = np.asarray(inputs["L1_attn_bconv"], f32).reshape(1, -1)
    w["WdtT"] = tr(inputs["L1_attn_Wdt"]); w["bdtv"] = np.asarray(inputs["L1_attn_bdt"], f32).reshape(1, -1)
    w["WBT"] = tr(inputs["L1_attn_WB"]); w["WCT"] = tr(inputs["L1_attn_WC"])
    w["WoT1"] = tr(inputs["L1_attn_Wo"])
    w["A_log"] = np.asarray(inputs["L1_attn_A_log"], f32).reshape(1, -1)
    w["w1fT"] = tr(inputs["L1_ffn_w1"]); w["b1f"] = np.asarray(inputs["L1_ffn_b1"], f32).reshape(1, -1)
    w["w2fT"] = tr(inputs["L1_ffn_w2"]); w["b2f"] = np.asarray(inputs["L1_ffn_b2"], f32).reshape(1, -1)
    w["lr_in"] = np.asarray(inputs["L0_attn_lr_scale"], f32).reshape(1, -1)
    w["trilm"] = np.tril(np.ones((128, 128), f16), -1)
    w["trium"] = np.triu(np.ones((128, 128), f16), 1)
    return w


def _dropout_mask(shape):
    import jax
    try:
        dev = jax.local_devices(backend="cpu")[0]
        with jax.default_device(dev):
            keep = np.asarray(jax.random.bernoulli(jax.random.key(42), 0.8, shape))
    except Exception:
        keep = np.asarray(jax.random.bernoulli(jax.random.key(42), 0.8, shape))
    return (keep.astype(np.float32) * 1.25).astype(np.float16)


def kernel(**inputs):
    x = np.asarray(inputs["L0_attn_x"], np.float32)
    B, L, D_ = x.shape
    T = L // 2
    w = _prep_inputs(inputs)
    mask = _dropout_mask((B, L, D_))
    if T not in _CACHE:
        _CACHE[T] = build_program(T)
    nc = _CACHE[T]
    in_maps = []
    for c in range(8):
        b, hf = c // 2, c % 2
        m = dict(w)
        m["x_in"] = np.ascontiguousarray(x[b, hf * T:(hf + 1) * T, :].T)
        m["mask_in"] = np.ascontiguousarray(mask[b, hf * T:(hf + 1) * T, :].T)
        m["role_in"] = np.full((1, 1), float(hf), np.float32)
        in_maps.append(m)
    res = run_bass_kernel_spmd(nc, in_maps, core_ids=list(range(8)))
    out = np.empty((B, L, D_), np.float32)
    for c in range(8):
        b, hf = c // 2, c % 2
        out[b, hf * T:(hf + 1) * T, :] = res.results[c]["out"].T
    return out



# version 04
# speedup vs baseline: 1.6753x; 1.0103x over previous
import os
import numpy as np
import ml_dtypes
import concourse.bass as bass
import concourse.mybir as mybir
from concourse.bass_utils import run_bass_kernel_spmd
from concourse.bass_isa import ReduceOp
from contextlib import ExitStack

F32 = mybir.dt.float32
F16 = mybir.dt.float16
F8 = mybir.dt.float8e4
AF = mybir.ActivationFunctionType
ALU = mybir.AluOpType
AX = mybir.AxisListType

ENGINES = ("tensor", "vector", "scalar", "gpsimd", "sync")
D = 1024
H = 16
FF = 4096
DI = 2048
EPS = 1e-5
MAGIC = float(1.5 * 2.0 ** 23)
GROUPS = [[0, 1], [2, 3], [4, 5], [6, 7]]


_COLLAPSE_PREFIX = ("pb",)
_COLLAPSE_NAMES = {"scr", "scr3", "fscr", "w8s", "xq8", "hqb", "lra_s",
                   "qmzv", "qbv", "w1v", "rqv", "szv", "egv", "mkv", "xs1v",
                   "afc_s", "apr_s", "masks", "lncol", "convw", "halos"}


def _norm(n):
    s = str(n)
    head, _, tail = s.rpartition("_")
    return head if head and tail.isdigit() else s


def _key(x):
    if isinstance(x, tuple):
        if x and not isinstance(x[0], (str, int, tuple)):
            n0 = getattr(x[0], "name", None)
            if n0 is not None:
                b = _norm(n0)
                if b in _COLLAPSE_NAMES or b.startswith(_COLLAPSE_PREFIX):
                    return b
        return tuple(_key(e) for e in x)
    if isinstance(x, (str, int)):
        return x
    n = getattr(x, "name", None)
    if n is None:
        return id(x)
    b = _norm(n)
    return b if (b in _COLLAPSE_NAMES or b.startswith(_COLLAPSE_PREFIX)) else n


class _Op:
    __slots__ = ("eng", "fn", "reads", "writes", "kind", "sem_key", "idx",
                 "waits", "need_inc", "drain_before", "tag")

    def __init__(self, eng, fn, reads, writes, kind, sem_key, tag=""):
        self.eng, self.fn, self.reads, self.writes = eng, fn, reads, writes
        self.kind, self.sem_key, self.tag = kind, sem_key, tag
        self.idx = -1
        self.waits = []
        self.need_inc = False
        self.drain_before = False


class Prog:
    def __init__(self, nc):
        self.nc = nc
        self.ops = []
        self.dma_sem_of_buf = {}
        self.n_dma_sems = 0

    def add(self, eng, fn, reads=(), writes=(), kind="c", sem_key=None, tag=""):
        op = _Op(eng, fn, tuple(_key(r) for r in reads),
                 tuple(_key(w) for w in writes), kind, sem_key, tag)
        op.idx = len(self.ops)
        self.ops.append(op)
        return op

    def c(self, eng, fn, reads=(), writes=(), tag=""):
        return self.add(eng, fn, reads, writes, "c", tag=tag)

    def dma(self, queue, fn, reads=(), writes=(), key=None, tag=""):
        if key is None:
            assert writes
            key = writes[0]
        key = _key(key)
        if key not in self.dma_sem_of_buf:
            self.dma_sem_of_buf[key] = f"ds{self.n_dma_sems}"
            self.n_dma_sems += 1
        return self.add(queue, fn, reads, writes, "d", self.dma_sem_of_buf[key], tag=tag)

    def cc(self, fn, reads=(), writes=(), tag=""):
        return self.add("gpsimd", fn, reads, writes, "cc", tag=tag)

    def fence(self, reads, writes, tag="fence"):
        self.c("vector", lambda h: h.memset(self.dummy, 0.0),
               reads=tuple(reads), writes=tuple(writes) + ("nopd",), tag=tag)

    def final_wait(self, queue, bufs):
        self.c(queue, lambda h: h.memset(self.dummy, 0.0),
               reads=tuple(bufs), writes=("nopd",), tag="final")

    def schedule(self):
        last_writer = {}
        readers_since = {}
        edges = {}
        for op in self.ops:
            deps = set()
            for r in op.reads:
                w = last_writer.get(r)
                if w is not None:
                    deps.add(w)
            for w_ in op.writes:
                pw = last_writer.get(w_)
                if pw is not None:
                    deps.add(pw)
                for rd in readers_since.get(w_, ()):
                    deps.add(rd)
            deps.discard(op)
            pl = [d for d in deps
                  if not (op.kind == "d" and d.kind == "d"
                          and d.sem_key == op.sem_key and d.eng == op.eng)]
            if pl:
                edges[op.idx] = pl
            for w_ in op.writes:
                last_writer[w_] = op
                readers_since[w_] = []
            for r in op.reads:
                readers_since.setdefault(r, []).append(op)
        for ci, pl in edges.items():
            c = self.ops[ci]
            for pp in pl:
                if pp.kind in ("d", "cc"):
                    continue
                if pp.eng == c.eng and c.kind == "c":
                    continue
                pp.need_inc = True
        inc_no = {e: 0 for e in ENGINES}
        inc_value = {}
        dval = {}
        for op in self.ops:
            if op.kind == "c" and op.need_inc:
                inc_no[op.eng] += 1
                inc_value[op.idx] = inc_no[op.eng]
            elif op.kind == "d":
                dval[op.idx] = dval.get(op.sem_key, 0) + 16
                dval[op.sem_key] = dval[op.idx]
            elif op.kind == "cc":
                dval[op.idx] = dval.get("__cc__", 0) + 1
                dval["__cc__"] = dval[op.idx]
        wm = {}
        pos = {}
        cnt = {e: 0 for e in ENGINES}
        for op in self.ops:
            pos[op.idx] = cnt[op.eng]
            cnt[op.eng] += 1
        drained = {e: -1 for e in ENGINES}
        for op in self.ops:
            for pp in edges.get(op.idx, ()):
                if pp.kind == "d":
                    k = ("dma", pp.sem_key)
                    v = dval[pp.idx]
                    if wm.get((op.eng,) + k, -1) < v:
                        op.waits.append((k, v))
                        wm[(op.eng,) + k] = v
                elif pp.kind == "cc":
                    v = dval[pp.idx]
                    if wm.get((op.eng, "cc"), -1) < v:
                        op.waits.append((("cc", "__cc__"), v))
                        wm[(op.eng, "cc")] = v
                elif pp.eng == op.eng and op.kind == "c":
                    if pp.eng != "tensor" and pos[pp.idx] > drained[pp.eng]:
                        op.drain_before = True
                else:
                    v = inc_value[pp.idx]
                    if wm.get((op.eng, "eng", pp.eng), -1) < v:
                        op.waits.append((("eng", pp.eng), v))
                        wm[(op.eng, "eng", pp.eng)] = v
            if op.drain_before:
                drained[op.eng] = pos[op.idx] - 1
            if len(op.waits) > 1:
                agg = {}
                for (k, v) in op.waits:
                    agg[k] = max(agg.get(k, 0), v)
                op.waits = list(agg.items())
        return inc_no

    def emit(self, stack):
        nc = self.nc
        self.schedule()
        if os.environ.get("PROG_DEBUG"):
            for k, name in self.dma_sem_of_buf.items():
                print("SEM", name, k)
        sems = {}
        for e in ENGINES:
            sems[("eng", e)] = stack.enter_context(nc.semaphore(f"pg_{e}"))
        for k, name in self.dma_sem_of_buf.items():
            sems[("dma", name)] = stack.enter_context(nc.semaphore(name))
        sems[("cc", "__cc__")] = stack.enter_context(nc.semaphore("ccsem"))
        by_eng = {e: [] for e in ENGINES}
        for op in self.ops:
            by_eng[op.eng].append(op)
        block = stack.enter_context(nc.Block())

        def runner(ename):
            def _run(hh):
                mysem = sems[("eng", ename)]
                for op in by_eng[ename]:
                    for (skey, val) in op.waits:
                        hh.wait_ge(sems[skey], val)
                    if op.drain_before:
                        hh.drain()
                    inst = op.fn(hh)
                    if op.kind == "d":
                        inst.then_inc(sems[("dma", op.sem_key)], 16)
                    elif op.kind == "cc":
                        inst.then_inc(sems[("cc", "__cc__")], 1)
                    elif op.need_inc:
                        inst.then_inc(mysem, 1)
            return _run

        for e in ENGINES:
            if by_eng[e]:
                getattr(block, e)(runner(e))


def bc_ap(handle, row, width):
    return bass.AP(handle, row * width, [[1, 1], [0, 128], [1, width]])


def hpk(g):
    return ((g // 2) % 2) * 64, (g % 2) * 256 + (g // 4) * 64



def build_program(T):
    NCH = T // 128
    TS = min(512, T)
    NT4 = T // TS
    TB = T // 4
    TSB = min(512, TB)
    NTB = TB // TSB
    NLVL = 7

    nc = bass.Bass("TRN2", target_bir_lowering=False)

    def din(name, shape, dt=F32):
        return nc.dram_tensor(name, shape, dt, kind="ExternalInput").ap()

    x_in = din("x_in", [D, T])
    role_in = din("role_in", [1, 1])
    trilm = din("trilm", [128, 128], F16)
    trium = din("trium", [128, 128], F16)
    mask_in = din("mask_in", [D, T], F16)
    lr_in = din("lr_in", [1, H])
    wqT = din("wqT", [D, D]); wkT = din("wkT", [D, D]); wvT = din("wvT", [D, D])
    woT = din("woT", [D, D])
    ln1w = din("ln1w", [1, D]); ln1b = din("ln1b", [1, D])
    ln2w = din("ln2w", [1, D]); ln2b = din("ln2b", [1, D])
    w1T = din("w1T", [D, FF]); b1v = din("b1v", [1, FF])
    w2T = din("w2T", [FF, D]); b2v = din("b2v", [1, D])
    WfcT = din("WfcT", [D, FF]); bfcv = din("bfcv", [1, FF])
    AfcT = din("AfcT", [D, 16]); BfcT = din("BfcT", [16, FF])
    WprojT = din("WprojT", [FF, D]); bprv = din("bprv", [1, D])
    AprojT = din("AprojT", [FF, 16]); BprojT = din("BprojT", [16, D])
    WiT = din("WiT", [D, 2 * DI])
    Wcv = din("Wcv", [DI, 4])
    bconv = din("bconv", [1, DI])
    WdtT = din("WdtT", [DI, DI]); bdtv = din("bdtv", [1, DI])
    WBT = din("WBT", [DI, DI]); WCT = din("WCT", [DI, DI])
    WoT1 = din("WoT1", [DI, D])
    A_log = din("A_log", [1, DI])
    w1fT = din("w1fT", [D, FF]); b1f = din("b1f", [1, FF])
    w2fT = din("w2fT", [FF, D]); b2f = din("b2f", [1, D])
    out_t = nc.dram_tensor("out", [D, T], F32, kind="ExternalOutput").ap()

    cc_s_in = nc.dram_tensor("cc_s_in", [128, 512], F32)
    cc_s_out = nc.dram_tensor("cc_s_out", [256, 512], F32)
    cc_e_in = nc.dram_tensor("cc_e_in", [4, 2 * DI], F32)
    cc_e_out = nc.dram_tensor("cc_e_out", [8, 2 * DI], F32)
    cc_h_in = nc.dram_tensor("cc_h_in", [128, 16], F32)
    cc_h_out = nc.dram_tensor("cc_h_out", [256, 16], F32)
    qmz_d = nc.dram_tensor("qmz_d", [128, 8 * NCH * 256], F16)
    s0_d = nc.dram_tensor("s0_d", [D, T], F16)
    s1_d = nc.dram_tensor("s1_d", [D, T], F16)
    y0_d = nc.dram_tensor("y0_d", [128, 16 * T], F16)
    gc_d = nc.dram_tensor("gc_d", [128, 16 * T], F16)
    w116_d = nc.dram_tensor("w116_d", [D, FF], F16)
    w216_d = nc.dram_tensor("w216_d", [FF, D], F16)
    rfc_d = nc.dram_tensor("rfc_d", [D, FF], F16)
    rpr_d = nc.dram_tensor("rpr_d", [FF, D], F16)
    q1_d = nc.dram_tensor("q1_d", [D, FF], F8)
    q2_d = nc.dram_tensor("q2_d", [FF, D], F8)
    wi16_d = nc.dram_tensor("wi16_d", [D, 2 * DI], F16)
    wdt16_d = nc.dram_tensor("wdt16_d", [DI, DI], F16)
    wb16_d = nc.dram_tensor("wb16_d", [DI, DI], F16)
    wc16_d = nc.dram_tensor("wc16_d", [DI, DI], F16)
    wo116_d = nc.dram_tensor("wo116_d", [DI, D], F16)

    p = Prog(nc)
    st_ = ExitStack()

    def sb(name, shape, dt):
        return st_.enter_context(nc.sbuf_tensor(name, shape, dt))

    def vt(name, shape, dt, backing, byte_off):
        h = nc.virtual_sbuf_tensor(name, shape, dt)
        nc._tensor(h.name, h.shape, h.dtype, type="SB")
        mloc = nc.lookup_mloc(h)
        mloc.allocated = not nc.target_bir_lowering
        mloc.addr = nc.lookup_mloc(backing).addr + byte_off
        return h

    with st_:
        A1 = sb("A1", [128, 8 * T], F16)
        A2 = sb("A2", [128, 8 * T], F16)
        A3 = sb("A3", [128, 8 * T], F16)
        A4 = sb("A4", [128, 8 * T], F16)
        ARC = sb("ARC", [128, 4 * T], F16)
        XNb = [sb("XN0", [128, 384], F16), sb("XN1", [128, 384], F16)]
        NTb = [XNb[0], XNb[1]]
        KQb = [sb("KQu0", [128, 128], F16), sb("KQu1", [128, 128], F16)]
        rows = sb("rows", [128, 3 * T], F32)
        bca = sb("bca", [128, 2 * T], F16)
        qmax = sb("qmax", [1, 132], F32)
        nopd = sb("nopd", [1, 2], F32)
        p.dummy = nopd[0:1, 0:2]
        onec = sb("onec", [128, 2], F16)
        bcols = sb("bcols", [128, 176], F32)
        lncol = sb("lncol", [128, 32], F32)
        rolec = sb("rolec", [128, 1], F32)
        wres = sb("wres", [128, 4096], F16)
        scr = sb("scr", [128, T], F16)
        scr3 = sb("scr3", [128, max(T // 2, 512)], F16)
        fscr = sb("fscr", [128, 4 * TB], F32)
        zbuf = sb("zbuf", [128, TB], F32)
        qsc = sb("qsc", [128, 12], F32)
        convw = sb("convw", [128, 64], F32)
        accol = sb("accol", [128, 16], F32)
        hcol = sb("hcol", [128, 16], F32)
        hini = sb("hini", [128, 16], F32)
        ucol = sb("ucol", [128, 16], F32)
        uprevc = sb("uprevc", [128, 16], F32)
        xce = sb("xce", [128, 16], F16)
        halos = sb("halos", [128, 64], F16)
        pb = [st_.enter_context(nc.psum_tensor(f"pb{i}", [128, 512], F32)) for i in range(8)]

        xs1v = vt("xs1v", [128, T], F16, A3, 8 * T)
        qmzv = vt("qmzv", [128, 2 * T], F16, A3, 10 * T)
        masks = vt("masks", [128, 256], F16, A3, 14 * T)
        if T >= 2048:
            sch = vt("sch", [128, 512], F32, A3, 8 * T)
            schb = vt("schb", [128, 512], F16, A3, 8 * T + 2048)
            wch = vt("wch", [128, 512], F32, A3, 8 * T + 3072)
            wchb = vt("wchb", [128, 512], F16, A3, 8 * T + 5120)
            sini = vt("sini", [128, 512], F32, A3, 8 * T + 6144)
            qbv = vt("qbv", [128, 2048], F16, A3, 8 * T + 8192)
        else:
            sch = sb("sch", [128, 512], F32)
            schb = sb("schb", [128, 512], F16)
            wch = sb("wch", [128, 512], F32)
            wchb = sb("wchb", [128, 512], F16)
            sini = sb("sini", [128, 512], F32)
            qbv = sb("qbv", [128, 2048], F16)
        w1v = vt("w1v", [128, 8 * TB], F16, rows, 4 * T)
        bcs = vt("bcs", [1, T], F16, rows, 8 * T)
        if 12 * T >= 2 * T + 11520 + 4 * T:
            lra_s = vt("lra_s", [16, T], F16, rows, 0)
            bpr_s = vt("bpr_s", [16, D], F16, rows, 2 * T)
            afc_s = vt("afc_s", [128, 128], F16, rows, 2 * T + 2048)
            apr_s = vt("apr_s", [128, 512], F16, rows, 2 * T + 2304)
            bfc_s = vt("bfc_s", [16, FF], F16, rows, 2 * T + 3328)
        else:
            lra_s = sb("lra_s", [16, T], F16)
            bpr_s = sb("bpr_s", [16, D], F16)
            afc_s = sb("afc_s", [128, 128], F16)
            apr_s = sb("apr_s", [128, 512], F16)
            bfc_s = sb("bfc_s", [16, FF], F16)
        rqv = vt("rqv", [128, T], F16, bca, 0)
        szv = vt("szv", [128, 8 * TB], F16, bca, 0)
        egv = (vt("egv", [4, 4096], F16, bca, 0) if 4 * T >= 8192
               else sb("egv", [4, 4096], F16))
        mkv = vt("mkv", [128, 8 * TB], F16, A2, 0)
        if 10 * T + 8192 <= 16 * T:
            hqb = vt("hqb", [128, 32 * TB], F8, A4, 0)
            w8s = vt("w8s", [128, 8192], F8, A4, 8 * T)
            xq8 = vt("xq8", [128, 8 * TB], F8, A4, 10 * T + 8192 - 2 * T)
        else:
            hqb = sb("hqb", [128, 32 * TB], F8)
            w8s = sb("w8s", [128, 8192], F8)
            xq8 = sb("xq8", [128, 8 * TB], F8)

        R_MAP = {0: (0, 0), 2: (0, 1),
                 1: (32, 0), 13: (32, 1), 3: (32, 2),
                 14: (64, 0), 15: (64, 1), 9: (64, 2),
                 8: (96, 0), 7: (96, 1)}

        def R(i):
            b, j = R_MAP[i]
            return rows[b:b + 1, j * T:(j + 1) * T]

        def dmal(dst, src, buf, queue="gpsimd", reads=(), tag=""):
            p.dma(queue, lambda h, d=dst, s=src: h.dma_start(out=d, in_=s),
                  reads=reads, writes=[buf], key=buf, tag=tag)

        def vop(fn, reads=(), writes=(), tag=""):
            p.c("vector", fn, reads, writes, tag)

        def aop(fn, reads=(), writes=(), tag=""):
            p.c("scalar", fn, reads, writes, tag)

        def top(fn, reads=(), writes=(), tag=""):
            p.c("tensor", fn, reads, writes, tag)

        def bcast(row_i, slot, tag):
            p.c("vector", lambda h, r=row_i: h.tensor_copy(bcs[0:1, 0:T], R(r)),
                reads=[rows], writes=[bcs], tag=tag + "c")
            p.dma("gpsimd", lambda h, s=slot: h.dma_start(
                out=bca[:, s * T:(s + 1) * T],
                in_=bass.AP(bcs, 0, [[1, 1], [0, 128], [1, T]])),
                reads=[bcs], writes=[(bca, slot)], key=(bca, slot), tag=tag)

        def rsqrt_row(v_row, o_row, tag):
            vop(lambda h: h.tensor_scalar_add(R(13), R(v_row), EPS),
                reads=[rows], writes=[rows], tag=tag + "a")
            vop(lambda h: h.reciprocal(R(13), R(13)),
                reads=[rows], writes=[rows], tag=tag + "r")
            aop(lambda h: h.activation(R(o_row), R(13), AF.Sqrt),
                reads=[rows], writes=[rows], tag=tag + "s")

        def negmul_tile(tag):
            vop(lambda h: h.scalar_tensor_tensor(
                bca[:, T:2 * T], bca[:, T:2 * T], -1.0, bca[:, 0:T],
                ALU.mult, ALU.mult),
                reads=[(bca, 0), (bca, 1)], writes=[(bca, 1)], tag=tag)

        def stats_of(srcbuf, src_key, m_row, v_row, bank, tag):
            skeys = list(src_key) if isinstance(src_key, list) else [src_key]
            mb_, mj = R_MAP[m_row]
            for t4 in range(NT4):
                for f in range(8):
                    top(lambda h, f_=f, t_=t4, s=srcbuf: h.matmul(
                        bank[0:1, 0:TS], onec[:, 0:1],
                        s[:, f_ * T + t_ * TS: f_ * T + (t_ + 1) * TS],
                        start=(f_ == 0), stop=(f_ == 7)),
                        reads=skeys + [onec], writes=[(bank, "st")], tag=f"{tag}s{f}.{t4}")
                vop(lambda h, t_=t4: h.tensor_scalar_mul(
                    rows[mb_:mb_ + 1, mj * T + t_ * TS: mj * T + (t_ + 1) * TS],
                    bank[0:1, 0:TS], 1.0 / D),
                    reads=[(bank, "st")], writes=[rows], tag=f"{tag}m{t4}")
            for t4 in range(NT4):
                for f in range(8):
                    aop(lambda h, f_=f, t_=t4, s=srcbuf: h.activation(
                        scr3[:, 0:TS], s[:, f_ * T + t_ * TS: f_ * T + (t_ + 1) * TS],
                        AF.Square),
                        reads=skeys, writes=[scr3], tag=f"{tag}q{f}.{t4}")
                    top(lambda h, f_=f: h.matmul(
                        bank[64:65, 0:TS], onec[:, 0:1], scr3[:, 0:TS],
                        start=(f_ == 0), stop=(f_ == 7)),
                        reads=[scr3, onec], writes=[(bank, "sq")], tag=f"{tag}k{f}.{t4}")
                vop(lambda h, t_=t4: h.tensor_scalar_mul(
                    rows[64:65, T + t_ * TS: T + (t_ + 1) * TS],
                    bank[64:65, 0:TS], 1.0 / D),
                    reads=[(bank, "sq")], writes=[rows], tag=f"{tag}e{t4}")
            vop(lambda h: h.tensor_mul(R(14), R(m_row), R(m_row)),
                reads=[rows], writes=[rows], tag=f"{tag}2")
            vop(lambda h: h.tensor_sub(R(v_row), R(15), R(14)),
                reads=[rows], writes=[rows], tag=f"{tag}v")

        def g3(srch, O_tot, row0, nrow, col0, W):
            return bass.AP(srch, row0 * O_tot + col0,
                           [[O_tot, 128], [128 * O_tot, nrow // 128], [1, W]])

        def d3(dsth, doff, nch, W, dcols):
            return bass.AP(dsth, doff, [[dcols, 128], [W, nch], [1, W]])

        def ldg(dsth, doff, dcols, srch, O_tot, row0, nrow, col0, W, buf,
                queue="gpsimd", tag="", rd=()):
            p.dma(queue, lambda h: h.dma_start(
                out=d3(dsth, doff, nrow // 128, W, dcols),
                in_=g3(srch, O_tot, row0, nrow, col0, W)),
                reads=tuple(rd), writes=[buf], key=buf, tag=tag)


        dmal(masks[:, 0:128], trilm, (masks, 0), tag="tril")
        dmal(masks[:, 128:256], trium, (masks, 1), tag="triu")
        p.dma("gpsimd", lambda h: h.dma_start(out=rolec[:, 0:1], in_=bc_ap(role_in.tensor, 0, 1)),
              writes=[rolec], tag="role")
        vop(lambda h: h.memset(onec[:, 0:2], 1.0), writes=[onec], tag="one")
        vop(lambda h: h.memset(zbuf[:, :], 0.0), writes=[zbuf], tag="zb")
        p.dma("gpsimd", lambda h: h.dma_start(out=bcols[:, 0:H], in_=bc_ap(lr_in.tensor, 0, H)),
              writes=[(bcols, "lr")], tag="lr")
        vop(lambda h: h.tensor_scalar_mul(bcols[:, 16:16 + H], bcols[:, 0:H], 0.01),
            reads=[(bcols, "lr")], writes=[(bcols, "beta")], tag="beta")

        def ldcols(off, src, nm, n=8, dst=None):
            d_ = lncol if dst is None else dst
            p.dma("gpsimd", lambda h, a=d_[:, off:off + n],
                  b=bass.AP(src.tensor, 0, [[1, 128], [128, n]]): h.dma_start(
                      out=a, in_=b, allow_slow_non_contiguous=True),
                  writes=[(d_, off)], key=(d_, off), tag=nm)
        ldcols(0, ln1w, "l1w"); ldcols(8, ln1b, "l1b")
        ldcols(16, ln2w, "l2w"); ldcols(24, ln2b, "l2b")

        xbuf = [scr, xs1v]
        for f in range(8):
            sl = f % 2
            dmal(xbuf[sl][:, 0:T], x_in[f * 128:(f + 1) * 128, :],
                 (xbuf[sl], "x"), tag=f"xs{f}")
            for t4 in range(NT4):
                top(lambda h, f_=f, t_=t4, s_=sl: h.matmul(
                    pb[t_][0:1, 0:TS], onec[:, 0:1],
                    xbuf[s_][:, t_ * TS:(t_ + 1) * TS],
                    start=(f_ == 0), stop=(f_ == 7)),
                    reads=[(xbuf[sl], "x"), onec], writes=[(pb[t4], "st")], tag=f"As{f}.{t4}")
            for t4 in range(NT4):
                aop(lambda h, s_=sl, t_=t4: h.activation(
                    scr3[:, 0:TS], xbuf[s_][:, t_ * TS:(t_ + 1) * TS], AF.Square),
                    reads=[(xbuf[sl], "x")], writes=[scr3], tag=f"Aq{f}.{t4}")
                top(lambda h, f_=f, t_=t4: h.matmul(
                    pb[t_][64:65, 0:TS], onec[:, 0:1],
                    scr3[:, 0:TS],
                    start=(f_ == 0), stop=(f_ == 7)),
                    reads=[scr3, onec], writes=[(pb[t4], "st")], tag=f"Ak{f}.{t4}")
        for t4 in range(NT4):
            vop(lambda h, t_=t4: h.tensor_scalar_mul(
                rows[0:1, t_ * TS:(t_ + 1) * TS], pb[t_][0:1, 0:TS], 1.0 / D),
                reads=[(pb[t4], "st")], writes=[rows], tag=f"Am{t4}")
            vop(lambda h, t_=t4: h.tensor_scalar_mul(
                rows[64:65, T + t_ * TS: T + (t_ + 1) * TS],
                pb[t_][64:65, 0:TS], 1.0 / D),
                reads=[(pb[t4], "st")], writes=[rows], tag=f"Ae{t4}")
        vop(lambda h: h.tensor_mul(R(14), R(0), R(0)),
            reads=[rows], writes=[rows], tag="A2")
        vop(lambda h: h.tensor_sub(R(1), R(15), R(14)),
            reads=[rows], writes=[rows], tag="Av")
        rsqrt_row(1, 2, "rs1")
        vop(lambda h: h.tensor_scalar_add(R(13), R(1), EPS),
            reads=[rows], writes=[rows], tag="r2a")
        vop(lambda h: h.reciprocal(R(13), R(13)),
            reads=[rows], writes=[rows], tag="r2b")
        vop(lambda h: h.tensor_mul(R(13), R(1), R(13)),
            reads=[rows], writes=[rows], tag="r2c")
        vop(lambda h: h.tensor_scalar_add(R(13), R(13), EPS),
            reads=[rows], writes=[rows], tag="r2d")
        vop(lambda h: h.reciprocal(R(13), R(13)),
            reads=[rows], writes=[rows], tag="r2e")
        aop(lambda h: h.activation(R(3), R(13), AF.Sqrt),
            reads=[rows], writes=[rows], tag="r2f")
        bcast(2, 0, "b_r1")
        bcast(3, 1, "b_r2")
        vop(lambda h: h.tensor_mul(bca[:, 0:T], bca[:, 0:T], bca[:, T:2 * T]),
            reads=[(bca, 0), (bca, 1)], writes=[(bca, 0)], tag="al1t")
        bcast(0, 1, "b_m1")
        negmul_tile("ma1t")
        SW3 = max(T // 2, 512)
        for f in range(8):
            sl = f % 2
            dmal(xbuf[sl][:, 0:T], x_in[f * 128:(f + 1) * 128, :],
                 (xbuf[sl], "x"), tag=f"xt{f}")
            for c2 in range(0, T, SW3):
                W3 = min(SW3, T - c2)
                vop(lambda h, s_=sl, c_=c2, W_=W3: h.tensor_mul(
                    scr3[:, 0:W_], xbuf[s_][:, c_:c_ + W_], bca[:, c_:c_ + W_]),
                    reads=[(xbuf[sl], "x"), (bca, 0)], writes=[scr3], tag=f"ta{f}.{c2}")
                vop(lambda h, c_=c2, W_=W3: h.tensor_add(
                    scr3[:, 0:W_], scr3[:, 0:W_], bca[:, T + c_:T + c_ + W_]),
                    reads=[scr3, (bca, 1)], writes=[scr3], tag=f"tb{f}.{c2}")
                vop(lambda h, f_=f, c_=c2, W_=W3: h.tensor_scalar(
                    A1[:, f_ * T + c_:f_ * T + c_ + W_], scr3[:, 0:W_],
                    lncol[:, f_:f_ + 1], lncol[:, 8 + f_:9 + f_], ALU.mult, ALU.add),
                    reads=[scr3, (lncol, 0), (lncol, 8)], writes=[(A1, "t0c")], tag=f"tc{f}.{c2}")
        bcast(2, 0, "b_rs1")
        bcast(0, 1, "b_m1b")
        negmul_tile("mr1t")

        kvt_off = 2 * T
        hal_off = 0
        gal_off = 4 * T

        for hp in range(8):
            sq, sk, sv_ = [((hp * 3 + j) % 4) * 1024 for j in range(3)]
            for (so, wT_ap, nm) in ((sq, wqT, "q"), (sk, wkT, "k"), (sv_, wvT, "v")):
                ldg(wres, so, 4096, wT_ap.tensor, D, 0, D, hp * 128, 128,
                    ("wq", so // 1024), tag=f"w{nm}{hp}")
            for (so, dsto, nm, bank) in ((sq, 0, "q", pb[0]), (sk, T, "k", pb[1])):
                for t4 in range(NT4):
                    for f in range(8):
                        top(lambda h, f_=f, t_=t4, w_=so, bk=bank: h.matmul(
                            bk[:, 0:TS], wres[:, w_ + f_ * 128: w_ + (f_ + 1) * 128],
                            A1[:, f_ * T + t_ * TS: f_ * T + (t_ + 1) * TS],
                            start=(f_ == 0), stop=(f_ == 7)),
                            reads=[("wq", so // 1024), (A1, "t0c")], writes=[(bank, "qk")],
                            tag=f"{nm}{hp}.{t4}.{f}")
                    vop(lambda h, t_=t4, o_=dsto, bk=bank: h.tensor_copy(
                        A3[:, o_ + t_ * TS: o_ + (t_ + 1) * TS], bk[:, 0:TS]),
                        reads=[(bank, "qk")], writes=[(A3, "qk", dsto)], tag=f"{nm}e{hp}.{t4}")
            def kvt_block(c):
                for f in range(8):
                    top(lambda h, c_=c, f_=f, w_=sk: h.matmul(
                        pb[2][:, 0:128],
                        A1[:, f_ * T + c_ * 128: f_ * T + (c_ + 1) * 128],
                        wres[:, w_ + f_ * 128: w_ + (f_ + 1) * 128],
                        start=(f_ == 0), stop=(f_ == 7)),
                        reads=[("wq", sk // 1024), (A1, "t0c")], writes=[(pb[2], "kt")],
                        tag=f"kt{hp}.{c}.{f}")
                for f in range(8):
                    top(lambda h, c_=c, f_=f, w_=sv_: h.matmul(
                        pb[2][:, 128:256],
                        A1[:, f_ * T + c_ * 128: f_ * T + (c_ + 1) * 128],
                        wres[:, w_ + f_ * 128: w_ + (f_ + 1) * 128],
                        start=(f_ == 0), stop=(f_ == 7)),
                        reads=[("wq", sv_ // 1024), (A1, "t0c")], writes=[(pb[2], "kt")],
                        tag=f"vt{hp}.{c}.{f}")
                vop(lambda h, c_=c: h.tensor_copy(
                    A3[:, kvt_off + c_ * 256: kvt_off + (c_ + 1) * 256], pb[2][:, 0:256]),
                    reads=[(pb[2], "kt")], writes=[(A3, "kvt", c)], tag=f"kv{hp}.{c}")

            def unit_steps(c, hh):
                hg = hp * 2 + hh
                Xn, Kq = XNb[hh], KQb[hh]
                pG = pb[3 + hh]
                pA = pb[5 + hh]
                pD = pb[7] if hh == 0 else pb[0]
                kfm = A3[hh * 64:hh * 64 + 64, T + c * 128: T + (c + 1) * 128]
                qfm = A3[hh * 64:hh * 64 + 64, c * 128:(c + 1) * 128]
                ktm = A3[:, kvt_off + c * 256 + hh * 64: kvt_off + c * 256 + hh * 64 + 64]
                vtm = A3[:, kvt_off + c * 256 + 128 + hh * 64: kvt_off + c * 256 + 128 + hh * 64 + 64]
                bco = bcols[:, 16 + hg:17 + hg]
                pp_, cc_ = hpk(hg)
                steps = []

                def s_gram():
                    top(lambda h, a=kfm, g=pG: h.matmul(g[:, 0:128], a, a, start=True, stop=True),
                        reads=[(A3, "qk", T)], writes=[(pG, "g")], tag=f"KK{hp}.{c}.{hh}")
                    top(lambda h, a=kfm, b=qfm, g=pG: h.matmul(g[:, 128:256], a, b, start=True, stop=True),
                        reads=[(A3, "qk", T), (A3, "qk", 0)], writes=[(pG, "g")], tag=f"KQ{hp}.{c}.{hh}")
                steps.append(s_gram)

                def s_mask():
                    vop(lambda h, g=pG, b=bco, X=Xn: h.scalar_tensor_tensor(
                        X[:, 128:256], g[:, 0:128], b, masks[:, 0:128], ALU.mult, ALU.mult),
                        reads=[(pG, "g"), (masks, 0), (bcols, "beta")], writes=[Xn], tag=f"mN{hp}.{c}.{hh}")
                    vop(lambda h, g=pG, b=bco, X=Xn: h.scalar_tensor_tensor(
                        X[:, 256:384], g[:, 0:128], b, masks[:, 128:256], ALU.mult, ALU.mult),
                        reads=[(pG, "g"), (masks, 1), (bcols, "beta")], writes=[Xn], tag=f"mT{hp}.{c}.{hh}")
                    vop(lambda h, g=pG, K=Kq: h.tensor_mul(K[:, :], g[:, 128:256], masks[:, 128:256]),
                        reads=[(pG, "g"), (masks, 1)], writes=[Kq], tag=f"mQ{hp}.{c}.{hh}")
                    vop(lambda h, a=ktm, b=bco, X=Xn: h.tensor_scalar_mul(X[:, 0:64], a, b),
                        reads=[(A3, "kvt", c), (bcols, "beta")], writes=[Xn], tag=f"Xa{hp}.{c}.{hh}")
                    vop(lambda h, a=vtm, b=bco, X=Xn: h.tensor_scalar_mul(X[:, 64:128], a, b),
                        reads=[(A3, "kvt", c), (bcols, "beta")], writes=[Xn], tag=f"Xb{hp}.{c}.{hh}")
                steps.append(s_mask)

                for k in range(NLVL):
                    last = (k == NLVL - 1)
                    ncols = 128 if last else 256

                    def s_mm(k=k, last=last, ncols=ncols):
                        top(lambda h, n_=ncols, X=Xn, A=pA: h.matmul(
                            A[:, 0:n_], X[:, 256:384], X[:, 0:n_], start=True, stop=True),
                            reads=[Xn], writes=[(pA, "a")], tag=f"sA{hp}.{c}.{hh}.{k}")
                        if not last:
                            top(lambda h, X=Xn, A=pA: h.matmul(
                                A[:, 256:384], X[:, 128:256], X[:, 256:384], start=True, stop=True),
                                reads=[Xn], writes=[(pA, "a")], tag=f"sB{hp}.{c}.{hh}.{k}")
                    steps.append(s_mm)

                    def s_up(k=k, last=last):
                        xo = ALU.subtract if k == 0 else ALU.add
                        vop(lambda h, o=xo, X=Xn, A=pA: h.tensor_tensor(
                            X[:, 0:128], X[:, 0:128], A[:, 0:128], o),
                            reads=[(pA, "a"), Xn], writes=[Xn], tag=f"sX{hp}.{c}.{hh}.{k}")
                        if not last:
                            vop(lambda h, X=Xn, A=pA: h.tensor_copy(X[:, 128:384], A[:, 128:384]),
                                reads=[(pA, "a")], writes=[Xn], tag=f"sN{hp}.{c}.{hh}.{k}")
                    steps.append(s_up)

                def s_hg():
                    top(lambda h, a=ktm, X=Xn, dd=pD: h.matmul(dd[0:64, 0:64], X[:, 0:64], a,
                                                               start=True, stop=True),
                        reads=[Xn, (A3, "kvt", c)], writes=[(pD, "d")], tag=f"Ht{hp}.{c}.{hh}")
                    top(lambda h, a=ktm, X=Xn, dd=pD: h.matmul(dd[0:64, 64:128], a, X[:, 64:128],
                                                               start=True, stop=True),
                        reads=[Xn, (A3, "kvt", c)], writes=[(pD, "d")], tag=f"Gt{hp}.{c}.{hh}")
                steps.append(s_hg)

                def s_hge():
                    vop(lambda h, P_=pp_, C_=cc_, c_=c, dd=pD: h.tensor_copy(
                        A4[P_:P_ + 64, hal_off + c_ * 512 + C_: hal_off + c_ * 512 + C_ + 64],
                        dd[0:64, 0:64]),
                        reads=[(pD, "d")], writes=[(A4, "hal")], tag=f"He{hp}.{c}.{hh}")
                    vop(lambda h, P_=pp_, C_=cc_, c_=c, dd=pD: h.tensor_copy(
                        A4[P_:P_ + 64, gal_off + c_ * 512 + C_: gal_off + c_ * 512 + C_ + 64],
                        dd[0:64, 64:128]),
                        reads=[(pD, "d")], writes=[(A4, "gal")], tag=f"Ge{hp}.{c}.{hh}")
                steps.append(s_hge)

                def s_zo():
                    top(lambda h, X=Xn, K=Kq, dd=pD: h.matmul(dd[0:64, 128:256], X[:, 0:64], K[:, :],
                                                              start=True, stop=True),
                        reads=[Xn, Kq], writes=[(pD, "d")], tag=f"Z{hp}.{c}.{hh}")
                    top(lambda h, X=Xn, K=Kq, dd=pD: h.matmul(dd[0:64, 256:384], X[:, 64:128], K[:, :],
                                                              start=True, stop=True),
                        reads=[Xn, Kq], writes=[(pD, "d")], tag=f"Op{hp}.{c}.{hh}")
                steps.append(s_zo)

                def s_ze():
                    vop(lambda h, q=qfm, P_=pp_, g=hg, c_=c, dd=pD: h.tensor_sub(
                        qmzv[P_:P_ + 64, c_ * 256 + (g % 2) * 128: c_ * 256 + (g % 2) * 128 + 128],
                        q, dd[0:64, 128:256]),
                        reads=[(pD, "d"), (A3, "qk", 0)], writes=[(qmzv, "qmz")], tag=f"QZ{hp}.{c}.{hh}")
                    vop(lambda h, g=hg, c_=c, dd=pD: h.tensor_copy(
                        A2[(g % 2) * 64:(g % 2) * 64 + 64,
                           (g // 2) * T + c_ * 128:(g // 2) * T + (c_ + 1) * 128],
                        dd[0:64, 256:384]),
                        reads=[(pD, "d")], writes=[(A2, "oall")], tag=f"Oe{hp}.{c}.{hh}")
                steps.append(s_ze)
                return steps

            kvt_block(0)
            for c in range(NCH):
                s0 = unit_steps(c, 0)
                s1 = unit_steps(c, 1)
                for si, (a, b) in enumerate(zip(s0, s1)):
                    a()
                    if si == 1 and c + 1 < NCH:
                        kvt_block(c + 1)
                    b()
            hpp = (hp % 2) * 64
            p.dma("sync", lambda h, hp_=hp, pp2=hpp: h.dma_start(
                out=qmz_d.ap()[pp2:pp2 + 64, hp_ * NCH * 256:(hp_ + 1) * NCH * 256],
                in_=qmzv[pp2:pp2 + 64, 0:NCH * 256]),
                reads=[(qmzv, "qmz")], writes=[("d", "qmz")], key=("d", "qmz"), tag=f"qz{hp}")

        p.fence([("d", "qmz")], [sch, schb, wch, wchb, sini, "xs1v"], tag="fchain")
        vop(lambda h: h.memset(sch[:, :], 0.0), writes=[sch], tag="Sz")
        vop(lambda h: h.memset(schb[:, :], 0.0), writes=[schb], tag="Szb")
        for c in range(NCH):
            vop(lambda h, c_=c: h.tensor_copy(ARC[:, c_ * 512:(c_ + 1) * 512], schb[:, :]),
                reads=[schb], writes=[(ARC, "s0log")], tag=f"sl{c}")
            for g in range(H):
                bk = pb[g % 2]
                pg, cg = hpk(g)
                top(lambda h, g_=g, c_=c, b_=bk, P_=pg, C_=cg: h.matmul(
                    b_[P_:P_ + 64, (g_ // 4) * 64:(g_ // 4) * 64 + 64],
                    A4[P_:P_ + 64, hal_off + c_ * 512 + C_: hal_off + c_ * 512 + C_ + 64],
                    schb[P_:P_ + 64, C_:C_ + 64],
                    start=True, stop=True),
                    reads=[(A4, "hal"), schb], writes=[(bk, "hs")], tag=f"HS{c}.{g}")
            vop(lambda h: h.tensor_sub(sch[:, 0:256], sch[:, 0:256], pb[0][:, 0:256]),
                reads=[(pb[0], "hs"), sch], writes=[sch], tag=f"Su0.{c}")
            vop(lambda h: h.tensor_sub(sch[:, 256:512], sch[:, 256:512], pb[1][:, 0:256]),
                reads=[(pb[1], "hs"), sch], writes=[sch], tag=f"Su1.{c}")
            vop(lambda h, c_=c: h.tensor_add(
                sch[:, :], sch[:, :], A4[:, gal_off + c_ * 512: gal_off + (c_ + 1) * 512]),
                reads=[(A4, "gal"), sch], writes=[sch], tag=f"Sg{c}")
            vop(lambda h: h.tensor_copy(schb[:, :], sch[:, :]), reads=[sch], writes=[schb],
                tag=f"Sc{c}")
        p.dma("sync", lambda h: h.dma_start(out=cc_s_in.ap(), in_=sch[:, :]),
              reads=[sch], writes=[("d", "ccs")], tag="sstg")
        p.cc(lambda h: h.collective_compute(
            "AllGather", ALU.bypass, replica_groups=GROUPS,
            ins=[cc_s_in.ap().opt()], outs=[cc_s_out.ap().opt()]),
            reads=[("d", "ccs")], writes=[("d", "ccso")], tag="sag")
        p.dma("sync", lambda h: h.dma_start(out=sini[:, :], in_=cc_s_out.ap()[0:128, :]),
              reads=[("d", "ccso")], writes=[sini], tag="srcv")
        vop(lambda h: h.tensor_scalar_mul(sini[:, :], sini[:, :], rolec[:, 0:1]),
            reads=[sini, rolec], writes=[sini], tag="smsk")
        vop(lambda h: h.tensor_copy(wch[:, :], sini[:, :]), reads=[sini], writes=[wch], tag="W0")
        vop(lambda h: h.tensor_copy(wchb[:, :], wch[:, :]), reads=[wch], writes=[wchb], tag="W0b")
        for c in range(NCH):
            vop(lambda h, c_=c: h.tensor_add(schb[:, :], ARC[:, c_ * 512:(c_ + 1) * 512], wchb[:, :]),
                reads=[(ARC, "s0log"), wchb], writes=[schb], tag=f"S0c{c}")
            p.dma("gpsimd", lambda h, c_=c: h.dma_start(
                out=bass.AP(qbv, 0, [[2048, 64], [512, 4], [1, 256]]),
                in_=bass.AP(qmz_d.ap().tensor, c_ * 256,
                            [[8 * NCH * 256, 64], [2 * NCH * 256, 4], [1, 256]])),
                reads=[("d", "qmz")], writes=[(qbv, "qb")], key=(qbv, "qb"), tag=f"qr{c}")
            p.dma("gpsimd", lambda h, c_=c: h.dma_start(
                out=bass.AP(qbv, 64 * 2048 + 256, [[2048, 64], [512, 4], [1, 256]]),
                in_=bass.AP(qmz_d.ap().tensor,
                            64 * 8 * NCH * 256 + NCH * 256 + c_ * 256,
                            [[8 * NCH * 256, 64], [2 * NCH * 256, 4], [1, 256]])),
                reads=[("d", "qmz")], writes=[(qbv, "qb")], key=(qbv, "qb"), tag=f"qro{c}")
            for g in range(H):
                bk = pb[2 + g % 2]
                pg, cg = hpk(g)
                top(lambda h, g_=g, b_=bk, P_=pg, C_=cg: h.matmul(
                    b_[P_:P_ + 64, (g_ // 4) * 128:(g_ // 4) * 128 + 128],
                    schb[P_:P_ + 64, C_:C_ + 64],
                    qbv[P_:P_ + 64, (g_ // 2) * 256 + (g_ % 2) * 128: (g_ // 2) * 256 + (g_ % 2) * 128 + 128],
                    start=True, stop=True),
                    reads=[schb, (qbv, "qb")], writes=[(bk, "of")], tag=f"Of{c}.{g}")
            for g in range(H):
                bk = pb[2 + g % 2]
                pg, cg = hpk(g)
                vop(lambda h, g_=g, c_=c, b_=bk, P_=pg: h.tensor_add(
                    A2[(g_ % 2) * 64:(g_ % 2) * 64 + 64,
                       (g_ // 2) * T + c_ * 128:(g_ // 2) * T + (c_ + 1) * 128],
                    A2[(g_ % 2) * 64:(g_ % 2) * 64 + 64,
                       (g_ // 2) * T + c_ * 128:(g_ // 2) * T + (c_ + 1) * 128],
                    b_[P_:P_ + 64, (g_ // 4) * 128:(g_ // 4) * 128 + 128]),
                    reads=[(bk, "of"), (A2, "oall")], writes=[(A2, "oall")], tag=f"Oa{c}.{g}")
            if c < NCH - 1:
                for g in range(H):
                    bk = pb[g % 2]
                    pg, cg = hpk(g)
                    top(lambda h, g_=g, c_=c, b_=bk, P_=pg, C_=cg: h.matmul(
                        b_[P_:P_ + 64, (g_ // 4) * 64:(g_ // 4) * 64 + 64],
                        A4[P_:P_ + 64, hal_off + c_ * 512 + C_: hal_off + c_ * 512 + C_ + 64],
                        wchb[P_:P_ + 64, C_:C_ + 64],
                        start=True, stop=True),
                        reads=[(A4, "hal"), wchb], writes=[(bk, "hw")], tag=f"HW{c}.{g}")
                vop(lambda h: h.tensor_sub(wch[:, 0:256], wch[:, 0:256], pb[0][:, 0:256]),
                    reads=[(pb[0], "hw"), wch], writes=[wch], tag=f"Wu0.{c}")
                vop(lambda h: h.tensor_sub(wch[:, 256:512], wch[:, 256:512], pb[1][:, 0:256]),
                    reads=[(pb[1], "hw"), wch], writes=[wch], tag=f"Wu1.{c}")
                vop(lambda h: h.tensor_copy(wchb[:, :], wch[:, :]), reads=[wch], writes=[wchb],
                    tag=f"Wb{c}")

        p.fence([(A3, "qk", 0), (A3, "qk", T), (A3, "kvt")],
                [(A3, "x"), "xs1v", "qmzv", "masks", sch, schb, wch, wchb, sini,
                 "qbv"], tag="fA3x")
        p.fence([], [("wq", 0), ("wq", 1), ("wq", 2), ("wq", 3),
                     ("ws", 0), ("ws", 1)], tag="fwres")
        for f in range(8):
            dmal(A3[:, f * T:(f + 1) * T], x_in[f * 128:(f + 1) * 128, :], (A3, "x", f), tag=f"xr{f}")
        p.fence([(A1, "t0c")], [(A1, "xnc")], tag="fA1x")
        for f in range(8):
            for c2 in range(0, T, SW3):
                W3 = min(SW3, T - c2)
                vop(lambda h, f_=f, c_=c2, W_=W3: h.tensor_mul(
                    scr3[:, 0:W_], A3[:, f_ * T + c_:f_ * T + c_ + W_], bca[:, c_:c_ + W_]),
                    reads=[(A3, "x", f), (bca, 0)], writes=[scr3], tag=f"xna{f}.{c2}")
                vop(lambda h, f_=f, c_=c2, W_=W3: h.tensor_add(
                    A1[:, f_ * T + c_:f_ * T + c_ + W_], scr3[:, 0:W_], bca[:, T + c_:T + c_ + W_]),
                    reads=[scr3, (bca, 1)], writes=[(A1, "xnc")], tag=f"xnb{f}.{c2}")
        p.fence([(A4, "hal"), (A4, "gal")], [(A4, "u")], tag="fA4u")
        for dg in range(4):
            so = (dg % 2) * 2048
            ldg(wres, so, 4096, woT.tensor, D, 0, D, dg * 256, 256, ("ws", so // 2048), tag=f"lwo{dg}")
            for oi in range(2):
                dt_i = dg * 2 + oi
                wob = pb[4 if dt_i % 2 == 0 else 5]
                for t4 in range(NT4):
                    for f in range(8):
                        top(lambda h, f_=f, t_=t4, o_=oi, s_=so, b_=wob: h.matmul(
                            b_[:, 0:TS], wres[:, s_ + f_ * 256 + o_ * 128: s_ + f_ * 256 + (o_ + 1) * 128],
                            A2[:, f_ * T + t_ * TS: f_ * T + (t_ + 1) * TS],
                            start=(f_ == 0), stop=(f_ == 7)),
                            reads=[("ws", so // 2048), (A2, "oall")], writes=[(wob, "wo")],
                            tag=f"wo{dt_i}.{t4}.{f}")
                    vop(lambda h, d_=dt_i, t_=t4, b_=wob: h.tensor_add(
                        A4[:, d_ * T + t_ * TS: d_ * T + (t_ + 1) * TS], b_[:, 0:TS],
                        A1[:, d_ * T + t_ * TS: d_ * T + (t_ + 1) * TS]),
                        reads=[(wob, "wo"), (A1, "xnc")], writes=[(A4, "u")], tag=f"u{dt_i}.{t4}")
        stats_of(A4, (A4, "u"), 7, 8, pb[0], "su")
        rsqrt_row(8, 9, "al2")
        bcast(9, 0, "b_al2")
        bcast(7, 1, "b_m2")
        negmul_tile("ma2t")
        p.dma("gpsimd", lambda h, a=bcols[:, 32:64], b=bass.AP(b1v.tensor, 0, [[1, 128], [128, 32]]): h.dma_start(out=a, in_=b, allow_slow_non_contiguous=True), writes=[(bcols, "b1")], key=(bcols, "b1"), tag="b1")
        p.dma("gpsimd", lambda h, a=bcols[:, 64:72], b=bass.AP(b2v.tensor, 0, [[1, 128], [128, 8]]): h.dma_start(out=a, in_=b, allow_slow_non_contiguous=True), writes=[(bcols, "b2")], key=(bcols, "b2"), tag="b2")
        def bounce16(srcT, dstD, K_, O_, nn):
            for kk in range(4):
                R4 = K_ // 4
                p.dma("gpsimd", lambda h, k_=kk, D_=dstD, R_=R4: h.dma_start(
                    out=D_.ap()[k_ * R_:(k_ + 1) * R_, :],
                    in_=srcT[k_ * R_:(k_ + 1) * R_, :]),
                    writes=[("d", nn)], key=("d", nn), tag=f"{nn}s{kk}")
        bounce16(w1T, w116_d, D, FF, "w1c")
        bounce16(w2T, w216_d, FF, D, "w2c")
        p.fence([(A2, "oall")], [(A2, "ghm")], tag="fA2g")
        p.fence([], [rows, "w1v"], tag="fw1v")
        for tb in range(4):
            for f in range(8):
                vop(lambda h, f_=f, b_=tb: h.tensor_mul(
                    scr3[:, 0:TB], A4[:, f_ * T + b_ * TB: f_ * T + (b_ + 1) * TB], bca[:, b_ * TB:(b_ + 1) * TB]),
                    reads=[(A4, "u"), (bca, 0)], writes=[scr3], tag=f"wia{tb}.{f}")
                vop(lambda h, f_=f, b_=tb: h.tensor_add(
                    scr3[:, 0:TB], scr3[:, 0:TB], bca[:, T + b_ * TB: T + (b_ + 1) * TB]),
                    reads=[scr3, (bca, 1)], writes=[scr3], tag=f"wib{tb}.{f}")
                vop(lambda h, f_=f: h.tensor_scalar(
                    w1v[:, f_ * TB:(f_ + 1) * TB], scr3[:, 0:TB],
                    lncol[:, 16 + f_:17 + f_], lncol[:, 24 + f_:25 + f_], ALU.mult, ALU.add),
                    reads=[scr3, (lncol, 16), (lncol, 24)], writes=[(w1v, "w1in", tb)], tag=f"wic{tb}.{f}")
            for hgrp in range(16):
                so = (hgrp % 2) * 2048
                ldg(wres, so, 4096, w116_d, FF, 0, D, hgrp * 256, 256,
                    ("ws", so // 2048), tag=f"lw1{tb}.{hgrp}", rd=[("d", "w1c")])
                for ho in range(2):
                    ht = hgrp * 2 + ho
                    wb = pb[5 if ht % 2 == 0 else 7]
                    for t4 in range(NTB):
                        for f in range(8):
                            top(lambda h, f_=f, o_=ho, t_=t4, s_=so, b_=wb: h.matmul(
                                b_[:, 0:TSB],
                                wres[:, s_ + f_ * 256 + o_ * 128: s_ + f_ * 256 + (o_ + 1) * 128],
                                w1v[:, f_ * TB + t_ * TSB: f_ * TB + (t_ + 1) * TSB],
                                start=(f_ == 0), stop=(f_ == 7)),
                                reads=[("ws", so // 2048), (w1v, "w1in", tb)], writes=[(wb, "w1")],
                                tag=f"w1m{tb}.{ht}.{t4}.{f}")
                        aop(lambda h, h_=ht, t_=t4, b_=wb: h.activation(
                            A2[:, h_ * TB + t_ * TSB: h_ * TB + (t_ + 1) * TSB], b_[:, 0:TSB],
                            AF.Gelu, bias=bcols[:, 32 + h_:33 + h_]),
                            reads=[(wb, "w1"), (bcols, "b1")], writes=[(A2, "ghm")],
                            tag=f"w1g{tb}.{ht}.{t4}")
            for dt_i in range(8):
                w2b = pb[6 if dt_i % 2 == 0 else 2]
                for hf in range(2):
                    so = ((dt_i * 2 + hf) % 2) * 2048
                    ldg(wres, so, 4096, w216_d, D, hf * 2048, 2048, dt_i * 128, 128,
                        ("ws", so // 2048), tag=f"lw2{tb}.{dt_i}.{hf}", rd=[("d", "w2c")])
                    for t4 in range(NTB):
                        for hc in range(16):
                            top(lambda h, hc_=hc, t_=t4, h2=hf, s_=so, b_=w2b: h.matmul(
                                b_[:, 0:TSB],
                                wres[:, s_ + hc_ * 128: s_ + (hc_ + 1) * 128],
                                A2[:, (h2 * 16 + hc_) * TB + t_ * TSB: (h2 * 16 + hc_) * TB + (t_ + 1) * TSB],
                                start=(h2 == 0 and hc_ == 0), stop=(h2 == 1 and hc_ == 15)),
                                reads=[("ws", so // 2048), (A2, "ghm")], writes=[(w2b, "w2")],
                                tag=f"w2m{tb}.{dt_i}.{hf}.{t4}.{hc}")
                if True:
                    for t4 in range(NTB):
                        aop(lambda h, d_=dt_i, t_=t4, b_=w2b: h.activation(
                            scr3[:, 0:TSB], b_[:, 0:TSB], AF.Identity,
                            bias=bcols[:, 64 + d_:65 + d_]),
                            reads=[(w2b, "w2"), (bcols, "b2")], writes=[scr3], tag=f"s0a{tb}.{dt_i}.{t4}")
                        vop(lambda h, d_=dt_i, t_=t4, b_=tb: h.tensor_add(
                            scr[:, 0:TSB], scr3[:, 0:TSB],
                            A4[:, d_ * T + b_ * TB + t_ * TSB: d_ * T + b_ * TB + (t_ + 1) * TSB]),
                            reads=[scr3, (A4, "u")], writes=[(scr, "s0")], tag=f"s0b{tb}.{dt_i}.{t4}")
                        p.dma("sync", lambda h, d_=dt_i, t_=t4, b_=tb: h.dma_start(
                            out=s0_d.ap()[d_ * 128:(d_ + 1) * 128,
                                          b_ * TB + t_ * TSB: b_ * TB + (t_ + 1) * TSB],
                            in_=scr[:, 0:TSB]),
                            reads=[(scr, "s0")], writes=[("d", "s0")], key=("d", "s0"),
                            tag=f"s0d{tb}.{dt_i}.{t4}")

        def quant_prep(wT_ap, K_, O_, scol, dstD, nn):
            PC = min(O_, T)
            vop(lambda h: h.memset(qsc[:, scol + 2:scol + 3], 0.0), writes=[(qsc, scol, "mx")],
                tag=nn + "z")
            for kk in range(K_ // 128):
                for ob in range(0, O_, PC):
                    dmal(scr[:, 0:PC], wT_ap[kk * 128:(kk + 1) * 128, ob:ob + PC], (scr, "wc"),
                         tag=f"{nn}a{kk}.{ob}")
                    vop(lambda h, P_=PC: h.reduce_max(qsc[:, scol + 3:scol + 4], scr[:, 0:P_],
                                                      axis=AX.X, apply_absolute_value=True),
                        reads=[(scr, "wc")], writes=[(qsc, scol, "t")], tag=f"{nn}r{kk}.{ob}")
                    vop(lambda h: h.tensor_max(qsc[:, scol + 2:scol + 3], qsc[:, scol + 2:scol + 3],
                                               qsc[:, scol + 3:scol + 4]),
                        reads=[(qsc, scol, "t"), (qsc, scol, "mx")], writes=[(qsc, scol, "mx")],
                        tag=f"{nn}m{kk}.{ob}")
            p.dma("gpsimd", lambda h: h.dma_start(
                out=qmax[0:1, 4:132],
                in_=bass.AP(qsc, scol + 2, [[12, 128], [1, 1]]),
                allow_slow_non_contiguous=True),
                reads=[(qsc, scol, "mx")], writes=[qmax], key=qmax, tag=f"{nn}tr")
            vop(lambda h: h.reduce_max(qmax[0:1, 0:1], qmax[0:1, 4:132], axis=AX.X),
                reads=[qmax], writes=[qmax], tag=f"{nn}rm")
            p.dma("gpsimd", lambda h: h.dma_start(
                out=qsc[:, scol + 4:scol + 5],
                in_=bass.AP(qmax, 0, [[1, 1], [0, 128], [1, 1]])),
                reads=[qmax], writes=[(qsc, scol, "gm")], key=(qsc, scol, "gm"),
                tag=f"{nn}ar")
            vop(lambda h: h.tensor_scalar_mul(qsc[:, scol:scol + 1], qsc[:, scol + 4:scol + 5],
                                              1.0 / 127.0),
                reads=[(qsc, scol, "gm")], writes=[(qsc, scol, "s")], tag=f"{nn}s")
            vop(lambda h: h.reciprocal(qsc[:, scol + 1:scol + 2], qsc[:, scol:scol + 1]),
                reads=[(qsc, scol, "s")], writes=[(qsc, scol, "rs")], tag=f"{nn}rs")
            for kk in range(K_ // 128):
                for ob in range(0, O_, PC):
                    dmal(scr[:, 0:PC], wT_ap[kk * 128:(kk + 1) * 128, ob:ob + PC], (scr, "wc"),
                         tag=f"{nn}b{kk}.{ob}")
                    S3 = max(T // 2, 512)
                    for o2 in range(0, PC, S3):
                        W2 = min(S3, PC - o2)
                        aop(lambda h, o_=o2, W_=W2: h.activation(
                            scr3[:, 0:W_], scr[:, o_:o_ + W_], AF.Identity,
                            scale=qsc[:, scol + 1:scol + 2]),
                            reads=[(scr, "wc"), (qsc, scol, "rs")], writes=[scr3], tag=f"{nn}q{kk}.{ob}.{o2}")
                        vop(lambda h, o_=o2, W_=W2: h.tensor_scalar(
                            rqv[:, o_:o_ + W_], scr3[:, 0:W_], MAGIC, MAGIC, ALU.add, ALU.subtract),
                            reads=[scr3], writes=[(rqv, "rq")], tag=f"{nn}w{kk}.{ob}.{o2}")
                    p.dma("sync", lambda h, k_=kk, o_=ob, D_=dstD, P_=PC: h.dma_start(
                        out=D_.ap()[k_ * 128:(k_ + 1) * 128, o_:o_ + P_], in_=rqv[:, 0:P_]),
                        reads=[(rqv, "rq")], writes=[("d", nn)], key=("d", nn),
                        tag=f"{nn}o{kk}.{ob}")
        p.fence([], [(bca, 0), (bca, 1), "rqv"], tag="frq")
        quant_prep(WfcT, D, FF, 0, rfc_d, "qfc")
        quant_prep(WprojT, FF, D, 4, rpr_d, "qpr")
        p.fence([], ["w1v", rows, "afc_s", "apr_s", bfc_s, bpr_s, "lra_s"],
                tag="fEsm")
        afc_v = afc_s[:, :]
        apr_v = apr_s[:, :]
        bfcs_v = bfc_s[:, :]
        bprs_v = bpr_s[:, :]
        lra_v = lra_s[:, :]
        for f in range(8):
            dmal(afc_v[:, f * 16:(f + 1) * 16], AfcT[f * 128:(f + 1) * 128, :], (afc_s, f),
                 tag=f"afc{f}")
        for hc in range(32):
            dmal(apr_v[:, hc * 16:(hc + 1) * 16], AprojT[hc * 128:(hc + 1) * 128, :],
                 (apr_s, hc), tag=f"apr{hc}")
        dmal(bfcs_v, BfcT, bfc_s, tag="bfc")
        dmal(bprs_v, BprojT, bpr_s, tag="bpr")
        p.dma("gpsimd", lambda h, a=bcols[:, 72:104], b=bass.AP(bfcv.tensor, 0, [[1, 128], [128, 32]]): h.dma_start(out=a, in_=b, allow_slow_non_contiguous=True), writes=[(bcols, "bfc")], key=(bcols, "bfc"), tag="bfcc")
        p.dma("gpsimd", lambda h, a=bcols[:, 104:112], b=bass.AP(bprv.tensor, 0, [[1, 128], [128, 8]]): h.dma_start(out=a, in_=b, allow_slow_non_contiguous=True), writes=[(bcols, "bpr")], key=(bcols, "bpr"), tag="bprc")
        for t4 in range(NT4):
            for f in range(8):
                top(lambda h, f_=f, t_=t4: h.matmul(
                    pb[0][0:16, 0:TS], afc_v[:, f_ * 16:(f_ + 1) * 16],
                    A1[:, f_ * T + t_ * TS: f_ * T + (t_ + 1) * TS],
                    start=(f_ == 0), stop=(f_ == 7)),
                    reads=[(afc_s, f), (A1, "xnc")], writes=[(pb[0], "lm")],
                    tag=f"lm{t4}.{f}")
            vop(lambda h, t_=t4: h.tensor_scalar_mul(lra_v[:, t_ * TS:(t_ + 1) * TS],
                                                     pb[0][0:16, 0:TS], 2.0),
                reads=[(pb[0], "lm")], writes=[(lra_s, "a")], tag=f"lme{t4}")
        p.fence([(A2, "ghm")], [(A2, "gf")], tag="fA2f")
        for tb in range(4):
            for hgrp in range(16):
                so = (hgrp % 2) * 2048
                ldg(wres, so, 4096, rfc_d, FF, 0, D, hgrp * 256, 256,
                    ("ws", so // 2048), tag=f"lrf{tb}.{hgrp}", rd=[("d", "qfc")])
                for ho in range(2):
                    ht = hgrp * 2 + ho
                    fcb = pb[5 if ht % 2 == 0 else 7]
                    for t4 in range(NTB):
                        for f in range(8):
                            top(lambda h, f_=f, o_=ho, t_=t4, b_=tb, s_=so, k_=fcb: h.matmul(
                                k_[:, 0:TSB],
                                wres[:, s_ + f_ * 256 + o_ * 128: s_ + f_ * 256 + (o_ + 1) * 128],
                                A1[:, f_ * T + b_ * TB + t_ * TSB: f_ * T + b_ * TB + (t_ + 1) * TSB],
                                start=(f_ == 0), stop=(f_ == 7)),
                                reads=[("ws", so // 2048), (A1, "xnc")], writes=[(fcb, "fc")],
                                tag=f"fcm{tb}.{ht}.{t4}.{f}")
                        top(lambda h, h_=ht, t_=t4, b_=tb: h.matmul(
                            pb[3][:, 0:TSB], bfcs_v[:, h_ * 128:(h_ + 1) * 128],
                            lra_v[:, b_ * TB + t_ * TSB: b_ * TB + (t_ + 1) * TSB],
                            start=True, stop=True),
                            reads=[bfc_s, (lra_s, "a")], writes=[(pb[3], "fl")],
                            tag=f"fcl{tb}.{ht}.{t4}")
                        aop(lambda h, t_=t4, k_=fcb: h.activation(
                            scr3[:, 0:TSB], k_[:, 0:TSB], AF.Identity, scale=qsc[:, 0:1]),
                            reads=[(fcb, "fc"), (qsc, 0, "s")], writes=[scr3], tag=f"fcs{tb}.{ht}.{t4}")
                        vop(lambda h, t_=t4: h.tensor_add(scr3[:, 0:TSB], scr3[:, 0:TSB], pb[3][:, 0:TSB]),
                            reads=[scr3, (pb[3], "fl")], writes=[scr3], tag=f"fca{tb}.{ht}.{t4}")
                        aop(lambda h, h_=ht, t_=t4: h.activation(
                            A2[:, h_ * TB + t_ * TSB: h_ * TB + (t_ + 1) * TSB], scr3[:, 0:TSB],
                            AF.Gelu, bias=bcols[:, 72 + h_:73 + h_]),
                            reads=[scr3, (bcols, "bfc")], writes=[(A2, "gf")],
                            tag=f"fcg{tb}.{ht}.{t4}")
            for t4 in range(NTB):
                for hc in range(32):
                    top(lambda h, hc_=hc, t_=t4: h.matmul(
                        pb[0][0:16, 0:TSB], apr_v[:, hc_ * 16:(hc_ + 1) * 16],
                        A2[:, hc_ * TB + t_ * TSB: hc_ * TB + (t_ + 1) * TSB],
                        start=(hc_ == 0), stop=(hc_ == 31)),
                        reads=[(apr_s, hc), (A2, "gf")], writes=[(pb[0], "lm2")],
                        tag=f"l2{tb}.{t4}.{hc}")
                vop(lambda h, t_=t4, b_=tb: h.tensor_scalar_mul(
                    lra_v[:, b_ * TB + t_ * TSB: b_ * TB + (t_ + 1) * TSB], pb[0][0:16, 0:TSB], 2.0),
                    reads=[(pb[0], "lm2")], writes=[(lra_s, "b", tb)], tag=f"l2e{tb}.{t4}")
            for dgp in range(4):
                p.dma("gpsimd", lambda h, b_=tb, g_=dgp: h.dma_start(
                    out=d3(scr, 0, 2, TB, T),
                    in_=g3(s0_d, T, g_ * 256, 256, b_ * TB, TB)),
                    reads=[("d", "s0")], writes=[(scr, "s0b")], key=(scr, "s0b"), tag=f"s0r{tb}.{dgp}")
                for do in range(2):
                    dt_i = dgp * 2 + do
                    prb = pb[6 if dt_i % 2 == 0 else 2]
                    for hf in range(2):
                        so = ((dt_i * 2 + hf) % 2) * 2048
                        ldg(wres, so, 4096, rpr_d, D, hf * 2048, 2048, dt_i * 128, 128,
                            ("ws", so // 2048), tag=f"lrp{tb}.{dt_i}.{hf}", rd=[("d", "qpr")])
                        for t4 in range(NTB):
                            for hc in range(16):
                                top(lambda h, hc_=hc, t_=t4, h2=hf, s_=so, k_=prb: h.matmul(
                                    k_[:, 0:TSB],
                                    wres[:, s_ + hc_ * 128: s_ + (hc_ + 1) * 128],
                                    A2[:, (h2 * 16 + hc_) * TB + t_ * TSB: (h2 * 16 + hc_) * TB + (t_ + 1) * TSB],
                                    start=(h2 == 0 and hc_ == 0), stop=(h2 == 1 and hc_ == 15)),
                                    reads=[("ws", so // 2048), (A2, "gf")], writes=[(prb, "pr")],
                                    tag=f"prm{tb}.{dt_i}.{hf}.{t4}.{hc}")
                    for t4 in range(NTB):
                        top(lambda h, d_=dt_i, t_=t4, b_=tb: h.matmul(
                            pb[4][:, 0:TSB], bprs_v[:, d_ * 128:(d_ + 1) * 128],
                            lra_v[:, b_ * TB + t_ * TSB: b_ * TB + (t_ + 1) * TSB],
                            start=True, stop=True),
                            reads=[bpr_s, (lra_s, "b", tb)], writes=[(pb[4], "pl")],
                            tag=f"prl{tb}.{dt_i}.{t4}")
                        aop(lambda h, t_=t4, k_=prb: h.activation(
                            scr3[:, 0:TSB], k_[:, 0:TSB], AF.Identity, scale=qsc[:, 4:5]),
                            reads=[(prb, "pr"), (qsc, 4, "s")], writes=[scr3], tag=f"prs{tb}.{dt_i}.{t4}")
                        vop(lambda h, t_=t4: h.tensor_add(scr3[:, 0:TSB], scr3[:, 0:TSB], pb[4][:, 0:TSB]),
                            reads=[scr3, (pb[4], "pl")], writes=[scr3], tag=f"pra{tb}.{dt_i}.{t4}")
                        aop(lambda h, d_=dt_i, t_=t4: h.activation(
                            rqv[:, 0:TSB],
                            scr3[:, 0:TSB], AF.Silu, bias=bcols[:, 104 + d_:105 + d_]),
                            reads=[scr3, (bcols, "bpr")], writes=[(rqv, "rq")], tag=f"prg{tb}.{dt_i}.{t4}")
                        vop(lambda h, d_=dt_i, t_=t4: h.tensor_mul(
                            scr3[:, 0:TSB], rqv[:, 0:TSB],
                            scr[:, (d_ % 2) * TB + t_ * TSB: (d_ % 2) * TB + (t_ + 1) * TSB]),
                            reads=[(rqv, "rq"), (scr, "s0b")], writes=[scr3], tag=f"g0m{tb}.{dt_i}.{t4}")
                        vop(lambda h, d_=dt_i, t_=t4, b_=tb: h.tensor_add(
                            A3[:, d_ * T + b_ * TB + t_ * TSB: d_ * T + b_ * TB + (t_ + 1) * TSB],
                            A3[:, d_ * T + b_ * TB + t_ * TSB: d_ * T + b_ * TB + (t_ + 1) * TSB],
                            scr3[:, 0:TSB]),
                            reads=[scr3, (A3, "x", dt_i)], writes=[(A3, "x", dt_i)],
                            tag=f"g0a{tb}.{dt_i}.{t4}")

        stats_of(A3, [(A3, "x", f) for f in range(8)], 0, 1, pb[0], "s1")
        rsqrt_row(1, 2, "rsL1")
        p.fence([], ["rqv", (bca, 0), (bca, 1)], tag="fFt")
        bcast(2, 0, "b_rsl1")
        bcast(0, 1, "b_ml1")
        negmul_tile("mrl1t")
        p.fence([(A1, "xnc")], [(A1, "x1c")], tag="fA1f")
        for f in range(8):
            for c2 in range(0, T, SW3):
                W3 = min(SW3, T - c2)
                vop(lambda h, f_=f, c_=c2, W_=W3: h.tensor_mul(
                    scr3[:, 0:W_], A3[:, f_ * T + c_:f_ * T + c_ + W_], bca[:, c_:c_ + W_]),
                    reads=[(A3, "x", f), (bca, 0)], writes=[scr3], tag=f"f1a{f}.{c2}")
                vop(lambda h, f_=f, c_=c2, W_=W3: h.tensor_add(
                    A1[:, f_ * T + c_:f_ * T + c_ + W_], scr3[:, 0:W_], bca[:, T + c_:T + c_ + W_]),
                    reads=[scr3, (bca, 1)], writes=[(A1, "x1c")], tag=f"f1b{f}.{c2}")
        p.dma("gpsimd", lambda h: h.dma_start(
            out=convw[:, 0:64], in_=bass.AP(Wcv.tensor, 0, [[4, 128], [512, 16], [1, 4]]),
            allow_slow_non_contiguous=True),
            writes=[convw], key=convw, tag="cw")
        p.dma("gpsimd", lambda h, a=bcols[:, 112:128], b=bass.AP(bconv.tensor, 0, [[1, 128], [128, 16]]): h.dma_start(out=a, in_=b, allow_slow_non_contiguous=True), writes=[(bcols, "bcv")], key=(bcols, "bcv"), tag="bcv")
        p.dma("gpsimd", lambda h, a=bcols[:, 128:144], b=bass.AP(bdtv.tensor, 0, [[1, 128], [128, 16]]): h.dma_start(out=a, in_=b, allow_slow_non_contiguous=True), writes=[(bcols, "bdt")], key=(bcols, "bdt"), tag="bdt")
        p.dma("gpsimd", lambda h, a=bcols[:, 144:160], b=bass.AP(A_log.tensor, 0, [[1, 128], [128, 16]]): h.dma_start(out=a, in_=b, allow_slow_non_contiguous=True), writes=[(bcols, "alog")], key=(bcols, "alog"), tag="alog")
        aop(lambda h: h.activation(bcols[:, 160:176], bcols[:, 144:160], AF.Exp),
            reads=[(bcols, "alog")], writes=[(bcols, "Ae")], tag="Aexp")
        vop(lambda h: h.tensor_scalar_mul(bcols[:, 160:176], bcols[:, 160:176], -1.0),
            reads=[(bcols, "Ae")], writes=[(bcols, "Ae")], tag="Aneg")
        bounce16(WiT, wi16_d, D, 2 * DI, "wic")
        bounce16(WdtT, wdt16_d, DI, DI, "dtc")
        bounce16(WBT, wb16_d, DI, DI, "wbc")
        bounce16(WCT, wc16_d, DI, DI, "wcc")
        bounce16(WoT1, wo116_d, DI, D, "woc")
        p.fence([], [(bca, 0), (bca, 1), "egv"], tag="feg")
        for og2 in range(16):
            so = (og2 % 2) * 2048
            ldg(wres, so, 4096, wi16_d, 2 * DI, 0, D, og2 * 256, 256,
                ("ws", so // 2048), tag=f"we{og2}", rd=[("d", "wic")])
            for f in range(8):
                top(lambda h, f_=f, s_=so: h.matmul(
                    pb[1][0:4, 0:256], A1[:, f_ * T + T - 4: f_ * T + T],
                    wres[:, s_ + f_ * 256: s_ + (f_ + 1) * 256],
                    start=(f_ == 0), stop=(f_ == 7)),
                    reads=[("ws", so // 2048), (A1, "x1c")], writes=[(pb[1], "eg")], tag=f"eg{og2}.{f}")
            vop(lambda h, o_=og2: h.tensor_copy(egv[0:4, o_ * 256:(o_ + 1) * 256], pb[1][0:4, 0:256]),
                reads=[(pb[1], "eg")], writes=[(egv, "eg")], tag=f"ege{og2}")
        p.dma("gpsimd", lambda h: h.dma_start(out=cc_e_in.ap(), in_=egv[0:4, 0:4096]),
              reads=[(egv, "eg")], writes=[("d", "cce")], tag="estg")
        p.cc(lambda h: h.collective_compute(
            "AllGather", ALU.bypass, replica_groups=GROUPS,
            ins=[cc_e_in.ap().opt()], outs=[cc_e_out.ap().opt()]),
            reads=[("d", "cce")], writes=[("d", "cceo")], tag="eag")
        for j in range(4):
            p.dma("gpsimd", lambda h, j_=j: h.dma_start(
                out=bass.AP(halos, j_, [[64, 128], [4, 16]]),
                in_=bass.AP(cc_e_out.ap().tensor, j_ * 2 * DI, [[1, 128], [128, 16]]),
                allow_slow_non_contiguous=True),
                reads=[("d", "cceo")], writes=[halos], key=halos, tag=f"ercv{j}")
        vop(lambda h: h.tensor_scalar_mul(halos[:, :], halos[:, :], rolec[:, 0:1]),
            reads=[halos, rolec], writes=[halos], tag="emsk")
        for oc in range(16):
            vop(lambda h, o_=oc: h.tensor_scalar_mul(
                xce[:, o_:o_ + 1], halos[:, o_ * 4:o_ * 4 + 1], convw[:, o_ * 4:o_ * 4 + 1]),
                reads=[halos, convw], writes=[xce], tag=f"xce{oc}.0")
            for j in range(1, 4):
                vop(lambda h, o_=oc, j_=j: h.scalar_tensor_tensor(
                    xce[:, o_:o_ + 1], halos[:, o_ * 4 + j_:o_ * 4 + j_ + 1],
                    convw[:, o_ * 4 + j_:o_ * 4 + j_ + 1], xce[:, o_:o_ + 1],
                    ALU.mult, ALU.add),
                    reads=[halos, convw, xce], writes=[xce], tag=f"xce{oc}.{j}")
            aop(lambda h, o_=oc: h.activation(xce[:, o_:o_ + 1], xce[:, o_:o_ + 1], AF.Silu,
                                              bias=bcols[:, 112 + o_:113 + o_]),
                reads=[xce, (bcols, "bcv")], writes=[xce], tag=f"xcs{oc}")
        vop(lambda h: h.memset(accol[:, :], 1.0), writes=[accol], tag="ac1")
        vop(lambda h: h.memset(hcol[:, :], 0.0), writes=[hcol], tag="hc0")
        vop(lambda h: h.memset(ucol[:, :], 0.0), writes=[ucol], tag="uc0")
        p.fence([], ["egv", "szv"], tag="fsz")
        x1o = 0
        dto = 4 * T
        p.fence([(A4, "u")], [(A4, "x1"), (A4, "dtc")], tag="fA4f")
        for blk in range(4):
            if blk > 0:
                for oc in range(16):
                    vop(lambda h, o_=oc: h.tensor_copy(
                        halos[:, o_ * 4 + 1:o_ * 4 + 4],
                        A4[:, x1o + o_ * TB + TB - 3: x1o + o_ * TB + TB]),
                        reads=[(A4, "x1")], writes=[halos], tag=f"hup{blk}.{oc}")
            for og2 in range(8):
                so = (og2 % 2) * 2048
                ldg(wres, so, 4096, wi16_d, 2 * DI, 0, D, og2 * 256, 256,
                    ("ws", so // 2048), tag=f"wx{blk}.{og2}", rd=[("d", "wic")])
                for oi in range(2):
                    oc = og2 * 2 + oi
                    xb_ = pb[2 if oc % 2 == 0 else 0]
                    for f in range(8):
                        top(lambda h, f_=f, o_=oi, b_=blk, s_=so, k_=xb_: h.matmul(
                            k_[:, 0:TB],
                            wres[:, s_ + f_ * 256 + o_ * 128: s_ + f_ * 256 + (o_ + 1) * 128],
                            A1[:, f_ * T + b_ * TB: f_ * T + (b_ + 1) * TB],
                            start=(f_ == 0), stop=(f_ == 7)),
                            reads=[("ws", so // 2048), (A1, "x1c")], writes=[(xb_, "x1")],
                            tag=f"x1m{blk}.{oc}.{f}")
                    vop(lambda h, o_=oc, k_=xb_: h.tensor_copy(
                        A4[:, x1o + o_ * TB: x1o + (o_ + 1) * TB], k_[:, 0:TB]),
                        reads=[(xb_, "x1")], writes=[(A4, "x1")], tag=f"x1e{blk}.{oc}")
            if blk == 0:
                p.fence([(ARC, "s0log")], [(ARC, "xc")], tag="fARC")
            for oc in range(16):
                for j in range(4):
                    nh = 3 - j
                    if nh > 0:
                        if j == 0:
                            vop(lambda h, o_=oc, j_=j, n_=nh: h.tensor_scalar_mul(
                                scr3[:, 0:n_], halos[:, o_ * 4 + 1 + j_:o_ * 4 + 1 + j_ + n_],
                                convw[:, o_ * 4 + j_:o_ * 4 + j_ + 1]),
                                reads=[halos, convw], writes=[scr3], tag=f"cve{blk}.{oc}.{j}")
                        else:
                            vop(lambda h, o_=oc, j_=j, n_=nh: h.scalar_tensor_tensor(
                                scr3[:, 0:n_], halos[:, o_ * 4 + 1 + j_:o_ * 4 + 1 + j_ + n_],
                                convw[:, o_ * 4 + j_:o_ * 4 + j_ + 1], scr3[:, 0:n_],
                                ALU.mult, ALU.add),
                                reads=[halos, convw, scr3], writes=[scr3], tag=f"cve{blk}.{oc}.{j}")
                    nb = TB - nh
                    if j == 0:
                        vop(lambda h, o_=oc, j_=j, n_=nh, m_=nb: h.tensor_scalar_mul(
                            scr3[:, n_:n_ + m_],
                            A4[:, x1o + o_ * TB: x1o + o_ * TB + m_],
                            convw[:, o_ * 4 + j_:o_ * 4 + j_ + 1]),
                            reads=[(A4, "x1"), convw], writes=[scr3], tag=f"cvb{blk}.{oc}.{j}")
                    else:
                        vop(lambda h, o_=oc, j_=j, n_=nh, m_=nb: h.scalar_tensor_tensor(
                            scr3[:, n_:n_ + m_],
                            A4[:, x1o + o_ * TB: x1o + o_ * TB + m_],
                            convw[:, o_ * 4 + j_:o_ * 4 + j_ + 1], scr3[:, n_:n_ + m_],
                            ALU.mult, ALU.add),
                            reads=[(A4, "x1"), convw, scr3], writes=[scr3],
                            tag=f"cvb{blk}.{oc}.{j}")
                aop(lambda h, o_=oc: h.activation(ARC[:, o_ * TB:(o_ + 1) * TB], scr3[:, 0:TB],
                                                  AF.Silu, bias=bcols[:, 112 + o_:113 + o_]),
                    reads=[scr3, (bcols, "bcv")], writes=[(ARC, "xc")], tag=f"cvs{blk}.{oc}")
            for og in range(4):
                for (mi, mdram, mk) in ((0, wdt16_d, "dt"), (1, wb16_d, "B"), (2, wc16_d, "C"),
                                        (3, wi16_d, "z")):
                    if mk == "z":
                        for oi2 in range(2):
                            so = (oi2 % 2) * 2048
                            ldg(wres, so, 4096, wi16_d, 2 * DI, 0, D,
                                DI + og * 512 + oi2 * 256, 256,
                                ("ws", so // 2048), tag=f"wz{blk}.{og}.{oi2}",
                                rd=[("d", "wic")])
                            for oi in range(2):
                                oc = og * 4 + oi2 * 2 + oi
                                zb_ = pb[3 if oc % 2 == 0 else 0]
                                for f in range(8):
                                    top(lambda h, f_=f, o_=oi, b_=blk, s_=so, k_=zb_: h.matmul(
                                        k_[:, 0:TB],
                                        wres[:, s_ + f_ * 256 + o_ * 128: s_ + f_ * 256 + (o_ + 1) * 128],
                                        A1[:, f_ * T + b_ * TB: f_ * T + (b_ + 1) * TB],
                                        start=(f_ == 0), stop=(f_ == 7)),
                                        reads=[("ws", so // 2048), (A1, "x1c")], writes=[(zb_, "z")],
                                        tag=f"zm{blk}.{oc}.{f}")
                                aop(lambda h, o_=oc % 4, k_=zb_: h.activation(
                                    szv[:, o_ * TB:(o_ + 1) * TB], k_[:, 0:TB], AF.Silu),
                                    reads=[(zb_, "z")], writes=[(szv, "sz")], tag=f"ze{blk}.{oc}")
                    else:
                        for oi in range(4):
                            oc = og * 4 + oi
                            so = (oc % 2) * 2048
                            if mk == "B":
                                mb_ = pb[5 if oc % 2 == 0 else 1]
                            else:
                                mb_ = pb[4 if oc % 2 == 0 else 6]
                            ldg(wres, so, 4096, mdram, DI, 0, DI, oc * 128, 128,
                                ("ws", so // 2048), tag=f"wm{mk}{blk}.{oc}",
                                rd=[("d", {"dt": "dtc", "B": "wbc", "C": "wcc"}[mk])])
                            for kk in range(16):
                                top(lambda h, k_=kk, s_=so, b_=mb_: h.matmul(
                                    b_[:, 0:TB],
                                    wres[:, s_ + k_ * 128: s_ + (k_ + 1) * 128],
                                    ARC[:, k_ * TB:(k_ + 1) * TB],
                                    start=(k_ == 0), stop=(k_ == 15)),
                                    reads=[("ws", so // 2048), (ARC, "xc")],
                                    writes=[(mb_, mk)], tag=f"{mk}m{blk}.{oc}.{kk}")
                            if mk == "dt":
                                aop(lambda h, o_=oc, b_=mb_: h.activation(
                                    scr3[:, 0:TB], b_[:, 0:TB], AF.Abs,
                                    bias=bcols[:, 128 + o_:129 + o_]),
                                    reads=[(mb_, "dt"), (bcols, "bdt")], writes=[scr3],
                                    tag=f"dta{blk}.{oc}")
                                aop(lambda h: h.activation(scr3[:, 0:TB], scr3[:, 0:TB], AF.Exp,
                                                           scale=-1.0),
                                    reads=[scr3], writes=[scr3], tag=f"dtb{blk}.{oc}")
                                aop(lambda h: h.activation(scr3[:, 0:TB], scr3[:, 0:TB], AF.Ln,
                                                           bias=1.0),
                                    reads=[scr3], writes=[scr3], tag=f"dtc{blk}.{oc}")
                                aop(lambda h, o_=oc, b_=mb_: h.activation(
                                    fscr[:, 0:TB], b_[:, 0:TB], AF.Relu,
                                    bias=bcols[:, 128 + o_:129 + o_]),
                                    reads=[(mb_, "dt"), (bcols, "bdt")], writes=[(fscr, "r")],
                                    tag=f"dtd{blk}.{oc}")
                                vop(lambda h, o_=oc: h.tensor_add(
                                    A4[:, dto + o_ * TB: dto + (o_ + 1) * TB], scr3[:, 0:TB],
                                    fscr[:, 0:TB]),
                                    reads=[scr3, (fscr, "r")], writes=[(A4, "dtc")], tag=f"dte{blk}.{oc}")
                            elif mk == "B":
                                if blk == 0:
                                    for kk in range(16):
                                        top(lambda h, k_=kk, oc2=oc, s_=so: h.matmul(
                                            pb[7][:, oc2:oc2 + 1],
                                            wres[:, s_ + k_ * 128: s_ + (k_ + 1) * 128],
                                            xce[:, k_:k_ + 1],
                                            start=(k_ == 0), stop=(k_ == 15)),
                                            reads=[("ws", so // 2048), xce], writes=[(pb[7], "bv")],
                                            tag=f"bv{oc}.{kk}")
                                aop(lambda h, b_=mb_: h.activation(
                                    scr3[:, 0:TB], b_[:, 0:TB], AF.Sigmoid),
                                    reads=[(mb_, "B")], writes=[scr3], tag=f"Bs{blk}.{oc}")
                                vop(lambda h, o_=oc, oo=oi: h.tensor_mul(
                                    scr[:, oo * TB:(oo + 1) * TB], scr3[:, 0:TB],
                                    ARC[:, o_ * TB:(o_ + 1) * TB]),
                                    reads=[scr3, (ARC, "xc")], writes=[(scr, "u")], tag=f"Bu{blk}.{oc}")
                            else:
                                aop(lambda h, b_=mb_: h.activation(
                                    scr3[:, 0:TB],
                                    b_[:, 0:TB], AF.Tanh),
                                    reads=[(mb_, "C")], writes=[scr3], tag=f"Ct{blk}.{oc}")
                                vop(lambda h, o_=oi: h.tensor_copy(
                                    szv[:, 4 * TB + o_ * TB: 4 * TB + (o_ + 1) * TB], scr3[:, 0:TB]),
                                    reads=[scr3], writes=[(szv, "ct")], tag=f"Cte{blk}.{oc}")
                if blk == 0:
                    for oi in range(4):
                        oc = og * 4 + oi
                        aop(lambda h, o_=oc: h.activation(
                            uprevc[:, o_:o_ + 1], pb[7][:, o_:o_ + 1], AF.Sigmoid),
                            reads=[(pb[7], "bv")], writes=[uprevc], tag=f"ups{oc}")
                        vop(lambda h, o_=oc: h.tensor_mul(uprevc[:, o_:o_ + 1], uprevc[:, o_:o_ + 1],
                                                          xce[:, o_:o_ + 1]),
                            reads=[uprevc, xce], writes=[uprevc], tag=f"upm{oc}")
                        vop(lambda h, o_=oc: h.tensor_mul(uprevc[:, o_:o_ + 1], uprevc[:, o_:o_ + 1],
                                                          rolec[:, 0:1]),
                            reads=[uprevc, rolec], writes=[uprevc], tag=f"upr{oc}")
                for oi in range(4):
                    oc = og * 4 + oi
                    vop(lambda h, o_=oc: h.tensor_scalar(
                        fscr[:, TB:2 * TB], A4[:, dto + o_ * TB: dto + (o_ + 1) * TB],
                        bcols[:, 160 + o_:161 + o_], -10.0, ALU.mult, ALU.max),
                        reads=[(A4, "dtc"), (bcols, "Ae")], writes=[(fscr, "la")], tag=f"la{blk}.{oc}")
                    vop(lambda h: h.tensor_scalar_min(fscr[:, TB:2 * TB], fscr[:, TB:2 * TB], -1e-4),
                        reads=[(fscr, "la")], writes=[(fscr, "la")], tag=f"lb{blk}.{oc}")
                    aop(lambda h: h.activation(fscr[:, TB:2 * TB], fscr[:, TB:2 * TB], AF.Exp),
                        reads=[(fscr, "la")], writes=[(fscr, "la")], tag=f"lc{blk}.{oc}")
                    vop(lambda h, o_=oc, oo=oi, b_=blk: h.tensor_copy(
                        fscr[:, 2 * TB + 1: 3 * TB], scr[:, oo * TB: (oo + 1) * TB - 1]),
                        reads=[(scr, "u")], writes=[(fscr, "be")], tag=f"us{blk}.{oc}")
                    vop(lambda h, o_=oc, b_=blk: h.tensor_copy(
                        fscr[:, 2 * TB: 2 * TB + 1],
                        (uprevc[:, o_:o_ + 1] if b_ == 0 else ucol[:, o_:o_ + 1])),
                        reads=[(uprevc if blk == 0 else ucol)], writes=[(fscr, "be")], tag=f"u0{blk}.{oc}")
                    vop(lambda h, oo=oi, o_=oc: h.tensor_copy(
                        ucol[:, o_:o_ + 1], scr[:, (oo + 1) * TB - 1:(oo + 1) * TB]),
                        reads=[(scr, "u")], writes=[ucol], tag=f"ue{blk}.{oc}")
                    vop(lambda h, oo=oi: h.tensor_add(
                        fscr[:, 2 * TB:3 * TB], fscr[:, 2 * TB:3 * TB], scr[:, oo * TB:(oo + 1) * TB]),
                        reads=[(fscr, "be"), (scr, "u")], writes=[(fscr, "be")], tag=f"ua{blk}.{oc}")
                    vop(lambda h, o_=oc: h.scalar_tensor_tensor(
                        fscr[:, 2 * TB:3 * TB], fscr[:, 2 * TB:3 * TB], 0.5,
                        A4[:, dto + o_ * TB: dto + (o_ + 1) * TB], ALU.mult, ALU.mult),
                        reads=[(fscr, "be"), (A4, "dtc")], writes=[(fscr, "be")], tag=f"ub{blk}.{oc}")
                    vop(lambda h, o_=oc: h.tensor_tensor_scan(
                        fscr[:, 0:TB], fscr[:, TB:2 * TB], fscr[:, 2 * TB:3 * TB],
                        hcol[:, o_:o_ + 1], ALU.mult, ALU.add),
                        reads=[(fscr, "la"), (fscr, "be"), hcol], writes=[(fscr, "h0")], tag=f"sc{blk}.{oc}")
                    vop(lambda h, o_=oc: h.tensor_copy(hcol[:, o_:o_ + 1], fscr[:, TB - 1:TB]),
                        reads=[(fscr, "h0")], writes=[hcol], tag=f"sce{blk}.{oc}")
                    vop(lambda h, o_=oc: h.tensor_tensor_scan(
                        fscr[:, 3 * TB:4 * TB], fscr[:, TB:2 * TB], zbuf[:, 0:TB],
                        accol[:, o_:o_ + 1], ALU.mult, ALU.add),
                        reads=[(fscr, "la"), zbuf, accol], writes=[(fscr, "ac")], tag=f"sa{blk}.{oc}")
                    vop(lambda h, o_=oc: h.tensor_copy(accol[:, o_:o_ + 1], fscr[:, 4 * TB - 1:4 * TB]),
                        reads=[(fscr, "ac")], writes=[accol], tag=f"sae{blk}.{oc}")
                    vop(lambda h, oo=oi: h.tensor_mul(
                        scr3[:, 0:TB], szv[:, 4 * TB + oo * TB: 4 * TB + (oo + 1) * TB],
                        szv[:, oo * TB:(oo + 1) * TB]),
                        reads=[(szv, "ct"), (szv, "sz")], writes=[scr3], tag=f"yw{blk}.{oc}")
                    vop(lambda h, o_=oc: h.tensor_mul(
                        A2[:, o_ * TB:(o_ + 1) * TB], fscr[:, 0:TB], scr3[:, 0:TB]),
                        reads=[(fscr, "h0"), scr3], writes=[(A2, "y0")], tag=f"y0{blk}.{oc}")
                    vop(lambda h, o_=oc: h.tensor_mul(
                        A2[:, 4 * T + o_ * TB: 4 * T + (o_ + 1) * TB],
                        fscr[:, 3 * TB:4 * TB], scr3[:, 0:TB]),
                        reads=[(fscr, "ac"), scr3], writes=[(A2, "gc")], tag=f"gc{blk}.{oc}")
            p.dma("sync", lambda h, b_=blk: h.dma_start(
                out=y0_d.ap()[:, b_ * 16 * TB:(b_ + 1) * 16 * TB], in_=A2[:, 0:16 * TB]),
                reads=[(A2, "y0")], writes=[("d", "y0")], key=("d", "y0"), tag=f"y0d{blk}")
            p.dma("sync", lambda h, b_=blk: h.dma_start(
                out=gc_d.ap()[:, b_ * 16 * TB:(b_ + 1) * 16 * TB], in_=A2[:, 4 * T:4 * T + 16 * TB]),
                reads=[(A2, "gc")], writes=[("d", "gc")], key=("d", "gc"), tag=f"gcd{blk}")
        p.dma("sync", lambda h: h.dma_start(out=cc_h_in.ap(), in_=hcol[:, :]),
              reads=[hcol], writes=[("d", "cch")], tag="hstg")
        p.cc(lambda h: h.collective_compute(
            "AllGather", ALU.bypass, replica_groups=GROUPS,
            ins=[cc_h_in.ap().opt()], outs=[cc_h_out.ap().opt()]),
            reads=[("d", "cch")], writes=[("d", "ccho")], tag="hag")
        p.dma("sync", lambda h: h.dma_start(out=hini[:, :], in_=cc_h_out.ap()[0:128, :]),
              reads=[("d", "ccho")], writes=[hini], tag="hrcv")
        vop(lambda h: h.tensor_scalar_mul(hini[:, :], hini[:, :], rolec[:, 0:1]),
            reads=[hini, rolec], writes=[hini], tag="hmsk")
        for blk in range(4):
            p.dma("gpsimd", lambda h, b_=blk: h.dma_start(
                out=A2[:, 0:16 * TB], in_=y0_d.ap()[:, b_ * 16 * TB:(b_ + 1) * 16 * TB]),
                reads=[("d", "y0")], writes=[(A2, "y0")], key=(A2, "y0"), tag=f"y0r{blk}")
            p.dma("gpsimd", lambda h, b_=blk: h.dma_start(
                out=A2[:, 4 * T:4 * T + 16 * TB], in_=gc_d.ap()[:, b_ * 16 * TB:(b_ + 1) * 16 * TB]),
                reads=[("d", "gc")], writes=[(A2, "gc")], key=(A2, "gc"), tag=f"gcr{blk}")
            for og in range(4):
                for hf in range(2):
                    so = ((og * 2 + hf) % 2) * 2048
                    ldg(wres, so, 4096, wo116_d, D, og * 512, 512, hf * 512, 512,
                        ("ws", so // 2048), tag=f"wog{blk}.{og}.{hf}",
                        rd=[("d", "woc")])
                    for oi in range(4):
                        oc = og * 4 + oi
                        vop(lambda h, o_=oc: h.scalar_tensor_tensor(
                            scr3[:, 0:TB], A2[:, 4 * T + o_ * TB: 4 * T + (o_ + 1) * TB],
                            hini[:, o_:o_ + 1], A2[:, o_ * TB:(o_ + 1) * TB], ALU.mult, ALU.add),
                            reads=[(A2, "y0"), (A2, "gc"), hini], writes=[scr3], tag=f"y{blk}.{oc}.{hf}")
                        vop(lambda h: h.tensor_copy(scr[:, 0:TB], scr3[:, 0:TB]),
                            reads=[scr3], writes=[(scr, "y16")], tag=f"y16{blk}.{oc}.{hf}")
                        for d4 in range(4):
                            dt_i = hf * 4 + d4
                            top(lambda h, o_=oi, d_=d4, b2=dt_i, oc_=oc, s_=so: h.matmul(
                                pb[b2][:, 0:TB], wres[:, s_ + o_ * 512 + d_ * 128: s_ + o_ * 512 + (d_ + 1) * 128],
                                scr[:, 0:TB],
                                start=(oc_ == 0), stop=(oc_ == 15)),
                                reads=[("ws", so // 2048), (scr, "y16")], writes=[(pb[dt_i], "s1")],
                                tag=f"s1m{blk}.{oc}.{dt_i}")
            for dt_i in range(8):
                vop(lambda h, d_=dt_i: h.tensor_copy(scr3[:, 0:TB], pb[d_][:, 0:TB]),
                    reads=[(pb[dt_i], "s1")], writes=[scr3], tag=f"s1e{blk}.{dt_i}")
                p.dma("sync", lambda h, d_=dt_i, b_=blk: h.dma_start(
                    out=s1_d.ap()[d_ * 128:(d_ + 1) * 128, b_ * TB:(b_ + 1) * TB], in_=scr3[:, 0:TB]),
                    reads=[scr3], writes=[("d", "s1")], key=("d", "s1"),
                    tag=f"s1d{blk}.{d_ if False else dt_i}")

        def quant8(wT_ap, K_, O_, scol, dstD, nn):
            for kk in range(4):
                R4 = K_ // 4
                p.dma("gpsimd", lambda h, k_=kk, D_=dstD, R_=R4: h.dma_start(
                    out=D_.ap()[k_ * R_:(k_ + 1) * R_, :],
                    in_=wT_ap[k_ * R_:(k_ + 1) * R_, :]),
                    writes=[("d", nn)], key=("d", nn), tag=f"{nn}o{kk}")
        p.fence([], [(A2, "y0"), (A2, "gc"), (mkv, "mk"),
                     (A4, "x1"), (A4, "dtc"), "hqb", "w8s", "xq8"], tag="fG")
        quant8(w1fT, D, FF, 8, q1_d, "q1")
        quant8(w2fT, FF, D, 8, q2_d, "q2")
        p.dma("gpsimd", lambda h, a=bcols[:, 32:64], b=bass.AP(b1f.tensor, 0, [[1, 128], [128, 32]]): h.dma_start(out=a, in_=b, allow_slow_non_contiguous=True), writes=[(bcols, "b1")], key=(bcols, "b1"), tag="b1f")
        p.dma("gpsimd", lambda h, a=bcols[:, 64:72], b=bass.AP(b2f.tensor, 0, [[1, 128], [128, 8]]): h.dma_start(out=a, in_=b, allow_slow_non_contiguous=True), writes=[(bcols, "b2")], key=(bcols, "b2"), tag="b2f")
        outw = 0
        for tb in range(4):
            for f in range(8):
                vop(lambda h, f_=f, b_=tb: h.tensor_copy(
                    xq8[:, f_ * TB:(f_ + 1) * TB],
                    A1[:, f_ * T + b_ * TB: f_ * T + (b_ + 1) * TB]),
                    reads=[(A1, "x1c")], writes=[(xq8, tb)], tag=f"xq{tb}.{f}")
            for hgrp in range(4):
                ldg(w8s, 0, 8192, q1_d, FF, 0, D, hgrp * 1024, 1024, (w8s, "g1"), tag=f"g1{tb}.{hgrp}", rd=[("d", "q1")])
                for ho in range(8):
                    ht = hgrp * 8 + ho
                    f1b = pb[5 if ht % 2 == 0 else 7]
                    for t4 in range(NTB):
                        for f in range(8):
                            top(lambda h, f_=f, o_=ho, t_=t4, k_=f1b: h.matmul(
                                k_[:, 0:TSB],
                                w8s[:, f_ * 1024 + o_ * 128: f_ * 1024 + (o_ + 1) * 128],
                                xq8[:, f_ * TB + t_ * TSB: f_ * TB + (t_ + 1) * TSB],
                                start=(f_ == 0), stop=(f_ == 7)),
                                reads=[(w8s, "g1"), (xq8, tb)], writes=[(f1b, "f1")],
                                tag=f"f1m{tb}.{ht}.{t4}.{f}")
                        aop(lambda h, h_=ht, t_=t4, k_=f1b: h.activation(
                            scr3[:, 0:TSB], k_[:, 0:TSB], AF.Identity,
                            bias=bcols[:, 32 + h_:33 + h_]),
                            reads=[(f1b, "f1"), (bcols, "b1")], writes=[scr3], tag=f"f1q{tb}.{ht}.{t4}")
                        aop(lambda h, h_=ht, t_=t4: h.activation(
                            hqb[:, h_ * TB + t_ * TSB: h_ * TB + (t_ + 1) * TSB], scr3[:, 0:TSB],
                            AF.Relu),
                            reads=[scr3], writes=[(hqb, tb)], tag=f"f1r{tb}.{ht}.{t4}")
            p.dma("gpsimd", lambda h, b_=tb: h.dma_start(
                out=d3(mkv, 0, 8, TB, 8 * TB),
                in_=g3(mask_in.tensor, T, 0, D, b_ * TB, TB)),
                reads=[], writes=[(mkv, "mk")], key=(mkv, "mk"), tag=f"mk{tb}")
            for dgp in range(4):
                p.dma("gpsimd", lambda h, b_=tb, g_=dgp: h.dma_start(
                    out=d3(scr, 0, 2, TB, T),
                    in_=g3(s1_d, T, g_ * 256, 256, b_ * TB, TB)),
                    reads=[("d", "s1")], writes=[(scr, "s1b")], key=(scr, "s1b"), tag=f"s1r{tb}.{dgp}")
                ldg(w8s, 0, 8192, q2_d, D, 0, FF, dgp * 256, 256, (w8s, "g2"), tag=f"g2{tb}.{dgp}", rd=[("d", "q2")])
                for do in range(2):
                    dt_i = dgp * 2 + do
                    f2b = pb[6 if dt_i % 2 == 0 else 2]
                    for t4 in range(NTB):
                        for hc in range(32):
                            top(lambda h, hc_=hc, t_=t4, d_=do, k_=f2b: h.matmul(
                                k_[:, 0:TSB],
                                w8s[:, hc_ * 256 + d_ * 128: hc_ * 256 + (d_ + 1) * 128],
                                hqb[:, hc_ * TB + t_ * TSB: hc_ * TB + (t_ + 1) * TSB],
                                start=(hc_ == 0), stop=(hc_ == 31)),
                                reads=[(w8s, "g2"), (hqb, tb)], writes=[(f2b, "f2")],
                                tag=f"f2m{tb}.{dt_i}.{t4}.{hc}")
                        aop(lambda h, d_=dt_i, t_=t4, k_=f2b: h.activation(
                            scr3[:, 0:TSB], k_[:, 0:TSB], AF.Identity,
                            bias=bcols[:, 64 + d_:65 + d_]),
                            reads=[(f2b, "f2"), (bcols, "b2")], writes=[scr3], tag=f"f2q{tb}.{dt_i}.{t4}")
                        vop(lambda h, d_=dt_i, t_=t4: h.tensor_mul(
                            scr3[:, 0:TSB], scr3[:, 0:TSB],
                            mkv[:, d_ * TB + t_ * TSB: d_ * TB + (t_ + 1) * TSB]),
                            reads=[scr3, (mkv, "mk")], writes=[scr3], tag=f"f2d{tb}.{dt_i}.{t4}")
                        aop(lambda h: h.activation(scr3[:, 0:TSB], scr3[:, 0:TSB], AF.Silu),
                            reads=[scr3], writes=[scr3], tag=f"f2s{tb}.{dt_i}.{t4}")
                        vop(lambda h, d_=dt_i, t_=t4: h.tensor_mul(
                            scr3[:, 0:TSB], scr3[:, 0:TSB],
                            scr[:, (d_ % 2) * TB + t_ * TSB: (d_ % 2) * TB + (t_ + 1) * TSB]),
                            reads=[scr3, (scr, "s1b")], writes=[scr3], tag=f"f2g{tb}.{dt_i}.{t4}")
                        vop(lambda h, d_=dt_i, t_=t4, b_=tb: h.tensor_add(
                            A3[:, d_ * T + b_ * TB + t_ * TSB: d_ * T + b_ * TB + (t_ + 1) * TSB],
                            A3[:, d_ * T + b_ * TB + t_ * TSB: d_ * T + b_ * TB + (t_ + 1) * TSB],
                            scr3[:, 0:TSB]),
                            reads=[scr3, (A3, "x", dt_i)], writes=[(A3, "x", dt_i), (A3, "xf", dt_i, tb)],
                            tag=f"g1a{tb}.{dt_i}.{t4}")
                    p.dma("gpsimd", lambda h, d_=dt_i, b_=tb: h.dma_start(
                        out=out_t[d_ * 128:(d_ + 1) * 128, b_ * TB:(b_ + 1) * TB],
                        in_=A3[:, d_ * T + b_ * TB: d_ * T + (b_ + 1) * TB]),
                        reads=[(A3, "xf", dt_i, tb)], writes=[("d", "outw", tb * 8 + dt_i)],
                        key=("d", "outw"), tag=f"out{tb}.{dt_i}")
        p.final_wait("gpsimd", [("d", "outw", i) for i in range(32)])
        p.emit(st_)
    return nc


_CACHE = {}


def _prep_inputs(inputs):
    f32 = np.float32
    f16 = np.float16
    tr = lambda a: np.ascontiguousarray(np.asarray(a, dtype=f32).T)
    w = {}
    w["wqT"] = tr(inputs["L0_attn_wq"]); w["wkT"] = tr(inputs["L0_attn_wk"])
    w["wvT"] = tr(inputs["L0_attn_wv"]); w["woT"] = tr(inputs["L0_attn_wo"])
    w["ln1w"] = np.asarray(inputs["L0_attn_ln1_w"], f32).reshape(1, -1)
    w["ln1b"] = np.asarray(inputs["L0_attn_ln1_b"], f32).reshape(1, -1)
    w["ln2w"] = np.asarray(inputs["L0_attn_ln2_w"], f32).reshape(1, -1)
    w["ln2b"] = np.asarray(inputs["L0_attn_ln2_b"], f32).reshape(1, -1)
    w["w1T"] = tr(inputs["L0_attn_w1"]); w["b1v"] = np.asarray(inputs["L0_attn_b1"], f32).reshape(1, -1)
    w["w2T"] = tr(inputs["L0_attn_w2"]); w["b2v"] = np.asarray(inputs["L0_attn_b2"], f32).reshape(1, -1)
    w["WfcT"] = tr(inputs["L0_ffn_W_fc"]); w["bfcv"] = np.asarray(inputs["L0_ffn_b_fc"], f32).reshape(1, -1)
    w["AfcT"] = tr(inputs["L0_ffn_A_fc"]); w["BfcT"] = np.ascontiguousarray(np.asarray(inputs["L0_ffn_B_fc"], f32).T)
    w["WprojT"] = tr(inputs["L0_ffn_W_proj"]); w["bprv"] = np.asarray(inputs["L0_ffn_b_proj"], f32).reshape(1, -1)
    w["AprojT"] = tr(inputs["L0_ffn_A_proj"]); w["BprojT"] = np.ascontiguousarray(np.asarray(inputs["L0_ffn_B_proj"], f32).T)
    w["WiT"] = tr(inputs["L1_attn_Wi"])
    w["Wcv"] = np.ascontiguousarray(np.asarray(inputs["L1_attn_Wconv"], f32).reshape(DI, 4))
    w["bconv"] = np.asarray(inputs["L1_attn_bconv"], f32).reshape(1, -1)
    w["WdtT"] = tr(inputs["L1_attn_Wdt"]); w["bdtv"] = np.asarray(inputs["L1_attn_bdt"], f32).reshape(1, -1)
    w["WBT"] = tr(inputs["L1_attn_WB"]); w["WCT"] = tr(inputs["L1_attn_WC"])
    w["WoT1"] = tr(inputs["L1_attn_Wo"])
    w["A_log"] = np.asarray(inputs["L1_attn_A_log"], f32).reshape(1, -1)
    w["w1fT"] = tr(inputs["L1_ffn_w1"]); w["b1f"] = np.asarray(inputs["L1_ffn_b1"], f32).reshape(1, -1)
    w["w2fT"] = tr(inputs["L1_ffn_w2"]); w["b2f"] = np.asarray(inputs["L1_ffn_b2"], f32).reshape(1, -1)
    w["lr_in"] = np.asarray(inputs["L0_attn_lr_scale"], f32).reshape(1, -1)
    w["trilm"] = np.tril(np.ones((128, 128), f16), -1)
    w["trium"] = np.triu(np.ones((128, 128), f16), 1)
    return w


def _dropout_mask(shape):
    import jax
    try:
        dev = jax.local_devices(backend="cpu")[0]
        with jax.default_device(dev):
            keep = np.asarray(jax.random.bernoulli(jax.random.key(42), 0.8, shape))
    except Exception:
        keep = np.asarray(jax.random.bernoulli(jax.random.key(42), 0.8, shape))
    return (keep.astype(np.float32) * 1.25).astype(np.float16)


def kernel(**inputs):
    x = np.asarray(inputs["L0_attn_x"], np.float32)
    B, L, D_ = x.shape
    T = L // 2
    w = _prep_inputs(inputs)
    mask = _dropout_mask((B, L, D_))
    if T not in _CACHE:
        _CACHE[T] = build_program(T)
    nc = _CACHE[T]
    in_maps = []
    for c in range(8):
        b, hf = c // 2, c % 2
        m = dict(w)
        m["x_in"] = np.ascontiguousarray(x[b, hf * T:(hf + 1) * T, :].T)
        m["mask_in"] = np.ascontiguousarray(mask[b, hf * T:(hf + 1) * T, :].T)
        m["role_in"] = np.full((1, 1), float(hf), np.float32)
        in_maps.append(m)
    res = run_bass_kernel_spmd(nc, in_maps, core_ids=list(range(8)))
    out = np.empty((B, L, D_), np.float32)
    for c in range(8):
        b, hf = c // 2, c % 2
        out[b, hf * T:(hf + 1) * T, :] = res.results[c]["out"].T
    return out



# version 05
# speedup vs baseline: 1.6979x; 1.0135x over previous
import os
import numpy as np
import ml_dtypes
import concourse.bass as bass
import concourse.mybir as mybir
from concourse.bass_utils import run_bass_kernel_spmd
from concourse.bass_isa import ReduceOp
from contextlib import ExitStack

F32 = mybir.dt.float32
F16 = mybir.dt.float16
F8 = mybir.dt.float8e4
AF = mybir.ActivationFunctionType
ALU = mybir.AluOpType
AX = mybir.AxisListType

ENGINES = ("tensor", "vector", "scalar", "gpsimd", "sync")
D = 1024
H = 16
FF = 4096
DI = 2048
EPS = 1e-5
MAGIC = float(1.5 * 2.0 ** 23)
GROUPS = [[0, 1], [2, 3], [4, 5], [6, 7]]


_COLLAPSE_PREFIX = ("pb",)
_COLLAPSE_NAMES = {"scr", "scr3", "fscr", "w8s", "xq8", "hqb", "lra_s",
                   "qmzv", "qbv", "w1v", "rqv", "szv", "egv", "mkv", "xs1v",
                   "afc_s", "apr_s", "masks", "lncol", "convw", "halos"}


def _norm(n):
    s = str(n)
    head, _, tail = s.rpartition("_")
    return head if head and tail.isdigit() else s


def _key(x):
    if isinstance(x, tuple):
        if x and not isinstance(x[0], (str, int, tuple)):
            n0 = getattr(x[0], "name", None)
            if n0 is not None:
                b = _norm(n0)
                if b in _COLLAPSE_NAMES or b.startswith(_COLLAPSE_PREFIX):
                    return b
        return tuple(_key(e) for e in x)
    if isinstance(x, (str, int)):
        return x
    n = getattr(x, "name", None)
    if n is None:
        return id(x)
    b = _norm(n)
    return b if (b in _COLLAPSE_NAMES or b.startswith(_COLLAPSE_PREFIX)) else n


class _Op:
    __slots__ = ("eng", "fn", "reads", "writes", "kind", "sem_key", "idx",
                 "waits", "need_inc", "drain_before", "tag")

    def __init__(self, eng, fn, reads, writes, kind, sem_key, tag=""):
        self.eng, self.fn, self.reads, self.writes = eng, fn, reads, writes
        self.kind, self.sem_key, self.tag = kind, sem_key, tag
        self.idx = -1
        self.waits = []
        self.need_inc = False
        self.drain_before = False


class Prog:
    def __init__(self, nc):
        self.nc = nc
        self.ops = []
        self.dma_sem_of_buf = {}
        self.n_dma_sems = 0

    def add(self, eng, fn, reads=(), writes=(), kind="c", sem_key=None, tag=""):
        op = _Op(eng, fn, tuple(_key(r) for r in reads),
                 tuple(_key(w) for w in writes), kind, sem_key, tag)
        op.idx = len(self.ops)
        self.ops.append(op)
        return op

    def c(self, eng, fn, reads=(), writes=(), tag=""):
        return self.add(eng, fn, reads, writes, "c", tag=tag)

    def dma(self, queue, fn, reads=(), writes=(), key=None, tag=""):
        if key is None:
            assert writes
            key = writes[0]
        key = _key(key)
        if key not in self.dma_sem_of_buf:
            self.dma_sem_of_buf[key] = f"ds{self.n_dma_sems}"
            self.n_dma_sems += 1
        return self.add(queue, fn, reads, writes, "d", self.dma_sem_of_buf[key], tag=tag)

    def cc(self, fn, reads=(), writes=(), tag=""):
        return self.add("gpsimd", fn, reads, writes, "cc", tag=tag)

    def fence(self, reads, writes, tag="fence"):
        self.c("vector", lambda h: h.memset(self.dummy, 0.0),
               reads=tuple(reads), writes=tuple(writes) + ("nopd",), tag=tag)

    def final_wait(self, queue, bufs):
        self.c(queue, lambda h: h.memset(self.dummy, 0.0),
               reads=tuple(bufs), writes=("nopd",), tag="final")

    def schedule(self):
        last_writer = {}
        readers_since = {}
        edges = {}
        for op in self.ops:
            deps = set()
            for r in op.reads:
                w = last_writer.get(r)
                if w is not None:
                    deps.add(w)
            for w_ in op.writes:
                pw = last_writer.get(w_)
                if pw is not None:
                    deps.add(pw)
                for rd in readers_since.get(w_, ()):
                    deps.add(rd)
            deps.discard(op)
            pl = [d for d in deps
                  if not (op.kind == "d" and d.kind == "d"
                          and d.sem_key == op.sem_key and d.eng == op.eng)]
            if pl:
                edges[op.idx] = pl
            for w_ in op.writes:
                last_writer[w_] = op
                readers_since[w_] = []
            for r in op.reads:
                readers_since.setdefault(r, []).append(op)
        for ci, pl in edges.items():
            c = self.ops[ci]
            for pp in pl:
                if pp.kind in ("d", "cc"):
                    continue
                if pp.eng == c.eng and c.kind == "c":
                    continue
                pp.need_inc = True
        inc_no = {e: 0 for e in ENGINES}
        inc_value = {}
        dval = {}
        for op in self.ops:
            if op.kind == "c" and op.need_inc:
                inc_no[op.eng] += 1
                inc_value[op.idx] = inc_no[op.eng]
            elif op.kind == "d":
                dval[op.idx] = dval.get(op.sem_key, 0) + 16
                dval[op.sem_key] = dval[op.idx]
            elif op.kind == "cc":
                dval[op.idx] = dval.get("__cc__", 0) + 1
                dval["__cc__"] = dval[op.idx]
        wm = {}
        pos = {}
        cnt = {e: 0 for e in ENGINES}
        for op in self.ops:
            pos[op.idx] = cnt[op.eng]
            cnt[op.eng] += 1
        drained = {e: -1 for e in ENGINES}
        for op in self.ops:
            for pp in edges.get(op.idx, ()):
                if pp.kind == "d":
                    k = ("dma", pp.sem_key)
                    v = dval[pp.idx]
                    if wm.get((op.eng,) + k, -1) < v:
                        op.waits.append((k, v))
                        wm[(op.eng,) + k] = v
                elif pp.kind == "cc":
                    v = dval[pp.idx]
                    if wm.get((op.eng, "cc"), -1) < v:
                        op.waits.append((("cc", "__cc__"), v))
                        wm[(op.eng, "cc")] = v
                elif pp.eng == op.eng and op.kind == "c":
                    if pp.eng != "tensor" and pos[pp.idx] > drained[pp.eng]:
                        op.drain_before = True
                else:
                    v = inc_value[pp.idx]
                    if wm.get((op.eng, "eng", pp.eng), -1) < v:
                        op.waits.append((("eng", pp.eng), v))
                        wm[(op.eng, "eng", pp.eng)] = v
            if op.drain_before:
                drained[op.eng] = pos[op.idx] - 1
            if len(op.waits) > 1:
                agg = {}
                for (k, v) in op.waits:
                    agg[k] = max(agg.get(k, 0), v)
                op.waits = list(agg.items())
        return inc_no

    def emit(self, stack):
        nc = self.nc
        self.schedule()
        if os.environ.get("PROG_DEBUG"):
            for k, name in self.dma_sem_of_buf.items():
                print("SEM", name, k)
        sems = {}
        for e in ENGINES:
            sems[("eng", e)] = stack.enter_context(nc.semaphore(f"pg_{e}"))
        for k, name in self.dma_sem_of_buf.items():
            sems[("dma", name)] = stack.enter_context(nc.semaphore(name))
        sems[("cc", "__cc__")] = stack.enter_context(nc.semaphore("ccsem"))
        by_eng = {e: [] for e in ENGINES}
        for op in self.ops:
            by_eng[op.eng].append(op)
        block = stack.enter_context(nc.Block())

        def runner(ename):
            def _run(hh):
                mysem = sems[("eng", ename)]
                for op in by_eng[ename]:
                    for (skey, val) in op.waits:
                        hh.wait_ge(sems[skey], val)
                    if op.drain_before:
                        hh.drain()
                    inst = op.fn(hh)
                    if op.kind == "d":
                        inst.then_inc(sems[("dma", op.sem_key)], 16)
                    elif op.kind == "cc":
                        inst.then_inc(sems[("cc", "__cc__")], 1)
                    elif op.need_inc:
                        inst.then_inc(mysem, 1)
            return _run

        for e in ENGINES:
            if by_eng[e]:
                getattr(block, e)(runner(e))


def bc_ap(handle, row, width):
    return bass.AP(handle, row * width, [[1, 1], [0, 128], [1, width]])


def hpk(g):
    return ((g // 2) % 2) * 64, (g % 2) * 256 + (g // 4) * 64



def build_program(T):
    NCH = T // 128
    TS = min(512, T)
    NT4 = T // TS
    TB = T // 4
    TSB = min(512, TB)
    NTB = TB // TSB
    NLVL = 7

    nc = bass.Bass("TRN2", target_bir_lowering=False)

    def din(name, shape, dt=F32):
        return nc.dram_tensor(name, shape, dt, kind="ExternalInput").ap()

    x_in = din("x_in", [D, T])
    role_in = din("role_in", [1, 1])
    trilm = din("trilm", [128, 128], F16)
    trium = din("trium", [128, 128], F16)
    mask_in = din("mask_in", [D, T], F16)
    lr_in = din("lr_in", [1, H])
    wqT = din("wqT", [D, D]); wkT = din("wkT", [D, D]); wvT = din("wvT", [D, D])
    woT = din("woT", [D, D])
    ln1w = din("ln1w", [1, D]); ln1b = din("ln1b", [1, D])
    ln2w = din("ln2w", [1, D]); ln2b = din("ln2b", [1, D])
    w1T = din("w1T", [D, FF]); b1v = din("b1v", [1, FF])
    w2T = din("w2T", [FF, D]); b2v = din("b2v", [1, D])
    WfcT = din("WfcT", [D, FF]); bfcv = din("bfcv", [1, FF])
    AfcT = din("AfcT", [D, 16]); BfcT = din("BfcT", [16, FF])
    WprojT = din("WprojT", [FF, D]); bprv = din("bprv", [1, D])
    AprojT = din("AprojT", [FF, 16]); BprojT = din("BprojT", [16, D])
    WiT = din("WiT", [D, 2 * DI])
    Wcv = din("Wcv", [DI, 4])
    bconv = din("bconv", [1, DI])
    WdtT = din("WdtT", [DI, DI]); bdtv = din("bdtv", [1, DI])
    WBT = din("WBT", [DI, DI]); WCT = din("WCT", [DI, DI])
    WoT1 = din("WoT1", [DI, D])
    A_log = din("A_log", [1, DI])
    w1fT = din("w1fT", [D, FF]); b1f = din("b1f", [1, FF])
    w2fT = din("w2fT", [FF, D]); b2f = din("b2f", [1, D])
    out_t = nc.dram_tensor("out", [D, T], F32, kind="ExternalOutput").ap()

    cc_s_in = nc.dram_tensor("cc_s_in", [128, 512], F32)
    cc_s_out = nc.dram_tensor("cc_s_out", [256, 512], F32)
    cc_e_in = nc.dram_tensor("cc_e_in", [4, 2 * DI], F32)
    cc_e_out = nc.dram_tensor("cc_e_out", [8, 2 * DI], F32)
    cc_h_in = nc.dram_tensor("cc_h_in", [128, 16], F32)
    cc_h_out = nc.dram_tensor("cc_h_out", [256, 16], F32)
    qmz_d = nc.dram_tensor("qmz_d", [128, 8 * NCH * 256], F16)
    s0_d = nc.dram_tensor("s0_d", [D, T], F16)
    s1_d = nc.dram_tensor("s1_d", [D, T], F16)
    y0_d = nc.dram_tensor("y0_d", [128, 16 * T], F16)
    gc_d = nc.dram_tensor("gc_d", [128, 16 * T], F16)
    w116_d = nc.dram_tensor("w116_d", [D, FF], F16)
    w216_d = nc.dram_tensor("w216_d", [FF, D], F16)
    rfc_d = nc.dram_tensor("rfc_d", [D, FF], F16)
    rpr_d = nc.dram_tensor("rpr_d", [FF, D], F16)
    q1_d = nc.dram_tensor("q1_d", [D, FF], F8)
    q2_d = nc.dram_tensor("q2_d", [FF, D], F8)
    wi16_d = nc.dram_tensor("wi16_d", [D, 2 * DI], F16)
    wdt16_d = nc.dram_tensor("wdt16_d", [DI, DI], F16)
    wb16_d = nc.dram_tensor("wb16_d", [DI, DI], F16)
    wc16_d = nc.dram_tensor("wc16_d", [DI, DI], F16)
    wo116_d = nc.dram_tensor("wo116_d", [DI, D], F16)

    p = Prog(nc)
    st_ = ExitStack()

    def sb(name, shape, dt):
        return st_.enter_context(nc.sbuf_tensor(name, shape, dt))

    def vt(name, shape, dt, backing, byte_off):
        h = nc.virtual_sbuf_tensor(name, shape, dt)
        nc._tensor(h.name, h.shape, h.dtype, type="SB")
        mloc = nc.lookup_mloc(h)
        mloc.allocated = not nc.target_bir_lowering
        mloc.addr = nc.lookup_mloc(backing).addr + byte_off
        return h

    with st_:
        A1 = sb("A1", [128, 8 * T], F16)
        A2 = sb("A2", [128, 8 * T], F16)
        A3 = sb("A3", [128, 8 * T], F16)
        A4 = sb("A4", [128, 8 * T], F16)
        ARC = sb("ARC", [128, 4 * T], F16)
        XNb = [sb("XN0", [128, 384], F16), sb("XN1", [128, 384], F16)]
        NTb = [XNb[0], XNb[1]]
        KQb = [sb("KQu0", [128, 128], F16), sb("KQu1", [128, 128], F16)]
        rows = sb("rows", [128, 3 * T], F32)
        bca = sb("bca", [128, 2 * T], F16)
        qmax = sb("qmax", [1, 132], F32)
        nopd = sb("nopd", [1, 2], F32)
        p.dummy = nopd[0:1, 0:2]
        onec = sb("onec", [128, 2], F16)
        bcols = sb("bcols", [128, 176], F32)
        lncol = sb("lncol", [128, 32], F32)
        rolec = sb("rolec", [128, 1], F32)
        wres = sb("wres", [128, 4096], F16)
        scr = sb("scr", [128, T], F16)
        scr3 = sb("scr3", [128, max(T // 2, 512)], F16)
        fscr = sb("fscr", [128, 4 * TB], F32)
        zbuf = sb("zbuf", [128, TB], F32)
        qsc = sb("qsc", [128, 12], F32)
        convw = sb("convw", [128, 64], F32)
        accol = sb("accol", [128, 16], F32)
        hcol = sb("hcol", [128, 16], F32)
        hini = sb("hini", [128, 16], F32)
        ucol = sb("ucol", [128, 16], F32)
        uprevc = sb("uprevc", [128, 16], F32)
        xce = sb("xce", [128, 16], F16)
        halos = sb("halos", [128, 64], F16)
        pb = [st_.enter_context(nc.psum_tensor(f"pb{i}", [128, 512], F32)) for i in range(8)]

        xs1v = vt("xs1v", [128, T], F16, A3, 8 * T)
        qmzv = vt("qmzv", [128, 2 * T], F16, A3, 10 * T)
        masks = vt("masks", [128, 256], F16, A3, 14 * T)
        if T >= 2048:
            sch = vt("sch", [128, 512], F32, A3, 8 * T)
            schb = vt("schb", [128, 512], F16, A3, 8 * T + 2048)
            wch = vt("wch", [128, 512], F32, A3, 8 * T + 3072)
            wchb = vt("wchb", [128, 512], F16, A3, 8 * T + 5120)
            sini = vt("sini", [128, 512], F32, A3, 8 * T + 6144)
            qbv = vt("qbv", [128, 2048], F16, A3, 8 * T + 8192)
        else:
            sch = sb("sch", [128, 512], F32)
            schb = sb("schb", [128, 512], F16)
            wch = sb("wch", [128, 512], F32)
            wchb = sb("wchb", [128, 512], F16)
            sini = sb("sini", [128, 512], F32)
            qbv = sb("qbv", [128, 2048], F16)
        w1v = vt("w1v", [128, 8 * TB], F16, rows, 4 * T)
        bcs = vt("bcs", [1, T], F16, rows, 8 * T)
        if 12 * T >= 2 * T + 11520 + 4 * T:
            lra_s = vt("lra_s", [16, T], F16, rows, 0)
            bpr_s = vt("bpr_s", [16, D], F16, rows, 2 * T)
            afc_s = vt("afc_s", [128, 128], F16, rows, 2 * T + 2048)
            apr_s = vt("apr_s", [128, 512], F16, rows, 2 * T + 2304)
            bfc_s = vt("bfc_s", [16, FF], F16, rows, 2 * T + 3328)
        else:
            lra_s = sb("lra_s", [16, T], F16)
            bpr_s = sb("bpr_s", [16, D], F16)
            afc_s = sb("afc_s", [128, 128], F16)
            apr_s = sb("apr_s", [128, 512], F16)
            bfc_s = sb("bfc_s", [16, FF], F16)
        rqv = vt("rqv", [128, T], F16, bca, 0)
        szv = vt("szv", [128, 8 * TB], F16, bca, 0)
        egv = (vt("egv", [4, 4096], F16, bca, 0) if 4 * T >= 8192
               else sb("egv", [4, 4096], F16))
        mkv = vt("mkv", [128, 8 * TB], F16, A2, 0)
        if 10 * T + 8192 <= 16 * T:
            hqb = vt("hqb", [128, 32 * TB], F8, A4, 0)
            w8s = vt("w8s", [128, 8192], F8, A4, 8 * T)
            xq8 = vt("xq8", [128, 8 * TB], F8, A4, 10 * T + 8192 - 2 * T)
        else:
            hqb = sb("hqb", [128, 32 * TB], F8)
            w8s = sb("w8s", [128, 8192], F8)
            xq8 = sb("xq8", [128, 8 * TB], F8)

        R_MAP = {0: (0, 0), 2: (0, 1),
                 1: (32, 0), 13: (32, 1), 3: (32, 2),
                 14: (64, 0), 15: (64, 1), 9: (64, 2),
                 8: (96, 0), 7: (96, 1)}

        def R(i):
            b, j = R_MAP[i]
            return rows[b:b + 1, j * T:(j + 1) * T]

        def dmal(dst, src, buf, queue="gpsimd", reads=(), tag=""):
            p.dma(queue, lambda h, d=dst, s=src: h.dma_start(out=d, in_=s),
                  reads=reads, writes=[buf], key=buf, tag=tag)

        def vop(fn, reads=(), writes=(), tag=""):
            p.c("vector", fn, reads, writes, tag)

        def aop(fn, reads=(), writes=(), tag=""):
            p.c("scalar", fn, reads, writes, tag)

        def top(fn, reads=(), writes=(), tag=""):
            p.c("tensor", fn, reads, writes, tag)

        def bcast(row_i, slot, tag):
            p.c("vector", lambda h, r=row_i: h.tensor_copy(bcs[0:1, 0:T], R(r)),
                reads=[rows], writes=[bcs], tag=tag + "c")
            p.dma("gpsimd", lambda h, s=slot: h.dma_start(
                out=bca[:, s * T:(s + 1) * T],
                in_=bass.AP(bcs, 0, [[1, 1], [0, 128], [1, T]])),
                reads=[bcs], writes=[(bca, slot)], key=(bca, slot), tag=tag)

        def rsqrt_row(v_row, o_row, tag):
            vop(lambda h: h.tensor_scalar_add(R(13), R(v_row), EPS),
                reads=[rows], writes=[rows], tag=tag + "a")
            vop(lambda h: h.reciprocal(R(13), R(13)),
                reads=[rows], writes=[rows], tag=tag + "r")
            aop(lambda h: h.activation(R(o_row), R(13), AF.Sqrt),
                reads=[rows], writes=[rows], tag=tag + "s")

        def negmul_tile(tag):
            vop(lambda h: h.scalar_tensor_tensor(
                bca[:, T:2 * T], bca[:, T:2 * T], -1.0, bca[:, 0:T],
                ALU.mult, ALU.mult),
                reads=[(bca, 0), (bca, 1)], writes=[(bca, 1)], tag=tag)

        def stats_of(srcbuf, src_key, m_row, v_row, bank, tag):
            skeys = list(src_key) if isinstance(src_key, list) else [src_key]
            mb_, mj = R_MAP[m_row]
            for t4 in range(NT4):
                for f in range(8):
                    top(lambda h, f_=f, t_=t4, s=srcbuf: h.matmul(
                        bank[0:1, 0:TS], onec[:, 0:1],
                        s[:, f_ * T + t_ * TS: f_ * T + (t_ + 1) * TS],
                        start=(f_ == 0), stop=(f_ == 7)),
                        reads=skeys + [onec], writes=[(bank, "st")], tag=f"{tag}s{f}.{t4}")
                vop(lambda h, t_=t4: h.tensor_scalar_mul(
                    rows[mb_:mb_ + 1, mj * T + t_ * TS: mj * T + (t_ + 1) * TS],
                    bank[0:1, 0:TS], 1.0 / D),
                    reads=[(bank, "st")], writes=[rows], tag=f"{tag}m{t4}")
            for t4 in range(NT4):
                for f in range(8):
                    aop(lambda h, f_=f, t_=t4, s=srcbuf: h.activation(
                        scr3[:, 0:TS], s[:, f_ * T + t_ * TS: f_ * T + (t_ + 1) * TS],
                        AF.Square),
                        reads=skeys, writes=[scr3], tag=f"{tag}q{f}.{t4}")
                    top(lambda h, f_=f: h.matmul(
                        bank[64:65, 0:TS], onec[:, 0:1], scr3[:, 0:TS],
                        start=(f_ == 0), stop=(f_ == 7)),
                        reads=[scr3, onec], writes=[(bank, "sq")], tag=f"{tag}k{f}.{t4}")
                vop(lambda h, t_=t4: h.tensor_scalar_mul(
                    rows[64:65, T + t_ * TS: T + (t_ + 1) * TS],
                    bank[64:65, 0:TS], 1.0 / D),
                    reads=[(bank, "sq")], writes=[rows], tag=f"{tag}e{t4}")
            vop(lambda h: h.tensor_mul(R(14), R(m_row), R(m_row)),
                reads=[rows], writes=[rows], tag=f"{tag}2")
            vop(lambda h: h.tensor_sub(R(v_row), R(15), R(14)),
                reads=[rows], writes=[rows], tag=f"{tag}v")

        def g3(srch, O_tot, row0, nrow, col0, W):
            return bass.AP(srch, row0 * O_tot + col0,
                           [[O_tot, 128], [128 * O_tot, nrow // 128], [1, W]])

        def d3(dsth, doff, nch, W, dcols):
            return bass.AP(dsth, doff, [[dcols, 128], [W, nch], [1, W]])

        def ldg(dsth, doff, dcols, srch, O_tot, row0, nrow, col0, W, buf,
                queue="gpsimd", tag="", rd=()):
            p.dma(queue, lambda h: h.dma_start(
                out=d3(dsth, doff, nrow // 128, W, dcols),
                in_=g3(srch, O_tot, row0, nrow, col0, W)),
                reads=tuple(rd), writes=[buf], key=buf, tag=tag)


        dmal(masks[:, 0:128], trilm, (masks, 0), tag="tril")
        dmal(masks[:, 128:256], trium, (masks, 1), tag="triu")
        p.dma("gpsimd", lambda h: h.dma_start(out=rolec[:, 0:1], in_=bc_ap(role_in.tensor, 0, 1)),
              writes=[rolec], tag="role")
        vop(lambda h: h.memset(onec[:, 0:2], 1.0), writes=[onec], tag="one")
        vop(lambda h: h.memset(zbuf[:, :], 0.0), writes=[zbuf], tag="zb")
        p.dma("gpsimd", lambda h: h.dma_start(out=bcols[:, 0:H], in_=bc_ap(lr_in.tensor, 0, H)),
              writes=[(bcols, "lr")], tag="lr")
        vop(lambda h: h.tensor_scalar_mul(bcols[:, 16:16 + H], bcols[:, 0:H], 0.01),
            reads=[(bcols, "lr")], writes=[(bcols, "beta")], tag="beta")

        def ldcols(off, src, nm, n=8, dst=None):
            d_ = lncol if dst is None else dst
            p.dma("gpsimd", lambda h, a=d_[:, off:off + n],
                  b=bass.AP(src.tensor, 0, [[1, 128], [128, n]]): h.dma_start(
                      out=a, in_=b, allow_slow_non_contiguous=True),
                  writes=[(d_, off)], key=(d_, off), tag=nm)
        ldcols(0, ln1w, "l1w"); ldcols(8, ln1b, "l1b")
        ldcols(16, ln2w, "l2w"); ldcols(24, ln2b, "l2b")

        xbuf = [scr, xs1v]
        for f in range(8):
            sl = f % 2
            dmal(xbuf[sl][:, 0:T], x_in[f * 128:(f + 1) * 128, :],
                 (xbuf[sl], "x"), tag=f"xs{f}")
            for t4 in range(NT4):
                top(lambda h, f_=f, t_=t4, s_=sl: h.matmul(
                    pb[t_][0:1, 0:TS], onec[:, 0:1],
                    xbuf[s_][:, t_ * TS:(t_ + 1) * TS],
                    start=(f_ == 0), stop=(f_ == 7)),
                    reads=[(xbuf[sl], "x"), onec], writes=[(pb[t4], "st")], tag=f"As{f}.{t4}")
            for t4 in range(NT4):
                aop(lambda h, s_=sl, t_=t4: h.activation(
                    scr3[:, 0:TS], xbuf[s_][:, t_ * TS:(t_ + 1) * TS], AF.Square),
                    reads=[(xbuf[sl], "x")], writes=[scr3], tag=f"Aq{f}.{t4}")
                top(lambda h, f_=f, t_=t4: h.matmul(
                    pb[t_][64:65, 0:TS], onec[:, 0:1],
                    scr3[:, 0:TS],
                    start=(f_ == 0), stop=(f_ == 7)),
                    reads=[scr3, onec], writes=[(pb[t4], "st")], tag=f"Ak{f}.{t4}")
        for t4 in range(NT4):
            vop(lambda h, t_=t4: h.tensor_scalar_mul(
                rows[0:1, t_ * TS:(t_ + 1) * TS], pb[t_][0:1, 0:TS], 1.0 / D),
                reads=[(pb[t4], "st")], writes=[rows], tag=f"Am{t4}")
            vop(lambda h, t_=t4: h.tensor_scalar_mul(
                rows[64:65, T + t_ * TS: T + (t_ + 1) * TS],
                pb[t_][64:65, 0:TS], 1.0 / D),
                reads=[(pb[t4], "st")], writes=[rows], tag=f"Ae{t4}")
        vop(lambda h: h.tensor_mul(R(14), R(0), R(0)),
            reads=[rows], writes=[rows], tag="A2")
        vop(lambda h: h.tensor_sub(R(1), R(15), R(14)),
            reads=[rows], writes=[rows], tag="Av")
        rsqrt_row(1, 2, "rs1")
        vop(lambda h: h.tensor_scalar_add(R(13), R(1), EPS),
            reads=[rows], writes=[rows], tag="r2a")
        vop(lambda h: h.reciprocal(R(13), R(13)),
            reads=[rows], writes=[rows], tag="r2b")
        vop(lambda h: h.tensor_mul(R(13), R(1), R(13)),
            reads=[rows], writes=[rows], tag="r2c")
        vop(lambda h: h.tensor_scalar_add(R(13), R(13), EPS),
            reads=[rows], writes=[rows], tag="r2d")
        vop(lambda h: h.reciprocal(R(13), R(13)),
            reads=[rows], writes=[rows], tag="r2e")
        aop(lambda h: h.activation(R(3), R(13), AF.Sqrt),
            reads=[rows], writes=[rows], tag="r2f")
        bcast(2, 0, "b_r1")
        bcast(3, 1, "b_r2")
        vop(lambda h: h.tensor_mul(bca[:, 0:T], bca[:, 0:T], bca[:, T:2 * T]),
            reads=[(bca, 0), (bca, 1)], writes=[(bca, 0)], tag="al1t")
        bcast(0, 1, "b_m1")
        negmul_tile("ma1t")
        SW3 = max(T // 2, 512)
        for f in range(8):
            sl = f % 2
            dmal(xbuf[sl][:, 0:T], x_in[f * 128:(f + 1) * 128, :],
                 (xbuf[sl], "x"), tag=f"xt{f}")
            for c2 in range(0, T, SW3):
                W3 = min(SW3, T - c2)
                vop(lambda h, s_=sl, c_=c2, W_=W3: h.tensor_mul(
                    scr3[:, 0:W_], xbuf[s_][:, c_:c_ + W_], bca[:, c_:c_ + W_]),
                    reads=[(xbuf[sl], "x"), (bca, 0)], writes=[scr3], tag=f"ta{f}.{c2}")
                vop(lambda h, c_=c2, W_=W3: h.tensor_add(
                    scr3[:, 0:W_], scr3[:, 0:W_], bca[:, T + c_:T + c_ + W_]),
                    reads=[scr3, (bca, 1)], writes=[scr3], tag=f"tb{f}.{c2}")
                vop(lambda h, f_=f, c_=c2, W_=W3: h.tensor_scalar(
                    A1[:, f_ * T + c_:f_ * T + c_ + W_], scr3[:, 0:W_],
                    lncol[:, f_:f_ + 1], lncol[:, 8 + f_:9 + f_], ALU.mult, ALU.add),
                    reads=[scr3, (lncol, 0), (lncol, 8)], writes=[(A1, "t0c")], tag=f"tc{f}.{c2}")
        bcast(2, 0, "b_rs1")
        bcast(0, 1, "b_m1b")
        negmul_tile("mr1t")

        def bounce16(srcT, dstD, K_, O_, nn):
            for kk in range(4):
                R4 = K_ // 4
                p.dma("gpsimd", lambda h, k_=kk, D_=dstD, R_=R4: h.dma_start(
                    out=D_.ap()[k_ * R_:(k_ + 1) * R_, :],
                    in_=srcT[k_ * R_:(k_ + 1) * R_, :]),
                    writes=[("d", nn)], key=("d", nn), tag=f"{nn}s{kk}")
        bounce16(w1T, w116_d, D, FF, "w1c")
        bounce16(w2T, w216_d, FF, D, "w2c")
        bounce16(WiT, wi16_d, D, 2 * DI, "wic")
        bounce16(WdtT, wdt16_d, DI, DI, "dtc")
        bounce16(WBT, wb16_d, DI, DI, "wbc")
        bounce16(WCT, wc16_d, DI, DI, "wcc")
        bounce16(WoT1, wo116_d, DI, D, "woc")
        bounce16(w1fT, q1_d, D, FF, "q1")
        bounce16(w2fT, q2_d, FF, D, "q2")

        kvt_off = 2 * T
        hal_off = 0
        gal_off = 4 * T

        for hp in range(8):
            sq, sk, sv_ = [((hp * 3 + j) % 4) * 1024 for j in range(3)]
            for (so, wT_ap, nm) in ((sq, wqT, "q"), (sk, wkT, "k"), (sv_, wvT, "v")):
                ldg(wres, so, 4096, wT_ap.tensor, D, 0, D, hp * 128, 128,
                    ("wq", so // 1024), tag=f"w{nm}{hp}")
            for (so, dsto, nm, bank) in ((sq, 0, "q", pb[0]), (sk, T, "k", pb[1])):
                for t4 in range(NT4):
                    for f in range(8):
                        top(lambda h, f_=f, t_=t4, w_=so, bk=bank: h.matmul(
                            bk[:, 0:TS], wres[:, w_ + f_ * 128: w_ + (f_ + 1) * 128],
                            A1[:, f_ * T + t_ * TS: f_ * T + (t_ + 1) * TS],
                            start=(f_ == 0), stop=(f_ == 7)),
                            reads=[("wq", so // 1024), (A1, "t0c")], writes=[(bank, "qk")],
                            tag=f"{nm}{hp}.{t4}.{f}")
                    vop(lambda h, t_=t4, o_=dsto, bk=bank: h.tensor_copy(
                        A3[:, o_ + t_ * TS: o_ + (t_ + 1) * TS], bk[:, 0:TS]),
                        reads=[(bank, "qk")], writes=[(A3, "qk", dsto)], tag=f"{nm}e{hp}.{t4}")
            def kvt_block(c):
                for f in range(8):
                    top(lambda h, c_=c, f_=f, w_=sk: h.matmul(
                        pb[2][:, 0:128],
                        A1[:, f_ * T + c_ * 128: f_ * T + (c_ + 1) * 128],
                        wres[:, w_ + f_ * 128: w_ + (f_ + 1) * 128],
                        start=(f_ == 0), stop=(f_ == 7)),
                        reads=[("wq", sk // 1024), (A1, "t0c")], writes=[(pb[2], "kt")],
                        tag=f"kt{hp}.{c}.{f}")
                for f in range(8):
                    top(lambda h, c_=c, f_=f, w_=sv_: h.matmul(
                        pb[2][:, 128:256],
                        A1[:, f_ * T + c_ * 128: f_ * T + (c_ + 1) * 128],
                        wres[:, w_ + f_ * 128: w_ + (f_ + 1) * 128],
                        start=(f_ == 0), stop=(f_ == 7)),
                        reads=[("wq", sv_ // 1024), (A1, "t0c")], writes=[(pb[2], "kt")],
                        tag=f"vt{hp}.{c}.{f}")
                vop(lambda h, c_=c: h.tensor_copy(
                    A3[:, kvt_off + c_ * 256: kvt_off + (c_ + 1) * 256], pb[2][:, 0:256]),
                    reads=[(pb[2], "kt")], writes=[(A3, "kvt", c)], tag=f"kv{hp}.{c}")

            def unit_steps(c, hh):
                hg = hp * 2 + hh
                Xn, Kq = XNb[hh], KQb[hh]
                pG = pb[3 + hh]
                pA = pb[5 + hh]
                pD = pb[7] if hh == 0 else pb[0]
                kfm = A3[hh * 64:hh * 64 + 64, T + c * 128: T + (c + 1) * 128]
                qfm = A3[hh * 64:hh * 64 + 64, c * 128:(c + 1) * 128]
                ktm = A3[:, kvt_off + c * 256 + hh * 64: kvt_off + c * 256 + hh * 64 + 64]
                vtm = A3[:, kvt_off + c * 256 + 128 + hh * 64: kvt_off + c * 256 + 128 + hh * 64 + 64]
                bco = bcols[:, 16 + hg:17 + hg]
                pp_, cc_ = hpk(hg)
                steps = []

                def s_gram():
                    top(lambda h, a=kfm, g=pG: h.matmul(g[:, 0:128], a, a, start=True, stop=True),
                        reads=[(A3, "qk", T)], writes=[(pG, "g")], tag=f"KK{hp}.{c}.{hh}")
                    top(lambda h, a=kfm, b=qfm, g=pG: h.matmul(g[:, 128:256], a, b, start=True, stop=True),
                        reads=[(A3, "qk", T), (A3, "qk", 0)], writes=[(pG, "g")], tag=f"KQ{hp}.{c}.{hh}")
                steps.append(s_gram)

                def s_mask():
                    vop(lambda h, g=pG, b=bco, X=Xn: h.scalar_tensor_tensor(
                        X[:, 128:256], g[:, 0:128], b, masks[:, 0:128], ALU.mult, ALU.mult),
                        reads=[(pG, "g"), (masks, 0), (bcols, "beta")], writes=[(Xn, "n")], tag=f"mN{hp}.{c}.{hh}")
                    vop(lambda h, g=pG, b=bco, X=Xn: h.scalar_tensor_tensor(
                        X[:, 256:384], g[:, 0:128], b, masks[:, 128:256], ALU.mult, ALU.mult),
                        reads=[(pG, "g"), (masks, 1), (bcols, "beta")], writes=[(Xn, "t")], tag=f"mT{hp}.{c}.{hh}")
                    vop(lambda h, g=pG, K=Kq: h.tensor_mul(K[:, :], g[:, 128:256], masks[:, 128:256]),
                        reads=[(pG, "g"), (masks, 1)], writes=[Kq], tag=f"mQ{hp}.{c}.{hh}")
                    vop(lambda h, a=ktm, b=bco, X=Xn: h.tensor_scalar_mul(X[:, 0:64], a, b),
                        reads=[(A3, "kvt", c), (bcols, "beta")], writes=[(Xn, "a")], tag=f"Xa{hp}.{c}.{hh}")
                    vop(lambda h, a=vtm, b=bco, X=Xn: h.tensor_scalar_mul(X[:, 64:128], a, b),
                        reads=[(A3, "kvt", c), (bcols, "beta")], writes=[(Xn, "b")], tag=f"Xb{hp}.{c}.{hh}")
                steps.append(s_mask)

                for k in range(NLVL):
                    last = (k == NLVL - 1)
                    ncols = 128 if last else 256

                    def s_mm(k=k, last=last, ncols=ncols):
                        top(lambda h, n_=ncols, X=Xn, A=pA: h.matmul(
                            A[:, 0:n_], X[:, 256:384], X[:, 0:n_], start=True, stop=True),
                            reads=[(Xn, "a"), (Xn, "b"), (Xn, "n"), (Xn, "t")],
                            writes=[(pA, "a")], tag=f"sA{hp}.{c}.{hh}.{k}")
                        if not last:
                            top(lambda h, X=Xn, A=pA: h.matmul(
                                A[:, 256:384], X[:, 128:256], X[:, 256:384], start=True, stop=True),
                                reads=[(Xn, "n"), (Xn, "t")], writes=[(pA, "a")], tag=f"sB{hp}.{c}.{hh}.{k}")
                    steps.append(s_mm)

                    def s_up(k=k, last=last):
                        xo = ALU.subtract if k == 0 else ALU.add
                        vop(lambda h, o=xo, X=Xn, A=pA: h.tensor_tensor(
                            X[:, 0:128], X[:, 0:128], A[:, 0:128], o),
                            reads=[(pA, "a"), (Xn, "a"), (Xn, "b")],
                            writes=[(Xn, "a"), (Xn, "b")], tag=f"sX{hp}.{c}.{hh}.{k}")
                        if not last:
                            vop(lambda h, X=Xn, A=pA: h.tensor_copy(X[:, 128:384], A[:, 128:384]),
                                reads=[(pA, "a")], writes=[(Xn, "n"), (Xn, "t")], tag=f"sN{hp}.{c}.{hh}.{k}")
                    steps.append(s_up)

                def s_hg():
                    top(lambda h, a=ktm, X=Xn, dd=pD: h.matmul(dd[0:64, 0:64], X[:, 0:64], a,
                                                               start=True, stop=True),
                        reads=[(Xn, "a"), (A3, "kvt", c)], writes=[(pD, "d")], tag=f"Ht{hp}.{c}.{hh}")
                    top(lambda h, a=ktm, X=Xn, dd=pD: h.matmul(dd[0:64, 64:128], a, X[:, 64:128],
                                                               start=True, stop=True),
                        reads=[(Xn, "b"), (A3, "kvt", c)], writes=[(pD, "d")], tag=f"Gt{hp}.{c}.{hh}")
                steps.append(s_hg)

                def s_hge():
                    vop(lambda h, P_=pp_, C_=cc_, c_=c, dd=pD: h.tensor_copy(
                        A4[P_:P_ + 64, hal_off + c_ * 512 + C_: hal_off + c_ * 512 + C_ + 64],
                        dd[0:64, 0:64]),
                        reads=[(pD, "d")], writes=[(A4, "hal")], tag=f"He{hp}.{c}.{hh}")
                    vop(lambda h, P_=pp_, C_=cc_, c_=c, dd=pD: h.tensor_copy(
                        A4[P_:P_ + 64, gal_off + c_ * 512 + C_: gal_off + c_ * 512 + C_ + 64],
                        dd[0:64, 64:128]),
                        reads=[(pD, "d")], writes=[(A4, "gal")], tag=f"Ge{hp}.{c}.{hh}")
                steps.append(s_hge)

                def s_zo():
                    top(lambda h, X=Xn, K=Kq, dd=pD: h.matmul(dd[0:64, 128:256], X[:, 0:64], K[:, :],
                                                              start=True, stop=True),
                        reads=[(Xn, "a"), Kq], writes=[(pD, "d")], tag=f"Z{hp}.{c}.{hh}")
                    top(lambda h, X=Xn, K=Kq, dd=pD: h.matmul(dd[0:64, 256:384], X[:, 64:128], K[:, :],
                                                              start=True, stop=True),
                        reads=[(Xn, "b"), Kq], writes=[(pD, "d")], tag=f"Op{hp}.{c}.{hh}")
                steps.append(s_zo)

                def s_ze():
                    vop(lambda h, q=qfm, P_=pp_, g=hg, c_=c, dd=pD: h.tensor_sub(
                        qmzv[P_:P_ + 64, c_ * 256 + (g % 2) * 128: c_ * 256 + (g % 2) * 128 + 128],
                        q, dd[0:64, 128:256]),
                        reads=[(pD, "d"), (A3, "qk", 0)], writes=[(qmzv, "qmz")], tag=f"QZ{hp}.{c}.{hh}")
                    vop(lambda h, g=hg, c_=c, dd=pD: h.tensor_copy(
                        A2[(g % 2) * 64:(g % 2) * 64 + 64,
                           (g // 2) * T + c_ * 128:(g // 2) * T + (c_ + 1) * 128],
                        dd[0:64, 256:384]),
                        reads=[(pD, "d")], writes=[(A2, "oall")], tag=f"Oe{hp}.{c}.{hh}")
                steps.append(s_ze)
                return steps

            kvt_block(0)
            for c in range(NCH):
                s0 = unit_steps(c, 0)
                s1 = unit_steps(c, 1)
                for si, (a, b) in enumerate(zip(s0, s1)):
                    a()
                    if si == 1 and c + 1 < NCH:
                        kvt_block(c + 1)
                    b()
            hpp = (hp % 2) * 64
            p.dma("sync", lambda h, hp_=hp, pp2=hpp: h.dma_start(
                out=qmz_d.ap()[pp2:pp2 + 64, hp_ * NCH * 256:(hp_ + 1) * NCH * 256],
                in_=qmzv[pp2:pp2 + 64, 0:NCH * 256]),
                reads=[(qmzv, "qmz")], writes=[("d", "qmz")], key=("d", "qmz"), tag=f"qz{hp}")

        p.fence([("d", "qmz")], [sch, schb, wch, wchb, sini, "xs1v"], tag="fchain")
        vop(lambda h: h.memset(sch[:, :], 0.0), writes=[sch], tag="Sz")
        vop(lambda h: h.memset(schb[:, :], 0.0), writes=[schb], tag="Szb")
        for c in range(NCH):
            vop(lambda h, c_=c: h.tensor_copy(ARC[:, c_ * 512:(c_ + 1) * 512], schb[:, :]),
                reads=[schb], writes=[(ARC, "s0log")], tag=f"sl{c}")
            for g in range(H):
                bk = pb[g % 2]
                pg, cg = hpk(g)
                top(lambda h, g_=g, c_=c, b_=bk, P_=pg, C_=cg: h.matmul(
                    b_[P_:P_ + 64, (g_ // 4) * 64:(g_ // 4) * 64 + 64],
                    A4[P_:P_ + 64, hal_off + c_ * 512 + C_: hal_off + c_ * 512 + C_ + 64],
                    schb[P_:P_ + 64, C_:C_ + 64],
                    start=True, stop=True),
                    reads=[(A4, "hal"), schb], writes=[(bk, "hs")], tag=f"HS{c}.{g}")
            vop(lambda h: h.tensor_sub(sch[:, 0:256], sch[:, 0:256], pb[0][:, 0:256]),
                reads=[(pb[0], "hs"), sch], writes=[sch], tag=f"Su0.{c}")
            vop(lambda h: h.tensor_sub(sch[:, 256:512], sch[:, 256:512], pb[1][:, 0:256]),
                reads=[(pb[1], "hs"), sch], writes=[sch], tag=f"Su1.{c}")
            vop(lambda h, c_=c: h.tensor_add(
                sch[:, :], sch[:, :], A4[:, gal_off + c_ * 512: gal_off + (c_ + 1) * 512]),
                reads=[(A4, "gal"), sch], writes=[sch], tag=f"Sg{c}")
            vop(lambda h: h.tensor_copy(schb[:, :], sch[:, :]), reads=[sch], writes=[schb],
                tag=f"Sc{c}")
        p.dma("sync", lambda h: h.dma_start(out=cc_s_in.ap(), in_=sch[:, :]),
              reads=[sch], writes=[("d", "ccs")], tag="sstg")
        p.cc(lambda h: h.collective_compute(
            "AllGather", ALU.bypass, replica_groups=GROUPS,
            ins=[cc_s_in.ap().opt()], outs=[cc_s_out.ap().opt()]),
            reads=[("d", "ccs")], writes=[("d", "ccso")], tag="sag")
        p.dma("sync", lambda h: h.dma_start(out=sini[:, :], in_=cc_s_out.ap()[0:128, :]),
              reads=[("d", "ccso")], writes=[sini], tag="srcv")
        vop(lambda h: h.tensor_scalar_mul(sini[:, :], sini[:, :], rolec[:, 0:1]),
            reads=[sini, rolec], writes=[sini], tag="smsk")
        vop(lambda h: h.tensor_copy(wch[:, :], sini[:, :]), reads=[sini], writes=[wch], tag="W0")
        vop(lambda h: h.tensor_copy(wchb[:, :], wch[:, :]), reads=[wch], writes=[wchb], tag="W0b")
        for c in range(NCH):
            vop(lambda h, c_=c: h.tensor_add(schb[:, :], ARC[:, c_ * 512:(c_ + 1) * 512], wchb[:, :]),
                reads=[(ARC, "s0log"), wchb], writes=[schb], tag=f"S0c{c}")
            p.dma("gpsimd", lambda h, c_=c: h.dma_start(
                out=bass.AP(qbv, 0, [[2048, 64], [512, 4], [1, 256]]),
                in_=bass.AP(qmz_d.ap().tensor, c_ * 256,
                            [[8 * NCH * 256, 64], [2 * NCH * 256, 4], [1, 256]])),
                reads=[("d", "qmz")], writes=[(qbv, "qb")], key=(qbv, "qb"), tag=f"qr{c}")
            p.dma("gpsimd", lambda h, c_=c: h.dma_start(
                out=bass.AP(qbv, 64 * 2048 + 256, [[2048, 64], [512, 4], [1, 256]]),
                in_=bass.AP(qmz_d.ap().tensor,
                            64 * 8 * NCH * 256 + NCH * 256 + c_ * 256,
                            [[8 * NCH * 256, 64], [2 * NCH * 256, 4], [1, 256]])),
                reads=[("d", "qmz")], writes=[(qbv, "qb")], key=(qbv, "qb"), tag=f"qro{c}")
            for g in range(H):
                bk = pb[2 + g % 2]
                pg, cg = hpk(g)
                top(lambda h, g_=g, b_=bk, P_=pg, C_=cg: h.matmul(
                    b_[P_:P_ + 64, (g_ // 4) * 128:(g_ // 4) * 128 + 128],
                    schb[P_:P_ + 64, C_:C_ + 64],
                    qbv[P_:P_ + 64, (g_ // 2) * 256 + (g_ % 2) * 128: (g_ // 2) * 256 + (g_ % 2) * 128 + 128],
                    start=True, stop=True),
                    reads=[schb, (qbv, "qb")], writes=[(bk, "of")], tag=f"Of{c}.{g}")
            for g in range(H):
                bk = pb[2 + g % 2]
                pg, cg = hpk(g)
                vop(lambda h, g_=g, c_=c, b_=bk, P_=pg: h.tensor_add(
                    A2[(g_ % 2) * 64:(g_ % 2) * 64 + 64,
                       (g_ // 2) * T + c_ * 128:(g_ // 2) * T + (c_ + 1) * 128],
                    A2[(g_ % 2) * 64:(g_ % 2) * 64 + 64,
                       (g_ // 2) * T + c_ * 128:(g_ // 2) * T + (c_ + 1) * 128],
                    b_[P_:P_ + 64, (g_ // 4) * 128:(g_ // 4) * 128 + 128]),
                    reads=[(bk, "of"), (A2, "oall")], writes=[(A2, "oall")], tag=f"Oa{c}.{g}")
            if c < NCH - 1:
                for g in range(H):
                    bk = pb[g % 2]
                    pg, cg = hpk(g)
                    top(lambda h, g_=g, c_=c, b_=bk, P_=pg, C_=cg: h.matmul(
                        b_[P_:P_ + 64, (g_ // 4) * 64:(g_ // 4) * 64 + 64],
                        A4[P_:P_ + 64, hal_off + c_ * 512 + C_: hal_off + c_ * 512 + C_ + 64],
                        wchb[P_:P_ + 64, C_:C_ + 64],
                        start=True, stop=True),
                        reads=[(A4, "hal"), wchb], writes=[(bk, "hw")], tag=f"HW{c}.{g}")
                vop(lambda h: h.tensor_sub(wch[:, 0:256], wch[:, 0:256], pb[0][:, 0:256]),
                    reads=[(pb[0], "hw"), wch], writes=[wch], tag=f"Wu0.{c}")
                vop(lambda h: h.tensor_sub(wch[:, 256:512], wch[:, 256:512], pb[1][:, 0:256]),
                    reads=[(pb[1], "hw"), wch], writes=[wch], tag=f"Wu1.{c}")
                vop(lambda h: h.tensor_copy(wchb[:, :], wch[:, :]), reads=[wch], writes=[wchb],
                    tag=f"Wb{c}")

        p.fence([(A3, "qk", 0), (A3, "qk", T), (A3, "kvt")],
                [(A3, "x"), "xs1v", "qmzv", "masks", sch, schb, wch, wchb, sini,
                 "qbv"], tag="fA3x")
        p.fence([], [("wq", 0), ("wq", 1), ("wq", 2), ("wq", 3),
                     ("ws", 0), ("ws", 1)], tag="fwres")
        for f in range(8):
            dmal(A3[:, f * T:(f + 1) * T], x_in[f * 128:(f + 1) * 128, :], (A3, "x", f), tag=f"xr{f}")
        p.fence([(A1, "t0c")], [(A1, "xnc")], tag="fA1x")
        for f in range(8):
            for c2 in range(0, T, SW3):
                W3 = min(SW3, T - c2)
                vop(lambda h, f_=f, c_=c2, W_=W3: h.tensor_mul(
                    scr3[:, 0:W_], A3[:, f_ * T + c_:f_ * T + c_ + W_], bca[:, c_:c_ + W_]),
                    reads=[(A3, "x", f), (bca, 0)], writes=[scr3], tag=f"xna{f}.{c2}")
                vop(lambda h, f_=f, c_=c2, W_=W3: h.tensor_add(
                    A1[:, f_ * T + c_:f_ * T + c_ + W_], scr3[:, 0:W_], bca[:, T + c_:T + c_ + W_]),
                    reads=[scr3, (bca, 1)], writes=[(A1, "xnc")], tag=f"xnb{f}.{c2}")
        p.fence([(A4, "hal"), (A4, "gal")], [(A4, "u")], tag="fA4u")
        for dg in range(4):
            so = (dg % 2) * 2048
            ldg(wres, so, 4096, woT.tensor, D, 0, D, dg * 256, 256, ("ws", so // 2048), tag=f"lwo{dg}")
            for oi in range(2):
                dt_i = dg * 2 + oi
                wob = pb[4 if dt_i % 2 == 0 else 5]
                for t4 in range(NT4):
                    for f in range(8):
                        top(lambda h, f_=f, t_=t4, o_=oi, s_=so, b_=wob: h.matmul(
                            b_[:, 0:TS], wres[:, s_ + f_ * 256 + o_ * 128: s_ + f_ * 256 + (o_ + 1) * 128],
                            A2[:, f_ * T + t_ * TS: f_ * T + (t_ + 1) * TS],
                            start=(f_ == 0), stop=(f_ == 7)),
                            reads=[("ws", so // 2048), (A2, "oall")], writes=[(wob, "wo")],
                            tag=f"wo{dt_i}.{t4}.{f}")
                    vop(lambda h, d_=dt_i, t_=t4, b_=wob: h.tensor_add(
                        A4[:, d_ * T + t_ * TS: d_ * T + (t_ + 1) * TS], b_[:, 0:TS],
                        A1[:, d_ * T + t_ * TS: d_ * T + (t_ + 1) * TS]),
                        reads=[(wob, "wo"), (A1, "xnc")], writes=[(A4, "u")], tag=f"u{dt_i}.{t4}")
        stats_of(A4, (A4, "u"), 7, 8, pb[0], "su")
        rsqrt_row(8, 9, "al2")
        bcast(9, 0, "b_al2")
        bcast(7, 1, "b_m2")
        negmul_tile("ma2t")
        p.dma("gpsimd", lambda h, a=bcols[:, 32:64], b=bass.AP(b1v.tensor, 0, [[1, 128], [128, 32]]): h.dma_start(out=a, in_=b, allow_slow_non_contiguous=True), writes=[(bcols, "b1")], key=(bcols, "b1"), tag="b1")
        p.dma("gpsimd", lambda h, a=bcols[:, 64:72], b=bass.AP(b2v.tensor, 0, [[1, 128], [128, 8]]): h.dma_start(out=a, in_=b, allow_slow_non_contiguous=True), writes=[(bcols, "b2")], key=(bcols, "b2"), tag="b2")
        p.fence([(A2, "oall")], [(A2, "ghm")], tag="fA2g")
        p.fence([], [rows, "w1v"], tag="fw1v")
        for tb in range(4):
            for f in range(8):
                vop(lambda h, f_=f, b_=tb: h.tensor_mul(
                    scr3[:, 0:TB], A4[:, f_ * T + b_ * TB: f_ * T + (b_ + 1) * TB], bca[:, b_ * TB:(b_ + 1) * TB]),
                    reads=[(A4, "u"), (bca, 0)], writes=[scr3], tag=f"wia{tb}.{f}")
                vop(lambda h, f_=f, b_=tb: h.tensor_add(
                    scr3[:, 0:TB], scr3[:, 0:TB], bca[:, T + b_ * TB: T + (b_ + 1) * TB]),
                    reads=[scr3, (bca, 1)], writes=[scr3], tag=f"wib{tb}.{f}")
                vop(lambda h, f_=f: h.tensor_scalar(
                    w1v[:, f_ * TB:(f_ + 1) * TB], scr3[:, 0:TB],
                    lncol[:, 16 + f_:17 + f_], lncol[:, 24 + f_:25 + f_], ALU.mult, ALU.add),
                    reads=[scr3, (lncol, 16), (lncol, 24)], writes=[(w1v, "w1in", tb)], tag=f"wic{tb}.{f}")
            for hgrp in range(16):
                so = (hgrp % 2) * 2048
                ldg(wres, so, 4096, w116_d, FF, 0, D, hgrp * 256, 256,
                    ("ws", so // 2048), tag=f"lw1{tb}.{hgrp}", rd=[("d", "w1c")])
                for ho in range(2):
                    ht = hgrp * 2 + ho
                    wb = pb[5 if ht % 2 == 0 else 7]
                    for t4 in range(NTB):
                        for f in range(8):
                            top(lambda h, f_=f, o_=ho, t_=t4, s_=so, b_=wb: h.matmul(
                                b_[:, 0:TSB],
                                wres[:, s_ + f_ * 256 + o_ * 128: s_ + f_ * 256 + (o_ + 1) * 128],
                                w1v[:, f_ * TB + t_ * TSB: f_ * TB + (t_ + 1) * TSB],
                                start=(f_ == 0), stop=(f_ == 7)),
                                reads=[("ws", so // 2048), (w1v, "w1in", tb)], writes=[(wb, "w1")],
                                tag=f"w1m{tb}.{ht}.{t4}.{f}")
                        aop(lambda h, h_=ht, t_=t4, b_=wb: h.activation(
                            A2[:, h_ * TB + t_ * TSB: h_ * TB + (t_ + 1) * TSB], b_[:, 0:TSB],
                            AF.Gelu, bias=bcols[:, 32 + h_:33 + h_]),
                            reads=[(wb, "w1"), (bcols, "b1")], writes=[(A2, "ghm")],
                            tag=f"w1g{tb}.{ht}.{t4}")
            for dt_i in range(8):
                w2b = pb[6 if dt_i % 2 == 0 else 2]
                for hf in range(2):
                    so = ((dt_i * 2 + hf) % 2) * 2048
                    ldg(wres, so, 4096, w216_d, D, hf * 2048, 2048, dt_i * 128, 128,
                        ("ws", so // 2048), tag=f"lw2{tb}.{dt_i}.{hf}", rd=[("d", "w2c")])
                    for t4 in range(NTB):
                        for hc in range(16):
                            top(lambda h, hc_=hc, t_=t4, h2=hf, s_=so, b_=w2b: h.matmul(
                                b_[:, 0:TSB],
                                wres[:, s_ + hc_ * 128: s_ + (hc_ + 1) * 128],
                                A2[:, (h2 * 16 + hc_) * TB + t_ * TSB: (h2 * 16 + hc_) * TB + (t_ + 1) * TSB],
                                start=(h2 == 0 and hc_ == 0), stop=(h2 == 1 and hc_ == 15)),
                                reads=[("ws", so // 2048), (A2, "ghm")], writes=[(w2b, "w2")],
                                tag=f"w2m{tb}.{dt_i}.{hf}.{t4}.{hc}")
                if True:
                    for t4 in range(NTB):
                        aop(lambda h, d_=dt_i, t_=t4, b_=w2b: h.activation(
                            scr3[:, 0:TSB], b_[:, 0:TSB], AF.Identity,
                            bias=bcols[:, 64 + d_:65 + d_]),
                            reads=[(w2b, "w2"), (bcols, "b2")], writes=[scr3], tag=f"s0a{tb}.{dt_i}.{t4}")
                        vop(lambda h, d_=dt_i, t_=t4, b_=tb: h.tensor_add(
                            scr[:, 0:TSB], scr3[:, 0:TSB],
                            A4[:, d_ * T + b_ * TB + t_ * TSB: d_ * T + b_ * TB + (t_ + 1) * TSB]),
                            reads=[scr3, (A4, "u")], writes=[(scr, "s0")], tag=f"s0b{tb}.{dt_i}.{t4}")
                        p.dma("sync", lambda h, d_=dt_i, t_=t4, b_=tb: h.dma_start(
                            out=s0_d.ap()[d_ * 128:(d_ + 1) * 128,
                                          b_ * TB + t_ * TSB: b_ * TB + (t_ + 1) * TSB],
                            in_=scr[:, 0:TSB]),
                            reads=[(scr, "s0")], writes=[("d", "s0")], key=("d", "s0"),
                            tag=f"s0d{tb}.{dt_i}.{t4}")

        def quant_prep(wT_ap, K_, O_, scol, dstD, nn):
            PC = min(O_, T)
            vop(lambda h: h.memset(qsc[:, scol + 2:scol + 3], 0.0), writes=[(qsc, scol, "mx")],
                tag=nn + "z")
            for kk in range(K_ // 128):
                for ob in range(0, O_, PC):
                    dmal(scr[:, 0:PC], wT_ap[kk * 128:(kk + 1) * 128, ob:ob + PC], (scr, "wc"),
                         tag=f"{nn}a{kk}.{ob}")
                    vop(lambda h, P_=PC: h.reduce_max(qsc[:, scol + 3:scol + 4], scr[:, 0:P_],
                                                      axis=AX.X, apply_absolute_value=True),
                        reads=[(scr, "wc")], writes=[(qsc, scol, "t")], tag=f"{nn}r{kk}.{ob}")
                    vop(lambda h: h.tensor_max(qsc[:, scol + 2:scol + 3], qsc[:, scol + 2:scol + 3],
                                               qsc[:, scol + 3:scol + 4]),
                        reads=[(qsc, scol, "t"), (qsc, scol, "mx")], writes=[(qsc, scol, "mx")],
                        tag=f"{nn}m{kk}.{ob}")
            p.dma("gpsimd", lambda h: h.dma_start(
                out=qmax[0:1, 4:132],
                in_=bass.AP(qsc, scol + 2, [[12, 128], [1, 1]]),
                allow_slow_non_contiguous=True),
                reads=[(qsc, scol, "mx")], writes=[qmax], key=qmax, tag=f"{nn}tr")
            vop(lambda h: h.reduce_max(qmax[0:1, 0:1], qmax[0:1, 4:132], axis=AX.X),
                reads=[qmax], writes=[qmax], tag=f"{nn}rm")
            p.dma("gpsimd", lambda h: h.dma_start(
                out=qsc[:, scol + 4:scol + 5],
                in_=bass.AP(qmax, 0, [[1, 1], [0, 128], [1, 1]])),
                reads=[qmax], writes=[(qsc, scol, "gm")], key=(qsc, scol, "gm"),
                tag=f"{nn}ar")
            vop(lambda h: h.tensor_scalar_mul(qsc[:, scol:scol + 1], qsc[:, scol + 4:scol + 5],
                                              1.0 / 127.0),
                reads=[(qsc, scol, "gm")], writes=[(qsc, scol, "s")], tag=f"{nn}s")
            vop(lambda h: h.reciprocal(qsc[:, scol + 1:scol + 2], qsc[:, scol:scol + 1]),
                reads=[(qsc, scol, "s")], writes=[(qsc, scol, "rs")], tag=f"{nn}rs")
            for kk in range(K_ // 128):
                for ob in range(0, O_, PC):
                    dmal(scr[:, 0:PC], wT_ap[kk * 128:(kk + 1) * 128, ob:ob + PC], (scr, "wc"),
                         tag=f"{nn}b{kk}.{ob}")
                    S3 = max(T // 2, 512)
                    for o2 in range(0, PC, S3):
                        W2 = min(S3, PC - o2)
                        aop(lambda h, o_=o2, W_=W2: h.activation(
                            scr3[:, 0:W_], scr[:, o_:o_ + W_], AF.Identity,
                            scale=qsc[:, scol + 1:scol + 2]),
                            reads=[(scr, "wc"), (qsc, scol, "rs")], writes=[scr3], tag=f"{nn}q{kk}.{ob}.{o2}")
                        vop(lambda h, o_=o2, W_=W2: h.tensor_scalar(
                            rqv[:, o_:o_ + W_], scr3[:, 0:W_], MAGIC, MAGIC, ALU.add, ALU.subtract),
                            reads=[scr3], writes=[(rqv, "rq")], tag=f"{nn}w{kk}.{ob}.{o2}")
                    p.dma("sync", lambda h, k_=kk, o_=ob, D_=dstD, P_=PC: h.dma_start(
                        out=D_.ap()[k_ * 128:(k_ + 1) * 128, o_:o_ + P_], in_=rqv[:, 0:P_]),
                        reads=[(rqv, "rq")], writes=[("d", nn)], key=("d", nn),
                        tag=f"{nn}o{kk}.{ob}")
        p.fence([], [(bca, 0), (bca, 1), "rqv"], tag="frq")
        quant_prep(WfcT, D, FF, 0, rfc_d, "qfc")
        quant_prep(WprojT, FF, D, 4, rpr_d, "qpr")
        p.fence([], ["w1v", rows, "afc_s", "apr_s", bfc_s, bpr_s, "lra_s"],
                tag="fEsm")
        afc_v = afc_s[:, :]
        apr_v = apr_s[:, :]
        bfcs_v = bfc_s[:, :]
        bprs_v = bpr_s[:, :]
        lra_v = lra_s[:, :]
        for f in range(8):
            dmal(afc_v[:, f * 16:(f + 1) * 16], AfcT[f * 128:(f + 1) * 128, :], (afc_s, f),
                 tag=f"afc{f}")
        for hc in range(32):
            dmal(apr_v[:, hc * 16:(hc + 1) * 16], AprojT[hc * 128:(hc + 1) * 128, :],
                 (apr_s, hc), tag=f"apr{hc}")
        dmal(bfcs_v, BfcT, bfc_s, tag="bfc")
        dmal(bprs_v, BprojT, bpr_s, tag="bpr")
        p.dma("gpsimd", lambda h, a=bcols[:, 72:104], b=bass.AP(bfcv.tensor, 0, [[1, 128], [128, 32]]): h.dma_start(out=a, in_=b, allow_slow_non_contiguous=True), writes=[(bcols, "bfc")], key=(bcols, "bfc"), tag="bfcc")
        p.dma("gpsimd", lambda h, a=bcols[:, 104:112], b=bass.AP(bprv.tensor, 0, [[1, 128], [128, 8]]): h.dma_start(out=a, in_=b, allow_slow_non_contiguous=True), writes=[(bcols, "bpr")], key=(bcols, "bpr"), tag="bprc")
        for t4 in range(NT4):
            for f in range(8):
                top(lambda h, f_=f, t_=t4: h.matmul(
                    pb[0][0:16, 0:TS], afc_v[:, f_ * 16:(f_ + 1) * 16],
                    A1[:, f_ * T + t_ * TS: f_ * T + (t_ + 1) * TS],
                    start=(f_ == 0), stop=(f_ == 7)),
                    reads=[(afc_s, f), (A1, "xnc")], writes=[(pb[0], "lm")],
                    tag=f"lm{t4}.{f}")
            vop(lambda h, t_=t4: h.tensor_scalar_mul(lra_v[:, t_ * TS:(t_ + 1) * TS],
                                                     pb[0][0:16, 0:TS], 2.0),
                reads=[(pb[0], "lm")], writes=[(lra_s, "a")], tag=f"lme{t4}")
        p.fence([(A2, "ghm")], [(A2, "gf")], tag="fA2f")
        for tb in range(4):
            for hgrp in range(16):
                so = (hgrp % 2) * 2048
                ldg(wres, so, 4096, rfc_d, FF, 0, D, hgrp * 256, 256,
                    ("ws", so // 2048), tag=f"lrf{tb}.{hgrp}", rd=[("d", "qfc")])
                for ho in range(2):
                    ht = hgrp * 2 + ho
                    fcb = pb[5 if ht % 2 == 0 else 7]
                    for t4 in range(NTB):
                        for f in range(8):
                            top(lambda h, f_=f, o_=ho, t_=t4, b_=tb, s_=so, k_=fcb: h.matmul(
                                k_[:, 0:TSB],
                                wres[:, s_ + f_ * 256 + o_ * 128: s_ + f_ * 256 + (o_ + 1) * 128],
                                A1[:, f_ * T + b_ * TB + t_ * TSB: f_ * T + b_ * TB + (t_ + 1) * TSB],
                                start=(f_ == 0), stop=(f_ == 7)),
                                reads=[("ws", so // 2048), (A1, "xnc")], writes=[(fcb, "fc")],
                                tag=f"fcm{tb}.{ht}.{t4}.{f}")
                        top(lambda h, h_=ht, t_=t4, b_=tb: h.matmul(
                            pb[3][:, 0:TSB], bfcs_v[:, h_ * 128:(h_ + 1) * 128],
                            lra_v[:, b_ * TB + t_ * TSB: b_ * TB + (t_ + 1) * TSB],
                            start=True, stop=True),
                            reads=[bfc_s, (lra_s, "a")], writes=[(pb[3], "fl")],
                            tag=f"fcl{tb}.{ht}.{t4}")
                        aop(lambda h, t_=t4, k_=fcb: h.activation(
                            scr3[:, 0:TSB], k_[:, 0:TSB], AF.Identity, scale=qsc[:, 0:1]),
                            reads=[(fcb, "fc"), (qsc, 0, "s")], writes=[scr3], tag=f"fcs{tb}.{ht}.{t4}")
                        vop(lambda h, t_=t4: h.tensor_add(scr3[:, 0:TSB], scr3[:, 0:TSB], pb[3][:, 0:TSB]),
                            reads=[scr3, (pb[3], "fl")], writes=[scr3], tag=f"fca{tb}.{ht}.{t4}")
                        aop(lambda h, h_=ht, t_=t4: h.activation(
                            A2[:, h_ * TB + t_ * TSB: h_ * TB + (t_ + 1) * TSB], scr3[:, 0:TSB],
                            AF.Gelu, bias=bcols[:, 72 + h_:73 + h_]),
                            reads=[scr3, (bcols, "bfc")], writes=[(A2, "gf")],
                            tag=f"fcg{tb}.{ht}.{t4}")
            for t4 in range(NTB):
                for hc in range(32):
                    top(lambda h, hc_=hc, t_=t4: h.matmul(
                        pb[0][0:16, 0:TSB], apr_v[:, hc_ * 16:(hc_ + 1) * 16],
                        A2[:, hc_ * TB + t_ * TSB: hc_ * TB + (t_ + 1) * TSB],
                        start=(hc_ == 0), stop=(hc_ == 31)),
                        reads=[(apr_s, hc), (A2, "gf")], writes=[(pb[0], "lm2")],
                        tag=f"l2{tb}.{t4}.{hc}")
                vop(lambda h, t_=t4, b_=tb: h.tensor_scalar_mul(
                    lra_v[:, b_ * TB + t_ * TSB: b_ * TB + (t_ + 1) * TSB], pb[0][0:16, 0:TSB], 2.0),
                    reads=[(pb[0], "lm2")], writes=[(lra_s, "b", tb)], tag=f"l2e{tb}.{t4}")
            for dgp in range(4):
                p.dma("gpsimd", lambda h, b_=tb, g_=dgp: h.dma_start(
                    out=d3(scr, 0, 2, TB, T),
                    in_=g3(s0_d, T, g_ * 256, 256, b_ * TB, TB)),
                    reads=[("d", "s0")], writes=[(scr, "s0b")], key=(scr, "s0b"), tag=f"s0r{tb}.{dgp}")
                for do in range(2):
                    dt_i = dgp * 2 + do
                    prb = pb[6 if dt_i % 2 == 0 else 2]
                    for hf in range(2):
                        so = ((dt_i * 2 + hf) % 2) * 2048
                        ldg(wres, so, 4096, rpr_d, D, hf * 2048, 2048, dt_i * 128, 128,
                            ("ws", so // 2048), tag=f"lrp{tb}.{dt_i}.{hf}", rd=[("d", "qpr")])
                        for t4 in range(NTB):
                            for hc in range(16):
                                top(lambda h, hc_=hc, t_=t4, h2=hf, s_=so, k_=prb: h.matmul(
                                    k_[:, 0:TSB],
                                    wres[:, s_ + hc_ * 128: s_ + (hc_ + 1) * 128],
                                    A2[:, (h2 * 16 + hc_) * TB + t_ * TSB: (h2 * 16 + hc_) * TB + (t_ + 1) * TSB],
                                    start=(h2 == 0 and hc_ == 0), stop=(h2 == 1 and hc_ == 15)),
                                    reads=[("ws", so // 2048), (A2, "gf")], writes=[(prb, "pr")],
                                    tag=f"prm{tb}.{dt_i}.{hf}.{t4}.{hc}")
                    for t4 in range(NTB):
                        top(lambda h, d_=dt_i, t_=t4, b_=tb: h.matmul(
                            pb[4][:, 0:TSB], bprs_v[:, d_ * 128:(d_ + 1) * 128],
                            lra_v[:, b_ * TB + t_ * TSB: b_ * TB + (t_ + 1) * TSB],
                            start=True, stop=True),
                            reads=[bpr_s, (lra_s, "b", tb)], writes=[(pb[4], "pl")],
                            tag=f"prl{tb}.{dt_i}.{t4}")
                        aop(lambda h, t_=t4, k_=prb: h.activation(
                            scr3[:, 0:TSB], k_[:, 0:TSB], AF.Identity, scale=qsc[:, 4:5]),
                            reads=[(prb, "pr"), (qsc, 4, "s")], writes=[scr3], tag=f"prs{tb}.{dt_i}.{t4}")
                        vop(lambda h, t_=t4: h.tensor_add(scr3[:, 0:TSB], scr3[:, 0:TSB], pb[4][:, 0:TSB]),
                            reads=[scr3, (pb[4], "pl")], writes=[scr3], tag=f"pra{tb}.{dt_i}.{t4}")
                        aop(lambda h, d_=dt_i, t_=t4: h.activation(
                            rqv[:, 0:TSB],
                            scr3[:, 0:TSB], AF.Silu, bias=bcols[:, 104 + d_:105 + d_]),
                            reads=[scr3, (bcols, "bpr")], writes=[(rqv, "rq")], tag=f"prg{tb}.{dt_i}.{t4}")
                        vop(lambda h, d_=dt_i, t_=t4: h.tensor_mul(
                            scr3[:, 0:TSB], rqv[:, 0:TSB],
                            scr[:, (d_ % 2) * TB + t_ * TSB: (d_ % 2) * TB + (t_ + 1) * TSB]),
                            reads=[(rqv, "rq"), (scr, "s0b")], writes=[scr3], tag=f"g0m{tb}.{dt_i}.{t4}")
                        vop(lambda h, d_=dt_i, t_=t4, b_=tb: h.tensor_add(
                            A3[:, d_ * T + b_ * TB + t_ * TSB: d_ * T + b_ * TB + (t_ + 1) * TSB],
                            A3[:, d_ * T + b_ * TB + t_ * TSB: d_ * T + b_ * TB + (t_ + 1) * TSB],
                            scr3[:, 0:TSB]),
                            reads=[scr3, (A3, "x", dt_i)], writes=[(A3, "x", dt_i)],
                            tag=f"g0a{tb}.{dt_i}.{t4}")

        stats_of(A3, [(A3, "x", f) for f in range(8)], 0, 1, pb[0], "s1")
        rsqrt_row(1, 2, "rsL1")
        p.fence([], ["rqv", (bca, 0), (bca, 1)], tag="fFt")
        bcast(2, 0, "b_rsl1")
        bcast(0, 1, "b_ml1")
        negmul_tile("mrl1t")
        p.fence([(A1, "xnc")], [(A1, "x1c")], tag="fA1f")
        for f in range(8):
            for c2 in range(0, T, SW3):
                W3 = min(SW3, T - c2)
                vop(lambda h, f_=f, c_=c2, W_=W3: h.tensor_mul(
                    scr3[:, 0:W_], A3[:, f_ * T + c_:f_ * T + c_ + W_], bca[:, c_:c_ + W_]),
                    reads=[(A3, "x", f), (bca, 0)], writes=[scr3], tag=f"f1a{f}.{c2}")
                vop(lambda h, f_=f, c_=c2, W_=W3: h.tensor_add(
                    A1[:, f_ * T + c_:f_ * T + c_ + W_], scr3[:, 0:W_], bca[:, T + c_:T + c_ + W_]),
                    reads=[scr3, (bca, 1)], writes=[(A1, "x1c")], tag=f"f1b{f}.{c2}")
        p.dma("gpsimd", lambda h: h.dma_start(
            out=convw[:, 0:64], in_=bass.AP(Wcv.tensor, 0, [[4, 128], [512, 16], [1, 4]]),
            allow_slow_non_contiguous=True),
            writes=[convw], key=convw, tag="cw")
        p.dma("gpsimd", lambda h, a=bcols[:, 112:128], b=bass.AP(bconv.tensor, 0, [[1, 128], [128, 16]]): h.dma_start(out=a, in_=b, allow_slow_non_contiguous=True), writes=[(bcols, "bcv")], key=(bcols, "bcv"), tag="bcv")
        p.dma("gpsimd", lambda h, a=bcols[:, 128:144], b=bass.AP(bdtv.tensor, 0, [[1, 128], [128, 16]]): h.dma_start(out=a, in_=b, allow_slow_non_contiguous=True), writes=[(bcols, "bdt")], key=(bcols, "bdt"), tag="bdt")
        p.dma("gpsimd", lambda h, a=bcols[:, 144:160], b=bass.AP(A_log.tensor, 0, [[1, 128], [128, 16]]): h.dma_start(out=a, in_=b, allow_slow_non_contiguous=True), writes=[(bcols, "alog")], key=(bcols, "alog"), tag="alog")
        aop(lambda h: h.activation(bcols[:, 160:176], bcols[:, 144:160], AF.Exp),
            reads=[(bcols, "alog")], writes=[(bcols, "Ae")], tag="Aexp")
        vop(lambda h: h.tensor_scalar_mul(bcols[:, 160:176], bcols[:, 160:176], -1.0),
            reads=[(bcols, "Ae")], writes=[(bcols, "Ae")], tag="Aneg")
        p.fence([], [(bca, 0), (bca, 1), "egv"], tag="feg")
        for og2 in range(16):
            so = (og2 % 2) * 2048
            ldg(wres, so, 4096, wi16_d, 2 * DI, 0, D, og2 * 256, 256,
                ("ws", so // 2048), tag=f"we{og2}", rd=[("d", "wic")])
            for f in range(8):
                top(lambda h, f_=f, s_=so: h.matmul(
                    pb[1][0:4, 0:256], A1[:, f_ * T + T - 4: f_ * T + T],
                    wres[:, s_ + f_ * 256: s_ + (f_ + 1) * 256],
                    start=(f_ == 0), stop=(f_ == 7)),
                    reads=[("ws", so // 2048), (A1, "x1c")], writes=[(pb[1], "eg")], tag=f"eg{og2}.{f}")
            vop(lambda h, o_=og2: h.tensor_copy(egv[0:4, o_ * 256:(o_ + 1) * 256], pb[1][0:4, 0:256]),
                reads=[(pb[1], "eg")], writes=[(egv, "eg")], tag=f"ege{og2}")
        p.dma("gpsimd", lambda h: h.dma_start(out=cc_e_in.ap(), in_=egv[0:4, 0:4096]),
              reads=[(egv, "eg")], writes=[("d", "cce")], tag="estg")
        p.cc(lambda h: h.collective_compute(
            "AllGather", ALU.bypass, replica_groups=GROUPS,
            ins=[cc_e_in.ap().opt()], outs=[cc_e_out.ap().opt()]),
            reads=[("d", "cce")], writes=[("d", "cceo")], tag="eag")
        for j in range(4):
            p.dma("gpsimd", lambda h, j_=j: h.dma_start(
                out=bass.AP(halos, j_, [[64, 128], [4, 16]]),
                in_=bass.AP(cc_e_out.ap().tensor, j_ * 2 * DI, [[1, 128], [128, 16]]),
                allow_slow_non_contiguous=True),
                reads=[("d", "cceo")], writes=[halos], key=halos, tag=f"ercv{j}")
        vop(lambda h: h.tensor_scalar_mul(halos[:, :], halos[:, :], rolec[:, 0:1]),
            reads=[halos, rolec], writes=[halos], tag="emsk")
        for oc in range(16):
            vop(lambda h, o_=oc: h.tensor_scalar_mul(
                xce[:, o_:o_ + 1], halos[:, o_ * 4:o_ * 4 + 1], convw[:, o_ * 4:o_ * 4 + 1]),
                reads=[halos, convw], writes=[xce], tag=f"xce{oc}.0")
            for j in range(1, 4):
                vop(lambda h, o_=oc, j_=j: h.scalar_tensor_tensor(
                    xce[:, o_:o_ + 1], halos[:, o_ * 4 + j_:o_ * 4 + j_ + 1],
                    convw[:, o_ * 4 + j_:o_ * 4 + j_ + 1], xce[:, o_:o_ + 1],
                    ALU.mult, ALU.add),
                    reads=[halos, convw, xce], writes=[xce], tag=f"xce{oc}.{j}")
            aop(lambda h, o_=oc: h.activation(xce[:, o_:o_ + 1], xce[:, o_:o_ + 1], AF.Silu,
                                              bias=bcols[:, 112 + o_:113 + o_]),
                reads=[xce, (bcols, "bcv")], writes=[xce], tag=f"xcs{oc}")
        vop(lambda h: h.memset(accol[:, :], 1.0), writes=[accol], tag="ac1")
        vop(lambda h: h.memset(hcol[:, :], 0.0), writes=[hcol], tag="hc0")
        vop(lambda h: h.memset(ucol[:, :], 0.0), writes=[ucol], tag="uc0")
        p.fence([], ["egv", "szv"], tag="fsz")
        x1o = 0
        dto = 4 * T
        p.fence([(A4, "u")], [(A4, "x1"), (A4, "dtc")], tag="fA4f")
        for blk in range(4):
            if blk > 0:
                for oc in range(16):
                    vop(lambda h, o_=oc: h.tensor_copy(
                        halos[:, o_ * 4 + 1:o_ * 4 + 4],
                        A4[:, x1o + o_ * TB + TB - 3: x1o + o_ * TB + TB]),
                        reads=[(A4, "x1")], writes=[halos], tag=f"hup{blk}.{oc}")
            for og2 in range(8):
                so = (og2 % 2) * 2048
                ldg(wres, so, 4096, wi16_d, 2 * DI, 0, D, og2 * 256, 256,
                    ("ws", so // 2048), tag=f"wx{blk}.{og2}", rd=[("d", "wic")])
                for oi in range(2):
                    oc = og2 * 2 + oi
                    xb_ = pb[2 if oc % 2 == 0 else 0]
                    for f in range(8):
                        top(lambda h, f_=f, o_=oi, b_=blk, s_=so, k_=xb_: h.matmul(
                            k_[:, 0:TB],
                            wres[:, s_ + f_ * 256 + o_ * 128: s_ + f_ * 256 + (o_ + 1) * 128],
                            A1[:, f_ * T + b_ * TB: f_ * T + (b_ + 1) * TB],
                            start=(f_ == 0), stop=(f_ == 7)),
                            reads=[("ws", so // 2048), (A1, "x1c")], writes=[(xb_, "x1")],
                            tag=f"x1m{blk}.{oc}.{f}")
                    vop(lambda h, o_=oc, k_=xb_: h.tensor_copy(
                        A4[:, x1o + o_ * TB: x1o + (o_ + 1) * TB], k_[:, 0:TB]),
                        reads=[(xb_, "x1")], writes=[(A4, "x1")], tag=f"x1e{blk}.{oc}")
            if blk == 0:
                p.fence([(ARC, "s0log")], [(ARC, "xc")], tag="fARC")
            for oc in range(16):
                for j in range(4):
                    nh = 3 - j
                    if nh > 0:
                        if j == 0:
                            vop(lambda h, o_=oc, j_=j, n_=nh: h.tensor_scalar_mul(
                                scr3[:, 0:n_], halos[:, o_ * 4 + 1 + j_:o_ * 4 + 1 + j_ + n_],
                                convw[:, o_ * 4 + j_:o_ * 4 + j_ + 1]),
                                reads=[halos, convw], writes=[scr3], tag=f"cve{blk}.{oc}.{j}")
                        else:
                            vop(lambda h, o_=oc, j_=j, n_=nh: h.scalar_tensor_tensor(
                                scr3[:, 0:n_], halos[:, o_ * 4 + 1 + j_:o_ * 4 + 1 + j_ + n_],
                                convw[:, o_ * 4 + j_:o_ * 4 + j_ + 1], scr3[:, 0:n_],
                                ALU.mult, ALU.add),
                                reads=[halos, convw, scr3], writes=[scr3], tag=f"cve{blk}.{oc}.{j}")
                    nb = TB - nh
                    if j == 0:
                        vop(lambda h, o_=oc, j_=j, n_=nh, m_=nb: h.tensor_scalar_mul(
                            scr3[:, n_:n_ + m_],
                            A4[:, x1o + o_ * TB: x1o + o_ * TB + m_],
                            convw[:, o_ * 4 + j_:o_ * 4 + j_ + 1]),
                            reads=[(A4, "x1"), convw], writes=[scr3], tag=f"cvb{blk}.{oc}.{j}")
                    else:
                        vop(lambda h, o_=oc, j_=j, n_=nh, m_=nb: h.scalar_tensor_tensor(
                            scr3[:, n_:n_ + m_],
                            A4[:, x1o + o_ * TB: x1o + o_ * TB + m_],
                            convw[:, o_ * 4 + j_:o_ * 4 + j_ + 1], scr3[:, n_:n_ + m_],
                            ALU.mult, ALU.add),
                            reads=[(A4, "x1"), convw, scr3], writes=[scr3],
                            tag=f"cvb{blk}.{oc}.{j}")
                aop(lambda h, o_=oc: h.activation(ARC[:, o_ * TB:(o_ + 1) * TB], scr3[:, 0:TB],
                                                  AF.Silu, bias=bcols[:, 112 + o_:113 + o_]),
                    reads=[scr3, (bcols, "bcv")], writes=[(ARC, "xc")], tag=f"cvs{blk}.{oc}")
            for og in range(4):
                for (mi, mdram, mk) in ((0, wdt16_d, "dt"), (1, wb16_d, "B"), (2, wc16_d, "C"),
                                        (3, wi16_d, "z")):
                    if mk == "z":
                        for oi2 in range(2):
                            so = (oi2 % 2) * 2048
                            ldg(wres, so, 4096, wi16_d, 2 * DI, 0, D,
                                DI + og * 512 + oi2 * 256, 256,
                                ("ws", so // 2048), tag=f"wz{blk}.{og}.{oi2}",
                                rd=[("d", "wic")])
                            for oi in range(2):
                                oc = og * 4 + oi2 * 2 + oi
                                zb_ = pb[3 if oc % 2 == 0 else 0]
                                for f in range(8):
                                    top(lambda h, f_=f, o_=oi, b_=blk, s_=so, k_=zb_: h.matmul(
                                        k_[:, 0:TB],
                                        wres[:, s_ + f_ * 256 + o_ * 128: s_ + f_ * 256 + (o_ + 1) * 128],
                                        A1[:, f_ * T + b_ * TB: f_ * T + (b_ + 1) * TB],
                                        start=(f_ == 0), stop=(f_ == 7)),
                                        reads=[("ws", so // 2048), (A1, "x1c")], writes=[(zb_, "z")],
                                        tag=f"zm{blk}.{oc}.{f}")
                                aop(lambda h, o_=oc % 4, k_=zb_: h.activation(
                                    szv[:, o_ * TB:(o_ + 1) * TB], k_[:, 0:TB], AF.Silu),
                                    reads=[(zb_, "z")], writes=[(szv, "sz")], tag=f"ze{blk}.{oc}")
                    else:
                        for oi in range(4):
                            oc = og * 4 + oi
                            so = (oc % 2) * 2048
                            if mk == "B":
                                mb_ = pb[5 if oc % 2 == 0 else 1]
                            else:
                                mb_ = pb[4 if oc % 2 == 0 else 6]
                            ldg(wres, so, 4096, mdram, DI, 0, DI, oc * 128, 128,
                                ("ws", so // 2048), tag=f"wm{mk}{blk}.{oc}",
                                rd=[("d", {"dt": "dtc", "B": "wbc", "C": "wcc"}[mk])])
                            for kk in range(16):
                                top(lambda h, k_=kk, s_=so, b_=mb_: h.matmul(
                                    b_[:, 0:TB],
                                    wres[:, s_ + k_ * 128: s_ + (k_ + 1) * 128],
                                    ARC[:, k_ * TB:(k_ + 1) * TB],
                                    start=(k_ == 0), stop=(k_ == 15)),
                                    reads=[("ws", so // 2048), (ARC, "xc")],
                                    writes=[(mb_, mk)], tag=f"{mk}m{blk}.{oc}.{kk}")
                            if mk == "dt":
                                aop(lambda h, o_=oc, b_=mb_: h.activation(
                                    scr3[:, 0:TB], b_[:, 0:TB], AF.Abs,
                                    bias=bcols[:, 128 + o_:129 + o_]),
                                    reads=[(mb_, "dt"), (bcols, "bdt")], writes=[scr3],
                                    tag=f"dta{blk}.{oc}")
                                aop(lambda h: h.activation(scr3[:, 0:TB], scr3[:, 0:TB], AF.Exp,
                                                           scale=-1.0),
                                    reads=[scr3], writes=[scr3], tag=f"dtb{blk}.{oc}")
                                aop(lambda h: h.activation(scr3[:, 0:TB], scr3[:, 0:TB], AF.Ln,
                                                           bias=1.0),
                                    reads=[scr3], writes=[scr3], tag=f"dtc{blk}.{oc}")
                                aop(lambda h, o_=oc, b_=mb_: h.activation(
                                    fscr[:, 0:TB], b_[:, 0:TB], AF.Relu,
                                    bias=bcols[:, 128 + o_:129 + o_]),
                                    reads=[(mb_, "dt"), (bcols, "bdt")], writes=[(fscr, "r")],
                                    tag=f"dtd{blk}.{oc}")
                                vop(lambda h, o_=oc: h.tensor_add(
                                    A4[:, dto + o_ * TB: dto + (o_ + 1) * TB], scr3[:, 0:TB],
                                    fscr[:, 0:TB]),
                                    reads=[scr3, (fscr, "r")], writes=[(A4, "dtc")], tag=f"dte{blk}.{oc}")
                            elif mk == "B":
                                if blk == 0:
                                    for kk in range(16):
                                        top(lambda h, k_=kk, oc2=oc, s_=so: h.matmul(
                                            pb[7][:, oc2:oc2 + 1],
                                            wres[:, s_ + k_ * 128: s_ + (k_ + 1) * 128],
                                            xce[:, k_:k_ + 1],
                                            start=(k_ == 0), stop=(k_ == 15)),
                                            reads=[("ws", so // 2048), xce], writes=[(pb[7], "bv")],
                                            tag=f"bv{oc}.{kk}")
                                aop(lambda h, b_=mb_: h.activation(
                                    scr3[:, 0:TB], b_[:, 0:TB], AF.Sigmoid),
                                    reads=[(mb_, "B")], writes=[scr3], tag=f"Bs{blk}.{oc}")
                                vop(lambda h, o_=oc, oo=oi: h.tensor_mul(
                                    scr[:, oo * TB:(oo + 1) * TB], scr3[:, 0:TB],
                                    ARC[:, o_ * TB:(o_ + 1) * TB]),
                                    reads=[scr3, (ARC, "xc")], writes=[(scr, "u")], tag=f"Bu{blk}.{oc}")
                            else:
                                aop(lambda h, b_=mb_: h.activation(
                                    scr3[:, 0:TB],
                                    b_[:, 0:TB], AF.Tanh),
                                    reads=[(mb_, "C")], writes=[scr3], tag=f"Ct{blk}.{oc}")
                                vop(lambda h, o_=oi: h.tensor_copy(
                                    szv[:, 4 * TB + o_ * TB: 4 * TB + (o_ + 1) * TB], scr3[:, 0:TB]),
                                    reads=[scr3], writes=[(szv, "ct")], tag=f"Cte{blk}.{oc}")
                if blk == 0:
                    for oi in range(4):
                        oc = og * 4 + oi
                        aop(lambda h, o_=oc: h.activation(
                            uprevc[:, o_:o_ + 1], pb[7][:, o_:o_ + 1], AF.Sigmoid),
                            reads=[(pb[7], "bv")], writes=[uprevc], tag=f"ups{oc}")
                        vop(lambda h, o_=oc: h.tensor_mul(uprevc[:, o_:o_ + 1], uprevc[:, o_:o_ + 1],
                                                          xce[:, o_:o_ + 1]),
                            reads=[uprevc, xce], writes=[uprevc], tag=f"upm{oc}")
                        vop(lambda h, o_=oc: h.tensor_mul(uprevc[:, o_:o_ + 1], uprevc[:, o_:o_ + 1],
                                                          rolec[:, 0:1]),
                            reads=[uprevc, rolec], writes=[uprevc], tag=f"upr{oc}")
                for oi in range(4):
                    oc = og * 4 + oi
                    vop(lambda h, o_=oc: h.tensor_scalar(
                        fscr[:, TB:2 * TB], A4[:, dto + o_ * TB: dto + (o_ + 1) * TB],
                        bcols[:, 160 + o_:161 + o_], -10.0, ALU.mult, ALU.max),
                        reads=[(A4, "dtc"), (bcols, "Ae")], writes=[(fscr, "la")], tag=f"la{blk}.{oc}")
                    vop(lambda h: h.tensor_scalar_min(fscr[:, TB:2 * TB], fscr[:, TB:2 * TB], -1e-4),
                        reads=[(fscr, "la")], writes=[(fscr, "la")], tag=f"lb{blk}.{oc}")
                    aop(lambda h: h.activation(fscr[:, TB:2 * TB], fscr[:, TB:2 * TB], AF.Exp),
                        reads=[(fscr, "la")], writes=[(fscr, "la")], tag=f"lc{blk}.{oc}")
                    vop(lambda h, o_=oc, oo=oi, b_=blk: h.tensor_copy(
                        fscr[:, 2 * TB + 1: 3 * TB], scr[:, oo * TB: (oo + 1) * TB - 1]),
                        reads=[(scr, "u")], writes=[(fscr, "be")], tag=f"us{blk}.{oc}")
                    vop(lambda h, o_=oc, b_=blk: h.tensor_copy(
                        fscr[:, 2 * TB: 2 * TB + 1],
                        (uprevc[:, o_:o_ + 1] if b_ == 0 else ucol[:, o_:o_ + 1])),
                        reads=[(uprevc if blk == 0 else ucol)], writes=[(fscr, "be")], tag=f"u0{blk}.{oc}")
                    vop(lambda h, oo=oi, o_=oc: h.tensor_copy(
                        ucol[:, o_:o_ + 1], scr[:, (oo + 1) * TB - 1:(oo + 1) * TB]),
                        reads=[(scr, "u")], writes=[ucol], tag=f"ue{blk}.{oc}")
                    vop(lambda h, oo=oi: h.tensor_add(
                        fscr[:, 2 * TB:3 * TB], fscr[:, 2 * TB:3 * TB], scr[:, oo * TB:(oo + 1) * TB]),
                        reads=[(fscr, "be"), (scr, "u")], writes=[(fscr, "be")], tag=f"ua{blk}.{oc}")
                    vop(lambda h, o_=oc: h.scalar_tensor_tensor(
                        fscr[:, 2 * TB:3 * TB], fscr[:, 2 * TB:3 * TB], 0.5,
                        A4[:, dto + o_ * TB: dto + (o_ + 1) * TB], ALU.mult, ALU.mult),
                        reads=[(fscr, "be"), (A4, "dtc")], writes=[(fscr, "be")], tag=f"ub{blk}.{oc}")
                    vop(lambda h, o_=oc: h.tensor_tensor_scan(
                        fscr[:, 0:TB], fscr[:, TB:2 * TB], fscr[:, 2 * TB:3 * TB],
                        hcol[:, o_:o_ + 1], ALU.mult, ALU.add),
                        reads=[(fscr, "la"), (fscr, "be"), hcol], writes=[(fscr, "h0")], tag=f"sc{blk}.{oc}")
                    vop(lambda h, o_=oc: h.tensor_copy(hcol[:, o_:o_ + 1], fscr[:, TB - 1:TB]),
                        reads=[(fscr, "h0")], writes=[hcol], tag=f"sce{blk}.{oc}")
                    vop(lambda h, o_=oc: h.tensor_tensor_scan(
                        fscr[:, 3 * TB:4 * TB], fscr[:, TB:2 * TB], zbuf[:, 0:TB],
                        accol[:, o_:o_ + 1], ALU.mult, ALU.add),
                        reads=[(fscr, "la"), zbuf, accol], writes=[(fscr, "ac")], tag=f"sa{blk}.{oc}")
                    vop(lambda h, o_=oc: h.tensor_copy(accol[:, o_:o_ + 1], fscr[:, 4 * TB - 1:4 * TB]),
                        reads=[(fscr, "ac")], writes=[accol], tag=f"sae{blk}.{oc}")
                    vop(lambda h, oo=oi: h.tensor_mul(
                        scr3[:, 0:TB], szv[:, 4 * TB + oo * TB: 4 * TB + (oo + 1) * TB],
                        szv[:, oo * TB:(oo + 1) * TB]),
                        reads=[(szv, "ct"), (szv, "sz")], writes=[scr3], tag=f"yw{blk}.{oc}")
                    vop(lambda h, o_=oc: h.tensor_mul(
                        A2[:, o_ * TB:(o_ + 1) * TB], fscr[:, 0:TB], scr3[:, 0:TB]),
                        reads=[(fscr, "h0"), scr3], writes=[(A2, "y0")], tag=f"y0{blk}.{oc}")
                    vop(lambda h, o_=oc: h.tensor_mul(
                        A2[:, 4 * T + o_ * TB: 4 * T + (o_ + 1) * TB],
                        fscr[:, 3 * TB:4 * TB], scr3[:, 0:TB]),
                        reads=[(fscr, "ac"), scr3], writes=[(A2, "gc")], tag=f"gc{blk}.{oc}")
            p.dma("sync", lambda h, b_=blk: h.dma_start(
                out=y0_d.ap()[:, b_ * 16 * TB:(b_ + 1) * 16 * TB], in_=A2[:, 0:16 * TB]),
                reads=[(A2, "y0")], writes=[("d", "y0")], key=("d", "y0"), tag=f"y0d{blk}")
            p.dma("sync", lambda h, b_=blk: h.dma_start(
                out=gc_d.ap()[:, b_ * 16 * TB:(b_ + 1) * 16 * TB], in_=A2[:, 4 * T:4 * T + 16 * TB]),
                reads=[(A2, "gc")], writes=[("d", "gc")], key=("d", "gc"), tag=f"gcd{blk}")
        p.dma("sync", lambda h: h.dma_start(out=cc_h_in.ap(), in_=hcol[:, :]),
              reads=[hcol], writes=[("d", "cch")], tag="hstg")
        p.cc(lambda h: h.collective_compute(
            "AllGather", ALU.bypass, replica_groups=GROUPS,
            ins=[cc_h_in.ap().opt()], outs=[cc_h_out.ap().opt()]),
            reads=[("d", "cch")], writes=[("d", "ccho")], tag="hag")
        p.dma("sync", lambda h: h.dma_start(out=hini[:, :], in_=cc_h_out.ap()[0:128, :]),
              reads=[("d", "ccho")], writes=[hini], tag="hrcv")
        vop(lambda h: h.tensor_scalar_mul(hini[:, :], hini[:, :], rolec[:, 0:1]),
            reads=[hini, rolec], writes=[hini], tag="hmsk")
        for blk in range(4):
            p.dma("gpsimd", lambda h, b_=blk: h.dma_start(
                out=A2[:, 0:16 * TB], in_=y0_d.ap()[:, b_ * 16 * TB:(b_ + 1) * 16 * TB]),
                reads=[("d", "y0")], writes=[(A2, "y0")], key=(A2, "y0"), tag=f"y0r{blk}")
            p.dma("gpsimd", lambda h, b_=blk: h.dma_start(
                out=A2[:, 4 * T:4 * T + 16 * TB], in_=gc_d.ap()[:, b_ * 16 * TB:(b_ + 1) * 16 * TB]),
                reads=[("d", "gc")], writes=[(A2, "gc")], key=(A2, "gc"), tag=f"gcr{blk}")
            for og in range(4):
                for hf in range(2):
                    so = ((og * 2 + hf) % 2) * 2048
                    ldg(wres, so, 4096, wo116_d, D, og * 512, 512, hf * 512, 512,
                        ("ws", so // 2048), tag=f"wog{blk}.{og}.{hf}",
                        rd=[("d", "woc")])
                    for oi in range(4):
                        oc = og * 4 + oi
                        vop(lambda h, o_=oc: h.scalar_tensor_tensor(
                            scr3[:, 0:TB], A2[:, 4 * T + o_ * TB: 4 * T + (o_ + 1) * TB],
                            hini[:, o_:o_ + 1], A2[:, o_ * TB:(o_ + 1) * TB], ALU.mult, ALU.add),
                            reads=[(A2, "y0"), (A2, "gc"), hini], writes=[scr3], tag=f"y{blk}.{oc}.{hf}")
                        vop(lambda h: h.tensor_copy(scr[:, 0:TB], scr3[:, 0:TB]),
                            reads=[scr3], writes=[(scr, "y16")], tag=f"y16{blk}.{oc}.{hf}")
                        for d4 in range(4):
                            dt_i = hf * 4 + d4
                            top(lambda h, o_=oi, d_=d4, b2=dt_i, oc_=oc, s_=so: h.matmul(
                                pb[b2][:, 0:TB], wres[:, s_ + o_ * 512 + d_ * 128: s_ + o_ * 512 + (d_ + 1) * 128],
                                scr[:, 0:TB],
                                start=(oc_ == 0), stop=(oc_ == 15)),
                                reads=[("ws", so // 2048), (scr, "y16")], writes=[(pb[dt_i], "s1")],
                                tag=f"s1m{blk}.{oc}.{dt_i}")
            for dt_i in range(8):
                vop(lambda h, d_=dt_i: h.tensor_copy(scr3[:, 0:TB], pb[d_][:, 0:TB]),
                    reads=[(pb[dt_i], "s1")], writes=[scr3], tag=f"s1e{blk}.{dt_i}")
                p.dma("sync", lambda h, d_=dt_i, b_=blk: h.dma_start(
                    out=s1_d.ap()[d_ * 128:(d_ + 1) * 128, b_ * TB:(b_ + 1) * TB], in_=scr3[:, 0:TB]),
                    reads=[scr3], writes=[("d", "s1")], key=("d", "s1"),
                    tag=f"s1d{blk}.{d_ if False else dt_i}")

        p.fence([], [(A2, "y0"), (A2, "gc"), (mkv, "mk"),
                     (A4, "x1"), (A4, "dtc"), "hqb", "w8s", "xq8"], tag="fG")
        p.dma("gpsimd", lambda h, a=bcols[:, 32:64], b=bass.AP(b1f.tensor, 0, [[1, 128], [128, 32]]): h.dma_start(out=a, in_=b, allow_slow_non_contiguous=True), writes=[(bcols, "b1")], key=(bcols, "b1"), tag="b1f")
        p.dma("gpsimd", lambda h, a=bcols[:, 64:72], b=bass.AP(b2f.tensor, 0, [[1, 128], [128, 8]]): h.dma_start(out=a, in_=b, allow_slow_non_contiguous=True), writes=[(bcols, "b2")], key=(bcols, "b2"), tag="b2f")
        outw = 0
        for tb in range(4):
            for f in range(8):
                vop(lambda h, f_=f, b_=tb: h.tensor_copy(
                    xq8[:, f_ * TB:(f_ + 1) * TB],
                    A1[:, f_ * T + b_ * TB: f_ * T + (b_ + 1) * TB]),
                    reads=[(A1, "x1c")], writes=[(xq8, tb)], tag=f"xq{tb}.{f}")
            for hgrp in range(4):
                ldg(w8s, 0, 8192, q1_d, FF, 0, D, hgrp * 1024, 1024, (w8s, "g1"), tag=f"g1{tb}.{hgrp}", rd=[("d", "q1")])
                for ho in range(8):
                    ht = hgrp * 8 + ho
                    f1b = pb[5 if ht % 2 == 0 else 7]
                    for t4 in range(NTB):
                        for f in range(8):
                            top(lambda h, f_=f, o_=ho, t_=t4, k_=f1b: h.matmul(
                                k_[:, 0:TSB],
                                w8s[:, f_ * 1024 + o_ * 128: f_ * 1024 + (o_ + 1) * 128],
                                xq8[:, f_ * TB + t_ * TSB: f_ * TB + (t_ + 1) * TSB],
                                start=(f_ == 0), stop=(f_ == 7)),
                                reads=[(w8s, "g1"), (xq8, tb)], writes=[(f1b, "f1")],
                                tag=f"f1m{tb}.{ht}.{t4}.{f}")
                        aop(lambda h, h_=ht, t_=t4, k_=f1b: h.activation(
                            scr3[:, 0:TSB], k_[:, 0:TSB], AF.Identity,
                            bias=bcols[:, 32 + h_:33 + h_]),
                            reads=[(f1b, "f1"), (bcols, "b1")], writes=[scr3], tag=f"f1q{tb}.{ht}.{t4}")
                        aop(lambda h, h_=ht, t_=t4: h.activation(
                            hqb[:, h_ * TB + t_ * TSB: h_ * TB + (t_ + 1) * TSB], scr3[:, 0:TSB],
                            AF.Relu),
                            reads=[scr3], writes=[(hqb, tb)], tag=f"f1r{tb}.{ht}.{t4}")
            p.dma("gpsimd", lambda h, b_=tb: h.dma_start(
                out=d3(mkv, 0, 8, TB, 8 * TB),
                in_=g3(mask_in.tensor, T, 0, D, b_ * TB, TB)),
                reads=[], writes=[(mkv, "mk")], key=(mkv, "mk"), tag=f"mk{tb}")
            for dgp in range(4):
                p.dma("gpsimd", lambda h, b_=tb, g_=dgp: h.dma_start(
                    out=d3(scr, 0, 2, TB, T),
                    in_=g3(s1_d, T, g_ * 256, 256, b_ * TB, TB)),
                    reads=[("d", "s1")], writes=[(scr, "s1b")], key=(scr, "s1b"), tag=f"s1r{tb}.{dgp}")
                ldg(w8s, 0, 8192, q2_d, D, 0, FF, dgp * 256, 256, (w8s, "g2"), tag=f"g2{tb}.{dgp}", rd=[("d", "q2")])
                for do in range(2):
                    dt_i = dgp * 2 + do
                    f2b = pb[6 if dt_i % 2 == 0 else 2]
                    for t4 in range(NTB):
                        for hc in range(32):
                            top(lambda h, hc_=hc, t_=t4, d_=do, k_=f2b: h.matmul(
                                k_[:, 0:TSB],
                                w8s[:, hc_ * 256 + d_ * 128: hc_ * 256 + (d_ + 1) * 128],
                                hqb[:, hc_ * TB + t_ * TSB: hc_ * TB + (t_ + 1) * TSB],
                                start=(hc_ == 0), stop=(hc_ == 31)),
                                reads=[(w8s, "g2"), (hqb, tb)], writes=[(f2b, "f2")],
                                tag=f"f2m{tb}.{dt_i}.{t4}.{hc}")
                        aop(lambda h, d_=dt_i, t_=t4, k_=f2b: h.activation(
                            scr3[:, 0:TSB], k_[:, 0:TSB], AF.Identity,
                            bias=bcols[:, 64 + d_:65 + d_]),
                            reads=[(f2b, "f2"), (bcols, "b2")], writes=[scr3], tag=f"f2q{tb}.{dt_i}.{t4}")
                        vop(lambda h, d_=dt_i, t_=t4: h.tensor_mul(
                            scr3[:, 0:TSB], scr3[:, 0:TSB],
                            mkv[:, d_ * TB + t_ * TSB: d_ * TB + (t_ + 1) * TSB]),
                            reads=[scr3, (mkv, "mk")], writes=[scr3], tag=f"f2d{tb}.{dt_i}.{t4}")
                        aop(lambda h: h.activation(scr3[:, 0:TSB], scr3[:, 0:TSB], AF.Silu),
                            reads=[scr3], writes=[scr3], tag=f"f2s{tb}.{dt_i}.{t4}")
                        vop(lambda h, d_=dt_i, t_=t4: h.tensor_mul(
                            scr3[:, 0:TSB], scr3[:, 0:TSB],
                            scr[:, (d_ % 2) * TB + t_ * TSB: (d_ % 2) * TB + (t_ + 1) * TSB]),
                            reads=[scr3, (scr, "s1b")], writes=[scr3], tag=f"f2g{tb}.{dt_i}.{t4}")
                        vop(lambda h, d_=dt_i, t_=t4, b_=tb: h.tensor_add(
                            A3[:, d_ * T + b_ * TB + t_ * TSB: d_ * T + b_ * TB + (t_ + 1) * TSB],
                            A3[:, d_ * T + b_ * TB + t_ * TSB: d_ * T + b_ * TB + (t_ + 1) * TSB],
                            scr3[:, 0:TSB]),
                            reads=[scr3, (A3, "x", dt_i)], writes=[(A3, "x", dt_i), (A3, "xf", dt_i, tb)],
                            tag=f"g1a{tb}.{dt_i}.{t4}")
                    p.dma("gpsimd", lambda h, d_=dt_i, b_=tb: h.dma_start(
                        out=out_t[d_ * 128:(d_ + 1) * 128, b_ * TB:(b_ + 1) * TB],
                        in_=A3[:, d_ * T + b_ * TB: d_ * T + (b_ + 1) * TB]),
                        reads=[(A3, "xf", dt_i, tb)], writes=[("d", "outw", tb * 8 + dt_i)],
                        key=("d", "outw"), tag=f"out{tb}.{dt_i}")
        p.final_wait("gpsimd", [("d", "outw", i) for i in range(32)])
        p.emit(st_)
    return nc


_CACHE = {}


def _prep_inputs(inputs):
    f32 = np.float32
    f16 = np.float16
    tr = lambda a: np.ascontiguousarray(np.asarray(a, dtype=f32).T)
    w = {}
    w["wqT"] = tr(inputs["L0_attn_wq"]); w["wkT"] = tr(inputs["L0_attn_wk"])
    w["wvT"] = tr(inputs["L0_attn_wv"]); w["woT"] = tr(inputs["L0_attn_wo"])
    w["ln1w"] = np.asarray(inputs["L0_attn_ln1_w"], f32).reshape(1, -1)
    w["ln1b"] = np.asarray(inputs["L0_attn_ln1_b"], f32).reshape(1, -1)
    w["ln2w"] = np.asarray(inputs["L0_attn_ln2_w"], f32).reshape(1, -1)
    w["ln2b"] = np.asarray(inputs["L0_attn_ln2_b"], f32).reshape(1, -1)
    w["w1T"] = tr(inputs["L0_attn_w1"]); w["b1v"] = np.asarray(inputs["L0_attn_b1"], f32).reshape(1, -1)
    w["w2T"] = tr(inputs["L0_attn_w2"]); w["b2v"] = np.asarray(inputs["L0_attn_b2"], f32).reshape(1, -1)
    w["WfcT"] = tr(inputs["L0_ffn_W_fc"]); w["bfcv"] = np.asarray(inputs["L0_ffn_b_fc"], f32).reshape(1, -1)
    w["AfcT"] = tr(inputs["L0_ffn_A_fc"]); w["BfcT"] = np.ascontiguousarray(np.asarray(inputs["L0_ffn_B_fc"], f32).T)
    w["WprojT"] = tr(inputs["L0_ffn_W_proj"]); w["bprv"] = np.asarray(inputs["L0_ffn_b_proj"], f32).reshape(1, -1)
    w["AprojT"] = tr(inputs["L0_ffn_A_proj"]); w["BprojT"] = np.ascontiguousarray(np.asarray(inputs["L0_ffn_B_proj"], f32).T)
    w["WiT"] = tr(inputs["L1_attn_Wi"])
    w["Wcv"] = np.ascontiguousarray(np.asarray(inputs["L1_attn_Wconv"], f32).reshape(DI, 4))
    w["bconv"] = np.asarray(inputs["L1_attn_bconv"], f32).reshape(1, -1)
    w["WdtT"] = tr(inputs["L1_attn_Wdt"]); w["bdtv"] = np.asarray(inputs["L1_attn_bdt"], f32).reshape(1, -1)
    w["WBT"] = tr(inputs["L1_attn_WB"]); w["WCT"] = tr(inputs["L1_attn_WC"])
    w["WoT1"] = tr(inputs["L1_attn_Wo"])
    w["A_log"] = np.asarray(inputs["L1_attn_A_log"], f32).reshape(1, -1)
    w["w1fT"] = tr(inputs["L1_ffn_w1"]); w["b1f"] = np.asarray(inputs["L1_ffn_b1"], f32).reshape(1, -1)
    w["w2fT"] = tr(inputs["L1_ffn_w2"]); w["b2f"] = np.asarray(inputs["L1_ffn_b2"], f32).reshape(1, -1)
    w["lr_in"] = np.asarray(inputs["L0_attn_lr_scale"], f32).reshape(1, -1)
    w["trilm"] = np.tril(np.ones((128, 128), f16), -1)
    w["trium"] = np.triu(np.ones((128, 128), f16), 1)
    return w


def _dropout_mask(shape):
    import jax
    try:
        dev = jax.local_devices(backend="cpu")[0]
        with jax.default_device(dev):
            keep = np.asarray(jax.random.bernoulli(jax.random.key(42), 0.8, shape))
    except Exception:
        keep = np.asarray(jax.random.bernoulli(jax.random.key(42), 0.8, shape))
    return (keep.astype(np.float32) * 1.25).astype(np.float16)


def kernel(**inputs):
    x = np.asarray(inputs["L0_attn_x"], np.float32)
    B, L, D_ = x.shape
    T = L // 2
    w = _prep_inputs(inputs)
    mask = _dropout_mask((B, L, D_))
    if T not in _CACHE:
        _CACHE[T] = build_program(T)
    nc = _CACHE[T]
    in_maps = []
    for c in range(8):
        b, hf = c // 2, c % 2
        m = dict(w)
        m["x_in"] = np.ascontiguousarray(x[b, hf * T:(hf + 1) * T, :].T)
        m["mask_in"] = np.ascontiguousarray(mask[b, hf * T:(hf + 1) * T, :].T)
        m["role_in"] = np.full((1, 1), float(hf), np.float32)
        in_maps.append(m)
    res = run_bass_kernel_spmd(nc, in_maps, core_ids=list(range(8)))
    out = np.empty((B, L, D_), np.float32)
    for c in range(8):
        b, hf = c // 2, c % 2
        out[b, hf * T:(hf + 1) * T, :] = res.results[c]["out"].T
    return out

